# version 12
# baseline (speedup 1.0000x reference)
import numpy as np
import ml_dtypes
from contextlib import ExitStack

import concourse.bass as bass
import concourse.mybir as mybir
from concourse.bass_utils import run_bass_kernel_spmd

F32 = mybir.dt.float32
BF16 = mybir.dt.bfloat16
AF = mybir.ActivationFunctionType
ALU = mybir.AluOpType
AX = mybir.AxisListType

NCORES = 8
D = 1024
SEQ = 4096
BATCH = 4
NH = 16
DH = 64
NHP = 8
T_OWN = 2048
HALO = 128
T_EXT = T_OWN + 2 * HALO
NT_OWN = T_OWN // 128
NT_EXT = T_EXT // 128
EPS = 1e-5

bf16 = ml_dtypes.bfloat16
_PFX = ""


def _set_pfx(p):
    global _PFX
    _PFX = p



class Buf:
    __slots__ = ("w", "readers", "name")

    def __init__(self, name=""):
        self.w = None
        self.readers = []
        self.name = name


class Sched:

    ENG = ("tensor", "vector", "scalar", "gpsimd", "sync")

    def __init__(self, nc, ctx):
        self.nc = nc
        self.ctx = ctx
        self.sems = {}
        self.cnt = {}
        for en in Sched.ENG:
            self.sems[en] = ctx.enter_context(nc.semaphore(f"se_{en}"))
            self.cnt[en] = 0
        self.dsems = {}
        self.dcnt = {}
        self.wm = {en: {} for en in Sched.ENG}

    def dma_key(self, key):
        if key not in self.dsems:
            self.dsems[key] = self.ctx.enter_context(
                self.nc.semaphore(f"sd_{key}"))
            self.dcnt[key] = 0
        return self.dsems[key]

    def _engine(self, en):
        return getattr(self.nc, en)

    def _wait(self, en, tok):
        sem_name, sem, val = tok
        if val <= 0:
            return
        if self.wm[en].get(sem_name, 0) >= val:
            return
        self._engine(en).wait_ge(sem, val)
        self.wm[en][sem_name] = val

    def _deps_of(self, reads, writes, deps):
        toks = list(getattr(self, "fence_toks", ())) + list(deps or [])
        for b in (reads or []):
            if b.w is not None:
                toks.append(b.w)
        for b in (writes or []):
            if b.w is not None:
                toks.append(b.w)
            toks.extend(b.readers)
        return toks

    def _commit(self, tok, reads, writes):
        for b in (reads or []):
            b.readers.append(tok)
        for b in (writes or []):
            b.w = tok
            b.readers = []

    def run(self, en, emit, reads=None, writes=None, deps=None):
        for tok in self._deps_of(reads, writes, deps):
            self._wait(en, tok)
        last = emit(self._engine(en))
        assert last is not None, "task emit() must return its last instruction"
        self.cnt[en] += 1
        last.then_inc(self.sems[en], 1)
        tok = (en, self.sems[en], self.cnt[en])
        self._commit(tok, reads, writes)
        return tok

    def dma(self, en, key, out, in_, reads=None, writes=None, deps=None,
            **kw):
        sem = self.dma_key(key)
        for tok in self._deps_of(reads, writes, deps):
            self._wait(en, tok)
        self._engine(en).dma_start(out, in_, **kw).then_inc(sem, 16)
        self.dcnt[key] += 16
        tok = (f"dma_{key}", sem, self.dcnt[key])
        self._commit(tok, reads, writes)
        return tok

    def idma(self, key, emit, reads=None, writes=None, deps=None):
        sem = self.dma_key(key)
        for tok in self._deps_of(reads, writes, deps):
            self._wait("gpsimd", tok)
        emit(self.nc.gpsimd).then_inc(sem, 16)
        self.dcnt[key] += 16
        tok = (f"dma_{key}", sem, self.dcnt[key])
        self._commit(tok, reads, writes)
        return tok

    def barrier_token(self, en):
        return (en, self.sems[en], self.cnt[en])

    def fence(self):
        toks = [(en, self.sems[en], self.cnt[en]) for en in Sched.ENG]
        toks += [(f"dma_{k}", self.dsems[k], self.dcnt[k]) for k in self.dsems]
        self.fence_toks = [t for t in toks if t[2] > 0]



def to_f32(a):
    return np.ascontiguousarray(a, dtype=np.float32)


def host_consts():
    c = {}
    c["identb"] = np.eye(128, dtype=np.float32)
    c["identf"] = np.eye(128, dtype=np.float32)
    c["ones_row"] = np.ones((1, 128), np.float32)
    c["ones_rowf"] = np.ones((1, 128), np.float32)
    c["ones_col"] = np.ones((128, 1), np.float32)
    c["eps_col"] = np.full((128, 1), EPS, np.float32)
    lt = np.zeros((128, 128), np.float32)
    for t in range(128):
        lt[:t, t] = 1.0
    c["lt_excl"] = lt
    c["zeros_t"] = np.zeros((128, 1024), np.float32)
    c["ones_colb"] = np.ones((128, 1), np.float32)
    return c


CONST_SPECS = [
    ("identb", [128, 128], BF16),
    ("identf", [128, 128], F32),
    ("ones_row", [1, 128], BF16),
    ("ones_rowf", [1, 128], F32),
    ("ones_col", [128, 1], F32),
    ("eps_col", [128, 1], F32),
    ("lt_excl", [128, 128], BF16),
    ("zeros_t", [128, 1024], BF16),
    ("ones_colb", [128, 1], BF16),
]


def declare_consts(nc, sch, ctx):
    sb = {}
    cb = Buf("consts")
    for name, shape, dt in CONST_SPECS:
        dram = nc.dram_tensor("c_" + name, shape, F32, kind="ExternalInput").ap()
        t = ctx.enter_context(nc.sbuf_tensor("sb_" + name, shape, dt))
        sch.dma("gpsimd", "consts", t[:, :], dram, writes=[cb])
        sb[name] = t
    return sb, cb



def emit_ln_tile(nc, sch, cb, x_buf, x_tile, tmp, want_f32=False, want_b16=None,
                 stats_out=None, consts=None):
    bn6, b_bn6 = tmp["bn6"]
    mv, b_mv = tmp["mv"]
    lnv, b_lnv = tmp["lnv"]
    rstd, b_rstd = tmp["rstd"]
    biasA, b_biasA = tmp["biasA"]
    eps_col = consts["eps_col"]

    sch.run("vector", lambda v: v.bn_stats(bn6[:, 0:6], x_tile[:, 0:512]),
            reads=[x_buf], writes=[b_bn6])
    sch.run("vector", lambda v: v.bn_stats(bn6[:, 6:12], x_tile[:, 512:1024]),
            reads=[x_buf], writes=[b_bn6])
    sch.run("vector", lambda v: v.bn_aggr(mv[:, :], bn6[:, :]),
            reads=[b_bn6], writes=[b_mv])
    sch.run("scalar", lambda s: s.activation(lnv[:, :], mv[:, 1:2], AF.Ln,
                                             bias=eps_col[:, 0:1], scale=1.0),
            reads=[b_mv, cb], writes=[b_lnv])
    sch.run("scalar", lambda s: s.activation(rstd[:, :], lnv[:, :], AF.Exp,
                                             scale=-0.5),
            reads=[b_lnv], writes=[b_rstd])
    sch.run("vector", lambda v: v.tensor_scalar(biasA[:, :], mv[:, 0:1],
                                                rstd[:, 0:1], -1.0,
                                                ALU.mult, ALU.mult),
            reads=[b_mv, b_rstd], writes=[b_biasA])
    tok_f = tok_b = None
    if want_f32 is not None:
        h, sl, wbuf = want_f32
        tok_f = sch.run("vector",
                        lambda v: v.tensor_scalar(h[:, sl], x_tile[:, :],
                                                  mv[:, 0:1], rstd[:, 0:1],
                                                  ALU.subtract, ALU.mult),
                        reads=[x_buf, b_mv, b_rstd], writes=[wbuf])
    if want_b16 is not None:
        h, sl, wbuf = want_b16
        tok_b = sch.run("scalar",
                        lambda s: s.activation(h[:, sl], x_tile[:, :],
                                               AF.Identity,
                                               bias=biasA[:, 0:1],
                                               scale=rstd[:, 0:1]),
                        reads=[x_buf, b_rstd, b_biasA], writes=[wbuf])
    if stats_out is not None:
        h, col, wbuf = stats_out
        sch.run("vector", lambda v: v.tensor_copy(h[:, col:col + 1], rstd[:, :]),
                reads=[b_rstd], writes=[wbuf])
        sch.run("vector", lambda v: v.tensor_copy(h[:, col + 1:col + 2],
                                                  biasA[:, :]),
                reads=[b_biasA], writes=[wbuf])
    return tok_f, tok_b


def mk_ln_tmp(nc, ctx, tag):
    tmp = {}
    for name, shape in [("bn6", [128, 12]), ("mv", [128, 2]), ("lnv", [128, 1]),
                        ("rstd", [128, 1]), ("biasA", [128, 1])]:
        h = ctx.enter_context(nc.sbuf_tensor(_PFX + f"{tag}_{name}", shape, F32))
        tmp[name] = (h, Buf(f"{tag}_{name}"))
    return tmp


def emit_transpose8(nc, sch, src, src_buf, pbank, pbank_buf, dst, dst_cols,
                    dst_buf, ident, cb, dtype_b16=True):
    def t_emit(t):
        last = None
        for ch in range(8):
            last = t.transpose(pbank[:, ch * 128:(ch + 1) * 128],
                               src[:, ch * 128:(ch + 1) * 128],
                               ident[:, :])
        return last
    sch.run("tensor", t_emit, reads=[src_buf, cb], writes=[pbank_buf])

    def v_emit(v):
        last = None
        for ch in range(8):
            last = v.tensor_copy(dst[:, dst_cols[ch]:dst_cols[ch] + 128],
                                 pbank[:, ch * 128:(ch + 1) * 128])
        return last
    return sch.run("vector", v_emit, reads=[pbank_buf], writes=[dst_buf])


def emit_top2(nc, sch, lgt, lgt_buf, nlog, tmp, out, out_buf, with_z=False):
    m1, b_m1 = tmp["m1"]
    nm1, b_nm1 = tmp["nm1"]
    e8, b_e8 = tmp["e8"]
    m1e, b_m1e = tmp["m1e"]
    is1, b_is1 = tmp["is1"]
    ms2, b_ms2 = tmp["ms2"]
    m2e, b_m2e = tmp["m2e"]
    is2, b_is2 = tmp["is2"]
    den, b_den = tmp["den"]
    rden, b_rden = tmp["rden"]
    p1n, b_p1n = tmp["p1n"]
    p2n, b_p2n = tmp["p2n"]
    B1, b_B1 = out["B1"]
    B2, b_B2 = out["B2"]
    n = nlog

    sch.run("vector", lambda v: v.reduce_max(m1[:, :], lgt[:, 0:n], axis=AX.X,
                                             negate=True),
            reads=[lgt_buf], writes=[b_m1])
    sch.run("scalar", lambda s: s.activation(e8[:, 0:n], lgt[:, 0:n], AF.Exp,
                                             bias=m1[:, 0:1], scale=1.0),
            reads=[lgt_buf, b_m1], writes=[b_e8])
    sch.run("vector", lambda v: v.reduce_max(m1e[:, :], e8[:, 0:n], axis=AX.X),
            reads=[b_e8], writes=[b_m1e])
    sch.run("vector", lambda v: v.tensor_single_scalar(is1[:, 0:n], e8[:, 0:n],
                                                       m1e[:, 0:1],
                                                       ALU.is_equal),
            reads=[b_e8, b_m1e], writes=[b_is1])
    sch.run("vector", lambda v: v.scalar_tensor_tensor(ms2[:, 0:n], is1[:, 0:n],
                                                       -1e30, e8[:, 0:n],
                                                       ALU.mult, ALU.add),
            reads=[b_is1, b_e8], writes=[b_ms2])
    sch.run("vector", lambda v: v.reduce_max(m2e[:, :], ms2[:, 0:n], axis=AX.X),
            reads=[b_ms2], writes=[b_m2e])
    sch.run("vector", lambda v: v.tensor_single_scalar(is2[:, 0:n], ms2[:, 0:n],
                                                       m2e[:, 0:1],
                                                       ALU.is_equal),
            reads=[b_ms2, b_m2e], writes=[b_is2])
    if with_z:
        sch.run("vector", lambda v: v.reduce_sum(den[:, :], e8[:, 0:n],
                                                 axis=AX.X),
                reads=[b_e8], writes=[b_den])
    else:
        sch.run("vector", lambda v: v.tensor_add(den[:, :], m1e[:, :],
                                                 m2e[:, :]),
                reads=[b_m1e, b_m2e], writes=[b_den])
    sch.run("vector", lambda v: v.reciprocal(rden[:, :], den[:, :]),
            reads=[b_den], writes=[b_rden])
    sch.run("vector", lambda v: v.tensor_mul(p1n[:, :], m1e[:, :], rden[:, :]),
            reads=[b_m1e, b_rden], writes=[b_p1n])
    sch.run("vector", lambda v: v.tensor_mul(p2n[:, :], m2e[:, :], rden[:, :]),
            reads=[b_m2e, b_rden], writes=[b_p2n])
    sch.run("vector", lambda v: v.tensor_single_scalar(B1[:, 0:n], is1[:, 0:n],
                                                       p1n[:, 0:1], ALU.mult),
            reads=[b_is1, b_p1n], writes=[b_B1])
    sch.run("vector", lambda v: v.tensor_single_scalar(B2[:, 0:n], is2[:, 0:n],
                                                       p2n[:, 0:1], ALU.mult),
            reads=[b_is2, b_p2n], writes=[b_B2])


def mk_top2_tmp(nc, ctx, tag, nlog):
    tmp = {}
    for name, w in [("m1", 1), ("nm1", 1), ("e8", nlog), ("m1e", 1),
                    ("is1", nlog), ("ms2", nlog), ("m2e", 1), ("is2", nlog),
                    ("den", 1), ("rden", 1), ("p1n", 1), ("p2n", 1)]:
        h = ctx.enter_context(nc.sbuf_tensor(_PFX + f"{tag}_{name}", [128, w], F32))
        tmp[name] = (h, Buf(f"{tag}_{name}"))
    return tmp



def build_launch_R(n_tiles, own_range):
    nc = bass.Bass("TRN2", target_bir_lowering=False)
    xr = nc.dram_tensor("xr", [n_tiles * 128, D], F32, kind="ExternalInput").ap()
    rw = nc.dram_tensor("rw", [D, 8], F32, kind="ExternalInput").ap()
    rb = nc.dram_tensor("rb", [1, 8], F32, kind="ExternalInput").ap()
    o_sp = nc.dram_tensor("o_sp", [8, 2], F32, kind="ExternalOutput").ap()
    with ExitStack() as ctx:
        sch = Sched(nc, ctx)
        consts, cb = declare_consts(nc, sch, ctx)
        emit_R(nc, sch, consts, cb, xr, rw, rb, o_sp, Buf("osp"),
               n_tiles, own_range, final_wait=True)
    return nc


def emit_R(nc, sch, consts, cb, xr, rw, rb, o_sp, b_osp, n_tiles, own_range,
           final_wait=False):
    global _PFX
    _PFX = "r_"
    with ExitStack() as ctx:
        e = ctx.enter_context

        NSLOT = 3
        NP = 2
        xs = [e(nc.sbuf_tensor(_PFX + f"xs{i}", [128, D], F32)) for i in range(NSLOT)]
        xb = [Buf(f"xs{i}") for i in range(NSLOT)]
        xn16 = [e(nc.sbuf_tensor(_PFX + f"xn16_{i}", [128, D], BF16)) for i in range(NP)]
        b_xn16 = [Buf(f"xn16_{i}") for i in range(NP)]
        xnf = [e(nc.sbuf_tensor(_PFX + f"xnf_{i}", [128, D], BF16)) for i in range(NP)]
        b_xnf = [Buf(f"xnf_{i}") for i in range(NP)]
        rw_sb = e(nc.sbuf_tensor(_PFX + "rw_sb", [128, 64], BF16))
        rb_sb = e(nc.sbuf_tensor(_PFX + "rb_sb", [1, 8], BF16))
        b_w = Buf("rw")
        lgt = [e(nc.sbuf_tensor(_PFX + f"lgt_{i}", [128, 8], F32)) for i in range(NP)]
        b_lgt = [Buf(f"lgt_{i}") for i in range(NP)]
        B1 = [e(nc.sbuf_tensor(_PFX + f"B1_{i}", [128, 8], F32)) for i in range(NP)]
        B2 = [e(nc.sbuf_tensor(_PFX + f"B2_{i}", [128, 8], F32)) for i in range(NP)]
        b_B1 = [Buf(f"B1_{i}") for i in range(NP)]
        b_B2 = [Buf(f"B2_{i}") for i in range(NP)]
        spart = e(nc.sbuf_tensor(_PFX + "spart", [8, 2], F32))
        b_sp = Buf("spart")

        pt = [e(nc.psum_tensor(_PFX + f"pt_{i}", [128, 1024], BF16)) for i in range(NP)]
        b_pt = [Buf(f"pt_{i}") for i in range(NP)]
        pr = [e(nc.psum_tensor(_PFX + f"pr_{i}", [128, 8], F32)) for i in range(NP)]
        b_pr = [Buf(f"pr_{i}") for i in range(NP)]
        pS1 = e(nc.psum_tensor(_PFX + "pS1", [8, 1], F32))
        pS2 = e(nc.psum_tensor(_PFX + "pS2", [8, 1], F32))
        b_pS = Buf("pS")

        ln_tmp = [mk_ln_tmp(nc, ctx, f"lnR{i}") for i in range(NP)]
        t2_tmp = [mk_top2_tmp(nc, ctx, f"t2R{i}", 8) for i in range(NP)]

        sch.dma("gpsimd", "w", rw_sb[:, :].rearrange("p (c e) -> p c e", c=8),
                rw.rearrange("(c p) e -> p c e", p=128), writes=[b_w])
        sch.dma("gpsimd", "w", rb_sb[:, :], rb, writes=[b_w])

        own_set = set(own_range)
        first_own, last_own = min(own_range), max(own_range)

        st2 = [None]

        def stage2_r(t, p):
            def t_emit(tns, p=p):
                last = None
                for ch in range(8):
                    last = tns.transpose(pt[p][:, ch * 128:(ch + 1) * 128],
                                         xn16[p][:, ch * 128:(ch + 1) * 128],
                                         consts["identb"][:, :])
                return last
            sch.run("tensor", t_emit, reads=[b_xn16[p], cb], writes=[b_pt[p]])

            def v_emit(v, p=p):
                last = None
                for ch in range(8):
                    last = v.tensor_copy(xnf[p][:, ch * 128:(ch + 1) * 128],
                                         pt[p][:, ch * 128:(ch + 1) * 128])
                return last
            sch.run("vector", v_emit, reads=[b_pt[p]], writes=[b_xnf[p]])

            def r_emit(tns, p=p):
                for ch in range(8):
                    tns.matmul(pr[p][:, :], xnf[p][:, ch * 128:(ch + 1) * 128],
                               rw_sb[:, ch * 8:(ch + 1) * 8],
                               start=(ch == 0), stop=False)
                return tns.matmul(pr[p][:, :], consts["ones_row"][:, :],
                                  rb_sb[:, :], start=False, stop=True)
            sch.run("tensor", r_emit, reads=[b_xnf[p], b_w, cb], writes=[b_pr[p]])
            sch.run("vector", lambda v, p=p: v.tensor_copy(lgt[p][:, :],
                                                           pr[p][:, :]),
                    reads=[b_pr[p]], writes=[b_lgt[p]])
            emit_top2(nc, sch, lgt[p], b_lgt[p], 8, t2_tmp[p],
                      {"B1": (B1[p], b_B1[p]), "B2": (B2[p], b_B2[p])}, None)
            if t in own_set:
                def s_emit(tns, p=p, first=(t == first_own), last=(t == last_own)):
                    tns.matmul(pS1[:, :], B1[p][:, :], consts["ones_col"][:, :],
                               start=first, stop=last)
                    return tns.matmul(pS2[:, :], B2[p][:, :],
                                      consts["ones_col"][:, :],
                                      start=first, stop=last)
                sch.run("tensor", s_emit, reads=[b_B1[p], b_B2[p], cb],
                        writes=[b_pS])

        for t in range(n_tiles):
            sl = t % NSLOT
            p = t % NP
            sch.dma("sync", f"x{sl}", xs[sl][:, :],
                    xr[t * 128:(t + 1) * 128, :], writes=[xb[sl]])
            emit_ln_tile(nc, sch, cb, xb[sl], xs[sl], ln_tmp[p],
                         want_f32=None,
                         want_b16=(xn16[p], slice(0, D), b_xn16[p]),
                         consts=consts)
            if st2[0]:
                st2[0]()
            st2[0] = (lambda t=t, p=p: stage2_r(t, p))
        st2[0]()

        def fin_emit(v):
            v.tensor_copy(spart[:, 0:1], pS1[:, :])
            return v.tensor_copy(spart[:, 1:2], pS2[:, :])
        sch.run("vector", fin_emit, reads=[b_pS], writes=[b_sp])
        tok = sch.dma("scalar", "spout", o_sp, spart[:, :], reads=[b_sp],
                      writes=[b_osp])
        if final_wait:
            sch._wait("scalar", tok)


def host_consts_named():
    return {"c_" + k: v for k, v in host_consts().items()}


def host_R_inputs(x_own, L0_ffn_router_w, L0_ffn_router_b):
    d = host_consts_named()
    d["xr"] = to_f32(x_own)
    d["rw"] = to_f32(L0_ffn_router_w)
    d["rb"] = to_f32(L0_ffn_router_b).reshape(1, 8)
    return d



NQT = 5
QT_N = [512, 512, 512, 512, 256]
QT_START = [0, 512, 1024, 1536, 2048]
UKB = [19, 23, 27, 31, 32]


def a0_maskpos(k):
    pos = set()
    for h in (0, 1):
        q0 = 2048 * h - 128 + 512 * k
        d0 = q0 // 128
        for j in range(4):
            kb = d0 + j
            if 0 <= kb < UKB[k]:
                pos.add(kb)
    return sorted(pos)


A0_MASKPOS = [a0_maskpos(k) for k in range(NQT)]
A0_NMASK = max(len(p) for p in A0_MASKPOS)
A0_DEBUG = None


def build_launch_A0():
    nc = bass.Bass("TRN2", target_bir_lowering=False)
    xf = nc.dram_tensor("xf", [SEQ, D], F32, kind="ExternalInput").ap()
    xe = nc.dram_tensor("xe", [T_EXT, D], F32, kind="ExternalInput").ap()
    wq_r = nc.dram_tensor("wq_r", [NHP, 8, 128, 128], F32, kind="ExternalInput").ap()
    wk_r = nc.dram_tensor("wk_r", [NHP, 8, 128, 128], F32, kind="ExternalInput").ap()
    wv_r = nc.dram_tensor("wv_r", [NHP, 8, 128, 128], F32, kind="ExternalInput").ap()
    woT_r = nc.dram_tensor("woT_r", [8, 128, D], F32, kind="ExternalInput").ap()
    dmask = nc.dram_tensor("dmask", [NQT, A0_NMASK, 128, 512], F32,
                           kind="ExternalInput").ap()
    bcol = nc.dram_tensor("bcol", [128, NQT, 32], F32, kind="ExternalInput").ap()
    vslab = nc.dram_tensor("vslab", [128, (SEQ // 128) * 4 * 65], F32,
                           kind="ExternalInput").ap()
    o_s = nc.dram_tensor("o_s", [T_EXT, D], F32, kind="ExternalOutput").ap()
    with ExitStack() as ctx:
        sch = Sched(nc, ctx)
        consts, cb = declare_consts(nc, sch, ctx)
        emit_A0(nc, sch, consts, cb, xf, xe, wq_r, wk_r, wv_r, woT_r, dmask,
                bcol, vslab, o_s, Buf("os"), final_wait=True)
    return nc


def emit_A0(nc, sch, consts, cb, xf, xe, wq_r, wk_r, wv_r, woT_r, dmask,
            bcol, vslab, o_s, b_os, final_wait=False):
    dbg = {}
    if A0_DEBUG is not None:
        for kb in A0_DEBUG[3]:
            dbg[kb] = nc.dram_tensor(f"dbg_pt_{kb}", [128, 512], F32,
                                     kind="ExternalOutput").ap()
    global _PFX
    _PFX = "a0_"
    with ExitStack() as ctx:
        e = ctx.enter_context

        qf = e(nc.sbuf_tensor(_PFX + "qf", [128, NHP * T_EXT], BF16))
        b_qf = Buf("qf")
        off = e(nc.sbuf_tensor(_PFX + "off", [128, NHP * T_EXT], BF16))
        b_off = Buf("off")
        bcol_sb = e(nc.sbuf_tensor(_PFX + "bcol_sb", [128, NQT * 32], F32))
        b_bcol = Buf("bcol")
        sch.dma("gpsimd", "bcol", bcol_sb[:, :].rearrange("p (q k) -> p q k", q=NQT),
                bcol, writes=[b_bcol])

        ln_tmp = mk_ln_tmp(nc, ctx, "lnA0")

        with ExitStack() as ph:
            pe = ph.enter_context
            xnf_e = pe(nc.sbuf_tensor(_PFX + "xnf_e", [128, 8 * T_EXT], BF16))
            b_xnfe = Buf("xnf_e")
            ptb = pe(nc.psum_tensor(_PFX + "ptb", [128, 1024], BF16))
            b_ptb = Buf("ptb")
            NSLOT = 3
            xs = [pe(nc.sbuf_tensor(_PFX + f"a1x{i}", [128, D], F32)) for i in range(NSLOT)]
            xb = [Buf(f"a1x{i}") for i in range(NSLOT)]
            xn16 = [pe(nc.sbuf_tensor(_PFX + f"a1xn16_{i}", [128, D], BF16))
                    for i in range(2)]
            b_xn16 = [Buf(f"a1xn16_{i}") for i in range(2)]
            ln2 = [ln_tmp, mk_ln_tmp(nc, ph, "lnA0b")]
            st2 = [None]
            for t in range(NT_EXT):
                sl = t % NSLOT
                p = t % 2
                sch.dma("sync", f"a1x{sl}", xs[sl][:, :],
                        xe[t * 128:(t + 1) * 128, :], writes=[xb[sl]])
                emit_ln_tile(nc, sch, cb, xb[sl], xs[sl], ln2[p],
                             want_f32=None,
                             want_b16=(xn16[p], slice(0, D), b_xn16[p]),
                             consts=consts)
                if st2[0]:
                    st2[0]()
                st2[0] = (lambda t=t, p=p: emit_transpose8(
                    nc, sch, xn16[p], b_xn16[p], ptb, b_ptb, xnf_e,
                    [ch * T_EXT + t * 128 for ch in range(8)],
                    b_xnfe, consts["identb"], cb))
            st2[0]()
            wq_sb = pe(nc.sbuf_tensor(_PFX + "wq_sb", [128, NHP * 8 * 128], BF16))
            b_wq = Buf("wq")
            sch.dma("gpsimd", "wq",
                    wq_sb[:, :].rearrange("p (hp dc c) -> p hp dc c", hp=NHP, dc=8),
                    wq_r.rearrange("hp dc p c -> p hp dc c"), writes=[b_wq])
            pqk = pe(nc.psum_tensor(_PFX + "pqk", [128, 1024], F32))
            b_pqk = [Buf("pqk0"), Buf("pqk1")]
            for hp in range(NHP):
                for g in range(NQT):
                    n = QT_N[g]
                    st = QT_START[g]
                    bank = (hp * NQT + g) % 2
                    po = pqk[:, bank * 512:bank * 512 + n]

                    def mm(tns, hp=hp, st=st, n=n, po=po):
                        last = None
                        for dc in range(8):
                            last = tns.matmul(
                                po, wq_sb[:, (hp * 8 + dc) * 128:(hp * 8 + dc + 1) * 128],
                                xnf_e[:, dc * T_EXT + st:dc * T_EXT + st + n],
                                start=(dc == 0), stop=(dc == 7))
                        return last
                    sch.run("tensor", mm, reads=[b_wq, b_xnfe, cb],
                            writes=[b_pqk[bank]])
                    sch.run("vector",
                            lambda v, hp=hp, st=st, n=n, po=po:
                            v.tensor_copy(qf[:, hp * T_EXT + st:hp * T_EXT + st + n], po),
                            reads=[b_pqk[bank]], writes=[b_qf])

        sch.fence()
        with ExitStack() as ph:
            pe = ph.enter_context
            xnf_f = pe(nc.sbuf_tensor(_PFX + "xnf_f", [128, 8 * SEQ], BF16))
            b_xnff = Buf("xnf_f")
            with ExitStack() as pa3:
                a3e = pa3.enter_context
                ptb = a3e(nc.psum_tensor(_PFX + "ptb2", [128, 1024], BF16))
                b_ptb = Buf("ptb2")
                NSLOT = 3
                xs = [a3e(nc.sbuf_tensor(_PFX + f"a3x{i}", [128, D], F32))
                      for i in range(NSLOT)]
                xb = [Buf(f"a3x{i}") for i in range(NSLOT)]
                xn16 = [a3e(nc.sbuf_tensor(_PFX + f"a3xn16_{i}", [128, D], BF16))
                        for i in range(2)]
                b_xn16 = [Buf(f"a3xn16_{i}") for i in range(2)]
                ln3 = [ln_tmp, mk_ln_tmp(nc, pa3, "lnA0c")]
                st3 = [None]
                for t in range(SEQ // 128):
                    sl = t % NSLOT
                    p = t % 2
                    sch.dma("sync", f"a3x{sl}", xs[sl][:, :],
                            xf[t * 128:(t + 1) * 128, :], writes=[xb[sl]])
                    emit_ln_tile(nc, sch, cb, xb[sl], xs[sl], ln3[p],
                                 want_f32=None,
                                 want_b16=(xn16[p], slice(0, D), b_xn16[p]),
                                 consts=consts)
                    if st3[0]:
                        st3[0]()
                    st3[0] = (lambda t=t, p=p: emit_transpose8(
                        nc, sch, xn16[p], b_xn16[p], ptb, b_ptb, xnf_f,
                        [ch * SEQ + t * 128 for ch in range(8)],
                        b_xnff, consts["identb"], cb))
                st3[0]()
            sch.fence()

            NKT = SEQ // 128
            for hg in range(4):
                sch.fence()
                with ExitStack() as hgc:
                    he = hgc.enter_context
                    wk_sb = he(nc.sbuf_tensor(_PFX + f"wk{hg}", [128, 2 * 8 * 128], BF16))
                    wv_sb = he(nc.sbuf_tensor(_PFX + f"wv{hg}", [128, 2 * 8 * 128], BF16))
                    b_wkv = Buf("wkv")
                    for i, hp in enumerate((2 * hg, 2 * hg + 1)):
                        sch.dma("gpsimd", "wkv",
                                wk_sb[:, i * 1024:(i + 1) * 1024].rearrange(
                                    "p (dc c) -> p dc c", dc=8),
                                wk_r[hp].rearrange("dc p c -> p dc c"),
                                writes=[b_wkv])
                        sch.dma("gpsimd", "wkv",
                                wv_sb[:, i * 1024:(i + 1) * 1024].rearrange(
                                    "p (dc c) -> p dc c", dc=8),
                                wv_r[hp].rearrange("dc p c -> p dc c"),
                                writes=[b_wkv])
                    kf = he(nc.sbuf_tensor(_PFX + f"kf{hg}", [128, 2 * SEQ], BF16))
                    b_kf = Buf("kf")
                    vaug = he(nc.sbuf_tensor(_PFX + f"va{hg}", [128, NKT * 4 * 65], BF16))
                    b_va = Buf("vaug")
                    pp = he(nc.psum_tensor(_PFX + f"pp{hg}", [128, 1024], F32))
                    b_pp = [Buf("pp0"), Buf("pp1")]
                    for i in range(2):
                        for g in range(8):
                            bank = (i * 8 + g) % 2
                            po = pp[:, bank * 512:(bank + 1) * 512]

                            def mm(tns, i=i, g=g, po=po):
                                last = None
                                for dc in range(8):
                                    last = tns.matmul(
                                        po,
                                        wk_sb[:, (i * 8 + dc) * 128:(i * 8 + dc + 1) * 128],
                                        xnf_f[:, dc * SEQ + g * 512:dc * SEQ + (g + 1) * 512],
                                        start=(dc == 0), stop=(dc == 7))
                                return last
                            sch.run("tensor", mm, reads=[b_wkv, b_xnff, cb],
                                    writes=[b_pp[bank]])
                            sch.run("vector",
                                    lambda v, i=i, g=g, po=po:
                                    v.tensor_copy(kf[:, i * SEQ + g * 512:i * SEQ + (g + 1) * 512], po),
                                    reads=[b_pp[bank]], writes=[b_kf])
                    VROW = NKT * 4 * 65
                    sch.dma("gpsimd", "vslab", vaug[:, :], vslab, writes=[b_va])
                    for kt in range(NKT):
                        for i in range(2):
                            bank = (kt * 2 + i) % 2
                            po = pp[:, bank * 512:bank * 512 + 128]

                            def mm(tns, kt=kt, i=i, po=po):
                                last = None
                                for dc in range(8):
                                    last = tns.matmul(
                                        po,
                                        xnf_f[:, dc * SEQ + kt * 128:dc * SEQ + (kt + 1) * 128],
                                        wv_sb[:, (i * 8 + dc) * 128:(i * 8 + dc + 1) * 128],
                                        start=(dc == 0), stop=(dc == 7))
                                return last
                            sch.run("tensor", mm, reads=[b_wkv, b_xnff, cb],
                                    writes=[b_pp[bank]])
                            base = (kt * 4 + i * 2) * 65

                            def ev(v, po=po, base=base):
                                return v.tensor_copy(
                                    bass.AP(vaug, base,
                                            [[VROW, 128], [65, 2], [1, 64]]),
                                    po.rearrange("p (h e) -> p h e", h=2))
                            sch.run("vector", ev, reads=[b_pp[bank]], writes=[b_va])

                    dm_sb = [he(nc.sbuf_tensor(_PFX + f"dm{hg}_0", [128, A0_NMASK * 512], BF16))]
                    b_dm = [Buf("dm0")]
                    pT = [he(nc.sbuf_tensor(_PFX + f"pT{hg}_{i}", [128, 512], BF16))
                          for i in range(8)]
                    b_pT = [Buf(f"pTs{i}") for i in range(4)]

                    recip = [he(nc.sbuf_tensor(_PFX + f"rc{hg}_{i}", [1, 512], F32))
                             for i in range(2)]
                    b_rc = [Buf("rc0"), Buf("rc1")]
                    bc_sb = [he(nc.sbuf_tensor(_PFX + f"bc{hg}_{i}", [64, 512], F32))
                             for i in range(2)]
                    b_bc = [Buf("bc0"), Buf("bc1")]
                    pS = [he(nc.psum_tensor(_PFX + f"pS{hg}_{i}", [128, 1024], F32))
                          for i in range(2)]
                    b_pS = [Buf("pS0"), Buf("pS1")]
                    pO = [he(nc.psum_tensor(_PFX + f"pO{hg}_{i}", [65, 512], F32))
                          for i in range(2)]
                    b_pO = [Buf("pO0"), Buf("pO1")]

                    for k in range(NQT):
                        n = QT_N[k]
                        qst = QT_START[k]
                        dsl = 0
                        sch.dma("gpsimd", f"dm{dsl}",
                                dm_sb[dsl][:, :].rearrange("p (m c) -> p m c",
                                                           m=A0_NMASK),
                                dmask[k].rearrange("m p c -> p m c"),
                                writes=[b_dm[dsl]])
                        mpos = {kb: j for j, kb in enumerate(A0_MASKPOS[k])}
                        for hlp in range(2):
                          ngrp = (UKB[k] + 1) // 2
                          for g in range(ngrp + 1):
                           for stage in range(4):
                            for hl in (2 * hlp, 2 * hlp + 1):
                                h = 4 * hg + hl
                                hp_loc = hl // 2
                                half = hl % 2
                                obank = hl % 2
                                par = hl % 2
                                q_ap = qf[64 * half:64 * half + 64,
                                          (2 * hg + hp_loc) * T_EXT + qst:
                                          (2 * hg + hp_loc) * T_EXT + qst + n]
                                kbs = [kb for kb in (2 * g, 2 * g + 1)
                                       if kb < UKB[k]]
                                pkbs = [kb for kb in (2 * g - 2, 2 * g - 1)
                                        if 0 <= kb < UKB[k]]
                                slot = par
                                pti = (g % 2) * 4 + par * 2
                                ppti = ((g - 1) % 2) * 4 + par * 2
                                if stage == 0 and g < ngrp:
                                    def qk_emit(t, kbs=kbs, slot=slot, q=q_ap,
                                                n=n, half=half, hp_loc=hp_loc):
                                        last = None
                                        for i, kb in enumerate(kbs):
                                            last = t.matmul(
                                                pS[slot][:, i * 512:i * 512 + n],
                                                kf[64 * half:64 * half + 64,
                                                   hp_loc * SEQ + kb * 128:
                                                   hp_loc * SEQ + (kb + 1) * 128],
                                                q, start=True, stop=True)
                                        return last
                                    sch.run("tensor", qk_emit,
                                            reads=[b_kf, b_qf],
                                            writes=[b_pS[slot]])
                                elif stage == 1 and g < ngrp:
                                    if any(kb in mpos for kb in kbs):
                                        def m_emit(v, kbs=kbs, slot=slot, n=n):
                                            last = None
                                            for i, kb in enumerate(kbs):
                                                if kb in mpos:
                                                    ap = pS[slot][:, i * 512:i * 512 + n]
                                                    last = v.tensor_tensor(
                                                        ap, ap,
                                                        dm_sb[dsl][:, mpos[kb] * 512:
                                                                   mpos[kb] * 512 + n],
                                                        ALU.add)
                                            return last
                                        sch.run("vector", m_emit,
                                                reads=[b_dm[dsl]],
                                                writes=[b_pS[slot]])
                                elif stage == 2 and g < ngrp:
                                    def e_emit(sce, kbs=kbs, slot=slot, n=n,
                                               k=k, pti=pti):
                                        last = None
                                        for i, kb in enumerate(kbs):
                                            src = pS[slot][:, i * 512:i * 512 + n]
                                            last = sce.activation(
                                                pT[pti + i][:, 0:n], src,
                                                AF.Exp, scale=0.125,
                                                bias=bcol_sb[:, k * 32 + kb:
                                                             k * 32 + kb + 1])
                                        return last
                                    sch.run("scalar", e_emit,
                                            reads=[b_pS[slot], b_bcol],
                                            writes=[b_pT[(g % 2) * 2 + par]])
                                elif stage == 3:
                                    if g >= 1:
                                        def pv_emit(t, pkbs=pkbs, n=n, k=k,
                                                    hl=hl, ob=obank, ppti=ppti):
                                            last = None
                                            for i, kb in enumerate(pkbs):
                                                last = t.matmul(
                                                    pO[ob][:, 0:n],
                                                    vaug[:, (kb * 4 + hl) * 65:
                                                         (kb * 4 + hl) * 65 + 65],
                                                    pT[ppti + i][:, 0:n],
                                                    start=(kb == 0),
                                                    stop=(kb == UKB[k] - 1))
                                            return last
                                        sch.run("tensor", pv_emit,
                                                reads=[b_va,
                                                       b_pT[((g - 1) % 2) * 2 + par]],
                                                writes=[b_pO[obank]])
                                    if g == ngrp:
                                        sch.run("vector",
                                                lambda v, n=n, ob=obank:
                                                v.reciprocal(recip[ob][:, 0:n],
                                                             pO[ob][64:65, 0:n]),
                                                reads=[b_pO[obank]],
                                                writes=[b_rc[obank]])
                                        sch.run("tensor",
                                                lambda t, n=n, ob=obank:
                                                t.matmul(pS[ob][0:64, 0:n],
                                                         consts["ones_rowf"][:, 0:64],
                                                         recip[ob][:, 0:n],
                                                         start=True, stop=True),
                                                reads=[b_rc[obank]],
                                                writes=[b_pS[obank]])
                                        sch.run("vector",
                                                lambda v, n=n, ob=obank:
                                                v.tensor_copy(bc_sb[ob][:, 0:n],
                                                              pS[ob][0:64, 0:n]),
                                                reads=[b_pS[obank]],
                                                writes=[b_bc[obank]])
                                        ocol = (2 * hg + hp_loc) * T_EXT + qst
                                        sch.run("vector",
                                                lambda v, n=n, ob=obank,
                                                half=half, ocol=ocol:
                                                v.tensor_tensor(
                                                    off[64 * half:64 * half + 64,
                                                        ocol:ocol + n],
                                                    pO[ob][0:64, 0:n],
                                                    bc_sb[ob][0:64, 0:n],
                                                    ALU.mult),
                                                reads=[b_pO[obank], b_bc[obank]],
                                                writes=[b_off])

        sch.fence()
        with ExitStack() as ph:
            pe = ph.enter_context
            wo_sb = pe(nc.sbuf_tensor(_PFX + "wo_sb", [128, 8 * D], BF16))
            b_wo = Buf("wo")
            sch.dma("gpsimd", "wo",
                    wo_sb[:, :].rearrange("p (cc o) -> p cc o", cc=8),
                    woT_r.rearrange("cc p o -> p cc o"), writes=[b_wo])
            ps = pe(nc.psum_tensor(_PFX + "psC", [128, 1024], F32))
            b_ps = [Buf("psC0"), Buf("psC1")]
            s_sb = [pe(nc.sbuf_tensor(_PFX + f"s_sb{i}", [128, 512], F32)) for i in range(2)]
            b_ssb = [Buf("ssb0"), Buf("ssb1")]
            for t in range(NT_EXT):
                for oc in range(2):
                    bank = (t * 2 + oc) % 2
                    po = ps[:, bank * 512:(bank + 1) * 512]

                    def mm(tns, t=t, oc=oc, po=po):
                        last = None
                        for cc in range(8):
                            last = tns.matmul(
                                po, off[:, cc * T_EXT + t * 128:cc * T_EXT + (t + 1) * 128],
                                wo_sb[:, cc * D + oc * 512:cc * D + (oc + 1) * 512],
                                start=(cc == 0), stop=(cc == 7))
                        return last
                    sch.run("tensor", mm, reads=[b_off, b_wo, cb],
                            writes=[b_ps[bank]])
                    sch.run("vector",
                            lambda v, bank=bank, po=po: v.tensor_copy(
                                s_sb[bank][:, :], po),
                            reads=[b_ps[bank]], writes=[b_ssb[bank]])
                    sch.dma("scalar", f"so{bank}",
                            o_s[t * 128:(t + 1) * 128, oc * 512:(oc + 1) * 512],
                            s_sb[bank][:, :], reads=[b_ssb[bank]],
                            writes=[b_os])
            if final_wait:
                for kk in ("so0", "so1"):
                    sch._wait("scalar",
                              (f"dma_{kk}", sch.dsems[kk], sch.dcnt[kk]))
        sch.fence()


def a0_vslab():
    slab = np.zeros((128, (SEQ // 128) * 4 * 65), np.float32)
    slab[:, 64::65] = 1.0
    return slab


def host_A0_inputs(core, x, wq, wk, wv, wo):
    b, h = core // 2, core % 2
    xb = to_f32(x[b])
    own0 = 2048 * h
    xe = np.zeros((T_EXT, D), np.float32)
    lo, hi = own0 - HALO, own0 + T_OWN + HALO
    slo, shi = max(lo, 0), min(hi, SEQ)
    xe[slo - lo:shi - lo] = xb[slo:shi]
    wq_r = to_f32(wq).reshape(8, 2, 64, 8, 128).transpose(0, 3, 4, 1, 2) \
        .reshape(8, 8, 128, 128)
    wk_r = to_f32(wk).reshape(8, 2, 64, 8, 128).transpose(0, 3, 4, 1, 2) \
        .reshape(8, 8, 128, 128)
    wv_r = to_f32(wv).reshape(8, 2, 64, 8, 128).transpose(0, 3, 4, 1, 2) \
        .reshape(8, 8, 128, 128)
    woT_r = to_f32(wo).T.reshape(8, 128, D)
    q0_ext = own0 - HALO
    dm = np.zeros((NQT, A0_NMASK, 128, 512), np.float32)
    bc = np.zeros((NQT, 32, 128), np.float32)
    for k in range(NQT):
        n = QT_N[k]
        q_abs = q0_ext + QT_START[k] + np.arange(n)
        d0 = (q0_ext + QT_START[k]) // 128
        diag = set(range(d0, d0 + 4))
        for j, kb in enumerate(A0_MASKPOS[k]):
            key_abs = kb * 128 + np.arange(128)
            m = np.where((key_abs[:, None] <= q_abs[None, :]) |
                         (q_abs[None, :] < 0), 0.0, -1e9)
            dm[k, j, :, 0:n] = m
        for kb in range(UKB[k]):
            if kb in A0_MASKPOS[k]:
                continue
            if kb > d0 + 3:
                bc[k, kb, :] = -1.25e8
    bc = np.ascontiguousarray(bc.transpose(2, 0, 1))
    return dict(xf=xb, xe=xe, wq_r=np.ascontiguousarray(wq_r),
                wk_r=np.ascontiguousarray(wk_r),
                wv_r=np.ascontiguousarray(wv_r),
                woT_r=np.ascontiguousarray(woT_r), dmask=dm, bcol=bc,
                vslab=a0_vslab(), **host_consts_named())



F0_GRP = [(0, 512), (512, 512), (1024, 512), (1536, 512), (2048, 256)]


def build_launch_F0():
    nc = bass.Bass("TRN2", target_bir_lowering=False)
    xe = nc.dram_tensor("xe", [T_EXT, D], F32, kind="ExternalInput").ap()
    se = nc.dram_tensor("se", [T_EXT, D], F32, kind="ExternalInput").ap()
    spall = nc.dram_tensor("spall", [8, 16], F32, kind="ExternalInput").ap()
    flag = nc.dram_tensor("flag", [8, 2], F32, kind="ExternalInput").ap()
    rw = nc.dram_tensor("rw", [D, 8], F32, kind="ExternalInput").ap()
    rb = nc.dram_tensor("rb", [1, 8], F32, kind="ExternalInput").ap()
    w1_r = nc.dram_tensor("w1_r", [8, 8, 128, 8, 128], F32, kind="ExternalInput").ap()
    w2_r = nc.dram_tensor("w2_r", [8, 8, 128, 2, 512], F32, kind="ExternalInput").ap()
    b1_r = nc.dram_tensor("b1_r", [128, 8, 8], F32, kind="ExternalInput").ap()
    b2_r = nc.dram_tensor("b2_r", [8, D], F32, kind="ExternalInput").ap()
    eye8 = nc.dram_tensor("eye8", [8, 8 * 128], F32, kind="ExternalInput").ap()
    o_x1 = nc.dram_tensor("o_x1", [T_EXT, D], F32, kind="ExternalOutput").ap()
    with ExitStack() as ctx:
        sch = Sched(nc, ctx)
        consts, cb = declare_consts(nc, sch, ctx)
        emit_F0(nc, sch, consts, cb, xe, se, None, spall, flag, rw, rb, w1_r,
                w2_r, b1_r, b2_r, eye8, o_x1, Buf("ox1"), spall_direct=False,
                final_wait=True)
    return nc


def emit_F0(nc, sch, consts, cb, xe, se, b_se, spall, flag, rw, rb, w1_r,
            w2_r, b1_r, b2_r, eye8, o_x1, b_ox1, spall_direct=False,
            b_spall=None, final_wait=False):
    global _PFX
    _PFX = "f0_"
    with ExitStack() as ctx:
        e = ctx.enter_context

        xnf16 = e(nc.sbuf_tensor(_PFX + "xnf16", [128, 8 * T_EXT], BF16))
        b_xnf = Buf("xnf16")
        acc = e(nc.sbuf_tensor(_PFX + "acc", [128, T_EXT * 8], F32))
        b_acc = Buf("acc")
        mask_all = e(nc.sbuf_tensor(_PFX + "mask_all", [128, NT_EXT * 8], F32))
        b_mask = Buf("mask")
        maskT = e(nc.sbuf_tensor(_PFX + "maskT", [8, T_EXT], F32))
        b_maskT = Buf("maskT")
        spall_sb = e(nc.sbuf_tensor(_PFX + "spall_sb", [8, 16], F32))
        flag_sb = e(nc.sbuf_tensor(_PFX + "flag_sb", [8, 2], F32))
        ssum = e(nc.sbuf_tensor(_PFX + "ssum", [8, 2], F32))
        b_ssum = Buf("ssum")
        eye8_sb = e(nc.sbuf_tensor(_PFX + "eye8_sb", [8, 8 * 128], F32))
        b_eye = Buf("eye8")
        rw_sb = e(nc.sbuf_tensor(_PFX + "rw_sb", [128, 64], F32))
        rb_sb = e(nc.sbuf_tensor(_PFX + "rb_sb", [1, 8], F32))
        b_w = Buf("rw")
        b1_sb = e(nc.sbuf_tensor(_PFX + "b1_sb", [128, 64], F32))
        b2_sb = e(nc.sbuf_tensor(_PFX + "b2_sb", [8, D], F32))
        b_b12 = Buf("b12")

        sch.dma("gpsimd", "ssumld", spall_sb[:, 0:2] if spall_direct
                else spall_sb[:, :], spall,
                reads=[b_spall] if b_spall else None, writes=[b_ssum])
        sch.dma("gpsimd", "ssumld", flag_sb[:, :], flag, writes=[b_ssum])
        sch.dma("gpsimd", "eyeld", eye8_sb[:, :], eye8, writes=[b_eye])
        sch.dma("gpsimd", "rwld", rw_sb[:, :].rearrange("p (c e) -> p c e", c=8),
                rw.rearrange("(c p) e -> p c e", p=128), writes=[b_w])
        sch.dma("gpsimd", "rwld", rb_sb[:, :], rb, writes=[b_w])
        sch.dma("gpsimd", "b12ld", b1_sb[:, :].rearrange("p (e h) -> p e h", e=8),
                b1_r, writes=[b_b12])
        sch.dma("gpsimd", "b12ld", b2_sb[:, :], b2_r, writes=[b_b12])

        ln_tmp = mk_ln_tmp(nc, ctx, "lnF0")
        t2_tmp = mk_top2_tmp(nc, ctx, "t2F0", 8)

        with ExitStack() as ph:
            pe = ph.enter_context
            NSLOT = 3
            xs = [pe(nc.sbuf_tensor(_PFX + f"f0x{i}", [128, D], F32)) for i in range(NSLOT)]
            xb = [Buf(f"f0x{i}") for i in range(NSLOT)]
            xn32 = pe(nc.sbuf_tensor(_PFX + "xn32", [128, D], F32))
            b_xn32 = Buf("xn32")
            xn16 = pe(nc.sbuf_tensor(_PFX + "xn16", [128, D], BF16))
            b_xn16 = Buf("xn16")
            xnf32 = pe(nc.sbuf_tensor(_PFX + "xnf32", [128, D], F32))
            b_xnf32 = Buf("xnf32")
            lgt = pe(nc.sbuf_tensor(_PFX + "lgt", [128, 8], F32))
            b_lgt = Buf("lgt")
            B1 = pe(nc.sbuf_tensor(_PFX + "B1", [128, 8], F32))
            B2 = pe(nc.sbuf_tensor(_PFX + "B2", [128, 8], F32))
            b_B1, b_B2 = Buf("B1"), Buf("B2")
            ptf = pe(nc.psum_tensor(_PFX + "ptf", [128, 1024], F32))
            b_ptf = Buf("ptf")
            ptb = pe(nc.psum_tensor(_PFX + "ptb", [128, 1024], BF16))
            b_ptb = Buf("ptb")
            pr = pe(nc.psum_tensor(_PFX + "pr", [128, 8], F32))
            b_pr = Buf("pr")
            pmt = pe(nc.psum_tensor(_PFX + "pmt", [8, 128], F32))
            b_pmt = Buf("pmt")

            ln2 = [ln_tmp, mk_ln_tmp(nc, ph, "lnF0b")]
            xn16b = [xn16, pe(nc.sbuf_tensor(_PFX + "xn16b", [128, D], BF16))]
            b_xn16b = [b_xn16, Buf("xn16b")]
            xn32b = [xn32, pe(nc.sbuf_tensor(_PFX + "xn32b", [128, D], F32))]
            b_xn32b = [b_xn32, Buf("xn32b")]
            st2 = [None]

            def stage2_f0(t, p):
                emit_transpose8(nc, sch, xn16b[p], b_xn16b[p], ptb, b_ptb,
                                xnf16, [ch * T_EXT + t * 128 for ch in range(8)],
                                b_xnf, consts["identb"], cb)
                def t_emit(tns):
                    last = None
                    for ch in range(8):
                        last = tns.transpose(ptf[:, ch * 128:(ch + 1) * 128],
                                             xn32b[p][:, ch * 128:(ch + 1) * 128],
                                             consts["identf"][:, :])
                    return last
                sch.run("tensor", t_emit, reads=[b_xn32b[p], cb], writes=[b_ptf])

                def v_emit(v):
                    last = None
                    for ch in range(8):
                        last = v.tensor_copy(xnf32[:, ch * 128:(ch + 1) * 128],
                                             ptf[:, ch * 128:(ch + 1) * 128])
                    return last
                sch.run("vector", v_emit, reads=[b_ptf], writes=[b_xnf32])

                def r_emit(tns):
                    for ch in range(8):
                        tns.matmul(pr[:, :], xnf32[:, ch * 128:(ch + 1) * 128],
                                   rw_sb[:, ch * 8:(ch + 1) * 8],
                                   start=(ch == 0), stop=False)
                    return tns.matmul(pr[:, :], consts["ones_rowf"][:, :],
                                      rb_sb[:, :], start=False, stop=True)
                sch.run("tensor", r_emit, reads=[b_xnf32, b_w, cb], writes=[b_pr])
                sch.run("vector", lambda v: v.tensor_copy(lgt[:, :], pr[:, :]),
                        reads=[b_pr], writes=[b_lgt])
                emit_top2(nc, sch, lgt, b_lgt, 8, t2_tmp,
                          {"B1": (B1, b_B1), "B2": (B2, b_B2)}, None)
                sch.run("vector",
                        lambda v: v.tensor_add(
                            mask_all[:, t * 8:(t + 1) * 8], B1[:, :], B2[:, :]),
                        reads=[b_B1, b_B2], writes=[b_mask])

            for t in range(NT_EXT):
                sl = t % NSLOT
                p = t % 2
                sch.dma("sync", f"f0x{sl}", xs[sl][:, :],
                        xe[t * 128:(t + 1) * 128, :], writes=[xb[sl]])
                emit_ln_tile(nc, sch, cb, xb[sl], xs[sl], ln2[p],
                             want_f32=(xn32b[p], slice(0, D), b_xn32b[p]),
                             want_b16=(xn16b[p], slice(0, D), b_xn16b[p]),
                             consts=consts)
                if st2[0]:
                    st2[0]()
                st2[0] = (lambda t=t, p=p: stage2_f0(t, p))
            st2[0]()

            if spall_direct:
                sch.run("vector", lambda v: v.tensor_copy(ssum[:, :],
                                                          spall_sb[:, 0:2]),
                        reads=[b_ssum], writes=[b_ssum])
            else:
                def red_emit(v):
                    v.reduce_sum(ssum[:, 0:1],
                                 bass.AP(spall_sb, 0, [[16, 8], [2, 8]]),
                                 axis=AX.X)
                    return v.reduce_sum(ssum[:, 1:2],
                                        bass.AP(spall_sb, 1, [[16, 8], [2, 8]]),
                                        axis=AX.X)
                sch.run("vector", red_emit, reads=[b_ssum], writes=[b_ssum])
            sch.run("vector", lambda v: v.tensor_mul(ssum[:, :], ssum[:, :],
                                                     flag_sb[:, :]),
                    reads=[b_ssum], writes=[b_ssum])
            sch.run("vector", lambda v: v.tensor_add(
                mask_all[0:8, 8:10], mask_all[0:8, 8:10], ssum[:, :]),
                    reads=[b_ssum], writes=[b_mask])
            sch.run("vector", lambda v: v.tensor_scalar_min(
                mask_all[0:8, 8:16], mask_all[0:8, 8:16], 2560.0),
                    reads=[], writes=[b_mask])

            for t in range(NT_EXT):
                sch.run("tensor",
                        lambda tns, t=t: tns.transpose(
                            pmt[:, :], mask_all[:, t * 8:(t + 1) * 8],
                            consts["identf"][:, :]),
                        reads=[b_mask, cb], writes=[b_pmt])
                sch.run("vector",
                        lambda v, t=t: v.tensor_copy(
                            maskT[:, t * 128:(t + 1) * 128], pmt[0:8, :]),
                        reads=[b_pmt], writes=[b_maskT])

        sch.fence()
        with ExitStack() as ph:
            pe = ph.enter_context
            pz = pe(nc.psum_tensor(_PFX + "pz", [128, 1024], F32))
            b_pz = [Buf("pz0"), Buf("pz1")]
            py = pe(nc.psum_tensor(_PFX + "py", [128, 1024], F32))
            b_py = [Buf("py0"), Buf("py1")]
            pBe = pe(nc.psum_tensor(_PFX + "pBe", [128, 512], F32))
            b_pBe = Buf("pBe")
            Be = pe(nc.sbuf_tensor(_PFX + "Be", [128, 512], F32))
            b_Be = Buf("Be")
            zz = pe(nc.sbuf_tensor(_PFX + "zz", [128, 512], F32))
            b_zz = Buf("zz")
            hf = pe(nc.sbuf_tensor(_PFX + "hf", [128, 8 * 512], BF16))
            b_hf = Buf("hf")
            w1_sb = [pe(nc.sbuf_tensor(_PFX + f"w1s{i}", [128, 8192], BF16)) for i in range(2)]
            w2_sb = [pe(nc.sbuf_tensor(_PFX + f"w2s{i}", [128, 8192], BF16)) for i in range(2)]
            b_w1 = [Buf("w1s0"), Buf("w1s1")]
            b_w2 = [Buf("w2s0"), Buf("w2s1")]

            for t in range(NT_EXT):
                for dch in range(2):
                    bank = (t * 2 + dch) % 2
                    po = py[:, bank * 512:(bank + 1) * 512]
                    sch.run("tensor",
                            lambda tns, t=t, dch=dch, po=po:
                            tns.matmul(po, maskT[:, t * 128:(t + 1) * 128],
                                       b2_sb[:, dch * 512:(dch + 1) * 512],
                                       start=True, stop=True),
                            reads=[b_maskT, b_b12], writes=[b_py[bank]])
                    sch.run("vector",
                            lambda v, t=t, dch=dch, po=po:
                            v.tensor_copy(
                                acc[:, t * 1024 + dch * 512:t * 1024 + (dch + 1) * 512],
                                po),
                            reads=[b_py[bank]], writes=[b_acc])

            for ei in range(8):
                wsl = ei % 2
                sch.dma("gpsimd", f"w1_{wsl}",
                        w1_sb[wsl][:, :].rearrange("p (dc hc c) -> p dc hc c",
                                                   dc=8, hc=8),
                        w1_r[ei].rearrange("dc d hc c -> d dc hc c"),
                        writes=[b_w1[wsl]])
                sch.dma("gpsimd", f"w2_{wsl}",
                        w2_sb[wsl][:, :].rearrange("p (hc dch o) -> p hc dch o",
                                                   hc=8, dch=2),
                        w2_r[ei].rearrange("hc h dch o -> h hc dch o"),
                        writes=[b_w2[wsl]])
                for gi, (gst, gn) in enumerate(F0_GRP):
                    sch.run("tensor",
                            lambda tns, ei=ei, gst=gst, gn=gn:
                            tns.matmul(pBe[:, 0:gn],
                                       eye8_sb[:, ei * 128:(ei + 1) * 128],
                                       maskT[:, gst:gst + gn],
                                       start=True, stop=True),
                            reads=[b_eye, b_maskT], writes=[b_pBe])
                    sch.run("vector",
                            lambda v, gn=gn: v.tensor_copy(Be[:, 0:gn],
                                                           pBe[:, 0:gn]),
                            reads=[b_pBe], writes=[b_Be])
                    for hc in range(8):
                        bank = hc % 2
                        po = pz[:, bank * 512:bank * 512 + gn]

                        def zmm(tns, wsl=wsl, hc=hc, gst=gst, gn=gn, po=po):
                            last = None
                            for dc in range(8):
                                last = tns.matmul(
                                    po,
                                    w1_sb[wsl][:, (dc * 8 + hc) * 128:(dc * 8 + hc + 1) * 128],
                                    xnf16[:, dc * T_EXT + gst:dc * T_EXT + gst + gn],
                                    start=(dc == 0), stop=(dc == 7))
                            return last
                        sch.run("tensor", zmm, reads=[b_w1[wsl], b_xnf, cb],
                                writes=[b_pz[bank]])
                        sch.run("vector",
                                lambda v, gn=gn, po=po:
                                v.tensor_tensor(zz[:, 0:gn], po, Be[:, 0:gn],
                                                ALU.mult),
                                reads=[b_pz[bank], b_Be], writes=[b_zz])
                        sch.run("scalar",
                                lambda s, hc=hc, gn=gn, ei=ei:
                                s.activation(hf[:, hc * 512:hc * 512 + gn],
                                             zz[:, 0:gn], AF.Gelu,
                                             bias=b1_sb[:, ei * 8 + hc:ei * 8 + hc + 1],
                                             scale=1.0),
                                reads=[b_zz, b_b12], writes=[b_hf])
                    ntt = (gn + 127) // 128
                    for ttl in range(ntt):
                        t = gi * 4 + ttl
                        for dch in range(2):
                            bank = (ttl * 2 + dch) % 2
                            po = py[:, bank * 512:(bank + 1) * 512]

                            def ymm(tns, wsl=wsl, ttl=ttl, dch=dch, po=po):
                                last = None
                                for hc in range(8):
                                    last = tns.matmul(
                                        po,
                                        hf[:, hc * 512 + ttl * 128:hc * 512 + (ttl + 1) * 128],
                                        w2_sb[wsl][:, (hc * 2 + dch) * 512:(hc * 2 + dch + 1) * 512],
                                        start=(hc == 0), stop=(hc == 7))
                                return last
                            sch.run("tensor", ymm, reads=[b_w2[wsl], b_hf, cb],
                                    writes=[b_py[bank]])
                            sch.run("vector",
                                    lambda v, t=t, ei=ei, dch=dch, po=po:
                                    v.scalar_tensor_tensor(
                                        acc[:, t * 1024 + dch * 512:t * 1024 + (dch + 1) * 512],
                                        po,
                                        mask_all[:, t * 8 + ei:t * 8 + ei + 1],
                                        acc[:, t * 1024 + dch * 512:t * 1024 + (dch + 1) * 512],
                                        ALU.mult, ALU.add),
                                    reads=[b_py[bank], b_mask], writes=[b_acc])

        sch.fence()
        with ExitStack() as ph:
            pe = ph.enter_context
            NSLOT = 2
            xs = [pe(nc.sbuf_tensor(_PFX + f"c4x{i}", [128, D], F32)) for i in range(NSLOT)]
            ss = [pe(nc.sbuf_tensor(_PFX + f"c4s{i}", [128, D], F32)) for i in range(NSLOT)]
            xbs = [Buf(f"c4x{i}") for i in range(NSLOT)]
            sbs = [Buf(f"c4s{i}") for i in range(NSLOT)]
            gsil = pe(nc.sbuf_tensor(_PFX + "gsil", [128, D], F32))
            b_g = Buf("gsil")
            x1t = [pe(nc.sbuf_tensor(_PFX + f"x1t{i}", [128, D], F32)) for i in range(NSLOT)]
            b_x1 = [Buf(f"x1t{i}") for i in range(NSLOT)]
            for t in range(NT_EXT):
                sl = t % NSLOT
                sch.dma("sync", f"c4x{sl}", xs[sl][:, :],
                        xe[t * 128:(t + 1) * 128, :], writes=[xbs[sl]])
                sch.dma("sync", f"c4s{sl}", ss[sl][:, :],
                        se[t * 128:(t + 1) * 128, :],
                        reads=[b_se] if b_se else None, writes=[sbs[sl]])
                sch.run("scalar",
                        lambda s, t=t: s.activation(
                            gsil[:, :], acc[:, t * 1024:(t + 1) * 1024], AF.Silu),
                        reads=[b_acc], writes=[b_g])
                sch.run("vector",
                        lambda v, sl=sl: v.tensor_mul(gsil[:, :], gsil[:, :],
                                                      ss[sl][:, :]),
                        reads=[sbs[sl]], writes=[b_g])
                sch.run("vector",
                        lambda v, sl=sl: v.tensor_add(x1t[sl][:, :], gsil[:, :],
                                                      xs[sl][:, :]),
                        reads=[b_g, xbs[sl]], writes=[b_x1[sl]])
                sch.dma("scalar", f"fo{sl}", o_x1[t * 128:(t + 1) * 128, :],
                        x1t[sl][:, :], reads=[b_x1[sl]], writes=[b_ox1])
            if final_wait:
                for kk in ("fo0", "fo1"):
                    sch._wait("scalar",
                              (f"dma_{kk}", sch.dsems[kk], sch.dcnt[kk]))
        sch.fence()


def host_F0_inputs(core, x, se, spart_all, rw, rb, w1, b1, w2, b2):
    b, h = core // 2, core % 2
    xb = to_f32(x[b])
    own0 = 2048 * h
    xe = np.zeros((T_EXT, D), np.float32)
    lo, hi = own0 - HALO, own0 + T_OWN + HALO
    slo, shi = max(lo, 0), min(hi, SEQ)
    xe[slo - lo:shi - lo] = xb[slo:shi]
    flag = np.full((8, 2), 1.0 if core == 0 else 0.0, np.float32)
    eye = np.zeros((8, 8 * 128), np.float32)
    for ei in range(8):
        eye[ei, ei * 128:(ei + 1) * 128] = 1.0
    w1_r = to_f32(w1).reshape(8, 8, 128, 8, 128)
    w2_r = to_f32(w2).reshape(8, 8, 128, 2, 512)
    b1_r = to_f32(b1).reshape(8, 8, 128).transpose(2, 0, 1)
    spall = to_f32(spart_all).transpose(1, 0, 2).reshape(8, 16)
    return dict(xe=xe, se=to_f32(se), spall=np.ascontiguousarray(spall),
                flag=flag, rw=to_f32(rw), rb=to_f32(rb).reshape(1, 8),
                w1_r=np.ascontiguousarray(w1_r),
                w2_r=np.ascontiguousarray(w2_r),
                b1_r=np.ascontiguousarray(b1_r), b2_r=to_f32(b2),
                eye8=eye, **host_consts_named())



VR1 = NT_EXT * 16 * 65
A1_DEBUG = None


def build_launch_A1():
    nc = bass.Bass("TRN2", target_bir_lowering=False)
    x1e = nc.dram_tensor("x1e", [T_EXT, D], F32, kind="ExternalInput").ap()
    wqkv_r = nc.dram_tensor("wqkv_r", [8, 128, 24, 128], F32, kind="ExternalInput").ap()
    qkvb_c = nc.dram_tensor("qkvb_c", [128, 24], F32, kind="ExternalInput").ap()
    qkvb_row = nc.dram_tensor("qkvb_row", [1, 3072], F32, kind="ExternalInput").ap()
    crope_r = nc.dram_tensor("crope_r", [128, T_EXT], F32, kind="ExternalInput").ap()
    srope_r = nc.dram_tensor("srope_r", [128, T_EXT], F32, kind="ExternalInput").ap()
    p2_r = nc.dram_tensor("p2_r", [128, 128], F32, kind="ExternalInput").ap()
    wmask_r = nc.dram_tensor("wmask_r", [6, 128, 512], F32, kind="ExternalInput").ap()
    vslab1 = nc.dram_tensor("vslab1", [128, VR1], F32, kind="ExternalInput").ap()
    bcol1 = nc.dram_tensor("bcol1", [128, NT_EXT], F32, kind="ExternalInput").ap()
    wo1_r = nc.dram_tensor("wo1_r", [8, 128, D], F32, kind="ExternalInput").ap()
    wob_row = nc.dram_tensor("wob_row", [1, D], F32, kind="ExternalInput").ap()
    o_s1 = nc.dram_tensor("o_s1", [T_OWN, D], F32, kind="ExternalOutput").ap()
    with ExitStack() as ctx:
        sch = Sched(nc, ctx)
        consts, cb = declare_consts(nc, sch, ctx)
        emit_A1(nc, sch, consts, cb, x1e, None, wqkv_r, qkvb_c, qkvb_row,
                crope_r, srope_r, p2_r, wmask_r, vslab1, bcol1, wo1_r,
                wob_row, o_s1, Buf("os1"), final_wait=True)
    return nc


def emit_A1(nc, sch, consts, cb, x1e, b_x1e, wqkv_r, qkvb_c, qkvb_row,
            crope_r, srope_r, p2_r, wmask_r, vslab1, bcol1, wo1_r, wob_row,
            o_s1, b_os1, final_wait=False):
    global A1_DBG_APS
    A1_DBG_APS = {}
    if A1_DEBUG is not None:
        for j in A1_DEBUG[2]:
            A1_DBG_APS[j] = nc.dram_tensor(f"dbg1_pt_{j}", [128, 512], F32,
                                           kind="ExternalOutput").ap()
        A1_DBG_APS["rc"] = nc.dram_tensor("dbg1_rc", [1, 512], F32,
                                          kind="ExternalOutput").ap()
    global _PFX
    _PFX = "a1_"
    with ExitStack() as ctx:
        e = ctx.enter_context

        Q1 = e(nc.sbuf_tensor(_PFX + "Q1", [128, 8 * T_EXT], BF16))
        K1 = e(nc.sbuf_tensor(_PFX + "K1", [128, 8 * T_EXT], BF16))
        b_Q1, b_K1 = Buf("Q1"), Buf("K1")
        vaug1 = e(nc.sbuf_tensor(_PFX + "vaug1", [128, VR1], BF16))
        b_va = Buf("va1")
        b_off = Buf("off1")

        ln_tmp = mk_ln_tmp(nc, ctx, "lnA1")

        with ExitStack() as ph:
            pe = ph.enter_context
            xnf1 = pe(nc.sbuf_tensor(_PFX + "xnf1", [128, 8 * T_EXT], BF16))
            b_xnf = Buf("xnf1")
            with ExitStack() as p1s:
                p1e = p1s.enter_context
                ptb = p1e(nc.psum_tensor(_PFX + "ptb", [128, 1024], BF16))
                b_ptb = Buf("ptb")
                NSLOT = 3
                xs = [p1e(nc.sbuf_tensor(_PFX + f"a1x{i}", [128, D], F32))
                      for i in range(NSLOT)]
                xb = [Buf(f"a1x{i}") for i in range(NSLOT)]
                xn16 = [p1e(nc.sbuf_tensor(_PFX + f"xn16_{i}", [128, D], BF16))
                        for i in range(2)]
                b_xn16 = [Buf(f"xn16_{i}") for i in range(2)]
                ln2 = [ln_tmp, mk_ln_tmp(nc, p1s, "lnA1b")]
                st2 = [None]
                for t in range(NT_EXT):
                    sl = t % NSLOT
                    p = t % 2
                    sch.dma("sync", f"a1x{sl}", xs[sl][:, :],
                            x1e[t * 128:(t + 1) * 128, :],
                            reads=[b_x1e] if b_x1e else None, writes=[xb[sl]])
                    emit_ln_tile(nc, sch, cb, xb[sl], xs[sl], ln2[p],
                                 want_f32=None,
                                 want_b16=(xn16[p], slice(0, D), b_xn16[p]),
                                 consts=consts)
                    if st2[0]:
                        st2[0]()
                    st2[0] = (lambda t=t, p=p: emit_transpose8(
                        nc, sch, xn16[p], b_xn16[p], ptb, b_ptb, xnf1,
                        [ch * T_EXT + t * 128 for ch in range(8)],
                        b_xnf, consts["identb"], cb))
                st2[0]()
            sch.fence()

            with ExitStack() as p2a:
                ae = p2a.enter_context
                wqk_sb = ae(nc.sbuf_tensor(_PFX + "wqk_sb", [128, 16 * 8 * 128], BF16))
                b_wqk = Buf("wqk")
                sch.dma("gpsimd", "wqk",
                        wqk_sb[:, :].rearrange("p (dc cc c) -> p dc cc c",
                                               cc=16, dc=8),
                        wqkv_r.rearrange("dc p cc c -> p dc cc c")[:, :, 0:16, :],
                        writes=[b_wqk])
                qb_sb = ae(nc.sbuf_tensor(_PFX + "qb_sb", [128, 24], F32))
                b_qb = Buf("qb")
                sch.dma("gpsimd", "qb", qb_sb[:, :], qkvb_c, writes=[b_qb])
                crope = ae(nc.sbuf_tensor(_PFX + "crope", [128, T_EXT], BF16))
                srope = ae(nc.sbuf_tensor(_PFX + "srope", [128, T_EXT], BF16))
                b_rope = Buf("rope")
                sch.dma("gpsimd", "rope", crope[:, :], crope_r, writes=[b_rope])
                sch.dma("gpsimd", "rope", srope[:, :], srope_r, writes=[b_rope])
                p2sb = ae(nc.sbuf_tensor(_PFX + "p2sb", [128, 128], BF16))
                b_p2 = Buf("p2")
                sch.dma("gpsimd", "p2", p2sb[:, :], p2_r, writes=[b_p2])

                pqk = ae(nc.psum_tensor(_PFX + "pqk", [128, 1024], F32))
                b_pqk = [Buf("pqk0"), Buf("pqk1")]
                prot = ae(nc.psum_tensor(_PFX + "prot", [128, 512], F32))
                b_prot = Buf("prot")
                qk16 = ae(nc.sbuf_tensor(_PFX + "qk16", [128, 512], BF16))
                b_qk16 = Buf("qk16")
                t1 = ae(nc.sbuf_tensor(_PFX + "t1", [128, 512], F32))
                b_t1 = Buf("t1")
                t2 = ae(nc.sbuf_tensor(_PFX + "t2", [128, 512], F32))
                b_t2 = Buf("t2")
                for cc in range(16):
                    dst, b_dst = (Q1, b_Q1) if cc < 8 else (K1, b_K1)
                    hp = cc % 8
                    for g in range(NQT):
                        n = QT_N[g]
                        st = QT_START[g]
                        bank = (cc * NQT + g) % 2
                        po = pqk[:, bank * 512:bank * 512 + n]

                        def mm(tns, cc=cc, st=st, n=n, po=po):
                            last = None
                            for dc in range(8):
                                last = tns.matmul(
                                    po,
                                    wqk_sb[:, (dc * 16 + cc) * 128:(dc * 16 + cc + 1) * 128],
                                    xnf1[:, dc * T_EXT + st:dc * T_EXT + st + n],
                                    start=(dc == 0), stop=(dc == 7))
                            return last
                        sch.run("tensor", mm, reads=[b_wqk, b_xnf, cb],
                                writes=[b_pqk[bank]])
                        sch.run("vector",
                                lambda v, cc=cc, n=n, po=po:
                                v.tensor_scalar_add(qk16[:, 0:n], po,
                                                    qb_sb[:, cc:cc + 1]),
                                reads=[b_pqk[bank], b_qb], writes=[b_qk16])
                        sch.run("tensor",
                                lambda t, n=n:
                                t.matmul(prot[:, 0:n], p2sb[:, :], qk16[:, 0:n],
                                         start=True, stop=True),
                                reads=[b_p2, b_qk16], writes=[b_prot])
                        sch.run("vector",
                                lambda v, st=st, n=n:
                                v.tensor_mul(t1[:, 0:n], qk16[:, 0:n],
                                             crope[:, st:st + n]),
                                reads=[b_qk16, b_rope], writes=[b_t1])
                        sch.run("vector",
                                lambda v, st=st, n=n:
                                v.tensor_tensor(t2[:, 0:n], prot[:, 0:n],
                                                srope[:, st:st + n], ALU.mult),
                                reads=[b_prot, b_rope], writes=[b_t2])
                        sch.run("vector",
                                lambda v, hp=hp, st=st, n=n, dst=dst:
                                v.tensor_add(dst[:, hp * T_EXT + st:hp * T_EXT + st + n],
                                             t1[:, 0:n], t2[:, 0:n]),
                                reads=[b_t1, b_t2], writes=[b_dst])

            sch.fence()
            with ExitStack() as p2b:
                be = p2b.enter_context
                wv_sb = be(nc.sbuf_tensor(_PFX + "wv1_sb", [128, 8 * 8 * 128], BF16))
                b_wv = Buf("wv1")
                sch.dma("gpsimd", "wv1",
                        wv_sb[:, :].rearrange("p (dc cc c) -> p dc cc c",
                                              cc=8, dc=8),
                        wqkv_r.rearrange("dc p cc c -> p dc cc c")[:, :, 16:24, :],
                        writes=[b_wv])
                vbrow = be(nc.sbuf_tensor(_PFX + "vbrow", [1, 3072], BF16))
                b_vb = Buf("vbrow")
                sch.dma("gpsimd", "vb", vbrow[:, :], qkvb_row, writes=[b_vb])
                sch.dma("gpsimd", "vslab1", vaug1[:, :], vslab1, writes=[b_va])
                ppv = be(nc.psum_tensor(_PFX + "ppv", [128, 1024], F32))
                b_ppv = [Buf("ppv0"), Buf("ppv1")]
                for kt in range(NT_EXT):
                    for hp in range(8):
                        bank = (kt * 8 + hp) % 2
                        po = ppv[:, bank * 512:bank * 512 + 128]

                        def mm(tns, kt=kt, hp=hp, po=po):
                            for dc in range(8):
                                tns.matmul(
                                    po,
                                    xnf1[:, dc * T_EXT + kt * 128:dc * T_EXT + (kt + 1) * 128],
                                    wv_sb[:, (dc * 8 + hp) * 128:(dc * 8 + hp + 1) * 128],
                                    start=(dc == 0), stop=False)
                            return tns.matmul(
                                po, consts["ones_row"][:, :],
                                vbrow[:, (16 + hp) * 128:(17 + hp) * 128],
                                start=False, stop=True)
                        sch.run("tensor", mm, reads=[b_wv, b_xnf, b_vb, cb],
                                writes=[b_ppv[bank]])
                        base = (kt * 16 + hp * 2) * 65

                        def ev(v, po=po, base=base):
                            return v.tensor_copy(
                                bass.AP(vaug1, base,
                                        [[VR1, 128], [65, 2], [1, 64]]),
                                po.rearrange("p (h c) -> p h c", h=2))
                        sch.run("vector", ev, reads=[b_ppv[bank]], writes=[b_va])

        sch.fence()
        with ExitStack() as ph34:
            off1 = ph34.enter_context(
                nc.sbuf_tensor(_PFX + "off1", [128, 8 * T_OWN], BF16))
            run_p34(nc, sch, ctx, ph34, consts, cb, Q1, K1, b_Q1, b_K1,
                    vaug1, b_va, off1, b_off, wmask_r, wo1_r, wob_row, o_s1,
                    bcol1, b_os1, final_wait)
        sch.fence()


def run_p34(nc, sch, ctx, ph34, consts, cb, Q1, K1, b_Q1, b_K1, vaug1, b_va,
            off1, b_off, wmask_r, wo1_r, wob_row, o_s1, bcol1, b_os1,
            final_wait):
        with ExitStack() as ph:
            pe = ph.enter_context
            wm_sb = pe(nc.sbuf_tensor(_PFX + "wm_sb", [128, 6 * 512], F32))
            b_wm = Buf("wm")
            bc1_sb = pe(nc.sbuf_tensor(_PFX + "bc1_sb", [128, NT_EXT], F32))
            b_bc1 = Buf("bc1c")
            sch.dma("gpsimd", "bc1", bc1_sb[:, :], bcol1, writes=[b_bc1])
            sch.dma("gpsimd", "wm",
                    wm_sb[:, :].rearrange("p (j c) -> p j c", j=6),
                    wmask_r.rearrange("j p c -> p j c"), writes=[b_wm])
            pT = [pe(nc.sbuf_tensor(_PFX + f"pT1_{i}", [128, 512], BF16))
                  for i in range(12)]
            b_pT = [Buf("pT6a"), Buf("pT6b")]
            smsk = [pe(nc.sbuf_tensor(_PFX + f"smsk1_{i}", [128, 512], F32))
                    for i in range(12)]
            b_smsk = [Buf("sm6a"), Buf("sm6b")]
            recip = [pe(nc.sbuf_tensor(_PFX + f"rc1_{i}", [1, 512], F32))
                     for i in range(2)]
            b_rc = [Buf("rc0"), Buf("rc1")]
            bc_sb = [pe(nc.sbuf_tensor(_PFX + f"bc1_{i}", [64, 512], F32))
                     for i in range(2)]
            b_bc = [Buf("bc0"), Buf("bc1")]
            pS = [pe(nc.psum_tensor(_PFX + f"pS1p_{i}", [128, 1024], F32))
                  for i in range(3)]
            b_pS = Buf("pS6")
            pO = [pe(nc.psum_tensor(_PFX + f"pO1_{i}", [65, 512], F32)) for i in range(2)]
            b_pO = [Buf("pO0"), Buf("pO1")]

            pending = [None]

            def flush():
                if pending[0] is not None:
                    pending[0]()
                    pending[0] = None

            for k in range(4):
                for h in range(16):
                    hp, half = h // 2, h % 2
                    obank = (k * 16 + h) % 2
                    par = h % 2
                    q_ap = Q1[64 * half:64 * half + 64,
                              hp * T_EXT + 128 + 512 * k:
                              hp * T_EXT + 128 + 512 * k + 512]

                    def qk_emit(t, k=k, hp=hp, half=half, q=q_ap):
                        last = None
                        for j in range(6):
                            kbe = 4 * k + j
                            last = t.matmul(
                                pS[j // 2][:, (j % 2) * 512:(j % 2 + 1) * 512],
                                K1[64 * half:64 * half + 64,
                                   hp * T_EXT + kbe * 128:hp * T_EXT + (kbe + 1) * 128],
                                q, start=True, stop=True)
                        return last
                    sch.run("tensor", qk_emit, reads=[b_K1, b_Q1], writes=[b_pS])

                    def m_emit(v, par=par):
                        last = None
                        for j in range(6):
                            last = v.tensor_tensor(
                                smsk[par * 6 + j][:, :],
                                pS[j // 2][:, (j % 2) * 512:(j % 2 + 1) * 512],
                                wm_sb[:, j * 512:(j + 1) * 512], ALU.add)
                        return last
                    sch.run("vector", m_emit, reads=[b_pS, b_wm],
                            writes=[b_smsk[par]])

                    def e_emit(sce, k=k, par=par):
                        last = None
                        for j in range(6):
                            kbe = 4 * k + j
                            last = sce.activation(
                                pT[par * 6 + j][:, :], smsk[par * 6 + j][:, :],
                                AF.Exp, scale=0.125,
                                bias=bc1_sb[:, kbe:kbe + 1])
                        return last
                    sch.run("scalar", e_emit, reads=[b_smsk[par], b_bc1],
                            writes=[b_pT[par]])

                    def tail(k=k, h=h, hp=hp, half=half, ob=obank, par=par):
                        def pv_emit(t):
                            last = None
                            for j in range(6):
                                kbe = 4 * k + j
                                last = t.matmul(
                                    pO[ob][:, :],
                                    vaug1[:, (kbe * 16 + h) * 65:(kbe * 16 + h) * 65 + 65],
                                    pT[par * 6 + j][:, :], start=(j == 0),
                                    stop=(j == 5))
                            return last
                        sch.run("tensor", pv_emit, reads=[b_va, b_pT[par]],
                                writes=[b_pO[ob]])
                        sch.run("vector",
                                lambda v: v.reciprocal(recip[ob][:, :],
                                                       pO[ob][64:65, :]),
                                reads=[b_pO[ob]], writes=[b_rc[ob]])
                        sch.run("tensor",
                                lambda t: t.matmul(
                                    pS[0][0:64, 0:512],
                                    consts["ones_rowf"][:, 0:64],
                                    recip[ob][:, :], start=True, stop=True),
                                reads=[b_rc[ob]], writes=[b_pS])
                        sch.run("vector",
                                lambda v: v.tensor_copy(bc_sb[ob][:, :],
                                                        pS[0][0:64, 0:512]),
                                reads=[b_pS], writes=[b_bc[ob]])
                        ocol = hp * T_OWN + 512 * k
                        sch.run("vector",
                                lambda v: v.tensor_tensor(
                                    off1[64 * half:64 * half + 64, ocol:ocol + 512],
                                    pO[ob][0:64, :], bc_sb[ob][:, :], ALU.mult),
                                reads=[b_pO[ob], b_bc[ob]], writes=[b_off])
                    flush()
                    pending[0] = tail
            flush()
        sch.fence()
        with ExitStack() as ph:
            pe = ph.enter_context
            wo_sb = pe(nc.sbuf_tensor(_PFX + "wo1_sb", [128, 8 * D], BF16))
            b_wo = Buf("wo1")
            sch.dma("gpsimd", "wo1",
                    wo_sb[:, :].rearrange("p (cc o) -> p cc o", cc=8),
                    wo1_r.rearrange("cc p o -> p cc o"), writes=[b_wo])
            wob_sb = pe(nc.sbuf_tensor(_PFX + "wob_sb", [1, D], BF16))
            b_wob = Buf("wob")
            sch.dma("gpsimd", "wob", wob_sb[:, :], wob_row, writes=[b_wob])
            ps = pe(nc.psum_tensor(_PFX + "psD", [128, 1024], F32))
            b_ps = [Buf("psD0"), Buf("psD1")]
            s_sb = [pe(nc.sbuf_tensor(_PFX + f"s1sb{i}", [128, 512], F32)) for i in range(2)]
            b_ssb = [Buf("s1sb0"), Buf("s1sb1")]
            for t in range(NT_OWN):
                for oc in range(2):
                    bank = (t * 2 + oc) % 2
                    po = ps[:, bank * 512:(bank + 1) * 512]

                    def mm(tns, t=t, oc=oc, po=po):
                        for cc in range(8):
                            tns.matmul(
                                po,
                                off1[:, cc * T_OWN + t * 128:cc * T_OWN + (t + 1) * 128],
                                wo_sb[:, cc * D + oc * 512:cc * D + (oc + 1) * 512],
                                start=(cc == 0), stop=False)
                        return tns.matmul(po, consts["ones_row"][:, :],
                                          wob_sb[:, oc * 512:(oc + 1) * 512],
                                          start=False, stop=True)
                    sch.run("tensor", mm, reads=[b_off, b_wo, b_wob, cb],
                            writes=[b_ps[bank]])
                    sch.run("vector",
                            lambda v, bank=bank, po=po:
                            v.tensor_copy(s_sb[bank][:, :], po),
                            reads=[b_ps[bank]], writes=[b_ssb[bank]])
                    sch.dma("scalar", f"s1o{bank}",
                            o_s1[t * 128:(t + 1) * 128, oc * 512:(oc + 1) * 512],
                            s_sb[bank][:, :], reads=[b_ssb[bank]],
                            writes=[b_os1])
            if final_wait:
                for kk in ("s1o0", "s1o1"):
                    sch._wait("scalar",
                              (f"dma_{kk}", sch.dsems[kk], sch.dcnt[kk]))


def host_A1_inputs(core, x1_full, Wqkv, bqkv, Wo, bo, rope_cache):
    b, h = core // 2, core % 2
    own0 = 2048 * h
    lo, hi = own0 - HALO, own0 + T_OWN + HALO
    slo, shi = max(lo, 0), min(hi, SEQ)
    x1e = np.zeros((T_EXT, D), np.float32)
    x1e[slo - lo:shi - lo] = to_f32(x1_full[b])[slo:shi]
    wqkv_r = to_f32(Wqkv).reshape(8, 128, 24, 128)
    qkvb_c = to_f32(bqkv).reshape(24, 128).T
    qkvb_row = to_f32(bqkv).reshape(1, 3072)
    cache = to_f32(rope_cache)
    pos = (lo + np.arange(T_EXT)) % 256
    pos = np.where(pos < 0, pos + 256, pos)
    cosv = cache[pos][:, :, 0]
    sinv = cache[pos][:, :, 1]
    cos64 = np.repeat(cosv, 2, axis=1)
    sin64 = np.repeat(sinv, 2, axis=1)
    sgn = np.tile(np.array([-1.0, 1.0], np.float32), 32)
    sin64s = sin64 * sgn[None, :]
    crope = np.concatenate([cos64, cos64], axis=1).T
    srope = np.concatenate([sin64s, sin64s], axis=1).T
    p2 = np.zeros((128, 128), np.float32)
    for i in range(64):
        p2[2 * i, 2 * i + 1] = 1.0
        p2[2 * i + 1, 2 * i] = 1.0
    wm = np.zeros((6, 128, 512), np.float32)
    for j in range(6):
        kr = np.arange(128)[:, None]
        ql = np.arange(512)[None, :]
        rel = 128 * (j - 1) + kr - ql
        wm[j] = np.where(np.abs(rel) <= 128, 0.0, -1e9)
    vones = np.zeros(T_EXT, np.float32)
    vones[slo - lo:shi - lo] = 1.0
    vs = np.zeros((128, VR1), np.float32)
    for kt in range(NT_EXT):
        for hh in range(16):
            vs[:, (kt * 16 + hh) * 65 + 64] = vones[kt * 128:(kt + 1) * 128]
    wo1_r = to_f32(Wo).reshape(8, 128, D)
    wob = to_f32(bo).reshape(1, D)
    bc1 = np.zeros((128, NT_EXT), np.float32)
    for kt in range(NT_EXT):
        bc1[:, kt] = np.where(vones[kt * 128:(kt + 1) * 128] > 0, 0.0, -1.25e8)
    return dict(x1e=x1e, bcol1=bc1,
                wqkv_r=wqkv_r, qkvb_c=np.ascontiguousarray(qkvb_c),
                qkvb_row=qkvb_row, crope_r=np.ascontiguousarray(crope),
                srope_r=np.ascontiguousarray(srope), p2_r=p2, wmask_r=wm,
                vslab1=vs, wo1_r=wo1_r, wob_row=wob, **host_consts_named())



F1_GRP = [(0, 512), (512, 512), (1024, 512), (1536, 512)]


def build_launch_F1():
    nc = bass.Bass("TRN2", target_bir_lowering=False)
    x1o = nc.dram_tensor("x1o", [T_OWN, D], F32, kind="ExternalInput").ap()
    s1 = nc.dram_tensor("s1", [T_OWN, D], F32, kind="ExternalInput").ap()
    r1w = nc.dram_tensor("r1w", [D, 9], F32, kind="ExternalInput").ap()
    r1b = nc.dram_tensor("r1b", [1, 9], F32, kind="ExternalInput").ap()
    w1_r = nc.dram_tensor("w1f_r", [8, 8, 128, 32, 128], F32, kind="ExternalInput").ap()
    w2_r = nc.dram_tensor("w2f_r", [8, 32, 128, 2, 512], F32, kind="ExternalInput").ap()
    o_out = nc.dram_tensor("o_out", [T_OWN, D], F32, kind="ExternalOutput").ap()
    with ExitStack() as ctx:
        sch = Sched(nc, ctx)
        consts, cb = declare_consts(nc, sch, ctx)
        emit_F1(nc, sch, consts, cb, x1o, None, s1, None, r1w, r1b, w1_r,
                w2_r, o_out, final_wait=True)
    return nc


F1_SPARSE = True
F1_C = 1024


def emit_F1(*a, **kw):
    if F1_SPARSE:
        return emit_F1_sparse(*a, **kw)
    kw.pop("interleave_fn", None)
    return emit_F1_dense(*a, **kw)


def emit_F1_sparse(nc, sch, consts, cb, x1o, b_x1o, s1, b_s1, r1w, r1b,
                   w1_r, w2_r, o_out, final_wait=False, interleave_fn=None):
    global _PFX
    _PFX = "f1_"
    C = F1_C
    SLOTS = 8 * C
    slab = nc.dram_tensor("f1_slab", [SLOTS, D], BF16).ap()
    yslab = nc.dram_tensor("f1_yslab", [SLOTS, D], BF16).ap()
    b_slab = Buf("slab")
    b_ysl = [Buf(f"ysl{e}") for e in range(8)]
    with ExitStack() as ctx:
        e_ = ctx.enter_context
        b_acc = Buf("acc")
        gates = e_(nc.sbuf_tensor(_PFX + "gates", [128, NT_OWN * 9], F32))
        b_gates = Buf("gates")
        stats = e_(nc.sbuf_tensor(_PFX + "stats", [128, NT_OWN * 2], F32))
        b_stats = Buf("stats")
        idx_sb = e_(nc.sbuf_tensor(_PFX + "idx", [128, NT_OWN * 9], mybir.dt.uint32))
        b_idx = Buf("idx")
        baseT = e_(nc.sbuf_tensor(_PFX + "baseT", [1, 9], F32))
        b_base = Buf("baseT")
        r1w_sb = e_(nc.sbuf_tensor(_PFX + "r1w_sb", [128, 72], F32))
        r1b_sb = e_(nc.sbuf_tensor(_PFX + "r1b_sb", [1, 9], F32))
        b_rw = Buf("r1w")
        sch.dma("gpsimd", "rwld", r1w_sb[:, :].rearrange("p (c e) -> p c e", c=8),
                r1w.rearrange("(c p) e -> p c e", p=128), writes=[b_rw])
        sch.dma("gpsimd", "rwld", r1b_sb[:, :], r1b, writes=[b_rw])

        ln_tmp = mk_ln_tmp(nc, ctx, "lnF1")
        breg = ctx.enter_context(nc.gpsimd.register("f1bnd"))
        sch.run("gpsimd", lambda g: g.reg_mov(breg, C - 1))
        for zi in range(SLOTS // 128):
            sch.dma("gpsimd", "zf", slab[zi * 128:(zi + 1) * 128, :],
                    consts["zeros_t"][:, :], reads=[cb], writes=[b_slab])
            sch.dma("gpsimd", f"zfy{zi // 8}", yslab[zi * 128:(zi + 1) * 128, :],
                    consts["zeros_t"][:, :], reads=[cb], writes=[b_ysl[zi // 8]])

        with ExitStack() as ph:
            pe = ph.enter_context
            t2_tmp = mk_top2_tmp(nc, ph, "t2F1", 9)
            NSLOT = 3
            xs = [pe(nc.sbuf_tensor(_PFX + f"f1x{i}", [128, D], F32))
                  for i in range(NSLOT)]
            xb = [Buf(f"f1x{i}") for i in range(NSLOT)]
            xn16b = [pe(nc.sbuf_tensor(_PFX + f"xn16_{i}", [128, D], BF16))
                     for i in range(2)]
            b_xn16b = [Buf(f"xn16_{i}") for i in range(2)]
            xn32b = [pe(nc.sbuf_tensor(_PFX + f"xn32_{i}", [128, D], F32))
                     for i in range(2)]
            b_xn32b = [Buf(f"xn32_{i}") for i in range(2)]
            xnf32 = pe(nc.sbuf_tensor(_PFX + "xnf32", [128, D], F32))
            b_xnf32 = Buf("xnf32")
            lgt = pe(nc.sbuf_tensor(_PFX + "lgt", [128, 9], F32))
            b_lgt = Buf("lgt")
            B1 = pe(nc.sbuf_tensor(_PFX + "B1", [128, 9], F32))
            B2 = pe(nc.sbuf_tensor(_PFX + "B2", [128, 9], F32))
            b_B1, b_B2 = Buf("B1"), Buf("B2")
            ind = pe(nc.sbuf_tensor(_PFX + "ind", [128, 9], BF16))
            b_ind = Buf("ind")
            indv = pe(nc.sbuf_tensor(_PFX + "indv", [128, 9], F32))
            b_indv = Buf("indv")
            over = pe(nc.sbuf_tensor(_PFX + "over", [128, 9], F32))
            b_over = Buf("over")
            df = pe(nc.sbuf_tensor(_PFX + "df", [128, 9], F32))
            b_df = Buf("df")
            ptf = pe(nc.psum_tensor(_PFX + "ptf", [128, 1024], F32))
            b_ptf = Buf("ptf")
            pr = pe(nc.psum_tensor(_PFX + "pr", [128, 9], F32))
            b_pr = Buf("pr")
            pPx = pe(nc.psum_tensor(_PFX + "pPx", [128, 9], F32))
            b_pPx = Buf("pPx")
            pcs = pe(nc.psum_tensor(_PFX + "pcs", [1, 9], F32))
            b_pcs = Buf("pcs")

            sch.run("vector", lambda v: v.memzero(baseT[:, :]), writes=[b_base])

            st2 = [None]

            def stage2(t, p):
                def t_emit(tns):
                    last = None
                    for ch in range(8):
                        last = tns.transpose(ptf[:, ch * 128:(ch + 1) * 128],
                                             xn32b[p][:, ch * 128:(ch + 1) * 128],
                                             consts["identf"][:, :])
                    return last
                sch.run("tensor", t_emit, reads=[b_xn32b[p], cb], writes=[b_ptf])

                def v_emit(v):
                    last = None
                    for ch in range(8):
                        last = v.tensor_copy(xnf32[:, ch * 128:(ch + 1) * 128],
                                             ptf[:, ch * 128:(ch + 1) * 128])
                    return last
                sch.run("vector", v_emit, reads=[b_ptf], writes=[b_xnf32])

                def r_emit(tns):
                    for ch in range(8):
                        tns.matmul(pr[:, :], xnf32[:, ch * 128:(ch + 1) * 128],
                                   r1w_sb[:, ch * 9:(ch + 1) * 9],
                                   start=(ch == 0), stop=False)
                    return tns.matmul(pr[:, :], consts["ones_rowf"][:, :],
                                      r1b_sb[:, :], start=False, stop=True)
                sch.run("tensor", r_emit, reads=[b_xnf32, b_rw, cb], writes=[b_pr])
                sch.run("vector", lambda v: v.tensor_copy(lgt[:, :], pr[:, :]),
                        reads=[b_pr], writes=[b_lgt])
                emit_top2(nc, sch, lgt, b_lgt, 9, t2_tmp,
                          {"B1": (B1, b_B1), "B2": (B2, b_B2)}, None,
                          with_z=True)
                sch.run("vector",
                        lambda v: v.tensor_add(
                            gates[:, t * 9:(t + 1) * 9], B1[:, :], B2[:, :]),
                        reads=[b_B1, b_B2], writes=[b_gates])
                sch.run("vector",
                        lambda v: v.tensor_single_scalar(
                            ind[:, :], gates[:, t * 9:(t + 1) * 9], 0.0,
                            ALU.is_gt),
                        reads=[b_gates], writes=[b_ind])
                sch.run("tensor",
                        lambda tns: tns.matmul(pcs[:, :],
                                               consts["ones_colb"][:, :],
                                               ind[:, :], start=True, stop=True),
                        reads=[b_ind, cb], writes=[b_pcs])

                def px_emit(tns):
                    tns.matmul(pPx[:, :], consts["lt_excl"][:, :], ind[:, :],
                               start=True, stop=False)
                    return tns.matmul(pPx[:, :], consts["ones_rowf"][:, :],
                                      baseT[:, :], start=False, stop=True)
                sch.run("tensor", px_emit, reads=[b_ind, b_base, cb],
                        writes=[b_pPx])
                sch.run("vector",
                        lambda v: v.tensor_scalar(indv[:, :], ind[:, :],
                                                  -1.0, 1.0, ALU.mult, ALU.add),
                        reads=[b_ind], writes=[b_indv])
                sch.run("vector",
                        lambda v: v.tensor_single_scalar(
                            over[:, :], pPx[:, :], float(C), ALU.is_ge),
                        reads=[b_pPx], writes=[b_over])
                sch.run("vector",
                        lambda v: v.scalar_tensor_tensor(
                            df[:, :], indv[:, :], 3e9, pPx[:, :],
                            ALU.mult, ALU.add),
                        reads=[b_indv, b_pPx], writes=[b_df])
                sch.run("vector",
                        lambda v: v.scalar_tensor_tensor(
                            df[:, :], over[:, :], 3e9, df[:, :],
                            ALU.mult, ALU.add),
                        reads=[b_over], writes=[b_df])
                sch.run("vector",
                        lambda v: v.tensor_copy(
                            idx_sb[:, t * 9:(t + 1) * 9], df[:, :]),
                        reads=[b_df], writes=[b_idx])
                sch.run("vector",
                        lambda v: v.tensor_tensor(baseT[:, :], baseT[:, :],
                                                  pcs[:, :], ALU.add),
                        reads=[b_pcs], writes=[b_base])
                for ei in range(8):
                    sch.idma("scat",
                             lambda g, t=t, p=p, ei=ei: g.indirect_dma_start(
                                 slab,
                                 bass.IndirectOffsetOnAxis(
                                     idx_sb[:, t * 9 + ei:t * 9 + ei + 1], 0),
                                 xn16b[p][:, :], None,
                                 element_offset=ei * C * D,
                                 bounds_check=breg, oob_is_err=False),
                             reads=[b_idx, b_xn16b[p]], writes=[b_slab])

            for t in range(NT_OWN):
                sl = t % NSLOT
                p = t % 2
                sch.dma("sync", f"f1x{sl}", xs[sl][:, :],
                        x1o[t * 128:(t + 1) * 128, :],
                        reads=[b_x1o] if b_x1o else None, writes=[xb[sl]])
                emit_ln_tile(nc, sch, cb, xb[sl], xs[sl],
                             ln_tmp if p == 0 else mk_f1_ln2(nc, ph),
                             want_f32=(xn32b[p], slice(0, D), b_xn32b[p]),
                             want_b16=(xn16b[p], slice(0, D), b_xn16b[p]),
                             stats_out=(stats, 2 * t, b_stats),
                             consts=consts)
                if st2[0]:
                    st2[0]()
                st2[0] = (lambda t=t, p=p: stage2(t, p))
            st2[0]()

        sch.fence()
        if interleave_fn is not None:
            interleave_fn()
            _set_pfx("f1_")
            sch.fence()
        with ExitStack() as ph:
            pe = ph.enter_context
            pz = pe(nc.psum_tensor(_PFX + "pz", [128, 1024], F32))
            b_pz = [Buf("pz0"), Buf("pz1")]
            py = pe(nc.psum_tensor(_PFX + "py", [128, 1024], F32))
            b_py = [Buf("py0"), Buf("py1")]
            ptb = pe(nc.psum_tensor(_PFX + "ptb", [128, 1024], BF16))
            b_ptb = Buf("ptb")
            xsl = [pe(nc.sbuf_tensor(_PFX + f"xsl{i}", [128, D], BF16))
                   for i in range(2)]
            b_xsl = [Buf(f"xsl{i}") for i in range(2)]
            xnfE = [pe(nc.sbuf_tensor(_PFX + f"xnfE{i}", [128, 8 * C], BF16))
                    for i in range(2)]
            b_xnfE = [Buf(f"xnfE{i}") for i in range(2)]
            hf = pe(nc.sbuf_tensor(_PFX + "hf", [128, 8 * C], BF16))
            b_hf = Buf("hf")
            w1_sb = [pe(nc.sbuf_tensor(_PFX + f"w1s{i}", [128, 8192], BF16))
                     for i in range(2)]
            w2_sb = [pe(nc.sbuf_tensor(_PFX + f"w2s{i}", [128, 8192], BF16))
                     for i in range(2)]
            b_w1 = [Buf("w1s0"), Buf("w1s1")]
            b_w2 = [Buf("w2s0"), Buf("w2s1")]
            yacc = pe(nc.sbuf_tensor(_PFX + "yacc", [128, C * 8], BF16))
            b_yacc = Buf("yacc")
            acc = pe(nc.sbuf_tensor(_PFX + "acc", [128, NT_OWN * 1024], F32))
            ygat = [pe(nc.sbuf_tensor(_PFX + f"ygat{i}", [128, D], BF16))
                    for i in range(2)]
            b_ygat = [Buf(f"ygat{i}") for i in range(2)]
            for i in range(2):
                sch.run("vector", lambda v, i=i: v.tensor_copy(
                    ygat[i][:, :], consts["zeros_t"][:, :]),
                        reads=[cb], writes=[b_ygat[i]])

            def load_slab(ei):
                ep = ei % 2
                for st in range(8):
                    ssl = st % 2
                    sch.dma("sync", f"xsl{ssl}", xsl[ssl][:, :],
                            slab[(ei * C + st * 128):(ei * C + (st + 1) * 128), :],
                            reads=[b_slab], writes=[b_xsl[ssl]])

            def trans_slab(ei):
                ep = ei % 2
                for st in range(8):
                    ssl = st % 2
                    emit_transpose8(nc, sch, xsl[ssl], b_xsl[ssl], ptb, b_ptb,
                                    xnfE[ep],
                                    [ch * C + st * 128 for ch in range(8)],
                                    b_xnfE[ep], consts["identb"], cb)

            def gather_acc(ei):
                for t in range(NT_OWN):
                    gsl = t % 2
                    sch.idma(f"ygat{gsl}",
                             lambda g, t=t, ei=ei, gsl=gsl:
                             g.indirect_dma_start(
                                 ygat[gsl][:, :], None, yslab,
                                 bass.IndirectOffsetOnAxis(
                                     idx_sb[:, t * 9 + ei:t * 9 + ei + 1], 0),
                                 element_offset=ei * C * D,
                                 bounds_check=breg, oob_is_err=False),
                             reads=[b_ysl[ei], b_idx], writes=[b_ygat[gsl]])
                    if ei == 0:
                        sch.run("vector",
                                lambda v, t=t, ei=ei, gsl=gsl:
                                v.tensor_single_scalar(
                                    acc[:, t * 1024:(t + 1) * 1024],
                                    ygat[gsl][:, :],
                                    gates[:, t * 9 + ei:t * 9 + ei + 1],
                                    ALU.mult),
                                reads=[b_ygat[gsl], b_gates], writes=[b_acc])
                    else:
                        sch.run("vector",
                                lambda v, t=t, ei=ei, gsl=gsl:
                                v.scalar_tensor_tensor(
                                    acc[:, t * 1024:(t + 1) * 1024],
                                    ygat[gsl][:, :],
                                    gates[:, t * 9 + ei:t * 9 + ei + 1],
                                    acc[:, t * 1024:(t + 1) * 1024],
                                    ALU.mult, ALU.add),
                                reads=[b_ygat[gsl], b_gates], writes=[b_acc])

            def load_w(gidx):
                ei, hs = gidx // 4, gidx % 4
                wsl = gidx % 2
                sch.dma("gpsimd", f"w1_{wsl}",
                        w1_sb[wsl][:, :].rearrange(
                            "p (dc hc c) -> p dc hc c", dc=8, hc=8),
                        w1_r[ei].rearrange(
                            "dc d hc c -> d dc hc c")[:, :, hs * 8:(hs + 1) * 8, :],
                        writes=[b_w1[wsl]])

            def load_w2(gidx):
                ei, hs = gidx // 4, gidx % 4
                wsl = gidx % 2
                sch.dma("gpsimd", f"w2_{wsl}",
                        w2_sb[wsl][:, :].rearrange(
                            "p (hc dch o) -> p hc dch o", hc=8, dch=2),
                        w2_r[ei].rearrange(
                            "hc h dch o -> h hc dch o")[:, hs * 8:(hs + 1) * 8, :, :],
                        writes=[b_w2[wsl]])

            load_slab(0)
            trans_slab(0)
            load_w(0)
            load_w2(0)
            for ei in range(8):
                ep = ei % 2
                for hs in range(4):
                    gidx = ei * 4 + hs
                    wsl = gidx % 2
                    if gidx + 1 < 32:
                        load_w(gidx + 1)
                        load_w2(gidx + 1)
                    if hs == 1 and ei + 1 < 8:
                        load_slab(ei + 1)
                    if hs == 3 and ei + 1 < 8:
                        trans_slab(ei + 1)
                    for hc in range(8):
                        for gi in range(C // 512):
                            bank = (hc * 2 + gi) % 2
                            po = pz[:, bank * 512:(bank + 1) * 512]

                            def zmm(tns, wsl=wsl, hc=hc, gi=gi, po=po, ep=ep):
                                last = None
                                for dc in range(8):
                                    last = tns.matmul(
                                        po,
                                        w1_sb[wsl][:, (dc * 8 + hc) * 128:(dc * 8 + hc + 1) * 128],
                                        xnfE[ep][:, dc * C + gi * 512:dc * C + (gi + 1) * 512],
                                        start=(dc == 0), stop=(dc == 7))
                                return last
                            sch.run("tensor", zmm,
                                    reads=[b_w1[wsl], b_xnfE[ep], cb],
                                    writes=[b_pz[bank]])
                            sch.run("scalar",
                                    lambda s, hc=hc, gi=gi, po=po:
                                    s.activation(
                                        hf[:, hc * C + gi * 512:hc * C + (gi + 1) * 512],
                                        po, AF.Gelu),
                                    reads=[b_pz[bank]], writes=[b_hf])
                    for tt in range(C // 128):
                        for dch in range(2):
                            bank = (tt * 2 + dch) % 2
                            po = py[:, bank * 512:(bank + 1) * 512]

                            def ymm(tns, wsl=wsl, tt=tt, dch=dch, po=po):
                                last = None
                                for hc in range(8):
                                    last = tns.matmul(
                                        po,
                                        hf[:, hc * C + tt * 128:hc * C + (tt + 1) * 128],
                                        w2_sb[wsl][:, (hc * 2 + dch) * 512:(hc * 2 + dch + 1) * 512],
                                        start=(hc == 0), stop=(hc == 7))
                                return last
                            sch.run("tensor", ymm, reads=[b_w2[wsl], b_hf, cb],
                                    writes=[b_py[bank]])
                            ysl_ap = yacc[:, tt * 1024 + dch * 512:
                                          tt * 1024 + (dch + 1) * 512]
                            if hs == 0:
                                sch.run("vector",
                                        lambda v, po=po, a=ysl_ap:
                                        v.tensor_copy(a, po),
                                        reads=[b_py[bank]], writes=[b_yacc])
                            else:
                                sch.run("vector",
                                        lambda v, po=po, a=ysl_ap:
                                        v.tensor_tensor(a, a, po, ALU.add),
                                        reads=[b_py[bank]], writes=[b_yacc])
                            if hs == 3:
                                sch.dma("scalar" if dch == 0 else "sync",
                                        f"yo{bank}",
                                        yslab[ei * C + tt * 128:
                                              ei * C + (tt + 1) * 128,
                                              dch * 512:(dch + 1) * 512],
                                        ysl_ap, reads=[b_yacc],
                                        writes=[b_ysl[ei]])
                gather_acc(ei)

        sch.fence()
        with ExitStack() as ph:
            pe = ph.enter_context
            NSLOT = 2
            xs = [pe(nc.sbuf_tensor(_PFX + f"c3x{i}", [128, D], F32))
                  for i in range(NSLOT)]
            ss = [pe(nc.sbuf_tensor(_PFX + f"c3s{i}", [128, D], F32))
                  for i in range(NSLOT)]
            xbs = [Buf(f"c3x{i}") for i in range(NSLOT)]
            sbs = [Buf(f"c3s{i}") for i in range(NSLOT)]
            xn1t = pe(nc.sbuf_tensor(_PFX + "xn1t", [128, D], BF16))
            b_xn1t = Buf("xn1t")
            gsil = pe(nc.sbuf_tensor(_PFX + "gsil", [128, D], F32))
            b_g = Buf("gsil")
            outt = [pe(nc.sbuf_tensor(_PFX + f"outt{i}", [128, D], F32))
                    for i in range(NSLOT)]
            b_ot = [Buf(f"outt{i}") for i in range(NSLOT)]
            for t in range(NT_OWN):
                sl = t % NSLOT
                sch.dma("sync", f"c3x{sl}", xs[sl][:, :],
                        x1o[t * 128:(t + 1) * 128, :],
                        reads=[b_x1o] if b_x1o else None, writes=[xbs[sl]])
                sch.dma("sync", f"c3s{sl}", ss[sl][:, :],
                        s1[t * 128:(t + 1) * 128, :],
                        reads=[b_s1] if b_s1 else None, writes=[sbs[sl]])

                def nrm(s, t=t, sl=sl):
                    return s.activation(xn1t[:, :], xs[sl][:, :], AF.Identity,
                                        bias=stats[:, 2 * t + 1:2 * t + 2],
                                        scale=stats[:, 2 * t:2 * t + 1])
                sch.run("scalar", nrm, reads=[xbs[sl], b_stats], writes=[b_xn1t])
                sch.run("vector",
                        lambda v, t=t: v.scalar_tensor_tensor(
                            acc[:, t * 1024:(t + 1) * 1024], xn1t[:, :],
                            gates[:, t * 9 + 8:t * 9 + 9],
                            acc[:, t * 1024:(t + 1) * 1024],
                            ALU.mult, ALU.add),
                        reads=[b_xn1t, b_gates], writes=[b_acc])
                sch.run("scalar",
                        lambda s, t=t: s.activation(
                            gsil[:, :], acc[:, t * 1024:(t + 1) * 1024], AF.Silu),
                        reads=[b_acc], writes=[b_g])
                sch.run("vector",
                        lambda v, sl=sl: v.tensor_mul(gsil[:, :], gsil[:, :],
                                                      ss[sl][:, :]),
                        reads=[sbs[sl]], writes=[b_g])
                sch.run("vector",
                        lambda v, sl=sl: v.tensor_add(outt[sl][:, :], gsil[:, :],
                                                      xs[sl][:, :]),
                        reads=[b_g, xbs[sl]], writes=[b_ot[sl]])
                sch.dma("scalar", f"fno{sl}", o_out[t * 128:(t + 1) * 128, :],
                        outt[sl][:, :], reads=[b_ot[sl]])
            if final_wait:
                for kk in ("fno0", "fno1"):
                    sch._wait("scalar",
                              (f"dma_{kk}", sch.dsems[kk], sch.dcnt[kk]))


_f1_ln2_cache = {}


def mk_f1_ln2(nc, ctx):
    key = id(nc)
    if key not in _f1_ln2_cache:
        _f1_ln2_cache[key] = mk_ln_tmp(nc, ctx, "lnF1b")
    return _f1_ln2_cache[key]


def emit_F1_dense(nc, sch, consts, cb, x1o, b_x1o, s1, b_s1, r1w, r1b, w1_r, w2_r,
            o_out, final_wait=False):
    global _PFX
    _PFX = "f1_"
    with ExitStack() as ctx:
        e = ctx.enter_context

        xnf16 = e(nc.sbuf_tensor(_PFX + "xnf16", [128, 8 * T_OWN], BF16))
        b_xnf = Buf("xnf16")
        acc = e(nc.sbuf_tensor(_PFX + "acc", [128, NT_OWN * 1024], F32))
        b_acc = Buf("acc")
        gates = e(nc.sbuf_tensor(_PFX + "gates", [128, NT_OWN * 9], F32))
        b_gates = Buf("gates")
        stats = e(nc.sbuf_tensor(_PFX + "stats", [128, NT_OWN * 2], F32))
        b_stats = Buf("stats")
        r1w_sb = e(nc.sbuf_tensor(_PFX + "r1w_sb", [128, 72], F32))
        r1b_sb = e(nc.sbuf_tensor(_PFX + "r1b_sb", [1, 9], F32))
        b_rw = Buf("r1w")
        sch.dma("gpsimd", "rwld", r1w_sb[:, :].rearrange("p (c e) -> p c e", c=8),
                r1w.rearrange("(c p) e -> p c e", p=128), writes=[b_rw])
        sch.dma("gpsimd", "rwld", r1b_sb[:, :], r1b, writes=[b_rw])

        ln_tmp = mk_ln_tmp(nc, ctx, "lnF1")
        t2_tmp = mk_top2_tmp(nc, ctx, "t2F1", 9)

        with ExitStack() as ph:
            pe = ph.enter_context
            NSLOT = 3
            xs = [pe(nc.sbuf_tensor(_PFX + f"f1x{i}", [128, D], F32)) for i in range(NSLOT)]
            xb = [Buf(f"f1x{i}") for i in range(NSLOT)]
            xn32 = pe(nc.sbuf_tensor(_PFX + "xn32", [128, D], F32))
            b_xn32 = Buf("xn32")
            xn16 = pe(nc.sbuf_tensor(_PFX + "xn16", [128, D], BF16))
            b_xn16 = Buf("xn16")
            xnf32 = pe(nc.sbuf_tensor(_PFX + "xnf32", [128, D], F32))
            b_xnf32 = Buf("xnf32")
            lgt = pe(nc.sbuf_tensor(_PFX + "lgt", [128, 9], F32))
            b_lgt = Buf("lgt")
            B1 = pe(nc.sbuf_tensor(_PFX + "B1", [128, 9], F32))
            B2 = pe(nc.sbuf_tensor(_PFX + "B2", [128, 9], F32))
            b_B1, b_B2 = Buf("B1"), Buf("B2")
            ptf = pe(nc.psum_tensor(_PFX + "ptf", [128, 1024], F32))
            b_ptf = Buf("ptf")
            ptb = pe(nc.psum_tensor(_PFX + "ptb", [128, 1024], BF16))
            b_ptb = Buf("ptb")
            pr = pe(nc.psum_tensor(_PFX + "pr", [128, 9], F32))
            b_pr = Buf("pr")

            ln2 = [ln_tmp, mk_ln_tmp(nc, ph, "lnF1b")]
            xn16b = [xn16, pe(nc.sbuf_tensor(_PFX + "xn16b", [128, D], BF16))]
            b_xn16b = [b_xn16, Buf("xn16b")]
            xn32b = [xn32, pe(nc.sbuf_tensor(_PFX + "xn32b", [128, D], F32))]
            b_xn32b = [b_xn32, Buf("xn32b")]
            st2 = [None]

            def stage2_f1(t, p):
                emit_transpose8(nc, sch, xn16b[p], b_xn16b[p], ptb, b_ptb,
                                xnf16, [ch * T_OWN + t * 128 for ch in range(8)],
                                b_xnf, consts["identb"], cb)
                def t_emit(tns):
                    last = None
                    for ch in range(8):
                        last = tns.transpose(ptf[:, ch * 128:(ch + 1) * 128],
                                             xn32b[p][:, ch * 128:(ch + 1) * 128],
                                             consts["identf"][:, :])
                    return last
                sch.run("tensor", t_emit, reads=[b_xn32b[p], cb], writes=[b_ptf])

                def v_emit(v):
                    last = None
                    for ch in range(8):
                        last = v.tensor_copy(xnf32[:, ch * 128:(ch + 1) * 128],
                                             ptf[:, ch * 128:(ch + 1) * 128])
                    return last
                sch.run("vector", v_emit, reads=[b_ptf], writes=[b_xnf32])

                def r_emit(tns):
                    for ch in range(8):
                        tns.matmul(pr[:, :], xnf32[:, ch * 128:(ch + 1) * 128],
                                   r1w_sb[:, ch * 9:(ch + 1) * 9],
                                   start=(ch == 0), stop=False)
                    return tns.matmul(pr[:, :], consts["ones_rowf"][:, :],
                                      r1b_sb[:, :], start=False, stop=True)
                sch.run("tensor", r_emit, reads=[b_xnf32, b_rw, cb], writes=[b_pr])
                sch.run("vector", lambda v: v.tensor_copy(lgt[:, :], pr[:, :]),
                        reads=[b_pr], writes=[b_lgt])
                emit_top2(nc, sch, lgt, b_lgt, 9, t2_tmp,
                          {"B1": (B1, b_B1), "B2": (B2, b_B2)}, None,
                          with_z=True)
                sch.run("vector",
                        lambda v: v.tensor_add(
                            gates[:, t * 9:(t + 1) * 9], B1[:, :], B2[:, :]),
                        reads=[b_B1, b_B2], writes=[b_gates])

            for t in range(NT_OWN):
                sl = t % NSLOT
                p = t % 2
                sch.dma("sync", f"f1x{sl}", xs[sl][:, :],
                        x1o[t * 128:(t + 1) * 128, :],
                        reads=[b_x1o] if b_x1o else None, writes=[xb[sl]])
                emit_ln_tile(nc, sch, cb, xb[sl], xs[sl], ln2[p],
                             want_f32=(xn32b[p], slice(0, D), b_xn32b[p]),
                             want_b16=(xn16b[p], slice(0, D), b_xn16b[p]),
                             stats_out=(stats, 2 * t, b_stats),
                             consts=consts)
                if st2[0]:
                    st2[0]()
                st2[0] = (lambda t=t, p=p: stage2_f1(t, p))
            st2[0]()

        sch.fence()
        with ExitStack() as ph:
            pe = ph.enter_context
            pz = pe(nc.psum_tensor(_PFX + "pz", [128, 1024], F32))
            b_pz = [Buf("pz0"), Buf("pz1")]
            py = pe(nc.psum_tensor(_PFX + "py", [128, 1024], F32))
            b_py = [Buf("py0"), Buf("py1")]
            hf = pe(nc.sbuf_tensor(_PFX + "hf", [128, 8 * T_OWN], BF16))
            b_hf = Buf("hf")
            w1_sb = [pe(nc.sbuf_tensor(_PFX + f"w1s{i}", [128, 8192], BF16)) for i in range(2)]
            w2_sb = [pe(nc.sbuf_tensor(_PFX + f"w2s{i}", [128, 8192], BF16)) for i in range(2)]
            b_w1 = [Buf("w1s0"), Buf("w1s1")]
            b_w2 = [Buf("w2s0"), Buf("w2s1")]

            first = True
            for ei in range(8):
                for hs in range(4):
                    wsl = (ei * 4 + hs) % 2
                    sch.dma("gpsimd", f"w1_{wsl}",
                            w1_sb[wsl][:, :].rearrange(
                                "p (dc hc c) -> p dc hc c", dc=8, hc=8),
                            w1_r[ei].rearrange(
                                "dc d hc c -> d dc hc c")[:, :, hs * 8:(hs + 1) * 8, :],
                            writes=[b_w1[wsl]])
                    sch.dma("gpsimd", f"w2_{wsl}",
                            w2_sb[wsl][:, :].rearrange(
                                "p (hc dch o) -> p hc dch o", hc=8, dch=2),
                            w2_r[ei].rearrange(
                                "hc h dch o -> h hc dch o")[:, hs * 8:(hs + 1) * 8, :, :],
                            writes=[b_w2[wsl]])
                    for hc in range(8):
                        for gi, (gst, gn) in enumerate(F1_GRP):
                            bank = (hc * 4 + gi) % 2
                            po = pz[:, bank * 512:(bank + 1) * 512]

                            def zmm(tns, wsl=wsl, hc=hc, gst=gst, po=po):
                                last = None
                                for dc in range(8):
                                    last = tns.matmul(
                                        po,
                                        w1_sb[wsl][:, (dc * 8 + hc) * 128:(dc * 8 + hc + 1) * 128],
                                        xnf16[:, dc * T_OWN + gst:dc * T_OWN + gst + 512],
                                        start=(dc == 0), stop=(dc == 7))
                                return last
                            sch.run("tensor", zmm, reads=[b_w1[wsl], b_xnf, cb],
                                    writes=[b_pz[bank]])
                            sch.run("scalar",
                                    lambda s, hc=hc, gst=gst, po=po:
                                    s.activation(
                                        hf[:, hc * T_OWN + gst:hc * T_OWN + gst + 512],
                                        po, AF.Gelu),
                                    reads=[b_pz[bank]], writes=[b_hf])
                    for tt in range(NT_OWN):
                        for dch in range(2):
                            bank = (tt * 2 + dch) % 2
                            po = py[:, bank * 512:(bank + 1) * 512]

                            def ymm(tns, wsl=wsl, tt=tt, dch=dch, po=po):
                                last = None
                                for hc in range(8):
                                    last = tns.matmul(
                                        po,
                                        hf[:, hc * T_OWN + tt * 128:hc * T_OWN + (tt + 1) * 128],
                                        w2_sb[wsl][:, (hc * 2 + dch) * 512:(hc * 2 + dch + 1) * 512],
                                        start=(hc == 0), stop=(hc == 7))
                                return last
                            sch.run("tensor", ymm, reads=[b_w2[wsl], b_hf, cb],
                                    writes=[b_py[bank]])
                            aslice = acc[:, tt * 1024 + dch * 512:tt * 1024 + (dch + 1) * 512]
                            if first:
                                sch.run("vector",
                                        lambda v, tt=tt, ei=ei, po=po, a=aslice:
                                        v.tensor_single_scalar(
                                            a, po,
                                            gates[:, tt * 9 + ei:tt * 9 + ei + 1],
                                            ALU.mult),
                                        reads=[b_py[bank], b_gates], writes=[b_acc])
                            else:
                                sch.run("vector",
                                        lambda v, tt=tt, ei=ei, po=po, a=aslice:
                                        v.scalar_tensor_tensor(
                                            a, po,
                                            gates[:, tt * 9 + ei:tt * 9 + ei + 1],
                                            a, ALU.mult, ALU.add),
                                        reads=[b_py[bank], b_gates], writes=[b_acc])
                    first = False

        sch.fence()
        with ExitStack() as ph:
            pe = ph.enter_context
            NSLOT = 2
            xs = [pe(nc.sbuf_tensor(_PFX + f"c3x{i}", [128, D], F32)) for i in range(NSLOT)]
            ss = [pe(nc.sbuf_tensor(_PFX + f"c3s{i}", [128, D], F32)) for i in range(NSLOT)]
            xbs = [Buf(f"c3x{i}") for i in range(NSLOT)]
            sbs = [Buf(f"c3s{i}") for i in range(NSLOT)]
            xn1t = pe(nc.sbuf_tensor(_PFX + "xn1t", [128, D], BF16))
            b_xn1t = Buf("xn1t")
            gsil = pe(nc.sbuf_tensor(_PFX + "gsil", [128, D], F32))
            b_g = Buf("gsil")
            outt = [pe(nc.sbuf_tensor(_PFX + f"outt{i}", [128, D], F32)) for i in range(NSLOT)]
            b_ot = [Buf(f"outt{i}") for i in range(NSLOT)]
            for t in range(NT_OWN):
                sl = t % NSLOT
                sch.dma("sync", f"c3x{sl}", xs[sl][:, :],
                        x1o[t * 128:(t + 1) * 128, :],
                        reads=[b_x1o] if b_x1o else None, writes=[xbs[sl]])
                sch.dma("sync", f"c3s{sl}", ss[sl][:, :],
                        s1[t * 128:(t + 1) * 128, :],
                        reads=[b_s1] if b_s1 else None, writes=[sbs[sl]])
                def nrm(s, t=t, sl=sl):
                    return s.activation(xn1t[:, :], xs[sl][:, :], AF.Identity,
                                        bias=stats[:, 2 * t + 1:2 * t + 2],
                                        scale=stats[:, 2 * t:2 * t + 1])
                sch.run("scalar", nrm, reads=[xbs[sl], b_stats], writes=[b_xn1t])
                sch.run("vector",
                        lambda v, t=t: v.scalar_tensor_tensor(
                            acc[:, t * 1024:(t + 1) * 1024], xn1t[:, :],
                            gates[:, t * 9 + 8:t * 9 + 9],
                            acc[:, t * 1024:(t + 1) * 1024],
                            ALU.mult, ALU.add),
                        reads=[b_xn1t, b_gates], writes=[b_acc])
                sch.run("scalar",
                        lambda s, t=t: s.activation(
                            gsil[:, :], acc[:, t * 1024:(t + 1) * 1024], AF.Silu),
                        reads=[b_acc], writes=[b_g])
                sch.run("vector",
                        lambda v, sl=sl: v.tensor_mul(gsil[:, :], gsil[:, :],
                                                      ss[sl][:, :]),
                        reads=[sbs[sl]], writes=[b_g])
                sch.run("vector",
                        lambda v, sl=sl: v.tensor_add(outt[sl][:, :], gsil[:, :],
                                                      xs[sl][:, :]),
                        reads=[b_g, xbs[sl]], writes=[b_ot[sl]])
                sch.dma("scalar", f"fno{sl}", o_out[t * 128:(t + 1) * 128, :],
                        outt[sl][:, :], reads=[b_ot[sl]])
            if final_wait:
                for kk in ("fno0", "fno1"):
                    sch._wait("scalar",
                              (f"dma_{kk}", sch.dsems[kk], sch.dcnt[kk]))


def host_F1_inputs(core, x1_full, s1_full, r1w, r1b, w1, w2):
    b, h = core // 2, core % 2
    own0 = 2048 * h
    x1e = to_f32(x1_full[b])[own0:own0 + T_OWN]
    s1e = to_f32(s1_full)
    w1_r = to_f32(w1).reshape(8, 8, 128, 32, 128)
    w2_r = to_f32(w2).reshape(8, 32, 128, 2, 512)
    return dict(x1o=x1e, s1=s1e, r1w=to_f32(r1w), r1b=to_f32(r1b).reshape(1, 9),
                w1f_r=w1_r, w2f_r=w2_r, **host_consts_named())



def build_fused():
    nc = bass.Bass("TRN2", target_bir_lowering=False)
    T = lambda n, sh: nc.dram_tensor(n, sh, F32, kind="ExternalInput").ap()
    x_all = T("x_all", [BATCH * SEQ, D])
    xf = T("xf", [SEQ, D])
    xe = T("xe", [T_EXT, D])
    wq_r = T("wq_r", [NHP, 8, 128, 128])
    wk_r = T("wk_r", [NHP, 8, 128, 128])
    wv_r = T("wv_r", [NHP, 8, 128, 128])
    woT_r = T("woT_r", [8, 128, D])
    dmask = T("dmask", [NQT, A0_NMASK, 128, 512])
    bcol = T("bcol", [128, NQT, 32])
    vslab = T("vslab", [128, (SEQ // 128) * 4 * 65])
    rw = T("rw", [D, 8])
    rb = T("rb", [1, 8])
    flag = T("flag", [8, 2])
    w1_r = T("w1_r", [8, 8, 128, 8, 128])
    w2_r = T("w2_r", [8, 8, 128, 2, 512])
    b1_r = T("b1_r", [128, 8, 8])
    b2_r = T("b2_r", [8, D])
    eye8 = T("eye8", [8, 8 * 128])
    wqkv_r = T("wqkv_r", [8, 128, 24, 128])
    qkvb_c = T("qkvb_c", [128, 24])
    qkvb_row = T("qkvb_row", [1, 3072])
    crope_r = T("crope_r", [128, T_EXT])
    srope_r = T("srope_r", [128, T_EXT])
    p2_r = T("p2_r", [128, 128])
    wmask_r = T("wmask_r", [6, 128, 512])
    vslab1 = T("vslab1", [128, VR1])
    bcol1 = T("bcol1", [128, NT_EXT])
    wo1_r = T("wo1_r", [8, 128, D])
    wob_row = T("wob_row", [1, D])
    r1w = T("r1w", [D, 9])
    r1b = T("r1b", [1, 9])
    w1f_r = T("w1f_r", [8, 8, 128, 32, 128])
    w2f_r = T("w2f_r", [8, 32, 128, 2, 512])
    o_out = nc.dram_tensor("o_out", [T_OWN, D], F32, kind="ExternalOutput").ap()
    sp_sc = nc.dram_tensor("sp_sc", [8, 2], F32).ap()
    s_sc = nc.dram_tensor("s_sc", [T_EXT, D], F32).ap()
    x1_sc = nc.dram_tensor("x1_sc", [T_EXT, D], F32).ap()
    s1_sc = nc.dram_tensor("s1_sc", [T_OWN, D], F32).ap()
    b_sp = Buf("sp_sc")
    b_s = Buf("s_sc")
    b_x1 = Buf("x1_sc")
    b_s1 = Buf("s1_sc")

    with ExitStack() as ctx:
        sch = Sched(nc, ctx)
        consts, cb = declare_consts(nc, sch, ctx)
        emit_R(nc, sch, consts, cb, x_all, rw, rb, sp_sc, b_sp,
               BATCH * SEQ // 128, list(range(BATCH * SEQ // 128)))
        sch.fence()
        emit_A0(nc, sch, consts, cb, xf, xe, wq_r, wk_r, wv_r, woT_r, dmask,
                bcol, vslab, s_sc, b_s)
        emit_F0(nc, sch, consts, cb, xe, s_sc, b_s, sp_sc, flag, rw, rb,
                w1_r, w2_r, b1_r, b2_r, eye8, x1_sc, b_x1,
                spall_direct=True, b_spall=b_sp)
        emit_F1(nc, sch, consts, cb, x1_sc[HALO:HALO + T_OWN, :], b_x1,
                s1_sc, b_s1, r1w, r1b, w1f_r, w2f_r, o_out, final_wait=True,
                interleave_fn=lambda: emit_A1(
                    nc, sch, consts, cb, x1_sc, b_x1, wqkv_r, qkvb_c,
                    qkvb_row, crope_r, srope_r, p2_r, wmask_r, vslab1, bcol1,
                    wo1_r, wob_row, s1_sc, b_s1))
    return nc


def host_fused_inputs(core, inputs):
    x = to_f32(inputs["L0_attn_x"])
    d = host_A0_inputs(core, x, inputs["L0_attn_wq"], inputs["L0_attn_wk"],
                       inputs["L0_attn_wv"], inputs["L0_attn_wo"])
    d["x_all"] = x.reshape(-1, D)
    d["rw"] = to_f32(inputs["L0_ffn_router_w"])
    d["rb"] = to_f32(inputs["L0_ffn_router_b"]).reshape(1, 8)
    d["flag"] = np.full((8, 2), 1.0 if core == 0 else 0.0, np.float32)
    d["w1_r"] = np.ascontiguousarray(to_f32(inputs["L0_ffn_w1"]).reshape(8, 8, 128, 8, 128))
    d["w2_r"] = np.ascontiguousarray(to_f32(inputs["L0_ffn_w2"]).reshape(8, 8, 128, 2, 512))
    d["b1_r"] = np.ascontiguousarray(to_f32(inputs["L0_ffn_b1"]).reshape(8, 8, 128).transpose(2, 0, 1))
    d["b2_r"] = to_f32(inputs["L0_ffn_b2"])
    eye = np.zeros((8, 8 * 128), np.float32)
    for ei in range(8):
        eye[ei, ei * 128:(ei + 1) * 128] = 1.0
    d["eye8"] = eye
    d1 = host_A1_inputs(core, np.broadcast_to(np.float32(0.0), (BATCH, SEQ, D)),
                        inputs["L1_attn_Wqkv_kernel"], inputs["L1_attn_Wqkv_bias"],
                        inputs["L1_attn_Wo_kernel"], inputs["L1_attn_Wo_bias"],
                        inputs["L1_attn_rope_cache"])
    del d1["x1e"]
    d.update(d1)
    d["r1w"] = to_f32(inputs["L1_ffn_router_w"])
    d["r1b"] = to_f32(inputs["L1_ffn_router_b"]).reshape(1, 9)
    d["w1f_r"] = np.ascontiguousarray(to_f32(inputs["L1_ffn_w1"]).reshape(8, 8, 128, 32, 128))
    d["w2f_r"] = np.ascontiguousarray(to_f32(inputs["L1_ffn_w2"]).reshape(8, 32, 128, 2, 512))
    return d



FUSED = True


def kernel(**inputs):
    if FUSED:
        nc = build_fused()
        in_maps = [host_fused_inputs(c, inputs) for c in range(NCORES)]
        res = run_bass_kernel_spmd(nc, in_maps, core_ids=list(range(NCORES)))
        out = np.empty((BATCH, SEQ, D), np.float32)
        for c in range(NCORES):
            b, h = c // 2, c % 2
            out[b, 2048 * h:2048 * h + T_OWN] = res.results[c]["o_out"]
        return out
    return kernel_multilaunch(**inputs)


def kernel_multilaunch(**inputs):
    x = to_f32(inputs["L0_attn_x"])
    flat = x.reshape(-1, D)

    nc_r = build_launch_R(NT_OWN, list(range(NT_OWN)))
    in_maps = [host_R_inputs(flat[c * T_OWN:(c + 1) * T_OWN],
                             inputs["L0_ffn_router_w"],
                             inputs["L0_ffn_router_b"])
               for c in range(NCORES)]
    res = run_bass_kernel_spmd(nc_r, in_maps, core_ids=list(range(NCORES)))
    spart_all = np.stack([res.results[c]["o_sp"] for c in range(NCORES)])

    nc_a0 = build_launch_A0()
    in_maps = [host_A0_inputs(c, x, inputs["L0_attn_wq"], inputs["L0_attn_wk"],
                              inputs["L0_attn_wv"], inputs["L0_attn_wo"])
               for c in range(NCORES)]
    res = run_bass_kernel_spmd(nc_a0, in_maps, core_ids=list(range(NCORES)))
    s_ext = [res.results[c]["o_s"] for c in range(NCORES)]

    nc_f0 = build_launch_F0()
    in_maps = [host_F0_inputs(c, x, s_ext[c], spart_all,
                              inputs["L0_ffn_router_w"],
                              inputs["L0_ffn_router_b"], inputs["L0_ffn_w1"],
                              inputs["L0_ffn_b1"], inputs["L0_ffn_w2"],
                              inputs["L0_ffn_b2"])
               for c in range(NCORES)]
    res = run_bass_kernel_spmd(nc_f0, in_maps, core_ids=list(range(NCORES)))
    x1 = np.empty((BATCH, SEQ, D), np.float32)
    for c in range(NCORES):
        b, h = c // 2, c % 2
        x1[b, 2048 * h:2048 * h + T_OWN] = \
            res.results[c]["o_x1"][HALO:HALO + T_OWN]

    nc_a1 = build_launch_A1()
    in_maps = [host_A1_inputs(c, x1, inputs["L1_attn_Wqkv_kernel"],
                              inputs["L1_attn_Wqkv_bias"],
                              inputs["L1_attn_Wo_kernel"],
                              inputs["L1_attn_Wo_bias"],
                              inputs["L1_attn_rope_cache"])
               for c in range(NCORES)]
    res = run_bass_kernel_spmd(nc_a1, in_maps, core_ids=list(range(NCORES)))
    s1 = [res.results[c]["o_s1"] for c in range(NCORES)]

    nc_f1 = build_launch_F1()
    in_maps = [host_F1_inputs(c, x1, s1[c], inputs["L1_ffn_router_w"],
                              inputs["L1_ffn_router_b"], inputs["L1_ffn_w1"],
                              inputs["L1_ffn_w2"])
               for c in range(NCORES)]
    res = run_bass_kernel_spmd(nc_f1, in_maps, core_ids=list(range(NCORES)))
    out = np.empty((BATCH, SEQ, D), np.float32)
    for c in range(NCORES):
        b, h = c // 2, c % 2
        out[b, 2048 * h:2048 * h + T_OWN] = res.results[c]["o_out"]
    return out


# version 15
# speedup vs baseline: 1.1621x; 1.1621x over previous
import numpy as np
import ml_dtypes
from contextlib import ExitStack

import concourse.bass as bass
import concourse.mybir as mybir
from concourse.bass_utils import run_bass_kernel_spmd

F32 = mybir.dt.float32
BF16 = mybir.dt.bfloat16
AF = mybir.ActivationFunctionType
ALU = mybir.AluOpType
AX = mybir.AxisListType

NCORES = 8
D = 1024
SEQ = 4096
BATCH = 4
NH = 16
DH = 64
NHP = 8
T_OWN = 2048
HALO = 128
T_EXT = T_OWN + 2 * HALO
NT_OWN = T_OWN // 128
NT_EXT = T_EXT // 128
EPS = 1e-5

bf16 = ml_dtypes.bfloat16
_PFX = ""



class Buf:
    __slots__ = ("w", "readers", "name")

    def __init__(self, name=""):
        self.w = None
        self.readers = []
        self.name = name


class Sched:

    ENG = ("tensor", "vector", "scalar", "gpsimd", "sync")

    def __init__(self, nc, ctx):
        self.nc = nc
        self.ctx = ctx
        self.sems = {}
        self.cnt = {}
        for en in Sched.ENG:
            self.sems[en] = ctx.enter_context(nc.semaphore(f"se_{en}"))
            self.cnt[en] = 0
        self.dsems = {}
        self.dcnt = {}
        self.wm = {en: {} for en in Sched.ENG}

    def dma_key(self, key):
        if key not in self.dsems:
            self.dsems[key] = self.ctx.enter_context(
                self.nc.semaphore(f"sd_{key}"))
            self.dcnt[key] = 0
        return self.dsems[key]

    def _engine(self, en):
        return getattr(self.nc, en)

    def _wait(self, en, tok):
        sem_name, sem, val = tok
        if val <= 0:
            return
        if self.wm[en].get(sem_name, 0) >= val:
            return
        self._engine(en).wait_ge(sem, val)
        self.wm[en][sem_name] = val

    def _deps_of(self, reads, writes, deps):
        toks = list(getattr(self, "fence_toks", ())) + list(deps or [])
        for b in (reads or []):
            if b.w is not None:
                toks.append(b.w)
        for b in (writes or []):
            if b.w is not None:
                toks.append(b.w)
            toks.extend(b.readers)
        return toks

    def _commit(self, tok, reads, writes):
        for b in (reads or []):
            b.readers.append(tok)
        for b in (writes or []):
            b.w = tok
            b.readers = []

    def run(self, en, emit, reads=None, writes=None, deps=None):
        for tok in self._deps_of(reads, writes, deps):
            self._wait(en, tok)
        last = emit(self._engine(en))
        assert last is not None, "task emit() must return its last instruction"
        self.cnt[en] += 1
        last.then_inc(self.sems[en], 1)
        tok = (en, self.sems[en], self.cnt[en])
        self._commit(tok, reads, writes)
        return tok

    def dma(self, en, key, out, in_, reads=None, writes=None, deps=None,
            **kw):
        sem = self.dma_key(key)
        for tok in self._deps_of(reads, writes, deps):
            self._wait(en, tok)
        self._engine(en).dma_start(out, in_, **kw).then_inc(sem, 16)
        self.dcnt[key] += 16
        tok = (f"dma_{key}", sem, self.dcnt[key])
        self._commit(tok, reads, writes)
        return tok

    def idma(self, key, emit, reads=None, writes=None, deps=None):
        sem = self.dma_key(key)
        for tok in self._deps_of(reads, writes, deps):
            self._wait("gpsimd", tok)
        emit(self.nc.gpsimd).then_inc(sem, 16)
        self.dcnt[key] += 16
        tok = (f"dma_{key}", sem, self.dcnt[key])
        self._commit(tok, reads, writes)
        return tok

    def barrier_token(self, en):
        return (en, self.sems[en], self.cnt[en])

    def fence(self):
        toks = [(en, self.sems[en], self.cnt[en]) for en in Sched.ENG]
        toks += [(f"dma_{k}", self.dsems[k], self.dcnt[k]) for k in self.dsems]
        self.fence_toks = [t for t in toks if t[2] > 0]



def to_f32(a):
    return np.ascontiguousarray(a, dtype=np.float32)


def host_consts():
    c = {}
    c["identb"] = np.eye(128, dtype=np.float32)
    c["identf"] = np.eye(128, dtype=np.float32)
    c["ones_row"] = np.ones((1, 128), np.float32)
    c["ones_rowf"] = np.ones((1, 128), np.float32)
    c["ones_col"] = np.ones((128, 1), np.float32)
    c["eps_col"] = np.full((128, 1), EPS, np.float32)
    lt = np.zeros((128, 128), np.float32)
    for t in range(128):
        lt[:t, t] = 1.0
    c["lt_excl"] = lt
    c["zeros_t"] = np.zeros((128, 1024), np.float32)
    c["ones_colb"] = np.ones((128, 1), np.float32)
    return c


CONST_SPECS = [
    ("identb", [128, 128], BF16),
    ("identf", [128, 128], F32),
    ("ones_row", [1, 128], BF16),
    ("ones_rowf", [1, 128], F32),
    ("ones_col", [128, 1], F32),
    ("eps_col", [128, 1], F32),
    ("lt_excl", [128, 128], BF16),
    ("zeros_t", [128, 1024], BF16),
    ("ones_colb", [128, 1], BF16),
]


def declare_consts(nc, sch, ctx):
    sb = {}
    cb = Buf("consts")
    for name, shape, dt in CONST_SPECS:
        dram = nc.dram_tensor("c_" + name, shape, F32, kind="ExternalInput").ap()
        t = ctx.enter_context(nc.sbuf_tensor("sb_" + name, shape, dt))
        sch.dma("gpsimd", "consts", t[:, :], dram, writes=[cb])
        sb[name] = t
    return sb, cb



def emit_ln_tile(nc, sch, cb, x_buf, x_tile, tmp, want_f32=False, want_b16=None,
                 stats_out=None, consts=None):
    bn6, b_bn6 = tmp["bn6"]
    mv, b_mv = tmp["mv"]
    lnv, b_lnv = tmp["lnv"]
    rstd, b_rstd = tmp["rstd"]
    biasA, b_biasA = tmp["biasA"]
    eps_col = consts["eps_col"]

    sch.run("vector", lambda v: v.bn_stats(bn6[:, 0:6], x_tile[:, 0:512]),
            reads=[x_buf], writes=[b_bn6])
    sch.run("vector", lambda v: v.bn_stats(bn6[:, 6:12], x_tile[:, 512:1024]),
            reads=[x_buf], writes=[b_bn6])
    sch.run("vector", lambda v: v.bn_aggr(mv[:, :], bn6[:, :]),
            reads=[b_bn6], writes=[b_mv])
    sch.run("scalar", lambda s: s.activation(lnv[:, :], mv[:, 1:2], AF.Ln,
                                             bias=eps_col[:, 0:1], scale=1.0),
            reads=[b_mv, cb], writes=[b_lnv])
    sch.run("scalar", lambda s: s.activation(rstd[:, :], lnv[:, :], AF.Exp,
                                             scale=-0.5),
            reads=[b_lnv], writes=[b_rstd])
    sch.run("vector", lambda v: v.tensor_scalar(biasA[:, :], mv[:, 0:1],
                                                rstd[:, 0:1], -1.0,
                                                ALU.mult, ALU.mult),
            reads=[b_mv, b_rstd], writes=[b_biasA])
    tok_f = tok_b = None
    if want_f32 is not None:
        h, sl, wbuf = want_f32
        tok_f = sch.run("vector",
                        lambda v: v.tensor_scalar(h[:, sl], x_tile[:, :],
                                                  mv[:, 0:1], rstd[:, 0:1],
                                                  ALU.subtract, ALU.mult),
                        reads=[x_buf, b_mv, b_rstd], writes=[wbuf])
    if want_b16 is not None:
        h, sl, wbuf = want_b16
        tok_b = sch.run("scalar",
                        lambda s: s.activation(h[:, sl], x_tile[:, :],
                                               AF.Identity,
                                               bias=biasA[:, 0:1],
                                               scale=rstd[:, 0:1]),
                        reads=[x_buf, b_rstd, b_biasA], writes=[wbuf])
    if stats_out is not None:
        h, col, wbuf = stats_out
        sch.run("vector", lambda v: v.tensor_copy(h[:, col:col + 1], rstd[:, :]),
                reads=[b_rstd], writes=[wbuf])
        sch.run("vector", lambda v: v.tensor_copy(h[:, col + 1:col + 2],
                                                  biasA[:, :]),
                reads=[b_biasA], writes=[wbuf])
    return tok_f, tok_b


def mk_ln_tmp(nc, ctx, tag):
    tmp = {}
    for name, shape in [("bn6", [128, 12]), ("mv", [128, 2]), ("lnv", [128, 1]),
                        ("rstd", [128, 1]), ("biasA", [128, 1])]:
        h = ctx.enter_context(nc.sbuf_tensor(_PFX + f"{tag}_{name}", shape, F32))
        tmp[name] = (h, Buf(f"{tag}_{name}"))
    return tmp


def emit_transpose8(nc, sch, src, src_buf, pbank, pbank_buf, dst, dst_cols,
                    dst_buf, ident, cb, dtype_b16=True):
    def t_emit(t):
        last = None
        for ch in range(8):
            last = t.transpose(pbank[:, ch * 128:(ch + 1) * 128],
                               src[:, ch * 128:(ch + 1) * 128],
                               ident[:, :])
        return last
    sch.run("tensor", t_emit, reads=[src_buf, cb], writes=[pbank_buf])

    def v_emit(v):
        last = None
        for ch in range(8):
            last = v.tensor_copy(dst[:, dst_cols[ch]:dst_cols[ch] + 128],
                                 pbank[:, ch * 128:(ch + 1) * 128])
        return last
    return sch.run("vector", v_emit, reads=[pbank_buf], writes=[dst_buf])


def emit_top2(nc, sch, lgt, lgt_buf, nlog, tmp, out, out_buf, with_z=False):
    m1, b_m1 = tmp["m1"]
    nm1, b_nm1 = tmp["nm1"]
    e8, b_e8 = tmp["e8"]
    m1e, b_m1e = tmp["m1e"]
    is1, b_is1 = tmp["is1"]
    ms2, b_ms2 = tmp["ms2"]
    m2e, b_m2e = tmp["m2e"]
    is2, b_is2 = tmp["is2"]
    den, b_den = tmp["den"]
    rden, b_rden = tmp["rden"]
    p1n, b_p1n = tmp["p1n"]
    p2n, b_p2n = tmp["p2n"]
    B1, b_B1 = out["B1"]
    B2, b_B2 = out["B2"]
    B1ap = out.get("B1ap")
    B2ap = out.get("B2ap")
    n = nlog

    sch.run("vector", lambda v: v.reduce_max(m1[:, :], lgt[:, 0:n], axis=AX.X,
                                             negate=True),
            reads=[lgt_buf], writes=[b_m1])
    sch.run("scalar", lambda s: s.activation(e8[:, 0:n], lgt[:, 0:n], AF.Exp,
                                             bias=m1[:, 0:1], scale=1.0),
            reads=[lgt_buf, b_m1], writes=[b_e8])
    sch.run("vector", lambda v: v.reduce_max(m1e[:, :], e8[:, 0:n], axis=AX.X),
            reads=[b_e8], writes=[b_m1e])
    sch.run("vector", lambda v: v.tensor_single_scalar(is1[:, 0:n], e8[:, 0:n],
                                                       m1e[:, 0:1],
                                                       ALU.is_equal),
            reads=[b_e8, b_m1e], writes=[b_is1])
    sch.run("vector", lambda v: v.scalar_tensor_tensor(ms2[:, 0:n], is1[:, 0:n],
                                                       -1e30, e8[:, 0:n],
                                                       ALU.mult, ALU.add),
            reads=[b_is1, b_e8], writes=[b_ms2])
    sch.run("vector", lambda v: v.reduce_max(m2e[:, :], ms2[:, 0:n], axis=AX.X),
            reads=[b_ms2], writes=[b_m2e])
    sch.run("vector", lambda v: v.tensor_single_scalar(is2[:, 0:n], ms2[:, 0:n],
                                                       m2e[:, 0:1],
                                                       ALU.is_equal),
            reads=[b_ms2, b_m2e], writes=[b_is2])
    if with_z:
        sch.run("vector", lambda v: v.reduce_sum(den[:, :], e8[:, 0:n],
                                                 axis=AX.X),
                reads=[b_e8], writes=[b_den])
    else:
        sch.run("vector", lambda v: v.tensor_add(den[:, :], m1e[:, :],
                                                 m2e[:, :]),
                reads=[b_m1e, b_m2e], writes=[b_den])
    sch.run("vector", lambda v: v.reciprocal(rden[:, :], den[:, :]),
            reads=[b_den], writes=[b_rden])
    sch.run("vector", lambda v: v.tensor_mul(p1n[:, :], m1e[:, :], rden[:, :]),
            reads=[b_m1e, b_rden], writes=[b_p1n])
    sch.run("vector", lambda v: v.tensor_mul(p2n[:, :], m2e[:, :], rden[:, :]),
            reads=[b_m2e, b_rden], writes=[b_p2n])
    sch.run("vector", lambda v: v.tensor_single_scalar(
        B1ap if B1ap is not None else B1[:, 0:n], is1[:, 0:n],
        p1n[:, 0:1], ALU.mult),
            reads=[b_is1, b_p1n], writes=[b_B1])
    sch.run("vector", lambda v: v.tensor_single_scalar(
        B2ap if B2ap is not None else B2[:, 0:n], is2[:, 0:n],
        p2n[:, 0:1], ALU.mult),
            reads=[b_is2, b_p2n], writes=[b_B2])


def mk_top2_tmp(nc, ctx, tag, nlog):
    tmp = {}
    for name, w in [("m1", 1), ("nm1", 1), ("e8", nlog), ("m1e", 1),
                    ("is1", nlog), ("ms2", nlog), ("m2e", 1), ("is2", nlog),
                    ("den", 1), ("rden", 1), ("p1n", 1), ("p2n", 1)]:
        h = ctx.enter_context(nc.sbuf_tensor(_PFX + f"{tag}_{name}", [128, w], F32))
        tmp[name] = (h, Buf(f"{tag}_{name}"))
    return tmp



def build_launch_R(n_tiles, own_range):
    nc = bass.Bass("TRN2", target_bir_lowering=False)
    xr = nc.dram_tensor("xr", [n_tiles * 128, D], F32, kind="ExternalInput").ap()
    rw = nc.dram_tensor("rw", [D, 8], F32, kind="ExternalInput").ap()
    rb = nc.dram_tensor("rb", [1, 8], F32, kind="ExternalInput").ap()
    o_sp = nc.dram_tensor("o_sp", [8, 2], F32, kind="ExternalOutput").ap()
    with ExitStack() as ctx:
        sch = Sched(nc, ctx)
        consts, cb = declare_consts(nc, sch, ctx)
        emit_R(nc, sch, consts, cb, xr, rw, rb, o_sp, Buf("osp"),
               n_tiles, own_range, final_wait=True)
    return nc


def emit_R(nc, sch, consts, cb, xr, rw, rb, o_sp, b_osp, n_tiles, own_range,
           final_wait=False):
    global _PFX
    _PFX = "r_"
    with ExitStack() as ctx:
        e = ctx.enter_context

        NSLOT = 3
        NP = 2
        xs = [e(nc.sbuf_tensor(_PFX + f"xs{i}", [128, D], F32)) for i in range(NSLOT)]
        xb = [Buf(f"xs{i}") for i in range(NSLOT)]
        xn16 = [e(nc.sbuf_tensor(_PFX + f"xn16_{i}", [128, D], BF16)) for i in range(NP)]
        b_xn16 = [Buf(f"xn16_{i}") for i in range(NP)]
        xnf = [e(nc.sbuf_tensor(_PFX + f"xnf_{i}", [128, D], BF16)) for i in range(NP)]
        b_xnf = [Buf(f"xnf_{i}") for i in range(NP)]
        rw_sb = e(nc.sbuf_tensor(_PFX + "rw_sb", [128, 64], BF16))
        rb_sb = e(nc.sbuf_tensor(_PFX + "rb_sb", [1, 8], BF16))
        b_w = Buf("rw")
        lgt = [e(nc.sbuf_tensor(_PFX + f"lgt_{i}", [128, 8], F32)) for i in range(NP)]
        b_lgt = [Buf(f"lgt_{i}") for i in range(NP)]
        B1 = [e(nc.sbuf_tensor(_PFX + f"B1_{i}", [128, 8], F32)) for i in range(NP)]
        B2 = [e(nc.sbuf_tensor(_PFX + f"B2_{i}", [128, 8], F32)) for i in range(NP)]
        b_B1 = [Buf(f"B1_{i}") for i in range(NP)]
        b_B2 = [Buf(f"B2_{i}") for i in range(NP)]
        spart = e(nc.sbuf_tensor(_PFX + "spart", [8, 2], F32))
        b_sp = Buf("spart")

        pt = [e(nc.psum_tensor(_PFX + f"pt_{i}", [128, 1024], BF16)) for i in range(NP)]
        b_pt = [Buf(f"pt_{i}") for i in range(NP)]
        pr = [e(nc.psum_tensor(_PFX + f"pr_{i}", [128, 8], F32)) for i in range(NP)]
        b_pr = [Buf(f"pr_{i}") for i in range(NP)]
        pS1 = e(nc.psum_tensor(_PFX + "pS1", [8, 1], F32))
        pS2 = e(nc.psum_tensor(_PFX + "pS2", [8, 1], F32))
        b_pS = Buf("pS")

        ln_tmp = [mk_ln_tmp(nc, ctx, f"lnR{i}") for i in range(NP)]
        t2_tmp = [mk_top2_tmp(nc, ctx, f"t2R{i}", 8) for i in range(NP)]

        sch.dma("gpsimd", "w", rw_sb[:, :].rearrange("p (c e) -> p c e", c=8),
                rw.rearrange("(c p) e -> p c e", p=128), writes=[b_w])
        sch.dma("gpsimd", "w", rb_sb[:, :], rb, writes=[b_w])

        own_set = set(own_range)
        first_own, last_own = min(own_range), max(own_range)

        st2 = [None]

        def stage2_r(t, p):
            def t_emit(tns, p=p):
                last = None
                for ch in range(8):
                    last = tns.transpose(pt[p][:, ch * 128:(ch + 1) * 128],
                                         xn16[p][:, ch * 128:(ch + 1) * 128],
                                         consts["identb"][:, :])
                return last
            sch.run("tensor", t_emit, reads=[b_xn16[p], cb], writes=[b_pt[p]])

            def v_emit(v, p=p):
                last = None
                for ch in range(8):
                    last = v.tensor_copy(xnf[p][:, ch * 128:(ch + 1) * 128],
                                         pt[p][:, ch * 128:(ch + 1) * 128])
                return last
            sch.run("vector", v_emit, reads=[b_pt[p]], writes=[b_xnf[p]])

            def r_emit(tns, p=p):
                for ch in range(8):
                    tns.matmul(pr[p][:, :], xnf[p][:, ch * 128:(ch + 1) * 128],
                               rw_sb[:, ch * 8:(ch + 1) * 8],
                               start=(ch == 0), stop=False)
                return tns.matmul(pr[p][:, :], consts["ones_row"][:, :],
                                  rb_sb[:, :], start=False, stop=True)
            sch.run("tensor", r_emit, reads=[b_xnf[p], b_w, cb], writes=[b_pr[p]])
            sch.run("vector", lambda v, p=p: v.tensor_copy(lgt[p][:, :],
                                                           pr[p][:, :]),
                    reads=[b_pr[p]], writes=[b_lgt[p]])
            emit_top2(nc, sch, lgt[p], b_lgt[p], 8, t2_tmp[p],
                      {"B1": (B1[p], b_B1[p]), "B2": (B2[p], b_B2[p])}, None)
            if t in own_set:
                def s_emit(tns, p=p, first=(t == first_own), last=(t == last_own)):
                    tns.matmul(pS1[:, :], B1[p][:, :], consts["ones_col"][:, :],
                               start=first, stop=last)
                    return tns.matmul(pS2[:, :], B2[p][:, :],
                                      consts["ones_col"][:, :],
                                      start=first, stop=last)
                sch.run("tensor", s_emit, reads=[b_B1[p], b_B2[p], cb],
                        writes=[b_pS])

        for t in range(n_tiles):
            sl = t % NSLOT
            p = t % NP
            sch.dma("sync", f"x{sl}", xs[sl][:, :],
                    xr[t * 128:(t + 1) * 128, :], writes=[xb[sl]])
            emit_ln_tile(nc, sch, cb, xb[sl], xs[sl], ln_tmp[p],
                         want_f32=None,
                         want_b16=(xn16[p], slice(0, D), b_xn16[p]),
                         consts=consts)
            if st2[0]:
                st2[0]()
            st2[0] = (lambda t=t, p=p: stage2_r(t, p))
        st2[0]()

        def fin_emit(v):
            v.tensor_copy(spart[:, 0:1], pS1[:, :])
            return v.tensor_copy(spart[:, 1:2], pS2[:, :])
        sch.run("vector", fin_emit, reads=[b_pS], writes=[b_sp])
        tok = sch.dma("scalar", "spout", o_sp, spart[:, :], reads=[b_sp],
                      writes=[b_osp])
        if final_wait:
            sch._wait("scalar", tok)


class RSteps:

    def __init__(self, nc, sch, consts, cb, xr, rw, rb, o_sp, b_osp, n_tiles):
        self.nc, self.sch = nc, sch
        self.o_sp, self.b_osp = o_sp, b_osp
        self.ctx = ExitStack()
        e = self.ctx.enter_context
        global _PFX
        old_pfx = _PFX
        _PFX = "r_"
        NSLOT = 3
        NP = 2
        xs = [e(nc.sbuf_tensor(f"r_xs{i}", [128, D], F32)) for i in range(NSLOT)]
        xb = [Buf(f"r_xs{i}") for i in range(NSLOT)]
        xn16 = [e(nc.sbuf_tensor(f"r_xn16_{i}", [128, D], BF16))
                for i in range(NP)]
        b_xn16 = [Buf(f"r_xn16_{i}") for i in range(NP)]
        xnf = [e(nc.sbuf_tensor(f"r_xnf_{i}", [128, D], BF16))
               for i in range(NP)]
        b_xnf_lo = [Buf(f"r_xnfl_{i}") for i in range(NP)]
        b_xnf_hi = [Buf(f"r_xnfh_{i}") for i in range(NP)]
        rw_sb = e(nc.sbuf_tensor("r_rw_sb", [128, 64], BF16))
        rb_sb = e(nc.sbuf_tensor("r_rb_sb", [1, 8], BF16))
        b_w = Buf("r_rw")
        lgt = [e(nc.sbuf_tensor(f"r_lgt_{i}", [128, 8], F32)) for i in range(NP)]
        b_lgt = [Buf(f"r_lgt_{i}") for i in range(NP)]
        B12 = [e(nc.sbuf_tensor(f"r_B12_{i}", [128, 16], F32)) for i in range(NP)]
        b_B12 = [Buf(f"r_B12_{i}") for i in range(NP)]
        sp16 = e(nc.sbuf_tensor("r_sp16", [16, 1], F32))
        b_sp16 = Buf("r_sp16")
        pt = [e(nc.psum_tensor(f"r_pt_{i}", [128, 1024], BF16)) for i in range(NP)]
        b_pt = [Buf(f"r_pt_{i}") for i in range(NP)]
        pr = e(nc.psum_tensor("r_pr", [128, 8], F32))
        b_pr = Buf("r_pr")
        pS = e(nc.psum_tensor("r_pS", [16, 1], F32))
        b_pS = Buf("r_pS")
        ln_tmp = [mk_ln_tmp(nc, self.ctx, f"lnR{i}") for i in range(NP)]
        t2_tmp = [mk_top2_tmp(nc, self.ctx, f"t2R{i}", 8) for i in range(NP)]
        _PFX = old_pfx

        sch.dma("gpsimd", "w", rw_sb[:, :].rearrange("p (c e) -> p c e", c=8),
                rw.rearrange("(c p) e -> p c e", p=128), writes=[b_w])
        sch.dma("gpsimd", "w", rb_sb[:, :], rb, writes=[b_w])

        st2 = [None]

        def stage2(t, p):
            def t_emit(tns, p=p):
                last = None
                for ch in range(8):
                    last = tns.transpose(pt[p][:, ch * 128:(ch + 1) * 128],
                                         xn16[p][:, ch * 128:(ch + 1) * 128],
                                         consts["identb"][:, :])
                return last
            sch.run("tensor", t_emit, reads=[b_xn16[p], cb], writes=[b_pt[p]])
            sch.run("vector",
                    lambda v, p=p: v.tensor_copy(xnf[p][:, 0:512],
                                                 pt[p][:, 0:512]),
                    reads=[b_pt[p]], writes=[b_xnf_lo[p]])
            sch.run("vector",
                    lambda v, p=p: v.tensor_copy(xnf[p][:, 512:1024],
                                                 pt[p][:, 512:1024]),
                    reads=[b_pt[p]], writes=[b_xnf_hi[p]])

            def r_emit(tns, p=p):
                for ch in range(8):
                    tns.matmul(pr[:, :], xnf[p][:, ch * 128:(ch + 1) * 128],
                               rw_sb[:, ch * 8:(ch + 1) * 8],
                               start=(ch == 0), stop=False)
                return tns.matmul(pr[:, :], consts["ones_row"][:, :],
                                  rb_sb[:, :], start=False, stop=True)
            sch.run("tensor", r_emit,
                    reads=[b_xnf_lo[p], b_xnf_hi[p], b_w, cb], writes=[b_pr])
            sch.run("vector", lambda v, p=p: v.tensor_copy(lgt[p][:, :],
                                                           pr[:, :]),
                    reads=[b_pr], writes=[b_lgt[p]])
            emit_top2(nc, sch, lgt[p], b_lgt[p], 8, t2_tmp[p],
                      {"B1": (B12[p], b_B12[p]), "B2": (B12[p], b_B12[p]),
                       "B1ap": bass.AP(B12[p], 0, [[16, 128], [2, 8]]),
                       "B2ap": bass.AP(B12[p], 1, [[16, 128], [2, 8]])}, None)

            def s_emit(tns, p=p, first=(t == 0), last=(t == n_tiles - 1)):
                return tns.matmul(pS[:, :], B12[p][:, :],
                                  consts["ones_col"][:, :],
                                  start=first, stop=last)
            sch.run("tensor", s_emit, reads=[b_B12[p], cb], writes=[b_pS])

        def tile_step(t):
            sl = t % NSLOT
            p = t % NP
            sch.dma("sync", f"x{sl}", xs[sl][:, :],
                    xr[t * 128:(t + 1) * 128, :], writes=[xb[sl]])
            emit_ln_tile(nc, sch, cb, xb[sl], xs[sl], ln_tmp[p],
                         want_f32=None,
                         want_b16=(xn16[p], slice(0, D), b_xn16[p]),
                         consts=consts)
            if st2[0] is not None:
                st2[0]()
            st2[0] = (lambda t=t, p=p: stage2(t, p))

        def fin_step():
            st2[0]()
            sch.run("vector", lambda v: v.tensor_copy(sp16[:, :], pS[:, :]),
                    reads=[b_pS], writes=[b_sp16])
            sch.dma("scalar", "spout", o_sp[:, :], sp16[:, :],
                    reads=[b_sp16], writes=[b_osp])

        self.steps = [(lambda t=t: tile_step(t)) for t in range(n_tiles)]
        self.steps.append(fin_step)

    def __call__(self):
        if self.steps:
            self.steps.pop(0)()

    def flush(self):
        while self.steps:
            self.steps.pop(0)()
        self.ctx.close()


def host_consts_named():
    return {"c_" + k: v for k, v in host_consts().items()}


def host_R_inputs(x_own, L0_ffn_router_w, L0_ffn_router_b):
    d = host_consts_named()
    d["xr"] = to_f32(x_own)
    d["rw"] = to_f32(L0_ffn_router_w)
    d["rb"] = to_f32(L0_ffn_router_b).reshape(1, 8)
    return d



NQT = 5
QT_N = [512, 512, 512, 512, 256]
QT_START = [0, 512, 1024, 1536, 2048]
UKB = [19, 23, 27, 31, 32]


def a0_maskpos(k):
    pos = set()
    for h in (0, 1):
        q0 = 2048 * h - 128 + 512 * k
        d0 = q0 // 128
        for j in range(4):
            kb = d0 + j
            if 0 <= kb < UKB[k]:
                pos.add(kb)
    return sorted(pos)


A0_MASKPOS = [a0_maskpos(k) for k in range(NQT)]
A0_NMASK = max(len(p) for p in A0_MASKPOS)
A0_DEBUG = None


def build_launch_A0():
    nc = bass.Bass("TRN2", target_bir_lowering=False)
    xf = nc.dram_tensor("xf", [SEQ, D], F32, kind="ExternalInput").ap()
    xe = nc.dram_tensor("xe", [T_EXT, D], F32, kind="ExternalInput").ap()
    wq_r = nc.dram_tensor("wq_r", [NHP, 8, 128, 128], F32, kind="ExternalInput").ap()
    wk_r = nc.dram_tensor("wk_r", [NHP, 8, 128, 128], F32, kind="ExternalInput").ap()
    wv_r = nc.dram_tensor("wv_r", [NHP, 8, 128, 128], F32, kind="ExternalInput").ap()
    woT_r = nc.dram_tensor("woT_r", [8, 128, D], F32, kind="ExternalInput").ap()
    dmask = nc.dram_tensor("dmask", [NQT, A0_NMASK, 128, 512], F32,
                           kind="ExternalInput").ap()
    bcol = nc.dram_tensor("bcol", [128, NQT, 32], F32, kind="ExternalInput").ap()
    vslab = nc.dram_tensor("vslab", [128, (SEQ // 128) * 4 * 65], F32,
                           kind="ExternalInput").ap()
    o_s = nc.dram_tensor("o_s", [T_EXT, D], F32, kind="ExternalOutput").ap()
    with ExitStack() as ctx:
        sch = Sched(nc, ctx)
        consts, cb = declare_consts(nc, sch, ctx)
        emit_A0(nc, sch, consts, cb, xf, xe, wq_r, wk_r, wv_r, woT_r, dmask,
                bcol, vslab, o_s, Buf("os"), final_wait=True)
    return nc


def emit_A0(nc, sch, consts, cb, xf, xe, wq_r, wk_r, wv_r, woT_r, dmask,
            bcol, vslab, o_s, b_os, final_wait=False, steal_mk=None):
    dbg = {}
    if A0_DEBUG is not None:
        for kb in A0_DEBUG[3]:
            dbg[kb] = nc.dram_tensor(f"dbg_pt_{kb}", [128, 512], F32,
                                     kind="ExternalOutput").ap()
    global _PFX
    _PFX = "a0_"
    with ExitStack() as ctx:
        e = ctx.enter_context

        qf = e(nc.sbuf_tensor(_PFX + "qf", [128, NHP * T_EXT], BF16))
        b_qf = Buf("qf")
        off = e(nc.sbuf_tensor(_PFX + "off", [128, NHP * T_EXT], BF16))
        b_off = Buf("off")
        bcol_sb = e(nc.sbuf_tensor(_PFX + "bcol_sb", [128, NQT * 32], F32))
        b_bcol = Buf("bcol")
        sch.dma("gpsimd", "bcol", bcol_sb[:, :].rearrange("p (q k) -> p q k", q=NQT),
                bcol, writes=[b_bcol])

        ln_tmp = mk_ln_tmp(nc, ctx, "lnA0")

        with ExitStack() as ph:
            pe = ph.enter_context
            xnf_e = pe(nc.sbuf_tensor(_PFX + "xnf_e", [128, 8 * T_EXT], BF16))
            b_xnfe = Buf("xnf_e")
            ptb = pe(nc.psum_tensor(_PFX + "ptb", [128, 1024], BF16))
            b_ptb = Buf("ptb")
            NSLOT = 3
            xs = [pe(nc.sbuf_tensor(_PFX + f"a1x{i}", [128, D], F32)) for i in range(NSLOT)]
            xb = [Buf(f"a1x{i}") for i in range(NSLOT)]
            xn16 = [pe(nc.sbuf_tensor(_PFX + f"a1xn16_{i}", [128, D], BF16))
                    for i in range(2)]
            b_xn16 = [Buf(f"a1xn16_{i}") for i in range(2)]
            ln2 = [ln_tmp, mk_ln_tmp(nc, ph, "lnA0b")]
            wq_sb = pe(nc.sbuf_tensor(_PFX + "wq_sb", [128, NHP * 8 * 128], BF16))
            b_wq = Buf("wq")
            sch.dma("gpsimd", "wq",
                    wq_sb[:, :].rearrange("p (hp dc c) -> p hp dc c", hp=NHP, dc=8),
                    wq_r.rearrange("hp dc p c -> p hp dc c"), writes=[b_wq])
            pqk = pe(nc.psum_tensor(_PFX + "pqk", [128, 1024], F32))
            b_pqk = [Buf("pqk0"), Buf("pqk1")]
            rst = steal_mk() if steal_mk is not None else None
            _steal = rst if rst is not None else (lambda: None)
            st2 = [None]
            for t in range(NT_EXT):
                _steal()
                _steal()
                _steal()
                sl = t % NSLOT
                p = t % 2
                sch.dma("sync", f"a1x{sl}", xs[sl][:, :],
                        xe[t * 128:(t + 1) * 128, :], writes=[xb[sl]])
                emit_ln_tile(nc, sch, cb, xb[sl], xs[sl], ln2[p],
                             want_f32=None,
                             want_b16=(xn16[p], slice(0, D), b_xn16[p]),
                             consts=consts)
                if st2[0]:
                    st2[0]()
                st2[0] = (lambda t=t, p=p: emit_transpose8(
                    nc, sch, xn16[p], b_xn16[p], ptb, b_ptb, xnf_e,
                    [ch * T_EXT + t * 128 for ch in range(8)],
                    b_xnfe, consts["identb"], cb))
            st2[0]()
            for hp in range(NHP):
                for g in range(NQT):
                    _steal()
                    _steal()
                    n = QT_N[g]
                    st = QT_START[g]
                    bank = (hp * NQT + g) % 2
                    po = pqk[:, bank * 512:bank * 512 + n]

                    def mm(tns, hp=hp, st=st, n=n, po=po):
                        last = None
                        for dc in range(8):
                            last = tns.matmul(
                                po, wq_sb[:, (hp * 8 + dc) * 128:(hp * 8 + dc + 1) * 128],
                                xnf_e[:, dc * T_EXT + st:dc * T_EXT + st + n],
                                start=(dc == 0), stop=(dc == 7))
                        return last
                    sch.run("tensor", mm, reads=[b_wq, b_xnfe, cb],
                            writes=[b_pqk[bank]])
                    sch.run("vector",
                            lambda v, hp=hp, st=st, n=n, po=po:
                            v.tensor_copy(qf[:, hp * T_EXT + st:hp * T_EXT + st + n], po),
                            reads=[b_pqk[bank]], writes=[b_qf])
            if rst is not None:
                rst.flush()

        sch.fence()
        with ExitStack() as ph:
            pe = ph.enter_context
            xnf_f = pe(nc.sbuf_tensor(_PFX + "xnf_f", [128, 8 * SEQ], BF16))
            b_xnff = Buf("xnf_f")
            with ExitStack() as pa3:
                a3e = pa3.enter_context
                ptb = a3e(nc.psum_tensor(_PFX + "ptb2", [128, 1024], BF16))
                b_ptb = Buf("ptb2")
                NSLOT = 3
                xs = [a3e(nc.sbuf_tensor(_PFX + f"a3x{i}", [128, D], F32))
                      for i in range(NSLOT)]
                xb = [Buf(f"a3x{i}") for i in range(NSLOT)]
                xn16 = [a3e(nc.sbuf_tensor(_PFX + f"a3xn16_{i}", [128, D], BF16))
                        for i in range(2)]
                b_xn16 = [Buf(f"a3xn16_{i}") for i in range(2)]
                ln3 = [ln_tmp, mk_ln_tmp(nc, pa3, "lnA0c")]
                st3 = [None]
                for t in range(SEQ // 128):
                    sl = t % NSLOT
                    p = t % 2
                    sch.dma("sync", f"a3x{sl}", xs[sl][:, :],
                            xf[t * 128:(t + 1) * 128, :], writes=[xb[sl]])
                    emit_ln_tile(nc, sch, cb, xb[sl], xs[sl], ln3[p],
                                 want_f32=None,
                                 want_b16=(xn16[p], slice(0, D), b_xn16[p]),
                                 consts=consts)
                    if st3[0]:
                        st3[0]()
                    st3[0] = (lambda t=t, p=p: emit_transpose8(
                        nc, sch, xn16[p], b_xn16[p], ptb, b_ptb, xnf_f,
                        [ch * SEQ + t * 128 for ch in range(8)],
                        b_xnff, consts["identb"], cb))
                st3[0]()
            sch.fence()

            NKT = SEQ // 128
            for hg in range(4):
                sch.fence()
                with ExitStack() as hgc:
                    he = hgc.enter_context
                    wk_sb = he(nc.sbuf_tensor(_PFX + f"wk{hg}", [128, 2 * 8 * 128], BF16))
                    wv_sb = he(nc.sbuf_tensor(_PFX + f"wv{hg}", [128, 2 * 8 * 128], BF16))
                    b_wkv = Buf("wkv")
                    for i, hp in enumerate((2 * hg, 2 * hg + 1)):
                        sch.dma("gpsimd", "wkv",
                                wk_sb[:, i * 1024:(i + 1) * 1024].rearrange(
                                    "p (dc c) -> p dc c", dc=8),
                                wk_r[hp].rearrange("dc p c -> p dc c"),
                                writes=[b_wkv])
                        sch.dma("gpsimd", "wkv",
                                wv_sb[:, i * 1024:(i + 1) * 1024].rearrange(
                                    "p (dc c) -> p dc c", dc=8),
                                wv_r[hp].rearrange("dc p c -> p dc c"),
                                writes=[b_wkv])
                    kf = he(nc.sbuf_tensor(_PFX + f"kf{hg}", [128, 2 * SEQ], BF16))
                    b_kf = Buf("kf")
                    vaug = he(nc.sbuf_tensor(_PFX + f"va{hg}", [128, NKT * 4 * 65], BF16))
                    b_va = Buf("vaug")
                    pp = he(nc.psum_tensor(_PFX + f"pp{hg}", [128, 1024], F32))
                    b_pp = [Buf("pp0"), Buf("pp1")]
                    for i in range(2):
                        for g in range(8):
                            bank = (i * 8 + g) % 2
                            po = pp[:, bank * 512:(bank + 1) * 512]

                            def mm(tns, i=i, g=g, po=po):
                                last = None
                                for dc in range(8):
                                    last = tns.matmul(
                                        po,
                                        wk_sb[:, (i * 8 + dc) * 128:(i * 8 + dc + 1) * 128],
                                        xnf_f[:, dc * SEQ + g * 512:dc * SEQ + (g + 1) * 512],
                                        start=(dc == 0), stop=(dc == 7))
                                return last
                            sch.run("tensor", mm, reads=[b_wkv, b_xnff, cb],
                                    writes=[b_pp[bank]])
                            sch.run("vector",
                                    lambda v, i=i, g=g, po=po:
                                    v.tensor_copy(kf[:, i * SEQ + g * 512:i * SEQ + (g + 1) * 512], po),
                                    reads=[b_pp[bank]], writes=[b_kf])
                    VROW = NKT * 4 * 65
                    sch.dma("gpsimd", "vslab", vaug[:, :], vslab, writes=[b_va])
                    for kt in range(NKT):
                        for i in range(2):
                            bank = (kt * 2 + i) % 2
                            po = pp[:, bank * 512:bank * 512 + 128]

                            def mm(tns, kt=kt, i=i, po=po):
                                last = None
                                for dc in range(8):
                                    last = tns.matmul(
                                        po,
                                        xnf_f[:, dc * SEQ + kt * 128:dc * SEQ + (kt + 1) * 128],
                                        wv_sb[:, (i * 8 + dc) * 128:(i * 8 + dc + 1) * 128],
                                        start=(dc == 0), stop=(dc == 7))
                                return last
                            sch.run("tensor", mm, reads=[b_wkv, b_xnff, cb],
                                    writes=[b_pp[bank]])
                            base = (kt * 4 + i * 2) * 65

                            def ev(v, po=po, base=base):
                                return v.tensor_copy(
                                    bass.AP(vaug, base,
                                            [[VROW, 128], [65, 2], [1, 64]]),
                                    po.rearrange("p (h e) -> p h e", h=2))
                            sch.run("vector", ev, reads=[b_pp[bank]], writes=[b_va])

                    dm_sb = [he(nc.sbuf_tensor(_PFX + f"dm{hg}_0", [128, A0_NMASK * 512], BF16))]
                    b_dm = [Buf("dm0")]
                    pT = [he(nc.sbuf_tensor(_PFX + f"pT{hg}_{i}", [128, 512], BF16))
                          for i in range(8)]
                    b_pT = [Buf(f"pTs{i}") for i in range(4)]

                    recip = [he(nc.sbuf_tensor(_PFX + f"rc{hg}_{i}", [1, 512], F32))
                             for i in range(2)]
                    b_rc = [Buf("rc0"), Buf("rc1")]
                    bc_sb = [he(nc.sbuf_tensor(_PFX + f"bc{hg}_{i}", [64, 512], F32))
                             for i in range(2)]
                    b_bc = [Buf("bc0"), Buf("bc1")]
                    pS = [he(nc.psum_tensor(_PFX + f"pS{hg}_{i}", [128, 1024], F32))
                          for i in range(2)]
                    b_pS = [Buf("pS0"), Buf("pS1")]
                    pO = [he(nc.psum_tensor(_PFX + f"pO{hg}_{i}", [65, 512], F32))
                          for i in range(2)]
                    b_pO = [Buf("pO0"), Buf("pO1")]

                    for k in range(NQT):
                        n = QT_N[k]
                        qst = QT_START[k]
                        dsl = 0
                        sch.dma("gpsimd", f"dm{dsl}",
                                dm_sb[dsl][:, :].rearrange("p (m c) -> p m c",
                                                           m=A0_NMASK),
                                dmask[k].rearrange("m p c -> p m c"),
                                writes=[b_dm[dsl]])
                        mpos = {kb: j for j, kb in enumerate(A0_MASKPOS[k])}
                        for hlp in range(2):
                          ngrp = (UKB[k] + 1) // 2
                          for g in range(ngrp + 1):
                           for stage in range(4):
                            for hl in (2 * hlp, 2 * hlp + 1):
                                h = 4 * hg + hl
                                hp_loc = hl // 2
                                half = hl % 2
                                obank = hl % 2
                                par = hl % 2
                                q_ap = qf[64 * half:64 * half + 64,
                                          (2 * hg + hp_loc) * T_EXT + qst:
                                          (2 * hg + hp_loc) * T_EXT + qst + n]
                                kbs = [kb for kb in (2 * g, 2 * g + 1)
                                       if kb < UKB[k]]
                                pkbs = [kb for kb in (2 * g - 2, 2 * g - 1)
                                        if 0 <= kb < UKB[k]]
                                slot = par
                                pti = (g % 2) * 4 + par * 2
                                ppti = ((g - 1) % 2) * 4 + par * 2
                                if stage == 0 and g < ngrp:
                                    def qk_emit(t, kbs=kbs, slot=slot, q=q_ap,
                                                n=n, half=half, hp_loc=hp_loc):
                                        last = None
                                        for i, kb in enumerate(kbs):
                                            last = t.matmul(
                                                pS[slot][:, i * 512:i * 512 + n],
                                                kf[64 * half:64 * half + 64,
                                                   hp_loc * SEQ + kb * 128:
                                                   hp_loc * SEQ + (kb + 1) * 128],
                                                q, start=True, stop=True)
                                        return last
                                    sch.run("tensor", qk_emit,
                                            reads=[b_kf, b_qf],
                                            writes=[b_pS[slot]])
                                elif stage == 1 and g < ngrp:
                                    if any(kb in mpos for kb in kbs):
                                        def m_emit(v, kbs=kbs, slot=slot, n=n):
                                            last = None
                                            for i, kb in enumerate(kbs):
                                                if kb in mpos:
                                                    ap = pS[slot][:, i * 512:i * 512 + n]
                                                    last = v.tensor_tensor(
                                                        ap, ap,
                                                        dm_sb[dsl][:, mpos[kb] * 512:
                                                                   mpos[kb] * 512 + n],
                                                        ALU.add)
                                            return last
                                        sch.run("vector", m_emit,
                                                reads=[b_dm[dsl]],
                                                writes=[b_pS[slot]])
                                elif stage == 2 and g < ngrp:
                                    def e_emit(sce, kbs=kbs, slot=slot, n=n,
                                               k=k, pti=pti):
                                        last = None
                                        for i, kb in enumerate(kbs):
                                            src = pS[slot][:, i * 512:i * 512 + n]
                                            last = sce.activation(
                                                pT[pti + i][:, 0:n], src,
                                                AF.Exp, scale=0.125,
                                                bias=bcol_sb[:, k * 32 + kb:
                                                             k * 32 + kb + 1])
                                        return last
                                    sch.run("scalar", e_emit,
                                            reads=[b_pS[slot], b_bcol],
                                            writes=[b_pT[(g % 2) * 2 + par]])
                                elif stage == 3:
                                    if g >= 1:
                                        def pv_emit(t, pkbs=pkbs, n=n, k=k,
                                                    hl=hl, ob=obank, ppti=ppti):
                                            last = None
                                            for i, kb in enumerate(pkbs):
                                                last = t.matmul(
                                                    pO[ob][:, 0:n],
                                                    vaug[:, (kb * 4 + hl) * 65:
                                                         (kb * 4 + hl) * 65 + 65],
                                                    pT[ppti + i][:, 0:n],
                                                    start=(kb == 0),
                                                    stop=(kb == UKB[k] - 1))
                                            return last
                                        sch.run("tensor", pv_emit,
                                                reads=[b_va,
                                                       b_pT[((g - 1) % 2) * 2 + par]],
                                                writes=[b_pO[obank]])
                                    if g == ngrp:
                                        sch.run("vector",
                                                lambda v, n=n, ob=obank:
                                                v.reciprocal(recip[ob][:, 0:n],
                                                             pO[ob][64:65, 0:n]),
                                                reads=[b_pO[obank]],
                                                writes=[b_rc[obank]])
                                        sch.run("tensor",
                                                lambda t, n=n, ob=obank:
                                                t.matmul(pS[ob][0:64, 0:n],
                                                         consts["ones_rowf"][:, 0:64],
                                                         recip[ob][:, 0:n],
                                                         start=True, stop=True),
                                                reads=[b_rc[obank]],
                                                writes=[b_pS[obank]])
                                        sch.run("vector",
                                                lambda v, n=n, ob=obank:
                                                v.tensor_copy(bc_sb[ob][:, 0:n],
                                                              pS[ob][0:64, 0:n]),
                                                reads=[b_pS[obank]],
                                                writes=[b_bc[obank]])
                                        ocol = (2 * hg + hp_loc) * T_EXT + qst
                                        sch.run("vector",
                                                lambda v, n=n, ob=obank,
                                                half=half, ocol=ocol:
                                                v.tensor_tensor(
                                                    off[64 * half:64 * half + 64,
                                                        ocol:ocol + n],
                                                    pO[ob][0:64, 0:n],
                                                    bc_sb[ob][0:64, 0:n],
                                                    ALU.mult),
                                                reads=[b_pO[obank], b_bc[obank]],
                                                writes=[b_off])

        sch.fence()
        with ExitStack() as ph:
            pe = ph.enter_context
            wo_sb = pe(nc.sbuf_tensor(_PFX + "wo_sb", [128, 8 * D], BF16))
            b_wo = Buf("wo")
            sch.dma("gpsimd", "wo",
                    wo_sb[:, :].rearrange("p (cc o) -> p cc o", cc=8),
                    woT_r.rearrange("cc p o -> p cc o"), writes=[b_wo])
            ps = pe(nc.psum_tensor(_PFX + "psC", [128, 1024], F32))
            b_ps = [Buf("psC0"), Buf("psC1")]
            s_sb = [pe(nc.sbuf_tensor(_PFX + f"s_sb{i}", [128, 512], F32)) for i in range(2)]
            b_ssb = [Buf("ssb0"), Buf("ssb1")]
            for t in range(NT_EXT):
                for oc in range(2):
                    bank = (t * 2 + oc) % 2
                    po = ps[:, bank * 512:(bank + 1) * 512]

                    def mm(tns, t=t, oc=oc, po=po):
                        last = None
                        for cc in range(8):
                            last = tns.matmul(
                                po, off[:, cc * T_EXT + t * 128:cc * T_EXT + (t + 1) * 128],
                                wo_sb[:, cc * D + oc * 512:cc * D + (oc + 1) * 512],
                                start=(cc == 0), stop=(cc == 7))
                        return last
                    sch.run("tensor", mm, reads=[b_off, b_wo, cb],
                            writes=[b_ps[bank]])
                    sch.run("vector",
                            lambda v, bank=bank, po=po: v.tensor_copy(
                                s_sb[bank][:, :], po),
                            reads=[b_ps[bank]], writes=[b_ssb[bank]])
                    sch.dma("scalar", f"so{bank}",
                            o_s[t * 128:(t + 1) * 128, oc * 512:(oc + 1) * 512],
                            s_sb[bank][:, :], reads=[b_ssb[bank]],
                            writes=[b_os])
            if final_wait:
                for kk in ("so0", "so1"):
                    sch._wait("scalar",
                              (f"dma_{kk}", sch.dsems[kk], sch.dcnt[kk]))
        sch.fence()


def a0_vslab():
    slab = np.zeros((128, (SEQ // 128) * 4 * 65), np.float32)
    slab[:, 64::65] = 1.0
    return slab


def host_A0_inputs(core, x, wq, wk, wv, wo):
    b, h = core // 2, core % 2
    xb = to_f32(x[b])
    own0 = 2048 * h
    xe = np.zeros((T_EXT, D), np.float32)
    lo, hi = own0 - HALO, own0 + T_OWN + HALO
    slo, shi = max(lo, 0), min(hi, SEQ)
    xe[slo - lo:shi - lo] = xb[slo:shi]
    wq_r = to_f32(wq).reshape(8, 2, 64, 8, 128).transpose(0, 3, 4, 1, 2) \
        .reshape(8, 8, 128, 128)
    wk_r = to_f32(wk).reshape(8, 2, 64, 8, 128).transpose(0, 3, 4, 1, 2) \
        .reshape(8, 8, 128, 128)
    wv_r = to_f32(wv).reshape(8, 2, 64, 8, 128).transpose(0, 3, 4, 1, 2) \
        .reshape(8, 8, 128, 128)
    woT_r = to_f32(wo).T.reshape(8, 128, D)
    q0_ext = own0 - HALO
    dm = np.zeros((NQT, A0_NMASK, 128, 512), np.float32)
    bc = np.zeros((NQT, 32, 128), np.float32)
    for k in range(NQT):
        n = QT_N[k]
        q_abs = q0_ext + QT_START[k] + np.arange(n)
        d0 = (q0_ext + QT_START[k]) // 128
        diag = set(range(d0, d0 + 4))
        for j, kb in enumerate(A0_MASKPOS[k]):
            key_abs = kb * 128 + np.arange(128)
            m = np.where((key_abs[:, None] <= q_abs[None, :]) |
                         (q_abs[None, :] < 0), 0.0, -1e9)
            dm[k, j, :, 0:n] = m
        for kb in range(UKB[k]):
            if kb in A0_MASKPOS[k]:
                continue
            if kb > d0 + 3:
                bc[k, kb, :] = -1.25e8
    bc = np.ascontiguousarray(bc.transpose(2, 0, 1))
    return dict(xf=xb, xe=xe, wq_r=np.ascontiguousarray(wq_r),
                wk_r=np.ascontiguousarray(wk_r),
                wv_r=np.ascontiguousarray(wv_r),
                woT_r=np.ascontiguousarray(woT_r), dmask=dm, bcol=bc,
                vslab=a0_vslab(), **host_consts_named())



F0_GRP = [(0, 512), (512, 512), (1024, 512), (1536, 512), (2048, 256)]


def build_launch_F0():
    nc = bass.Bass("TRN2", target_bir_lowering=False)
    xe = nc.dram_tensor("xe", [T_EXT, D], F32, kind="ExternalInput").ap()
    se = nc.dram_tensor("se", [T_EXT, D], F32, kind="ExternalInput").ap()
    spall = nc.dram_tensor("spall", [8, 16], F32, kind="ExternalInput").ap()
    flag = nc.dram_tensor("flag", [8, 2], F32, kind="ExternalInput").ap()
    rw = nc.dram_tensor("rw", [D, 8], F32, kind="ExternalInput").ap()
    rb = nc.dram_tensor("rb", [1, 8], F32, kind="ExternalInput").ap()
    w1_r = nc.dram_tensor("w1_r", [8, 8, 128, 8, 128], F32, kind="ExternalInput").ap()
    w2_r = nc.dram_tensor("w2_r", [8, 8, 128, 2, 512], F32, kind="ExternalInput").ap()
    b1_r = nc.dram_tensor("b1_r", [128, 8, 8], F32, kind="ExternalInput").ap()
    b2_r = nc.dram_tensor("b2_r", [8, D], F32, kind="ExternalInput").ap()
    eye8 = nc.dram_tensor("eye8", [8, 8 * 128], F32, kind="ExternalInput").ap()
    o_x1 = nc.dram_tensor("o_x1", [T_EXT, D], F32, kind="ExternalOutput").ap()
    with ExitStack() as ctx:
        sch = Sched(nc, ctx)
        consts, cb = declare_consts(nc, sch, ctx)
        emit_F0(nc, sch, consts, cb, xe, se, None, spall, flag, rw, rb, w1_r,
                w2_r, b1_r, b2_r, eye8, o_x1, Buf("ox1"), spall_direct=False,
                final_wait=True)
    return nc


def emit_F0(nc, sch, consts, cb, xe, se, b_se, spall, flag, rw, rb, w1_r,
            w2_r, b1_r, b2_r, eye8, o_x1, b_ox1, spall_direct=False,
            b_spall=None, final_wait=False):
    global _PFX
    _PFX = "f0_"
    with ExitStack() as ctx:
        e = ctx.enter_context

        xnf16 = e(nc.sbuf_tensor(_PFX + "xnf16", [128, 8 * T_EXT], BF16))
        b_xnf = Buf("xnf16")
        acc = e(nc.sbuf_tensor(_PFX + "acc", [128, T_EXT * 8], F32))
        b_acc = Buf("acc")
        mask_all = e(nc.sbuf_tensor(_PFX + "mask_all", [128, NT_EXT * 8], F32))
        b_mask = Buf("mask")
        maskT = e(nc.sbuf_tensor(_PFX + "maskT", [8, T_EXT], F32))
        b_maskT = Buf("maskT")
        spall_sb = e(nc.sbuf_tensor(_PFX + "spall_sb", [8, 16], F32))
        flag_sb = e(nc.sbuf_tensor(_PFX + "flag_sb", [8, 2], F32))
        ssum = e(nc.sbuf_tensor(_PFX + "ssum", [8, 2], F32))
        b_ssum = Buf("ssum")
        eye8_sb = e(nc.sbuf_tensor(_PFX + "eye8_sb", [8, 8 * 128], F32))
        b_eye = Buf("eye8")
        rw_sb = e(nc.sbuf_tensor(_PFX + "rw_sb", [128, 64], F32))
        rb_sb = e(nc.sbuf_tensor(_PFX + "rb_sb", [1, 8], F32))
        b_w = Buf("rw")
        b1_sb = e(nc.sbuf_tensor(_PFX + "b1_sb", [128, 64], F32))
        b2_sb = e(nc.sbuf_tensor(_PFX + "b2_sb", [8, D], F32))
        b_b12 = Buf("b12")

        sch.dma("gpsimd", "ssumld", spall_sb[:, 0:2] if spall_direct
                else spall_sb[:, :], spall,
                reads=[b_spall] if b_spall else None, writes=[b_ssum])
        sch.dma("gpsimd", "ssumld", flag_sb[:, :], flag, writes=[b_ssum])
        sch.dma("gpsimd", "eyeld", eye8_sb[:, :], eye8, writes=[b_eye])
        sch.dma("gpsimd", "rwld", rw_sb[:, :].rearrange("p (c e) -> p c e", c=8),
                rw.rearrange("(c p) e -> p c e", p=128), writes=[b_w])
        sch.dma("gpsimd", "rwld", rb_sb[:, :], rb, writes=[b_w])
        sch.dma("gpsimd", "b12ld", b1_sb[:, :].rearrange("p (e h) -> p e h", e=8),
                b1_r, writes=[b_b12])
        sch.dma("gpsimd", "b12ld", b2_sb[:, :], b2_r, writes=[b_b12])

        ln_tmp = mk_ln_tmp(nc, ctx, "lnF0")
        t2_tmp = mk_top2_tmp(nc, ctx, "t2F0", 8)

        with ExitStack() as ph:
            pe = ph.enter_context
            NSLOT = 3
            xs = [pe(nc.sbuf_tensor(_PFX + f"f0x{i}", [128, D], F32)) for i in range(NSLOT)]
            xb = [Buf(f"f0x{i}") for i in range(NSLOT)]
            xn32 = pe(nc.sbuf_tensor(_PFX + "xn32", [128, D], F32))
            b_xn32 = Buf("xn32")
            xn16 = pe(nc.sbuf_tensor(_PFX + "xn16", [128, D], BF16))
            b_xn16 = Buf("xn16")
            xnf32 = pe(nc.sbuf_tensor(_PFX + "xnf32", [128, D], F32))
            b_xnf32 = Buf("xnf32")
            lgt = pe(nc.sbuf_tensor(_PFX + "lgt", [128, 8], F32))
            b_lgt = Buf("lgt")
            B1 = pe(nc.sbuf_tensor(_PFX + "B1", [128, 8], F32))
            B2 = pe(nc.sbuf_tensor(_PFX + "B2", [128, 8], F32))
            b_B1, b_B2 = Buf("B1"), Buf("B2")
            ptf = pe(nc.psum_tensor(_PFX + "ptf", [128, 1024], F32))
            b_ptf = Buf("ptf")
            ptb = pe(nc.psum_tensor(_PFX + "ptb", [128, 1024], BF16))
            b_ptb = Buf("ptb")
            pr = pe(nc.psum_tensor(_PFX + "pr", [128, 8], F32))
            b_pr = Buf("pr")
            pmt = pe(nc.psum_tensor(_PFX + "pmt", [8, 128], F32))
            b_pmt = Buf("pmt")

            ln2 = [ln_tmp, mk_ln_tmp(nc, ph, "lnF0b")]
            xn16b = [xn16, pe(nc.sbuf_tensor(_PFX + "xn16b", [128, D], BF16))]
            b_xn16b = [b_xn16, Buf("xn16b")]
            xn32b = [xn32, pe(nc.sbuf_tensor(_PFX + "xn32b", [128, D], F32))]
            b_xn32b = [b_xn32, Buf("xn32b")]
            st2 = [None]

            def stage2_f0(t, p):
                emit_transpose8(nc, sch, xn16b[p], b_xn16b[p], ptb, b_ptb,
                                xnf16, [ch * T_EXT + t * 128 for ch in range(8)],
                                b_xnf, consts["identb"], cb)
                def t_emit(tns):
                    last = None
                    for ch in range(8):
                        last = tns.transpose(ptf[:, ch * 128:(ch + 1) * 128],
                                             xn32b[p][:, ch * 128:(ch + 1) * 128],
                                             consts["identf"][:, :])
                    return last
                sch.run("tensor", t_emit, reads=[b_xn32b[p], cb], writes=[b_ptf])

                def v_emit(v):
                    last = None
                    for ch in range(8):
                        last = v.tensor_copy(xnf32[:, ch * 128:(ch + 1) * 128],
                                             ptf[:, ch * 128:(ch + 1) * 128])
                    return last
                sch.run("vector", v_emit, reads=[b_ptf], writes=[b_xnf32])

                def r_emit(tns):
                    for ch in range(8):
                        tns.matmul(pr[:, :], xnf32[:, ch * 128:(ch + 1) * 128],
                                   rw_sb[:, ch * 8:(ch + 1) * 8],
                                   start=(ch == 0), stop=False)
                    return tns.matmul(pr[:, :], consts["ones_rowf"][:, :],
                                      rb_sb[:, :], start=False, stop=True)
                sch.run("tensor", r_emit, reads=[b_xnf32, b_w, cb], writes=[b_pr])
                sch.run("vector", lambda v: v.tensor_copy(lgt[:, :], pr[:, :]),
                        reads=[b_pr], writes=[b_lgt])
                emit_top2(nc, sch, lgt, b_lgt, 8, t2_tmp,
                          {"B1": (B1, b_B1), "B2": (B2, b_B2)}, None)
                sch.run("vector",
                        lambda v: v.tensor_add(
                            mask_all[:, t * 8:(t + 1) * 8], B1[:, :], B2[:, :]),
                        reads=[b_B1, b_B2], writes=[b_mask])

            for t in range(NT_EXT):
                sl = t % NSLOT
                p = t % 2
                sch.dma("sync", f"f0x{sl}", xs[sl][:, :],
                        xe[t * 128:(t + 1) * 128, :], writes=[xb[sl]])
                emit_ln_tile(nc, sch, cb, xb[sl], xs[sl], ln2[p],
                             want_f32=(xn32b[p], slice(0, D), b_xn32b[p]),
                             want_b16=(xn16b[p], slice(0, D), b_xn16b[p]),
                             consts=consts)
                if st2[0]:
                    st2[0]()
                st2[0] = (lambda t=t, p=p: stage2_f0(t, p))
            st2[0]()

            if spall_direct:
                sch.run("vector", lambda v: v.tensor_copy(ssum[:, :],
                                                          spall_sb[:, 0:2]),
                        reads=[b_ssum], writes=[b_ssum])
            else:
                def red_emit(v):
                    v.reduce_sum(ssum[:, 0:1],
                                 bass.AP(spall_sb, 0, [[16, 8], [2, 8]]),
                                 axis=AX.X)
                    return v.reduce_sum(ssum[:, 1:2],
                                        bass.AP(spall_sb, 1, [[16, 8], [2, 8]]),
                                        axis=AX.X)
                sch.run("vector", red_emit, reads=[b_ssum], writes=[b_ssum])
            sch.run("vector", lambda v: v.tensor_mul(ssum[:, :], ssum[:, :],
                                                     flag_sb[:, :]),
                    reads=[b_ssum], writes=[b_ssum])
            sch.run("vector", lambda v: v.tensor_add(
                mask_all[0:8, 8:10], mask_all[0:8, 8:10], ssum[:, :]),
                    reads=[b_ssum], writes=[b_mask])
            sch.run("vector", lambda v: v.tensor_scalar_min(
                mask_all[0:8, 8:16], mask_all[0:8, 8:16], 2560.0),
                    reads=[], writes=[b_mask])

            for t in range(NT_EXT):
                sch.run("tensor",
                        lambda tns, t=t: tns.transpose(
                            pmt[:, :], mask_all[:, t * 8:(t + 1) * 8],
                            consts["identf"][:, :]),
                        reads=[b_mask, cb], writes=[b_pmt])
                sch.run("vector",
                        lambda v, t=t: v.tensor_copy(
                            maskT[:, t * 128:(t + 1) * 128], pmt[0:8, :]),
                        reads=[b_pmt], writes=[b_maskT])

        sch.fence()
        with ExitStack() as ph:
            pe = ph.enter_context
            pz = pe(nc.psum_tensor(_PFX + "pz", [128, 1024], F32))
            b_pz = [Buf("pz0"), Buf("pz1")]
            py = pe(nc.psum_tensor(_PFX + "py", [128, 1024], F32))
            b_py = [Buf("py0"), Buf("py1")]
            pBe = pe(nc.psum_tensor(_PFX + "pBe", [128, 512], F32))
            b_pBe = Buf("pBe")
            Be = pe(nc.sbuf_tensor(_PFX + "Be", [128, 512], F32))
            b_Be = Buf("Be")
            zz = pe(nc.sbuf_tensor(_PFX + "zz", [128, 512], F32))
            b_zz = Buf("zz")
            hf = pe(nc.sbuf_tensor(_PFX + "hf", [128, 8 * 512], BF16))
            b_hf = Buf("hf")
            w1_sb = [pe(nc.sbuf_tensor(_PFX + f"w1s{i}", [128, 8192], BF16)) for i in range(2)]
            w2_sb = [pe(nc.sbuf_tensor(_PFX + f"w2s{i}", [128, 8192], BF16)) for i in range(2)]
            b_w1 = [Buf("w1s0"), Buf("w1s1")]
            b_w2 = [Buf("w2s0"), Buf("w2s1")]

            for t in range(NT_EXT):
                for dch in range(2):
                    bank = (t * 2 + dch) % 2
                    po = py[:, bank * 512:(bank + 1) * 512]
                    sch.run("tensor",
                            lambda tns, t=t, dch=dch, po=po:
                            tns.matmul(po, maskT[:, t * 128:(t + 1) * 128],
                                       b2_sb[:, dch * 512:(dch + 1) * 512],
                                       start=True, stop=True),
                            reads=[b_maskT, b_b12], writes=[b_py[bank]])
                    sch.run("vector",
                            lambda v, t=t, dch=dch, po=po:
                            v.tensor_copy(
                                acc[:, t * 1024 + dch * 512:t * 1024 + (dch + 1) * 512],
                                po),
                            reads=[b_py[bank]], writes=[b_acc])

            for ei in range(8):
                wsl = ei % 2
                sch.dma("gpsimd", f"w1_{wsl}",
                        w1_sb[wsl][:, :].rearrange("p (dc hc c) -> p dc hc c",
                                                   dc=8, hc=8),
                        w1_r[ei].rearrange("dc d hc c -> d dc hc c"),
                        writes=[b_w1[wsl]])
                sch.dma("gpsimd", f"w2_{wsl}",
                        w2_sb[wsl][:, :].rearrange("p (hc dch o) -> p hc dch o",
                                                   hc=8, dch=2),
                        w2_r[ei].rearrange("hc h dch o -> h hc dch o"),
                        writes=[b_w2[wsl]])
                for gi, (gst, gn) in enumerate(F0_GRP):
                    sch.run("tensor",
                            lambda tns, ei=ei, gst=gst, gn=gn:
                            tns.matmul(pBe[:, 0:gn],
                                       eye8_sb[:, ei * 128:(ei + 1) * 128],
                                       maskT[:, gst:gst + gn],
                                       start=True, stop=True),
                            reads=[b_eye, b_maskT], writes=[b_pBe])
                    sch.run("vector",
                            lambda v, gn=gn: v.tensor_copy(Be[:, 0:gn],
                                                           pBe[:, 0:gn]),
                            reads=[b_pBe], writes=[b_Be])
                    for hc in range(8):
                        bank = hc % 2
                        po = pz[:, bank * 512:bank * 512 + gn]

                        def zmm(tns, wsl=wsl, hc=hc, gst=gst, gn=gn, po=po):
                            last = None
                            for dc in range(8):
                                last = tns.matmul(
                                    po,
                                    w1_sb[wsl][:, (dc * 8 + hc) * 128:(dc * 8 + hc + 1) * 128],
                                    xnf16[:, dc * T_EXT + gst:dc * T_EXT + gst + gn],
                                    start=(dc == 0), stop=(dc == 7))
                            return last
                        sch.run("tensor", zmm, reads=[b_w1[wsl], b_xnf, cb],
                                writes=[b_pz[bank]])
                        sch.run("vector",
                                lambda v, gn=gn, po=po:
                                v.tensor_tensor(zz[:, 0:gn], po, Be[:, 0:gn],
                                                ALU.mult),
                                reads=[b_pz[bank], b_Be], writes=[b_zz])
                        sch.run("scalar",
                                lambda s, hc=hc, gn=gn, ei=ei:
                                s.activation(hf[:, hc * 512:hc * 512 + gn],
                                             zz[:, 0:gn], AF.Gelu,
                                             bias=b1_sb[:, ei * 8 + hc:ei * 8 + hc + 1],
                                             scale=1.0),
                                reads=[b_zz, b_b12], writes=[b_hf])
                    ntt = (gn + 127) // 128
                    for ttl in range(ntt):
                        t = gi * 4 + ttl
                        for dch in range(2):
                            bank = (ttl * 2 + dch) % 2
                            po = py[:, bank * 512:(bank + 1) * 512]

                            def ymm(tns, wsl=wsl, ttl=ttl, dch=dch, po=po):
                                last = None
                                for hc in range(8):
                                    last = tns.matmul(
                                        po,
                                        hf[:, hc * 512 + ttl * 128:hc * 512 + (ttl + 1) * 128],
                                        w2_sb[wsl][:, (hc * 2 + dch) * 512:(hc * 2 + dch + 1) * 512],
                                        start=(hc == 0), stop=(hc == 7))
                                return last
                            sch.run("tensor", ymm, reads=[b_w2[wsl], b_hf, cb],
                                    writes=[b_py[bank]])
                            sch.run("vector",
                                    lambda v, t=t, ei=ei, dch=dch, po=po:
                                    v.scalar_tensor_tensor(
                                        acc[:, t * 1024 + dch * 512:t * 1024 + (dch + 1) * 512],
                                        po,
                                        mask_all[:, t * 8 + ei:t * 8 + ei + 1],
                                        acc[:, t * 1024 + dch * 512:t * 1024 + (dch + 1) * 512],
                                        ALU.mult, ALU.add),
                                    reads=[b_py[bank], b_mask], writes=[b_acc])

        sch.fence()
        with ExitStack() as ph:
            pe = ph.enter_context
            NSLOT = 2
            xs = [pe(nc.sbuf_tensor(_PFX + f"c4x{i}", [128, D], F32)) for i in range(NSLOT)]
            ss = [pe(nc.sbuf_tensor(_PFX + f"c4s{i}", [128, D], F32)) for i in range(NSLOT)]
            xbs = [Buf(f"c4x{i}") for i in range(NSLOT)]
            sbs = [Buf(f"c4s{i}") for i in range(NSLOT)]
            gsil = pe(nc.sbuf_tensor(_PFX + "gsil", [128, D], F32))
            b_g = Buf("gsil")
            x1t = [pe(nc.sbuf_tensor(_PFX + f"x1t{i}", [128, D], F32)) for i in range(NSLOT)]
            b_x1 = [Buf(f"x1t{i}") for i in range(NSLOT)]
            for t in range(NT_EXT):
                sl = t % NSLOT
                sch.dma("sync", f"c4x{sl}", xs[sl][:, :],
                        xe[t * 128:(t + 1) * 128, :], writes=[xbs[sl]])
                sch.dma("sync", f"c4s{sl}", ss[sl][:, :],
                        se[t * 128:(t + 1) * 128, :],
                        reads=[b_se] if b_se else None, writes=[sbs[sl]])
                sch.run("scalar",
                        lambda s, t=t: s.activation(
                            gsil[:, :], acc[:, t * 1024:(t + 1) * 1024], AF.Silu),
                        reads=[b_acc], writes=[b_g])
                sch.run("vector",
                        lambda v, sl=sl: v.tensor_mul(gsil[:, :], gsil[:, :],
                                                      ss[sl][:, :]),
                        reads=[sbs[sl]], writes=[b_g])
                sch.run("vector",
                        lambda v, sl=sl: v.tensor_add(x1t[sl][:, :], gsil[:, :],
                                                      xs[sl][:, :]),
                        reads=[b_g, xbs[sl]], writes=[b_x1[sl]])
                sch.dma("scalar", f"fo{sl}", o_x1[t * 128:(t + 1) * 128, :],
                        x1t[sl][:, :], reads=[b_x1[sl]], writes=[b_ox1])
            if final_wait:
                for kk in ("fo0", "fo1"):
                    sch._wait("scalar",
                              (f"dma_{kk}", sch.dsems[kk], sch.dcnt[kk]))
        sch.fence()


def host_F0_inputs(core, x, se, spart_all, rw, rb, w1, b1, w2, b2):
    b, h = core // 2, core % 2
    xb = to_f32(x[b])
    own0 = 2048 * h
    xe = np.zeros((T_EXT, D), np.float32)
    lo, hi = own0 - HALO, own0 + T_OWN + HALO
    slo, shi = max(lo, 0), min(hi, SEQ)
    xe[slo - lo:shi - lo] = xb[slo:shi]
    flag = np.full((8, 2), 1.0 if core == 0 else 0.0, np.float32)
    eye = np.zeros((8, 8 * 128), np.float32)
    for ei in range(8):
        eye[ei, ei * 128:(ei + 1) * 128] = 1.0
    w1_r = to_f32(w1).reshape(8, 8, 128, 8, 128)
    w2_r = to_f32(w2).reshape(8, 8, 128, 2, 512)
    b1_r = to_f32(b1).reshape(8, 8, 128).transpose(2, 0, 1)
    spall = to_f32(spart_all).transpose(1, 0, 2).reshape(8, 16)
    return dict(xe=xe, se=to_f32(se), spall=np.ascontiguousarray(spall),
                flag=flag, rw=to_f32(rw), rb=to_f32(rb).reshape(1, 8),
                w1_r=np.ascontiguousarray(w1_r),
                w2_r=np.ascontiguousarray(w2_r),
                b1_r=np.ascontiguousarray(b1_r), b2_r=to_f32(b2),
                eye8=eye, **host_consts_named())



VR1 = NT_EXT * 16 * 65
A1_DEBUG = None


def build_launch_A1():
    nc = bass.Bass("TRN2", target_bir_lowering=False)
    x1e = nc.dram_tensor("x1e", [T_EXT, D], F32, kind="ExternalInput").ap()
    wqkv_r = nc.dram_tensor("wqkv_r", [8, 128, 24, 128], F32, kind="ExternalInput").ap()
    qkvb_c = nc.dram_tensor("qkvb_c", [128, 24], F32, kind="ExternalInput").ap()
    qkvb_row = nc.dram_tensor("qkvb_row", [1, 3072], F32, kind="ExternalInput").ap()
    crope_r = nc.dram_tensor("crope_r", [128, T_EXT], F32, kind="ExternalInput").ap()
    srope_r = nc.dram_tensor("srope_r", [128, T_EXT], F32, kind="ExternalInput").ap()
    p2_r = nc.dram_tensor("p2_r", [128, 128], F32, kind="ExternalInput").ap()
    wmask_r = nc.dram_tensor("wmask_r", [6, 128, 512], F32, kind="ExternalInput").ap()
    vslab1 = nc.dram_tensor("vslab1", [128, VR1], F32, kind="ExternalInput").ap()
    bcol1 = nc.dram_tensor("bcol1", [128, NT_EXT], F32, kind="ExternalInput").ap()
    wo1_r = nc.dram_tensor("wo1_r", [8, 128, D], F32, kind="ExternalInput").ap()
    wob_row = nc.dram_tensor("wob_row", [1, D], F32, kind="ExternalInput").ap()
    o_s1 = nc.dram_tensor("o_s1", [T_OWN, D], F32, kind="ExternalOutput").ap()
    with ExitStack() as ctx:
        sch = Sched(nc, ctx)
        consts, cb = declare_consts(nc, sch, ctx)
        emit_A1(nc, sch, consts, cb, x1e, None, wqkv_r, qkvb_c, qkvb_row,
                crope_r, srope_r, p2_r, wmask_r, vslab1, bcol1, wo1_r,
                wob_row, o_s1, Buf("os1"), final_wait=True)
    return nc


def emit_A1(nc, sch, consts, cb, x1e, b_x1e, wqkv_r, qkvb_c, qkvb_row,
            crope_r, srope_r, p2_r, wmask_r, vslab1, bcol1, wo1_r, wob_row,
            o_s1, b_os1, final_wait=False):
    global A1_DBG_APS
    A1_DBG_APS = {}
    if A1_DEBUG is not None:
        for j in A1_DEBUG[2]:
            A1_DBG_APS[j] = nc.dram_tensor(f"dbg1_pt_{j}", [128, 512], F32,
                                           kind="ExternalOutput").ap()
        A1_DBG_APS["rc"] = nc.dram_tensor("dbg1_rc", [1, 512], F32,
                                          kind="ExternalOutput").ap()
    global _PFX
    _PFX = "a1_"
    with ExitStack() as ctx:
        e = ctx.enter_context

        Q1 = e(nc.sbuf_tensor(_PFX + "Q1", [128, 8 * T_EXT], BF16))
        K1 = e(nc.sbuf_tensor(_PFX + "K1", [128, 8 * T_EXT], BF16))
        b_Q1, b_K1 = Buf("Q1"), Buf("K1")
        vaug1 = e(nc.sbuf_tensor(_PFX + "vaug1", [128, VR1], BF16))
        b_va = Buf("va1")
        b_off = Buf("off1")

        ln_tmp = mk_ln_tmp(nc, ctx, "lnA1")

        with ExitStack() as ph:
            pe = ph.enter_context
            xnf1 = pe(nc.sbuf_tensor(_PFX + "xnf1", [128, 8 * T_EXT], BF16))
            b_xnf = Buf("xnf1")
            with ExitStack() as p1s:
                p1e = p1s.enter_context
                ptb = p1e(nc.psum_tensor(_PFX + "ptb", [128, 1024], BF16))
                b_ptb = Buf("ptb")
                NSLOT = 3
                xs = [p1e(nc.sbuf_tensor(_PFX + f"a1x{i}", [128, D], F32))
                      for i in range(NSLOT)]
                xb = [Buf(f"a1x{i}") for i in range(NSLOT)]
                xn16 = [p1e(nc.sbuf_tensor(_PFX + f"xn16_{i}", [128, D], BF16))
                        for i in range(2)]
                b_xn16 = [Buf(f"xn16_{i}") for i in range(2)]
                ln2 = [ln_tmp, mk_ln_tmp(nc, p1s, "lnA1b")]
                st2 = [None]
                for t in range(NT_EXT):
                    sl = t % NSLOT
                    p = t % 2
                    sch.dma("sync", f"a1x{sl}", xs[sl][:, :],
                            x1e[t * 128:(t + 1) * 128, :],
                            reads=[b_x1e] if b_x1e else None, writes=[xb[sl]])
                    emit_ln_tile(nc, sch, cb, xb[sl], xs[sl], ln2[p],
                                 want_f32=None,
                                 want_b16=(xn16[p], slice(0, D), b_xn16[p]),
                                 consts=consts)
                    if st2[0]:
                        st2[0]()
                    st2[0] = (lambda t=t, p=p: emit_transpose8(
                        nc, sch, xn16[p], b_xn16[p], ptb, b_ptb, xnf1,
                        [ch * T_EXT + t * 128 for ch in range(8)],
                        b_xnf, consts["identb"], cb))
                st2[0]()
            sch.fence()

            with ExitStack() as p2a:
                ae = p2a.enter_context
                wqk_sb = ae(nc.sbuf_tensor(_PFX + "wqk_sb", [128, 16 * 8 * 128], BF16))
                b_wqk = Buf("wqk")
                sch.dma("gpsimd", "wqk",
                        wqk_sb[:, :].rearrange("p (dc cc c) -> p dc cc c",
                                               cc=16, dc=8),
                        wqkv_r.rearrange("dc p cc c -> p dc cc c")[:, :, 0:16, :],
                        writes=[b_wqk])
                qb_sb = ae(nc.sbuf_tensor(_PFX + "qb_sb", [128, 24], F32))
                b_qb = Buf("qb")
                sch.dma("gpsimd", "qb", qb_sb[:, :], qkvb_c, writes=[b_qb])
                crope = ae(nc.sbuf_tensor(_PFX + "crope", [128, T_EXT], BF16))
                srope = ae(nc.sbuf_tensor(_PFX + "srope", [128, T_EXT], BF16))
                b_rope = Buf("rope")
                sch.dma("gpsimd", "rope", crope[:, :], crope_r, writes=[b_rope])
                sch.dma("gpsimd", "rope", srope[:, :], srope_r, writes=[b_rope])
                p2sb = ae(nc.sbuf_tensor(_PFX + "p2sb", [128, 128], BF16))
                b_p2 = Buf("p2")
                sch.dma("gpsimd", "p2", p2sb[:, :], p2_r, writes=[b_p2])

                pqk = ae(nc.psum_tensor(_PFX + "pqk", [128, 1024], F32))
                b_pqk = [Buf("pqk0"), Buf("pqk1")]
                prot = ae(nc.psum_tensor(_PFX + "prot", [128, 512], F32))
                b_prot = Buf("prot")
                qk16 = ae(nc.sbuf_tensor(_PFX + "qk16", [128, 512], BF16))
                b_qk16 = Buf("qk16")
                t1 = ae(nc.sbuf_tensor(_PFX + "t1", [128, 512], F32))
                b_t1 = Buf("t1")
                t2 = ae(nc.sbuf_tensor(_PFX + "t2", [128, 512], F32))
                b_t2 = Buf("t2")
                for cc in range(16):
                    dst, b_dst = (Q1, b_Q1) if cc < 8 else (K1, b_K1)
                    hp = cc % 8
                    for g in range(NQT):
                        n = QT_N[g]
                        st = QT_START[g]
                        bank = (cc * NQT + g) % 2
                        po = pqk[:, bank * 512:bank * 512 + n]

                        def mm(tns, cc=cc, st=st, n=n, po=po):
                            last = None
                            for dc in range(8):
                                last = tns.matmul(
                                    po,
                                    wqk_sb[:, (dc * 16 + cc) * 128:(dc * 16 + cc + 1) * 128],
                                    xnf1[:, dc * T_EXT + st:dc * T_EXT + st + n],
                                    start=(dc == 0), stop=(dc == 7))
                            return last
                        sch.run("tensor", mm, reads=[b_wqk, b_xnf, cb],
                                writes=[b_pqk[bank]])
                        sch.run("vector",
                                lambda v, cc=cc, n=n, po=po:
                                v.tensor_scalar_add(qk16[:, 0:n], po,
                                                    qb_sb[:, cc:cc + 1]),
                                reads=[b_pqk[bank], b_qb], writes=[b_qk16])
                        sch.run("tensor",
                                lambda t, n=n:
                                t.matmul(prot[:, 0:n], p2sb[:, :], qk16[:, 0:n],
                                         start=True, stop=True),
                                reads=[b_p2, b_qk16], writes=[b_prot])
                        sch.run("vector",
                                lambda v, st=st, n=n:
                                v.tensor_mul(t1[:, 0:n], qk16[:, 0:n],
                                             crope[:, st:st + n]),
                                reads=[b_qk16, b_rope], writes=[b_t1])
                        sch.run("vector",
                                lambda v, st=st, n=n:
                                v.tensor_tensor(t2[:, 0:n], prot[:, 0:n],
                                                srope[:, st:st + n], ALU.mult),
                                reads=[b_prot, b_rope], writes=[b_t2])
                        sch.run("vector",
                                lambda v, hp=hp, st=st, n=n, dst=dst:
                                v.tensor_add(dst[:, hp * T_EXT + st:hp * T_EXT + st + n],
                                             t1[:, 0:n], t2[:, 0:n]),
                                reads=[b_t1, b_t2], writes=[b_dst])

            sch.fence()
            with ExitStack() as p2b:
                be = p2b.enter_context
                wv_sb = be(nc.sbuf_tensor(_PFX + "wv1_sb", [128, 8 * 8 * 128], BF16))
                b_wv = Buf("wv1")
                sch.dma("gpsimd", "wv1",
                        wv_sb[:, :].rearrange("p (dc cc c) -> p dc cc c",
                                              cc=8, dc=8),
                        wqkv_r.rearrange("dc p cc c -> p dc cc c")[:, :, 16:24, :],
                        writes=[b_wv])
                vbrow = be(nc.sbuf_tensor(_PFX + "vbrow", [1, 3072], BF16))
                b_vb = Buf("vbrow")
                sch.dma("gpsimd", "vb", vbrow[:, :], qkvb_row, writes=[b_vb])
                sch.dma("gpsimd", "vslab1", vaug1[:, :], vslab1, writes=[b_va])
                ppv = be(nc.psum_tensor(_PFX + "ppv", [128, 1024], F32))
                b_ppv = [Buf("ppv0"), Buf("ppv1")]
                for kt in range(NT_EXT):
                    for hp in range(8):
                        bank = (kt * 8 + hp) % 2
                        po = ppv[:, bank * 512:bank * 512 + 128]

                        def mm(tns, kt=kt, hp=hp, po=po):
                            for dc in range(8):
                                tns.matmul(
                                    po,
                                    xnf1[:, dc * T_EXT + kt * 128:dc * T_EXT + (kt + 1) * 128],
                                    wv_sb[:, (dc * 8 + hp) * 128:(dc * 8 + hp + 1) * 128],
                                    start=(dc == 0), stop=False)
                            return tns.matmul(
                                po, consts["ones_row"][:, :],
                                vbrow[:, (16 + hp) * 128:(17 + hp) * 128],
                                start=False, stop=True)
                        sch.run("tensor", mm, reads=[b_wv, b_xnf, b_vb, cb],
                                writes=[b_ppv[bank]])
                        base = (kt * 16 + hp * 2) * 65

                        def ev(v, po=po, base=base):
                            return v.tensor_copy(
                                bass.AP(vaug1, base,
                                        [[VR1, 128], [65, 2], [1, 64]]),
                                po.rearrange("p (h c) -> p h c", h=2))
                        sch.run("vector", ev, reads=[b_ppv[bank]], writes=[b_va])

        sch.fence()
        with ExitStack() as ph34:
            off1 = ph34.enter_context(
                nc.sbuf_tensor(_PFX + "off1", [128, 8 * T_OWN], BF16))
            run_p34(nc, sch, ctx, ph34, consts, cb, Q1, K1, b_Q1, b_K1,
                    vaug1, b_va, off1, b_off, wmask_r, wo1_r, wob_row, o_s1,
                    bcol1, b_os1, final_wait)
        sch.fence()


def run_p34(nc, sch, ctx, ph34, consts, cb, Q1, K1, b_Q1, b_K1, vaug1, b_va,
            off1, b_off, wmask_r, wo1_r, wob_row, o_s1, bcol1, b_os1,
            final_wait):
        with ExitStack() as ph:
            pe = ph.enter_context
            wm_sb = pe(nc.sbuf_tensor(_PFX + "wm_sb", [128, 6 * 512], F32))
            b_wm = Buf("wm")
            bc1_sb = pe(nc.sbuf_tensor(_PFX + "bc1_sb", [128, NT_EXT], F32))
            b_bc1 = Buf("bc1c")
            sch.dma("gpsimd", "bc1", bc1_sb[:, :], bcol1, writes=[b_bc1])
            sch.dma("gpsimd", "wm",
                    wm_sb[:, :].rearrange("p (j c) -> p j c", j=6),
                    wmask_r.rearrange("j p c -> p j c"), writes=[b_wm])
            pT = [pe(nc.sbuf_tensor(_PFX + f"pT1_{i}", [128, 512], BF16))
                  for i in range(12)]
            b_pT = [Buf("pT6a"), Buf("pT6b")]
            smsk = [pe(nc.sbuf_tensor(_PFX + f"smsk1_{i}", [128, 512], F32))
                    for i in range(12)]
            b_smsk = [Buf("sm6a"), Buf("sm6b")]
            recip = [pe(nc.sbuf_tensor(_PFX + f"rc1_{i}", [1, 512], F32))
                     for i in range(2)]
            b_rc = [Buf("rc0"), Buf("rc1")]
            bc_sb = [pe(nc.sbuf_tensor(_PFX + f"bc1_{i}", [64, 512], F32))
                     for i in range(2)]
            b_bc = [Buf("bc0"), Buf("bc1")]
            pS = [pe(nc.psum_tensor(_PFX + f"pS1p_{i}", [128, 1024], F32))
                  for i in range(3)]
            b_pS = Buf("pS6")
            pO = [pe(nc.psum_tensor(_PFX + f"pO1_{i}", [65, 512], F32)) for i in range(2)]
            b_pO = [Buf("pO0"), Buf("pO1")]

            pending = [None]

            def flush():
                if pending[0] is not None:
                    pending[0]()
                    pending[0] = None

            for k in range(4):
                for h in range(16):
                    hp, half = h // 2, h % 2
                    obank = (k * 16 + h) % 2
                    par = h % 2
                    q_ap = Q1[64 * half:64 * half + 64,
                              hp * T_EXT + 128 + 512 * k:
                              hp * T_EXT + 128 + 512 * k + 512]

                    def qk_emit(t, k=k, hp=hp, half=half, q=q_ap):
                        last = None
                        for j in range(6):
                            kbe = 4 * k + j
                            last = t.matmul(
                                pS[j // 2][:, (j % 2) * 512:(j % 2 + 1) * 512],
                                K1[64 * half:64 * half + 64,
                                   hp * T_EXT + kbe * 128:hp * T_EXT + (kbe + 1) * 128],
                                q, start=True, stop=True)
                        return last
                    sch.run("tensor", qk_emit, reads=[b_K1, b_Q1], writes=[b_pS])

                    def m_emit(v, par=par):
                        last = None
                        for j in range(6):
                            last = v.tensor_tensor(
                                smsk[par * 6 + j][:, :],
                                pS[j // 2][:, (j % 2) * 512:(j % 2 + 1) * 512],
                                wm_sb[:, j * 512:(j + 1) * 512], ALU.add)
                        return last
                    sch.run("vector", m_emit, reads=[b_pS, b_wm],
                            writes=[b_smsk[par]])

                    def e_emit(sce, k=k, par=par):
                        last = None
                        for j in range(6):
                            kbe = 4 * k + j
                            last = sce.activation(
                                pT[par * 6 + j][:, :], smsk[par * 6 + j][:, :],
                                AF.Exp, scale=0.125,
                                bias=bc1_sb[:, kbe:kbe + 1])
                        return last
                    sch.run("scalar", e_emit, reads=[b_smsk[par], b_bc1],
                            writes=[b_pT[par]])

                    def tail(k=k, h=h, hp=hp, half=half, ob=obank, par=par):
                        def pv_emit(t):
                            last = None
                            for j in range(6):
                                kbe = 4 * k + j
                                last = t.matmul(
                                    pO[ob][:, :],
                                    vaug1[:, (kbe * 16 + h) * 65:(kbe * 16 + h) * 65 + 65],
                                    pT[par * 6 + j][:, :], start=(j == 0),
                                    stop=(j == 5))
                            return last
                        sch.run("tensor", pv_emit, reads=[b_va, b_pT[par]],
                                writes=[b_pO[ob]])
                        sch.run("vector",
                                lambda v: v.reciprocal(recip[ob][:, :],
                                                       pO[ob][64:65, :]),
                                reads=[b_pO[ob]], writes=[b_rc[ob]])
                        sch.run("tensor",
                                lambda t: t.matmul(
                                    pS[0][0:64, 0:512],
                                    consts["ones_rowf"][:, 0:64],
                                    recip[ob][:, :], start=True, stop=True),
                                reads=[b_rc[ob]], writes=[b_pS])
                        sch.run("vector",
                                lambda v: v.tensor_copy(bc_sb[ob][:, :],
                                                        pS[0][0:64, 0:512]),
                                reads=[b_pS], writes=[b_bc[ob]])
                        ocol = hp * T_OWN + 512 * k
                        sch.run("vector",
                                lambda v: v.tensor_tensor(
                                    off1[64 * half:64 * half + 64, ocol:ocol + 512],
                                    pO[ob][0:64, :], bc_sb[ob][:, :], ALU.mult),
                                reads=[b_pO[ob], b_bc[ob]], writes=[b_off])
                    flush()
                    pending[0] = tail
            flush()
        sch.fence()
        with ExitStack() as ph:
            pe = ph.enter_context
            wo_sb = pe(nc.sbuf_tensor(_PFX + "wo1_sb", [128, 8 * D], BF16))
            b_wo = Buf("wo1")
            sch.dma("gpsimd", "wo1",
                    wo_sb[:, :].rearrange("p (cc o) -> p cc o", cc=8),
                    wo1_r.rearrange("cc p o -> p cc o"), writes=[b_wo])
            wob_sb = pe(nc.sbuf_tensor(_PFX + "wob_sb", [1, D], BF16))
            b_wob = Buf("wob")
            sch.dma("gpsimd", "wob", wob_sb[:, :], wob_row, writes=[b_wob])
            ps = pe(nc.psum_tensor(_PFX + "psD", [128, 1024], F32))
            b_ps = [Buf("psD0"), Buf("psD1")]
            s_sb = [pe(nc.sbuf_tensor(_PFX + f"s1sb{i}", [128, 512], F32)) for i in range(2)]
            b_ssb = [Buf("s1sb0"), Buf("s1sb1")]
            for t in range(NT_OWN):
                for oc in range(2):
                    bank = (t * 2 + oc) % 2
                    po = ps[:, bank * 512:(bank + 1) * 512]

                    def mm(tns, t=t, oc=oc, po=po):
                        for cc in range(8):
                            tns.matmul(
                                po,
                                off1[:, cc * T_OWN + t * 128:cc * T_OWN + (t + 1) * 128],
                                wo_sb[:, cc * D + oc * 512:cc * D + (oc + 1) * 512],
                                start=(cc == 0), stop=False)
                        return tns.matmul(po, consts["ones_row"][:, :],
                                          wob_sb[:, oc * 512:(oc + 1) * 512],
                                          start=False, stop=True)
                    sch.run("tensor", mm, reads=[b_off, b_wo, b_wob, cb],
                            writes=[b_ps[bank]])
                    sch.run("vector",
                            lambda v, bank=bank, po=po:
                            v.tensor_copy(s_sb[bank][:, :], po),
                            reads=[b_ps[bank]], writes=[b_ssb[bank]])
                    sch.dma("scalar", f"s1o{bank}",
                            o_s1[t * 128:(t + 1) * 128, oc * 512:(oc + 1) * 512],
                            s_sb[bank][:, :], reads=[b_ssb[bank]],
                            writes=[b_os1])
            if final_wait:
                for kk in ("s1o0", "s1o1"):
                    sch._wait("scalar",
                              (f"dma_{kk}", sch.dsems[kk], sch.dcnt[kk]))


def host_A1_inputs(core, x1_full, Wqkv, bqkv, Wo, bo, rope_cache):
    b, h = core // 2, core % 2
    own0 = 2048 * h
    lo, hi = own0 - HALO, own0 + T_OWN + HALO
    slo, shi = max(lo, 0), min(hi, SEQ)
    x1e = np.zeros((T_EXT, D), np.float32)
    x1e[slo - lo:shi - lo] = to_f32(x1_full[b])[slo:shi]
    wqkv_r = to_f32(Wqkv).reshape(8, 128, 24, 128)
    qkvb_c = to_f32(bqkv).reshape(24, 128).T
    qkvb_row = to_f32(bqkv).reshape(1, 3072)
    cache = to_f32(rope_cache)
    pos = (lo + np.arange(T_EXT)) % 256
    pos = np.where(pos < 0, pos + 256, pos)
    cosv = cache[pos][:, :, 0]
    sinv = cache[pos][:, :, 1]
    cos64 = np.repeat(cosv, 2, axis=1)
    sin64 = np.repeat(sinv, 2, axis=1)
    sgn = np.tile(np.array([-1.0, 1.0], np.float32), 32)
    sin64s = sin64 * sgn[None, :]
    crope = np.concatenate([cos64, cos64], axis=1).T
    srope = np.concatenate([sin64s, sin64s], axis=1).T
    p2 = np.zeros((128, 128), np.float32)
    for i in range(64):
        p2[2 * i, 2 * i + 1] = 1.0
        p2[2 * i + 1, 2 * i] = 1.0
    wm = np.zeros((6, 128, 512), np.float32)
    for j in range(6):
        kr = np.arange(128)[:, None]
        ql = np.arange(512)[None, :]
        rel = 128 * (j - 1) + kr - ql
        wm[j] = np.where(np.abs(rel) <= 128, 0.0, -1e9)
    vones = np.zeros(T_EXT, np.float32)
    vones[slo - lo:shi - lo] = 1.0
    vs = np.zeros((128, VR1), np.float32)
    for kt in range(NT_EXT):
        for hh in range(16):
            vs[:, (kt * 16 + hh) * 65 + 64] = vones[kt * 128:(kt + 1) * 128]
    wo1_r = to_f32(Wo).reshape(8, 128, D)
    wob = to_f32(bo).reshape(1, D)
    bc1 = np.zeros((128, NT_EXT), np.float32)
    for kt in range(NT_EXT):
        bc1[:, kt] = np.where(vones[kt * 128:(kt + 1) * 128] > 0, 0.0, -1.25e8)
    return dict(x1e=x1e, bcol1=bc1,
                wqkv_r=wqkv_r, qkvb_c=np.ascontiguousarray(qkvb_c),
                qkvb_row=qkvb_row, crope_r=np.ascontiguousarray(crope),
                srope_r=np.ascontiguousarray(srope), p2_r=p2, wmask_r=wm,
                vslab1=vs, wo1_r=wo1_r, wob_row=wob, **host_consts_named())



F1_GRP = [(0, 512), (512, 512), (1024, 512), (1536, 512)]


def build_launch_F1():
    nc = bass.Bass("TRN2", target_bir_lowering=False)
    x1o = nc.dram_tensor("x1o", [T_OWN, D], F32, kind="ExternalInput").ap()
    s1 = nc.dram_tensor("s1", [T_OWN, D], F32, kind="ExternalInput").ap()
    r1w = nc.dram_tensor("r1w", [D, 9], F32, kind="ExternalInput").ap()
    r1b = nc.dram_tensor("r1b", [1, 9], F32, kind="ExternalInput").ap()
    w1_r = nc.dram_tensor("w1f_r", [8, 8, 128, 32, 128], F32, kind="ExternalInput").ap()
    w2_r = nc.dram_tensor("w2f_r", [8, 32, 128, 2, 512], F32, kind="ExternalInput").ap()
    o_out = nc.dram_tensor("o_out", [T_OWN, D], F32, kind="ExternalOutput").ap()
    with ExitStack() as ctx:
        sch = Sched(nc, ctx)
        consts, cb = declare_consts(nc, sch, ctx)
        emit_F1(nc, sch, consts, cb, x1o, None, s1, None, r1w, r1b, w1_r,
                w2_r, o_out, final_wait=True)
    return nc


F1_SPARSE = True
F1_C = 1024


def emit_F1(*a, **kw):
    if F1_SPARSE:
        return emit_F1_sparse(*a, **kw)
    return emit_F1_dense(*a, **kw)


def emit_F1_sparse(nc, sch, consts, cb, x1o, b_x1o, s1, b_s1, r1w, r1b,
                   w1_r, w2_r, o_out, final_wait=False):
    global _PFX
    _PFX = "f1_"
    C = F1_C
    SLOTS = 8 * C
    slab = nc.dram_tensor("f1_slab", [SLOTS, D], BF16).ap()
    yslab = nc.dram_tensor("f1_yslab", [SLOTS, D], BF16).ap()
    b_slab = Buf("slab")
    b_ysl = [Buf(f"ysl{e}") for e in range(8)]
    with ExitStack() as ctx:
        e_ = ctx.enter_context
        b_acc = Buf("acc")
        gates = e_(nc.sbuf_tensor(_PFX + "gates", [128, NT_OWN * 9], F32))
        b_gates = Buf("gates")
        stats = e_(nc.sbuf_tensor(_PFX + "stats", [128, NT_OWN * 2], F32))
        b_stats = Buf("stats")
        idx_sb = e_(nc.sbuf_tensor(_PFX + "idx", [128, NT_OWN * 9], mybir.dt.uint32))
        b_idx = Buf("idx")
        baseT = e_(nc.sbuf_tensor(_PFX + "baseT", [1, 9], F32))
        b_base = Buf("baseT")
        r1w_sb = e_(nc.sbuf_tensor(_PFX + "r1w_sb", [128, 72], F32))
        r1b_sb = e_(nc.sbuf_tensor(_PFX + "r1b_sb", [1, 9], F32))
        b_rw = Buf("r1w")
        sch.dma("gpsimd", "rwld", r1w_sb[:, :].rearrange("p (c e) -> p c e", c=8),
                r1w.rearrange("(c p) e -> p c e", p=128), writes=[b_rw])
        sch.dma("gpsimd", "rwld", r1b_sb[:, :], r1b, writes=[b_rw])

        ln_tmp = mk_ln_tmp(nc, ctx, "lnF1")
        breg = ctx.enter_context(nc.gpsimd.register("f1bnd"))
        sch.run("gpsimd", lambda g: g.reg_mov(breg, C - 1))
        for zi in range(SLOTS // 128):
            sch.dma("gpsimd", "zf", slab[zi * 128:(zi + 1) * 128, :],
                    consts["zeros_t"][:, :], reads=[cb], writes=[b_slab])
            sch.dma("gpsimd", f"zfy{zi // 8}", yslab[zi * 128:(zi + 1) * 128, :],
                    consts["zeros_t"][:, :], reads=[cb], writes=[b_ysl[zi // 8]])

        with ExitStack() as ph:
            pe = ph.enter_context
            t2_tmp = mk_top2_tmp(nc, ph, "t2F1", 9)
            NSLOT = 3
            xs = [pe(nc.sbuf_tensor(_PFX + f"f1x{i}", [128, D], F32))
                  for i in range(NSLOT)]
            xb = [Buf(f"f1x{i}") for i in range(NSLOT)]
            xn16b = [pe(nc.sbuf_tensor(_PFX + f"xn16_{i}", [128, D], BF16))
                     for i in range(2)]
            b_xn16b = [Buf(f"xn16_{i}") for i in range(2)]
            xn32b = [pe(nc.sbuf_tensor(_PFX + f"xn32_{i}", [128, D], F32))
                     for i in range(2)]
            b_xn32b = [Buf(f"xn32_{i}") for i in range(2)]
            xnf32 = pe(nc.sbuf_tensor(_PFX + "xnf32", [128, D], F32))
            b_xnf32 = Buf("xnf32")
            lgt = pe(nc.sbuf_tensor(_PFX + "lgt", [128, 9], F32))
            b_lgt = Buf("lgt")
            B1 = pe(nc.sbuf_tensor(_PFX + "B1", [128, 9], F32))
            B2 = pe(nc.sbuf_tensor(_PFX + "B2", [128, 9], F32))
            b_B1, b_B2 = Buf("B1"), Buf("B2")
            ind = pe(nc.sbuf_tensor(_PFX + "ind", [128, 9], BF16))
            b_ind = Buf("ind")
            indv = pe(nc.sbuf_tensor(_PFX + "indv", [128, 9], F32))
            b_indv = Buf("indv")
            over = pe(nc.sbuf_tensor(_PFX + "over", [128, 9], F32))
            b_over = Buf("over")
            df = pe(nc.sbuf_tensor(_PFX + "df", [128, 9], F32))
            b_df = Buf("df")
            ptf = pe(nc.psum_tensor(_PFX + "ptf", [128, 1024], F32))
            b_ptf = Buf("ptf")
            pr = pe(nc.psum_tensor(_PFX + "pr", [128, 9], F32))
            b_pr = Buf("pr")
            pPx = pe(nc.psum_tensor(_PFX + "pPx", [128, 9], F32))
            b_pPx = Buf("pPx")
            pcs = pe(nc.psum_tensor(_PFX + "pcs", [1, 9], F32))
            b_pcs = Buf("pcs")

            sch.run("vector", lambda v: v.memzero(baseT[:, :]), writes=[b_base])

            st2 = [None]

            def stage2(t, p):
                def t_emit(tns):
                    last = None
                    for ch in range(8):
                        last = tns.transpose(ptf[:, ch * 128:(ch + 1) * 128],
                                             xn32b[p][:, ch * 128:(ch + 1) * 128],
                                             consts["identf"][:, :])
                    return last
                sch.run("tensor", t_emit, reads=[b_xn32b[p], cb], writes=[b_ptf])

                def v_emit(v):
                    last = None
                    for ch in range(8):
                        last = v.tensor_copy(xnf32[:, ch * 128:(ch + 1) * 128],
                                             ptf[:, ch * 128:(ch + 1) * 128])
                    return last
                sch.run("vector", v_emit, reads=[b_ptf], writes=[b_xnf32])

                def r_emit(tns):
                    for ch in range(8):
                        tns.matmul(pr[:, :], xnf32[:, ch * 128:(ch + 1) * 128],
                                   r1w_sb[:, ch * 9:(ch + 1) * 9],
                                   start=(ch == 0), stop=False)
                    return tns.matmul(pr[:, :], consts["ones_rowf"][:, :],
                                      r1b_sb[:, :], start=False, stop=True)
                sch.run("tensor", r_emit, reads=[b_xnf32, b_rw, cb], writes=[b_pr])
                sch.run("vector", lambda v: v.tensor_copy(lgt[:, :], pr[:, :]),
                        reads=[b_pr], writes=[b_lgt])
                emit_top2(nc, sch, lgt, b_lgt, 9, t2_tmp,
                          {"B1": (B1, b_B1), "B2": (B2, b_B2)}, None,
                          with_z=True)
                sch.run("vector",
                        lambda v: v.tensor_add(
                            gates[:, t * 9:(t + 1) * 9], B1[:, :], B2[:, :]),
                        reads=[b_B1, b_B2], writes=[b_gates])
                sch.run("vector",
                        lambda v: v.tensor_single_scalar(
                            ind[:, :], gates[:, t * 9:(t + 1) * 9], 0.0,
                            ALU.is_gt),
                        reads=[b_gates], writes=[b_ind])
                sch.run("tensor",
                        lambda tns: tns.matmul(pcs[:, :],
                                               consts["ones_colb"][:, :],
                                               ind[:, :], start=True, stop=True),
                        reads=[b_ind, cb], writes=[b_pcs])

                def px_emit(tns):
                    tns.matmul(pPx[:, :], consts["lt_excl"][:, :], ind[:, :],
                               start=True, stop=False)
                    return tns.matmul(pPx[:, :], consts["ones_rowf"][:, :],
                                      baseT[:, :], start=False, stop=True)
                sch.run("tensor", px_emit, reads=[b_ind, b_base, cb],
                        writes=[b_pPx])
                sch.run("vector",
                        lambda v: v.tensor_scalar(indv[:, :], ind[:, :],
                                                  -1.0, 1.0, ALU.mult, ALU.add),
                        reads=[b_ind], writes=[b_indv])
                sch.run("vector",
                        lambda v: v.tensor_single_scalar(
                            over[:, :], pPx[:, :], float(C), ALU.is_ge),
                        reads=[b_pPx], writes=[b_over])
                sch.run("vector",
                        lambda v: v.scalar_tensor_tensor(
                            df[:, :], indv[:, :], 3e9, pPx[:, :],
                            ALU.mult, ALU.add),
                        reads=[b_indv, b_pPx], writes=[b_df])
                sch.run("vector",
                        lambda v: v.scalar_tensor_tensor(
                            df[:, :], over[:, :], 3e9, df[:, :],
                            ALU.mult, ALU.add),
                        reads=[b_over], writes=[b_df])
                sch.run("vector",
                        lambda v: v.tensor_copy(
                            idx_sb[:, t * 9:(t + 1) * 9], df[:, :]),
                        reads=[b_df], writes=[b_idx])
                sch.run("vector",
                        lambda v: v.tensor_tensor(baseT[:, :], baseT[:, :],
                                                  pcs[:, :], ALU.add),
                        reads=[b_pcs], writes=[b_base])
                for ei in range(8):
                    sch.idma("scat",
                             lambda g, t=t, p=p, ei=ei: g.indirect_dma_start(
                                 slab,
                                 bass.IndirectOffsetOnAxis(
                                     idx_sb[:, t * 9 + ei:t * 9 + ei + 1], 0),
                                 xn16b[p][:, :], None,
                                 element_offset=ei * C * D,
                                 bounds_check=breg, oob_is_err=False),
                             reads=[b_idx, b_xn16b[p]], writes=[b_slab])

            for t in range(NT_OWN):
                sl = t % NSLOT
                p = t % 2
                sch.dma("sync", f"f1x{sl}", xs[sl][:, :],
                        x1o[t * 128:(t + 1) * 128, :],
                        reads=[b_x1o] if b_x1o else None, writes=[xb[sl]])
                emit_ln_tile(nc, sch, cb, xb[sl], xs[sl],
                             ln_tmp if p == 0 else mk_f1_ln2(nc, ph),
                             want_f32=(xn32b[p], slice(0, D), b_xn32b[p]),
                             want_b16=(xn16b[p], slice(0, D), b_xn16b[p]),
                             stats_out=(stats, 2 * t, b_stats),
                             consts=consts)
                if st2[0]:
                    st2[0]()
                st2[0] = (lambda t=t, p=p: stage2(t, p))
            st2[0]()

        sch.fence()
        with ExitStack() as ph:
            pe = ph.enter_context
            pz = pe(nc.psum_tensor(_PFX + "pz", [128, 1024], F32))
            b_pz = [Buf("pz0"), Buf("pz1")]
            py = pe(nc.psum_tensor(_PFX + "py", [128, 1024], F32))
            b_py = [Buf("py0"), Buf("py1")]
            ptb = pe(nc.psum_tensor(_PFX + "ptb", [128, 1024], BF16))
            b_ptb = Buf("ptb")
            xsl = [pe(nc.sbuf_tensor(_PFX + f"xsl{i}", [128, D], BF16))
                   for i in range(2)]
            b_xsl = [Buf(f"xsl{i}") for i in range(2)]
            xnfE = [pe(nc.sbuf_tensor(_PFX + f"xnfE{i}", [128, 8 * C], BF16))
                    for i in range(2)]
            b_xnfE = [Buf(f"xnfE{i}") for i in range(2)]
            hf = pe(nc.sbuf_tensor(_PFX + "hf", [128, 8 * C], BF16))
            b_hf = Buf("hf")
            w1_sb = [pe(nc.sbuf_tensor(_PFX + f"w1s{i}", [128, 8192], BF16))
                     for i in range(2)]
            w2_sb = [pe(nc.sbuf_tensor(_PFX + f"w2s{i}", [128, 8192], BF16))
                     for i in range(2)]
            b_w1 = [Buf("w1s0"), Buf("w1s1")]
            b_w2 = [Buf("w2s0"), Buf("w2s1")]
            yacc = pe(nc.sbuf_tensor(_PFX + "yacc", [128, C * 8], BF16))
            b_yacc = Buf("yacc")
            acc = pe(nc.sbuf_tensor(_PFX + "acc", [128, NT_OWN * 1024], F32))
            ygat = [pe(nc.sbuf_tensor(_PFX + f"ygat{i}", [128, D], BF16))
                    for i in range(2)]
            b_ygat = [Buf(f"ygat{i}") for i in range(2)]
            for i in range(2):
                sch.run("vector", lambda v, i=i: v.tensor_copy(
                    ygat[i][:, :], consts["zeros_t"][:, :]),
                        reads=[cb], writes=[b_ygat[i]])

            def load_slab(ei):
                ep = ei % 2
                for st in range(8):
                    ssl = st % 2
                    sch.dma("sync", f"xsl{ssl}", xsl[ssl][:, :],
                            slab[(ei * C + st * 128):(ei * C + (st + 1) * 128), :],
                            reads=[b_slab], writes=[b_xsl[ssl]])

            def trans_slab(ei):
                ep = ei % 2
                for st in range(8):
                    ssl = st % 2
                    emit_transpose8(nc, sch, xsl[ssl], b_xsl[ssl], ptb, b_ptb,
                                    xnfE[ep],
                                    [ch * C + st * 128 for ch in range(8)],
                                    b_xnfE[ep], consts["identb"], cb)

            def gather_acc(ei):
                for t in range(NT_OWN):
                    gsl = t % 2
                    sch.idma(f"ygat{gsl}",
                             lambda g, t=t, ei=ei, gsl=gsl:
                             g.indirect_dma_start(
                                 ygat[gsl][:, :], None, yslab,
                                 bass.IndirectOffsetOnAxis(
                                     idx_sb[:, t * 9 + ei:t * 9 + ei + 1], 0),
                                 element_offset=ei * C * D,
                                 bounds_check=breg, oob_is_err=False),
                             reads=[b_ysl[ei], b_idx], writes=[b_ygat[gsl]])
                    if ei == 0:
                        sch.run("vector",
                                lambda v, t=t, ei=ei, gsl=gsl:
                                v.tensor_single_scalar(
                                    acc[:, t * 1024:(t + 1) * 1024],
                                    ygat[gsl][:, :],
                                    gates[:, t * 9 + ei:t * 9 + ei + 1],
                                    ALU.mult),
                                reads=[b_ygat[gsl], b_gates], writes=[b_acc])
                    else:
                        sch.run("vector",
                                lambda v, t=t, ei=ei, gsl=gsl:
                                v.scalar_tensor_tensor(
                                    acc[:, t * 1024:(t + 1) * 1024],
                                    ygat[gsl][:, :],
                                    gates[:, t * 9 + ei:t * 9 + ei + 1],
                                    acc[:, t * 1024:(t + 1) * 1024],
                                    ALU.mult, ALU.add),
                                reads=[b_ygat[gsl], b_gates], writes=[b_acc])

            def load_w(gidx):
                ei, hs = gidx // 4, gidx % 4
                wsl = gidx % 2
                sch.dma("gpsimd", f"w1_{wsl}",
                        w1_sb[wsl][:, :].rearrange(
                            "p (dc hc c) -> p dc hc c", dc=8, hc=8),
                        w1_r[ei].rearrange(
                            "dc d hc c -> d dc hc c")[:, :, hs * 8:(hs + 1) * 8, :],
                        writes=[b_w1[wsl]])

            def load_w2(gidx):
                ei, hs = gidx // 4, gidx % 4
                wsl = gidx % 2
                sch.dma("gpsimd", f"w2_{wsl}",
                        w2_sb[wsl][:, :].rearrange(
                            "p (hc dch o) -> p hc dch o", hc=8, dch=2),
                        w2_r[ei].rearrange(
                            "hc h dch o -> h hc dch o")[:, hs * 8:(hs + 1) * 8, :, :],
                        writes=[b_w2[wsl]])

            load_slab(0)
            trans_slab(0)
            load_w(0)
            load_w2(0)
            for ei in range(8):
                ep = ei % 2
                for hs in range(4):
                    gidx = ei * 4 + hs
                    wsl = gidx % 2
                    if gidx + 1 < 32:
                        load_w(gidx + 1)
                        load_w2(gidx + 1)
                    if hs == 1 and ei + 1 < 8:
                        load_slab(ei + 1)
                    if hs == 3 and ei + 1 < 8:
                        trans_slab(ei + 1)
                    for hc in range(8):
                        for gi in range(C // 512):
                            bank = (hc * 2 + gi) % 2
                            po = pz[:, bank * 512:(bank + 1) * 512]

                            def zmm(tns, wsl=wsl, hc=hc, gi=gi, po=po, ep=ep):
                                last = None
                                for dc in range(8):
                                    last = tns.matmul(
                                        po,
                                        w1_sb[wsl][:, (dc * 8 + hc) * 128:(dc * 8 + hc + 1) * 128],
                                        xnfE[ep][:, dc * C + gi * 512:dc * C + (gi + 1) * 512],
                                        start=(dc == 0), stop=(dc == 7))
                                return last
                            sch.run("tensor", zmm,
                                    reads=[b_w1[wsl], b_xnfE[ep], cb],
                                    writes=[b_pz[bank]])
                            sch.run("scalar",
                                    lambda s, hc=hc, gi=gi, po=po:
                                    s.activation(
                                        hf[:, hc * C + gi * 512:hc * C + (gi + 1) * 512],
                                        po, AF.Gelu),
                                    reads=[b_pz[bank]], writes=[b_hf])
                    for tt in range(C // 128):
                        for dch in range(2):
                            bank = (tt * 2 + dch) % 2
                            po = py[:, bank * 512:(bank + 1) * 512]

                            def ymm(tns, wsl=wsl, tt=tt, dch=dch, po=po):
                                last = None
                                for hc in range(8):
                                    last = tns.matmul(
                                        po,
                                        hf[:, hc * C + tt * 128:hc * C + (tt + 1) * 128],
                                        w2_sb[wsl][:, (hc * 2 + dch) * 512:(hc * 2 + dch + 1) * 512],
                                        start=(hc == 0), stop=(hc == 7))
                                return last
                            sch.run("tensor", ymm, reads=[b_w2[wsl], b_hf, cb],
                                    writes=[b_py[bank]])
                            ysl_ap = yacc[:, tt * 1024 + dch * 512:
                                          tt * 1024 + (dch + 1) * 512]
                            if hs == 0:
                                sch.run("vector",
                                        lambda v, po=po, a=ysl_ap:
                                        v.tensor_copy(a, po),
                                        reads=[b_py[bank]], writes=[b_yacc])
                            else:
                                sch.run("vector",
                                        lambda v, po=po, a=ysl_ap:
                                        v.tensor_tensor(a, a, po, ALU.add),
                                        reads=[b_py[bank]], writes=[b_yacc])
                            if hs == 3:
                                sch.dma("scalar" if dch == 0 else "sync",
                                        f"yo{bank}",
                                        yslab[ei * C + tt * 128:
                                              ei * C + (tt + 1) * 128,
                                              dch * 512:(dch + 1) * 512],
                                        ysl_ap, reads=[b_yacc],
                                        writes=[b_ysl[ei]])
                gather_acc(ei)

        sch.fence()
        with ExitStack() as ph:
            pe = ph.enter_context
            NSLOT = 2
            xs = [pe(nc.sbuf_tensor(_PFX + f"c3x{i}", [128, D], F32))
                  for i in range(NSLOT)]
            ss = [pe(nc.sbuf_tensor(_PFX + f"c3s{i}", [128, D], F32))
                  for i in range(NSLOT)]
            xbs = [Buf(f"c3x{i}") for i in range(NSLOT)]
            sbs = [Buf(f"c3s{i}") for i in range(NSLOT)]
            xn1t = pe(nc.sbuf_tensor(_PFX + "xn1t", [128, D], BF16))
            b_xn1t = Buf("xn1t")
            gsil = pe(nc.sbuf_tensor(_PFX + "gsil", [128, D], F32))
            b_g = Buf("gsil")
            outt = [pe(nc.sbuf_tensor(_PFX + f"outt{i}", [128, D], F32))
                    for i in range(NSLOT)]
            b_ot = [Buf(f"outt{i}") for i in range(NSLOT)]
            for t in range(NT_OWN):
                sl = t % NSLOT
                sch.dma("sync", f"c3x{sl}", xs[sl][:, :],
                        x1o[t * 128:(t + 1) * 128, :],
                        reads=[b_x1o] if b_x1o else None, writes=[xbs[sl]])
                sch.dma("sync", f"c3s{sl}", ss[sl][:, :],
                        s1[t * 128:(t + 1) * 128, :],
                        reads=[b_s1] if b_s1 else None, writes=[sbs[sl]])

                def nrm(s, t=t, sl=sl):
                    return s.activation(xn1t[:, :], xs[sl][:, :], AF.Identity,
                                        bias=stats[:, 2 * t + 1:2 * t + 2],
                                        scale=stats[:, 2 * t:2 * t + 1])
                sch.run("scalar", nrm, reads=[xbs[sl], b_stats], writes=[b_xn1t])
                sch.run("vector",
                        lambda v, t=t: v.scalar_tensor_tensor(
                            acc[:, t * 1024:(t + 1) * 1024], xn1t[:, :],
                            gates[:, t * 9 + 8:t * 9 + 9],
                            acc[:, t * 1024:(t + 1) * 1024],
                            ALU.mult, ALU.add),
                        reads=[b_xn1t, b_gates], writes=[b_acc])
                sch.run("scalar",
                        lambda s, t=t: s.activation(
                            gsil[:, :], acc[:, t * 1024:(t + 1) * 1024], AF.Silu),
                        reads=[b_acc], writes=[b_g])
                sch.run("vector",
                        lambda v, sl=sl: v.tensor_mul(gsil[:, :], gsil[:, :],
                                                      ss[sl][:, :]),
                        reads=[sbs[sl]], writes=[b_g])
                sch.run("vector",
                        lambda v, sl=sl: v.tensor_add(outt[sl][:, :], gsil[:, :],
                                                      xs[sl][:, :]),
                        reads=[b_g, xbs[sl]], writes=[b_ot[sl]])
                sch.dma("scalar", f"fno{sl}", o_out[t * 128:(t + 1) * 128, :],
                        outt[sl][:, :], reads=[b_ot[sl]])
            if final_wait:
                for kk in ("fno0", "fno1"):
                    sch._wait("scalar",
                              (f"dma_{kk}", sch.dsems[kk], sch.dcnt[kk]))


_f1_ln2_cache = {}


def mk_f1_ln2(nc, ctx):
    key = id(nc)
    if key not in _f1_ln2_cache:
        _f1_ln2_cache[key] = mk_ln_tmp(nc, ctx, "lnF1b")
    return _f1_ln2_cache[key]


def emit_F1_dense(nc, sch, consts, cb, x1o, b_x1o, s1, b_s1, r1w, r1b, w1_r, w2_r,
            o_out, final_wait=False):
    global _PFX
    _PFX = "f1_"
    with ExitStack() as ctx:
        e = ctx.enter_context

        xnf16 = e(nc.sbuf_tensor(_PFX + "xnf16", [128, 8 * T_OWN], BF16))
        b_xnf = Buf("xnf16")
        acc = e(nc.sbuf_tensor(_PFX + "acc", [128, NT_OWN * 1024], F32))
        b_acc = Buf("acc")
        gates = e(nc.sbuf_tensor(_PFX + "gates", [128, NT_OWN * 9], F32))
        b_gates = Buf("gates")
        stats = e(nc.sbuf_tensor(_PFX + "stats", [128, NT_OWN * 2], F32))
        b_stats = Buf("stats")
        r1w_sb = e(nc.sbuf_tensor(_PFX + "r1w_sb", [128, 72], F32))
        r1b_sb = e(nc.sbuf_tensor(_PFX + "r1b_sb", [1, 9], F32))
        b_rw = Buf("r1w")
        sch.dma("gpsimd", "rwld", r1w_sb[:, :].rearrange("p (c e) -> p c e", c=8),
                r1w.rearrange("(c p) e -> p c e", p=128), writes=[b_rw])
        sch.dma("gpsimd", "rwld", r1b_sb[:, :], r1b, writes=[b_rw])

        ln_tmp = mk_ln_tmp(nc, ctx, "lnF1")
        t2_tmp = mk_top2_tmp(nc, ctx, "t2F1", 9)

        with ExitStack() as ph:
            pe = ph.enter_context
            NSLOT = 3
            xs = [pe(nc.sbuf_tensor(_PFX + f"f1x{i}", [128, D], F32)) for i in range(NSLOT)]
            xb = [Buf(f"f1x{i}") for i in range(NSLOT)]
            xn32 = pe(nc.sbuf_tensor(_PFX + "xn32", [128, D], F32))
            b_xn32 = Buf("xn32")
            xn16 = pe(nc.sbuf_tensor(_PFX + "xn16", [128, D], BF16))
            b_xn16 = Buf("xn16")
            xnf32 = pe(nc.sbuf_tensor(_PFX + "xnf32", [128, D], F32))
            b_xnf32 = Buf("xnf32")
            lgt = pe(nc.sbuf_tensor(_PFX + "lgt", [128, 9], F32))
            b_lgt = Buf("lgt")
            B1 = pe(nc.sbuf_tensor(_PFX + "B1", [128, 9], F32))
            B2 = pe(nc.sbuf_tensor(_PFX + "B2", [128, 9], F32))
            b_B1, b_B2 = Buf("B1"), Buf("B2")
            ptf = pe(nc.psum_tensor(_PFX + "ptf", [128, 1024], F32))
            b_ptf = Buf("ptf")
            ptb = pe(nc.psum_tensor(_PFX + "ptb", [128, 1024], BF16))
            b_ptb = Buf("ptb")
            pr = pe(nc.psum_tensor(_PFX + "pr", [128, 9], F32))
            b_pr = Buf("pr")

            ln2 = [ln_tmp, mk_ln_tmp(nc, ph, "lnF1b")]
            xn16b = [xn16, pe(nc.sbuf_tensor(_PFX + "xn16b", [128, D], BF16))]
            b_xn16b = [b_xn16, Buf("xn16b")]
            xn32b = [xn32, pe(nc.sbuf_tensor(_PFX + "xn32b", [128, D], F32))]
            b_xn32b = [b_xn32, Buf("xn32b")]
            st2 = [None]

            def stage2_f1(t, p):
                emit_transpose8(nc, sch, xn16b[p], b_xn16b[p], ptb, b_ptb,
                                xnf16, [ch * T_OWN + t * 128 for ch in range(8)],
                                b_xnf, consts["identb"], cb)
                def t_emit(tns):
                    last = None
                    for ch in range(8):
                        last = tns.transpose(ptf[:, ch * 128:(ch + 1) * 128],
                                             xn32b[p][:, ch * 128:(ch + 1) * 128],
                                             consts["identf"][:, :])
                    return last
                sch.run("tensor", t_emit, reads=[b_xn32b[p], cb], writes=[b_ptf])

                def v_emit(v):
                    last = None
                    for ch in range(8):
                        last = v.tensor_copy(xnf32[:, ch * 128:(ch + 1) * 128],
                                             ptf[:, ch * 128:(ch + 1) * 128])
                    return last
                sch.run("vector", v_emit, reads=[b_ptf], writes=[b_xnf32])

                def r_emit(tns):
                    for ch in range(8):
                        tns.matmul(pr[:, :], xnf32[:, ch * 128:(ch + 1) * 128],
                                   r1w_sb[:, ch * 9:(ch + 1) * 9],
                                   start=(ch == 0), stop=False)
                    return tns.matmul(pr[:, :], consts["ones_rowf"][:, :],
                                      r1b_sb[:, :], start=False, stop=True)
                sch.run("tensor", r_emit, reads=[b_xnf32, b_rw, cb], writes=[b_pr])
                sch.run("vector", lambda v: v.tensor_copy(lgt[:, :], pr[:, :]),
                        reads=[b_pr], writes=[b_lgt])
                emit_top2(nc, sch, lgt, b_lgt, 9, t2_tmp,
                          {"B1": (B1, b_B1), "B2": (B2, b_B2)}, None,
                          with_z=True)
                sch.run("vector",
                        lambda v: v.tensor_add(
                            gates[:, t * 9:(t + 1) * 9], B1[:, :], B2[:, :]),
                        reads=[b_B1, b_B2], writes=[b_gates])

            for t in range(NT_OWN):
                sl = t % NSLOT
                p = t % 2
                sch.dma("sync", f"f1x{sl}", xs[sl][:, :],
                        x1o[t * 128:(t + 1) * 128, :],
                        reads=[b_x1o] if b_x1o else None, writes=[xb[sl]])
                emit_ln_tile(nc, sch, cb, xb[sl], xs[sl], ln2[p],
                             want_f32=(xn32b[p], slice(0, D), b_xn32b[p]),
                             want_b16=(xn16b[p], slice(0, D), b_xn16b[p]),
                             stats_out=(stats, 2 * t, b_stats),
                             consts=consts)
                if st2[0]:
                    st2[0]()
                st2[0] = (lambda t=t, p=p: stage2_f1(t, p))
            st2[0]()

        sch.fence()
        with ExitStack() as ph:
            pe = ph.enter_context
            pz = pe(nc.psum_tensor(_PFX + "pz", [128, 1024], F32))
            b_pz = [Buf("pz0"), Buf("pz1")]
            py = pe(nc.psum_tensor(_PFX + "py", [128, 1024], F32))
            b_py = [Buf("py0"), Buf("py1")]
            hf = pe(nc.sbuf_tensor(_PFX + "hf", [128, 8 * T_OWN], BF16))
            b_hf = Buf("hf")
            w1_sb = [pe(nc.sbuf_tensor(_PFX + f"w1s{i}", [128, 8192], BF16)) for i in range(2)]
            w2_sb = [pe(nc.sbuf_tensor(_PFX + f"w2s{i}", [128, 8192], BF16)) for i in range(2)]
            b_w1 = [Buf("w1s0"), Buf("w1s1")]
            b_w2 = [Buf("w2s0"), Buf("w2s1")]

            first = True
            for ei in range(8):
                for hs in range(4):
                    wsl = (ei * 4 + hs) % 2
                    sch.dma("gpsimd", f"w1_{wsl}",
                            w1_sb[wsl][:, :].rearrange(
                                "p (dc hc c) -> p dc hc c", dc=8, hc=8),
                            w1_r[ei].rearrange(
                                "dc d hc c -> d dc hc c")[:, :, hs * 8:(hs + 1) * 8, :],
                            writes=[b_w1[wsl]])
                    sch.dma("gpsimd", f"w2_{wsl}",
                            w2_sb[wsl][:, :].rearrange(
                                "p (hc dch o) -> p hc dch o", hc=8, dch=2),
                            w2_r[ei].rearrange(
                                "hc h dch o -> h hc dch o")[:, hs * 8:(hs + 1) * 8, :, :],
                            writes=[b_w2[wsl]])
                    for hc in range(8):
                        for gi, (gst, gn) in enumerate(F1_GRP):
                            bank = (hc * 4 + gi) % 2
                            po = pz[:, bank * 512:(bank + 1) * 512]

                            def zmm(tns, wsl=wsl, hc=hc, gst=gst, po=po):
                                last = None
                                for dc in range(8):
                                    last = tns.matmul(
                                        po,
                                        w1_sb[wsl][:, (dc * 8 + hc) * 128:(dc * 8 + hc + 1) * 128],
                                        xnf16[:, dc * T_OWN + gst:dc * T_OWN + gst + 512],
                                        start=(dc == 0), stop=(dc == 7))
                                return last
                            sch.run("tensor", zmm, reads=[b_w1[wsl], b_xnf, cb],
                                    writes=[b_pz[bank]])
                            sch.run("scalar",
                                    lambda s, hc=hc, gst=gst, po=po:
                                    s.activation(
                                        hf[:, hc * T_OWN + gst:hc * T_OWN + gst + 512],
                                        po, AF.Gelu),
                                    reads=[b_pz[bank]], writes=[b_hf])
                    for tt in range(NT_OWN):
                        for dch in range(2):
                            bank = (tt * 2 + dch) % 2
                            po = py[:, bank * 512:(bank + 1) * 512]

                            def ymm(tns, wsl=wsl, tt=tt, dch=dch, po=po):
                                last = None
                                for hc in range(8):
                                    last = tns.matmul(
                                        po,
                                        hf[:, hc * T_OWN + tt * 128:hc * T_OWN + (tt + 1) * 128],
                                        w2_sb[wsl][:, (hc * 2 + dch) * 512:(hc * 2 + dch + 1) * 512],
                                        start=(hc == 0), stop=(hc == 7))
                                return last
                            sch.run("tensor", ymm, reads=[b_w2[wsl], b_hf, cb],
                                    writes=[b_py[bank]])
                            aslice = acc[:, tt * 1024 + dch * 512:tt * 1024 + (dch + 1) * 512]
                            if first:
                                sch.run("vector",
                                        lambda v, tt=tt, ei=ei, po=po, a=aslice:
                                        v.tensor_single_scalar(
                                            a, po,
                                            gates[:, tt * 9 + ei:tt * 9 + ei + 1],
                                            ALU.mult),
                                        reads=[b_py[bank], b_gates], writes=[b_acc])
                            else:
                                sch.run("vector",
                                        lambda v, tt=tt, ei=ei, po=po, a=aslice:
                                        v.scalar_tensor_tensor(
                                            a, po,
                                            gates[:, tt * 9 + ei:tt * 9 + ei + 1],
                                            a, ALU.mult, ALU.add),
                                        reads=[b_py[bank], b_gates], writes=[b_acc])
                    first = False

        sch.fence()
        with ExitStack() as ph:
            pe = ph.enter_context
            NSLOT = 2
            xs = [pe(nc.sbuf_tensor(_PFX + f"c3x{i}", [128, D], F32)) for i in range(NSLOT)]
            ss = [pe(nc.sbuf_tensor(_PFX + f"c3s{i}", [128, D], F32)) for i in range(NSLOT)]
            xbs = [Buf(f"c3x{i}") for i in range(NSLOT)]
            sbs = [Buf(f"c3s{i}") for i in range(NSLOT)]
            xn1t = pe(nc.sbuf_tensor(_PFX + "xn1t", [128, D], BF16))
            b_xn1t = Buf("xn1t")
            gsil = pe(nc.sbuf_tensor(_PFX + "gsil", [128, D], F32))
            b_g = Buf("gsil")
            outt = [pe(nc.sbuf_tensor(_PFX + f"outt{i}", [128, D], F32)) for i in range(NSLOT)]
            b_ot = [Buf(f"outt{i}") for i in range(NSLOT)]
            for t in range(NT_OWN):
                sl = t % NSLOT
                sch.dma("sync", f"c3x{sl}", xs[sl][:, :],
                        x1o[t * 128:(t + 1) * 128, :],
                        reads=[b_x1o] if b_x1o else None, writes=[xbs[sl]])
                sch.dma("sync", f"c3s{sl}", ss[sl][:, :],
                        s1[t * 128:(t + 1) * 128, :],
                        reads=[b_s1] if b_s1 else None, writes=[sbs[sl]])
                def nrm(s, t=t, sl=sl):
                    return s.activation(xn1t[:, :], xs[sl][:, :], AF.Identity,
                                        bias=stats[:, 2 * t + 1:2 * t + 2],
                                        scale=stats[:, 2 * t:2 * t + 1])
                sch.run("scalar", nrm, reads=[xbs[sl], b_stats], writes=[b_xn1t])
                sch.run("vector",
                        lambda v, t=t: v.scalar_tensor_tensor(
                            acc[:, t * 1024:(t + 1) * 1024], xn1t[:, :],
                            gates[:, t * 9 + 8:t * 9 + 9],
                            acc[:, t * 1024:(t + 1) * 1024],
                            ALU.mult, ALU.add),
                        reads=[b_xn1t, b_gates], writes=[b_acc])
                sch.run("scalar",
                        lambda s, t=t: s.activation(
                            gsil[:, :], acc[:, t * 1024:(t + 1) * 1024], AF.Silu),
                        reads=[b_acc], writes=[b_g])
                sch.run("vector",
                        lambda v, sl=sl: v.tensor_mul(gsil[:, :], gsil[:, :],
                                                      ss[sl][:, :]),
                        reads=[sbs[sl]], writes=[b_g])
                sch.run("vector",
                        lambda v, sl=sl: v.tensor_add(outt[sl][:, :], gsil[:, :],
                                                      xs[sl][:, :]),
                        reads=[b_g, xbs[sl]], writes=[b_ot[sl]])
                sch.dma("scalar", f"fno{sl}", o_out[t * 128:(t + 1) * 128, :],
                        outt[sl][:, :], reads=[b_ot[sl]])
            if final_wait:
                for kk in ("fno0", "fno1"):
                    sch._wait("scalar",
                              (f"dma_{kk}", sch.dsems[kk], sch.dcnt[kk]))


def host_F1_inputs(core, x1_full, s1_full, r1w, r1b, w1, w2):
    b, h = core // 2, core % 2
    own0 = 2048 * h
    x1e = to_f32(x1_full[b])[own0:own0 + T_OWN]
    s1e = to_f32(s1_full)
    w1_r = to_f32(w1).reshape(8, 8, 128, 32, 128)
    w2_r = to_f32(w2).reshape(8, 32, 128, 2, 512)
    return dict(x1o=x1e, s1=s1e, r1w=to_f32(r1w), r1b=to_f32(r1b).reshape(1, 9),
                w1f_r=w1_r, w2f_r=w2_r, **host_consts_named())



def build_fused():
    nc = bass.Bass("TRN2", target_bir_lowering=False)
    T = lambda n, sh: nc.dram_tensor(n, sh, F32, kind="ExternalInput").ap()
    x_all = T("x_all", [BATCH * SEQ, D])
    xf = T("xf", [SEQ, D])
    xe = T("xe", [T_EXT, D])
    wq_r = T("wq_r", [NHP, 8, 128, 128])
    wk_r = T("wk_r", [NHP, 8, 128, 128])
    wv_r = T("wv_r", [NHP, 8, 128, 128])
    woT_r = T("woT_r", [8, 128, D])
    dmask = T("dmask", [NQT, A0_NMASK, 128, 512])
    bcol = T("bcol", [128, NQT, 32])
    vslab = T("vslab", [128, (SEQ // 128) * 4 * 65])
    rw = T("rw", [D, 8])
    rb = T("rb", [1, 8])
    flag = T("flag", [8, 2])
    w1_r = T("w1_r", [8, 8, 128, 8, 128])
    w2_r = T("w2_r", [8, 8, 128, 2, 512])
    b1_r = T("b1_r", [128, 8, 8])
    b2_r = T("b2_r", [8, D])
    eye8 = T("eye8", [8, 8 * 128])
    wqkv_r = T("wqkv_r", [8, 128, 24, 128])
    qkvb_c = T("qkvb_c", [128, 24])
    qkvb_row = T("qkvb_row", [1, 3072])
    crope_r = T("crope_r", [128, T_EXT])
    srope_r = T("srope_r", [128, T_EXT])
    p2_r = T("p2_r", [128, 128])
    wmask_r = T("wmask_r", [6, 128, 512])
    vslab1 = T("vslab1", [128, VR1])
    bcol1 = T("bcol1", [128, NT_EXT])
    wo1_r = T("wo1_r", [8, 128, D])
    wob_row = T("wob_row", [1, D])
    r1w = T("r1w", [D, 9])
    r1b = T("r1b", [1, 9])
    w1f_r = T("w1f_r", [8, 8, 128, 32, 128])
    w2f_r = T("w2f_r", [8, 32, 128, 2, 512])
    o_out = nc.dram_tensor("o_out", [T_OWN, D], F32, kind="ExternalOutput").ap()
    sp_sc = nc.dram_tensor("sp_sc", [16, 1], F32).ap()
    s_sc = nc.dram_tensor("s_sc", [T_EXT, D], F32).ap()
    x1_sc = nc.dram_tensor("x1_sc", [T_EXT, D], F32).ap()
    s1_sc = nc.dram_tensor("s1_sc", [T_OWN, D], F32).ap()
    b_sp = Buf("sp_sc")
    b_s = Buf("s_sc")
    b_x1 = Buf("x1_sc")
    b_s1 = Buf("s1_sc")

    with ExitStack() as ctx:
        sch = Sched(nc, ctx)
        consts, cb = declare_consts(nc, sch, ctx)
        emit_A0(nc, sch, consts, cb, xf, xe, wq_r, wk_r, wv_r, woT_r, dmask,
                bcol, vslab, s_sc, b_s,
                steal_mk=lambda: RSteps(nc, sch, consts, cb, x_all, rw, rb,
                                        sp_sc, b_sp, BATCH * SEQ // 128))
        emit_F0(nc, sch, consts, cb, xe, s_sc, b_s,
                sp_sc.rearrange("(e k) one -> e (k one)", k=2), flag, rw, rb,
                w1_r, w2_r, b1_r, b2_r, eye8, x1_sc, b_x1,
                spall_direct=True, b_spall=b_sp)
        emit_A1(nc, sch, consts, cb, x1_sc, b_x1, wqkv_r, qkvb_c, qkvb_row,
                crope_r, srope_r, p2_r, wmask_r, vslab1, bcol1, wo1_r,
                wob_row, s1_sc, b_s1)
        emit_F1(nc, sch, consts, cb, x1_sc[HALO:HALO + T_OWN, :], b_x1,
                s1_sc, b_s1, r1w, r1b, w1f_r, w2f_r, o_out, final_wait=True)
    return nc


def host_fused_inputs(core, inputs):
    x = to_f32(inputs["L0_attn_x"])
    d = host_A0_inputs(core, x, inputs["L0_attn_wq"], inputs["L0_attn_wk"],
                       inputs["L0_attn_wv"], inputs["L0_attn_wo"])
    d["x_all"] = x.reshape(-1, D)
    d["rw"] = to_f32(inputs["L0_ffn_router_w"])
    d["rb"] = to_f32(inputs["L0_ffn_router_b"]).reshape(1, 8)
    d["flag"] = np.full((8, 2), 1.0 if core == 0 else 0.0, np.float32)
    d["w1_r"] = np.ascontiguousarray(to_f32(inputs["L0_ffn_w1"]).reshape(8, 8, 128, 8, 128))
    d["w2_r"] = np.ascontiguousarray(to_f32(inputs["L0_ffn_w2"]).reshape(8, 8, 128, 2, 512))
    d["b1_r"] = np.ascontiguousarray(to_f32(inputs["L0_ffn_b1"]).reshape(8, 8, 128).transpose(2, 0, 1))
    d["b2_r"] = to_f32(inputs["L0_ffn_b2"])
    eye = np.zeros((8, 8 * 128), np.float32)
    for ei in range(8):
        eye[ei, ei * 128:(ei + 1) * 128] = 1.0
    d["eye8"] = eye
    d1 = host_A1_inputs(core, np.broadcast_to(np.float32(0.0), (BATCH, SEQ, D)),
                        inputs["L1_attn_Wqkv_kernel"], inputs["L1_attn_Wqkv_bias"],
                        inputs["L1_attn_Wo_kernel"], inputs["L1_attn_Wo_bias"],
                        inputs["L1_attn_rope_cache"])
    del d1["x1e"]
    d.update(d1)
    d["r1w"] = to_f32(inputs["L1_ffn_router_w"])
    d["r1b"] = to_f32(inputs["L1_ffn_router_b"]).reshape(1, 9)
    d["w1f_r"] = np.ascontiguousarray(to_f32(inputs["L1_ffn_w1"]).reshape(8, 8, 128, 32, 128))
    d["w2f_r"] = np.ascontiguousarray(to_f32(inputs["L1_ffn_w2"]).reshape(8, 32, 128, 2, 512))
    return d



FUSED = True


def kernel(**inputs):
    if FUSED:
        nc = build_fused()
        in_maps = [host_fused_inputs(c, inputs) for c in range(NCORES)]
        res = run_bass_kernel_spmd(nc, in_maps, core_ids=list(range(NCORES)))
        out = np.empty((BATCH, SEQ, D), np.float32)
        for c in range(NCORES):
            b, h = c // 2, c % 2
            out[b, 2048 * h:2048 * h + T_OWN] = res.results[c]["o_out"]
        return out
    return kernel_multilaunch(**inputs)


def kernel_multilaunch(**inputs):
    x = to_f32(inputs["L0_attn_x"])
    flat = x.reshape(-1, D)

    nc_r = build_launch_R(NT_OWN, list(range(NT_OWN)))
    in_maps = [host_R_inputs(flat[c * T_OWN:(c + 1) * T_OWN],
                             inputs["L0_ffn_router_w"],
                             inputs["L0_ffn_router_b"])
               for c in range(NCORES)]
    res = run_bass_kernel_spmd(nc_r, in_maps, core_ids=list(range(NCORES)))
    spart_all = np.stack([res.results[c]["o_sp"] for c in range(NCORES)])

    nc_a0 = build_launch_A0()
    in_maps = [host_A0_inputs(c, x, inputs["L0_attn_wq"], inputs["L0_attn_wk"],
                              inputs["L0_attn_wv"], inputs["L0_attn_wo"])
               for c in range(NCORES)]
    res = run_bass_kernel_spmd(nc_a0, in_maps, core_ids=list(range(NCORES)))
    s_ext = [res.results[c]["o_s"] for c in range(NCORES)]

    nc_f0 = build_launch_F0()
    in_maps = [host_F0_inputs(c, x, s_ext[c], spart_all,
                              inputs["L0_ffn_router_w"],
                              inputs["L0_ffn_router_b"], inputs["L0_ffn_w1"],
                              inputs["L0_ffn_b1"], inputs["L0_ffn_w2"],
                              inputs["L0_ffn_b2"])
               for c in range(NCORES)]
    res = run_bass_kernel_spmd(nc_f0, in_maps, core_ids=list(range(NCORES)))
    x1 = np.empty((BATCH, SEQ, D), np.float32)
    for c in range(NCORES):
        b, h = c // 2, c % 2
        x1[b, 2048 * h:2048 * h + T_OWN] = \
            res.results[c]["o_x1"][HALO:HALO + T_OWN]

    nc_a1 = build_launch_A1()
    in_maps = [host_A1_inputs(c, x1, inputs["L1_attn_Wqkv_kernel"],
                              inputs["L1_attn_Wqkv_bias"],
                              inputs["L1_attn_Wo_kernel"],
                              inputs["L1_attn_Wo_bias"],
                              inputs["L1_attn_rope_cache"])
               for c in range(NCORES)]
    res = run_bass_kernel_spmd(nc_a1, in_maps, core_ids=list(range(NCORES)))
    s1 = [res.results[c]["o_s1"] for c in range(NCORES)]

    nc_f1 = build_launch_F1()
    in_maps = [host_F1_inputs(c, x1, s1[c], inputs["L1_ffn_router_w"],
                              inputs["L1_ffn_router_b"], inputs["L1_ffn_w1"],
                              inputs["L1_ffn_w2"])
               for c in range(NCORES)]
    res = run_bass_kernel_spmd(nc_f1, in_maps, core_ids=list(range(NCORES)))
    out = np.empty((BATCH, SEQ, D), np.float32)
    for c in range(NCORES):
        b, h = c // 2, c % 2
        out[b, 2048 * h:2048 * h + T_OWN] = res.results[c]["o_out"]
    return out


# version 16
# speedup vs baseline: 1.1643x; 1.0019x over previous
import numpy as np
import ml_dtypes
from contextlib import ExitStack

import concourse.bass as bass
import concourse.mybir as mybir
from concourse.bass_utils import run_bass_kernel_spmd

F32 = mybir.dt.float32
BF16 = mybir.dt.bfloat16
AF = mybir.ActivationFunctionType
ALU = mybir.AluOpType
AX = mybir.AxisListType

NCORES = 8
D = 1024
SEQ = 4096
BATCH = 4
NH = 16
DH = 64
NHP = 8
T_OWN = 2048
HALO = 128
T_EXT = T_OWN + 2 * HALO
NT_OWN = T_OWN // 128
NT_EXT = T_EXT // 128
EPS = 1e-5

bf16 = ml_dtypes.bfloat16
_PFX = ""



class Buf:
    __slots__ = ("w", "readers", "name")

    def __init__(self, name=""):
        self.w = None
        self.readers = []
        self.name = name


class Sched:

    ENG = ("tensor", "vector", "scalar", "gpsimd", "sync")

    def __init__(self, nc, ctx):
        self.nc = nc
        self.ctx = ctx
        self.sems = {}
        self.cnt = {}
        for en in Sched.ENG:
            self.sems[en] = ctx.enter_context(nc.semaphore(f"se_{en}"))
            self.cnt[en] = 0
        self.dsems = {}
        self.dcnt = {}
        self.wm = {en: {} for en in Sched.ENG}

    def dma_key(self, key):
        if key not in self.dsems:
            self.dsems[key] = self.ctx.enter_context(
                self.nc.semaphore(f"sd_{key}"))
            self.dcnt[key] = 0
        return self.dsems[key]

    def _engine(self, en):
        return getattr(self.nc, en)

    def _wait(self, en, tok):
        sem_name, sem, val = tok
        if val <= 0:
            return
        if self.wm[en].get(sem_name, 0) >= val:
            return
        self._engine(en).wait_ge(sem, val)
        self.wm[en][sem_name] = val

    def _deps_of(self, reads, writes, deps):
        toks = list(getattr(self, "fence_toks", ())) + list(deps or [])
        for b in (reads or []):
            if b.w is not None:
                toks.append(b.w)
        for b in (writes or []):
            if b.w is not None:
                toks.append(b.w)
            toks.extend(b.readers)
        return toks

    def _commit(self, tok, reads, writes):
        for b in (reads or []):
            b.readers.append(tok)
        for b in (writes or []):
            b.w = tok
            b.readers = []

    def run(self, en, emit, reads=None, writes=None, deps=None):
        for tok in self._deps_of(reads, writes, deps):
            self._wait(en, tok)
        last = emit(self._engine(en))
        assert last is not None, "task emit() must return its last instruction"
        self.cnt[en] += 1
        last.then_inc(self.sems[en], 1)
        tok = (en, self.sems[en], self.cnt[en])
        self._commit(tok, reads, writes)
        return tok

    def dma(self, en, key, out, in_, reads=None, writes=None, deps=None,
            **kw):
        sem = self.dma_key(key)
        for tok in self._deps_of(reads, writes, deps):
            self._wait(en, tok)
        self._engine(en).dma_start(out, in_, **kw).then_inc(sem, 16)
        self.dcnt[key] += 16
        tok = (f"dma_{key}", sem, self.dcnt[key])
        self._commit(tok, reads, writes)
        return tok

    def idma(self, key, emit, reads=None, writes=None, deps=None):
        sem = self.dma_key(key)
        for tok in self._deps_of(reads, writes, deps):
            self._wait("gpsimd", tok)
        emit(self.nc.gpsimd).then_inc(sem, 16)
        self.dcnt[key] += 16
        tok = (f"dma_{key}", sem, self.dcnt[key])
        self._commit(tok, reads, writes)
        return tok

    def barrier_token(self, en):
        return (en, self.sems[en], self.cnt[en])

    def fence(self):
        toks = [(en, self.sems[en], self.cnt[en]) for en in Sched.ENG]
        toks += [(f"dma_{k}", self.dsems[k], self.dcnt[k]) for k in self.dsems]
        self.fence_toks = [t for t in toks if t[2] > 0]



def to_f32(a):
    return np.ascontiguousarray(a, dtype=np.float32)


def host_consts():
    c = {}
    c["identb"] = np.eye(128, dtype=np.float32)
    c["identf"] = np.eye(128, dtype=np.float32)
    c["ones_row"] = np.ones((1, 128), np.float32)
    c["ones_rowf"] = np.ones((1, 128), np.float32)
    c["ones_col"] = np.ones((128, 1), np.float32)
    c["eps_col"] = np.full((128, 1), EPS, np.float32)
    lt = np.zeros((128, 128), np.float32)
    for t in range(128):
        lt[:t, t] = 1.0
    c["lt_excl"] = lt
    c["zeros_t"] = np.zeros((128, 1024), np.float32)
    c["ones_colb"] = np.ones((128, 1), np.float32)
    return c


CONST_SPECS = [
    ("identb", [128, 128], BF16),
    ("identf", [128, 128], F32),
    ("ones_row", [1, 128], BF16),
    ("ones_rowf", [1, 128], F32),
    ("ones_col", [128, 1], F32),
    ("eps_col", [128, 1], F32),
    ("lt_excl", [128, 128], BF16),
    ("zeros_t", [128, 1024], BF16),
    ("ones_colb", [128, 1], BF16),
]


def declare_consts(nc, sch, ctx):
    sb = {}
    cb = Buf("consts")
    for name, shape, dt in CONST_SPECS:
        dram = nc.dram_tensor("c_" + name, shape, F32, kind="ExternalInput").ap()
        t = ctx.enter_context(nc.sbuf_tensor("sb_" + name, shape, dt))
        sch.dma("gpsimd", "consts", t[:, :], dram, writes=[cb])
        sb[name] = t
    return sb, cb



def emit_ln_tile(nc, sch, cb, x_buf, x_tile, tmp, want_f32=False, want_b16=None,
                 stats_out=None, consts=None):
    bn6, b_bn6 = tmp["bn6"]
    mv, b_mv = tmp["mv"]
    lnv, b_lnv = tmp["lnv"]
    rstd, b_rstd = tmp["rstd"]
    biasA, b_biasA = tmp["biasA"]
    eps_col = consts["eps_col"]

    sch.run("vector", lambda v: v.bn_stats(bn6[:, 0:6], x_tile[:, 0:512]),
            reads=[x_buf], writes=[b_bn6])
    sch.run("vector", lambda v: v.bn_stats(bn6[:, 6:12], x_tile[:, 512:1024]),
            reads=[x_buf], writes=[b_bn6])
    sch.run("vector", lambda v: v.bn_aggr(mv[:, :], bn6[:, :]),
            reads=[b_bn6], writes=[b_mv])
    sch.run("scalar", lambda s: s.activation(lnv[:, :], mv[:, 1:2], AF.Ln,
                                             bias=eps_col[:, 0:1], scale=1.0),
            reads=[b_mv, cb], writes=[b_lnv])
    sch.run("scalar", lambda s: s.activation(rstd[:, :], lnv[:, :], AF.Exp,
                                             scale=-0.5),
            reads=[b_lnv], writes=[b_rstd])
    sch.run("vector", lambda v: v.tensor_scalar(biasA[:, :], mv[:, 0:1],
                                                rstd[:, 0:1], -1.0,
                                                ALU.mult, ALU.mult),
            reads=[b_mv, b_rstd], writes=[b_biasA])
    tok_f = tok_b = None
    if want_f32 is not None:
        h, sl, wbuf = want_f32
        tok_f = sch.run("vector",
                        lambda v: v.tensor_scalar(h[:, sl], x_tile[:, :],
                                                  mv[:, 0:1], rstd[:, 0:1],
                                                  ALU.subtract, ALU.mult),
                        reads=[x_buf, b_mv, b_rstd], writes=[wbuf])
    if want_b16 is not None:
        h, sl, wbuf = want_b16
        tok_b = sch.run("scalar",
                        lambda s: s.activation(h[:, sl], x_tile[:, :],
                                               AF.Identity,
                                               bias=biasA[:, 0:1],
                                               scale=rstd[:, 0:1]),
                        reads=[x_buf, b_rstd, b_biasA], writes=[wbuf])
    if stats_out is not None:
        h, col, wbuf = stats_out
        sch.run("vector", lambda v: v.tensor_copy(h[:, col:col + 1], rstd[:, :]),
                reads=[b_rstd], writes=[wbuf])
        sch.run("vector", lambda v: v.tensor_copy(h[:, col + 1:col + 2],
                                                  biasA[:, :]),
                reads=[b_biasA], writes=[wbuf])
    return tok_f, tok_b


def mk_ln_tmp(nc, ctx, tag):
    tmp = {}
    for name, shape in [("bn6", [128, 12]), ("mv", [128, 2]), ("lnv", [128, 1]),
                        ("rstd", [128, 1]), ("biasA", [128, 1])]:
        h = ctx.enter_context(nc.sbuf_tensor(_PFX + f"{tag}_{name}", shape, F32))
        tmp[name] = (h, Buf(f"{tag}_{name}"))
    return tmp


def emit_transpose8(nc, sch, src, src_buf, pbank, pbank_buf, dst, dst_cols,
                    dst_buf, ident, cb, dtype_b16=True):
    def t_emit(t):
        last = None
        for ch in range(8):
            last = t.transpose(pbank[:, ch * 128:(ch + 1) * 128],
                               src[:, ch * 128:(ch + 1) * 128],
                               ident[:, :])
        return last
    sch.run("tensor", t_emit, reads=[src_buf, cb], writes=[pbank_buf])

    def v_emit(v):
        last = None
        for ch in range(8):
            last = v.tensor_copy(dst[:, dst_cols[ch]:dst_cols[ch] + 128],
                                 pbank[:, ch * 128:(ch + 1) * 128])
        return last
    return sch.run("vector", v_emit, reads=[pbank_buf], writes=[dst_buf])


def emit_top2(nc, sch, lgt, lgt_buf, nlog, tmp, out, out_buf, with_z=False):
    m1, b_m1 = tmp["m1"]
    nm1, b_nm1 = tmp["nm1"]
    e8, b_e8 = tmp["e8"]
    m1e, b_m1e = tmp["m1e"]
    is1, b_is1 = tmp["is1"]
    ms2, b_ms2 = tmp["ms2"]
    m2e, b_m2e = tmp["m2e"]
    is2, b_is2 = tmp["is2"]
    den, b_den = tmp["den"]
    rden, b_rden = tmp["rden"]
    p1n, b_p1n = tmp["p1n"]
    p2n, b_p2n = tmp["p2n"]
    B1, b_B1 = out["B1"]
    B2, b_B2 = out["B2"]
    B1ap = out.get("B1ap")
    B2ap = out.get("B2ap")
    n = nlog

    sch.run("vector", lambda v: v.reduce_max(m1[:, :], lgt[:, 0:n], axis=AX.X,
                                             negate=True),
            reads=[lgt_buf], writes=[b_m1])
    sch.run("scalar", lambda s: s.activation(e8[:, 0:n], lgt[:, 0:n], AF.Exp,
                                             bias=m1[:, 0:1], scale=1.0),
            reads=[lgt_buf, b_m1], writes=[b_e8])
    sch.run("vector", lambda v: v.reduce_max(m1e[:, :], e8[:, 0:n], axis=AX.X),
            reads=[b_e8], writes=[b_m1e])
    sch.run("vector", lambda v: v.tensor_single_scalar(is1[:, 0:n], e8[:, 0:n],
                                                       m1e[:, 0:1],
                                                       ALU.is_equal),
            reads=[b_e8, b_m1e], writes=[b_is1])
    sch.run("vector", lambda v: v.scalar_tensor_tensor(ms2[:, 0:n], is1[:, 0:n],
                                                       -1e30, e8[:, 0:n],
                                                       ALU.mult, ALU.add),
            reads=[b_is1, b_e8], writes=[b_ms2])
    sch.run("vector", lambda v: v.reduce_max(m2e[:, :], ms2[:, 0:n], axis=AX.X),
            reads=[b_ms2], writes=[b_m2e])
    sch.run("vector", lambda v: v.tensor_single_scalar(is2[:, 0:n], ms2[:, 0:n],
                                                       m2e[:, 0:1],
                                                       ALU.is_equal),
            reads=[b_ms2, b_m2e], writes=[b_is2])
    if with_z:
        sch.run("vector", lambda v: v.reduce_sum(den[:, :], e8[:, 0:n],
                                                 axis=AX.X),
                reads=[b_e8], writes=[b_den])
    else:
        sch.run("vector", lambda v: v.tensor_add(den[:, :], m1e[:, :],
                                                 m2e[:, :]),
                reads=[b_m1e, b_m2e], writes=[b_den])
    sch.run("vector", lambda v: v.reciprocal(rden[:, :], den[:, :]),
            reads=[b_den], writes=[b_rden])
    sch.run("vector", lambda v: v.tensor_mul(p1n[:, :], m1e[:, :], rden[:, :]),
            reads=[b_m1e, b_rden], writes=[b_p1n])
    sch.run("vector", lambda v: v.tensor_mul(p2n[:, :], m2e[:, :], rden[:, :]),
            reads=[b_m2e, b_rden], writes=[b_p2n])
    sch.run("vector", lambda v: v.tensor_single_scalar(
        B1ap if B1ap is not None else B1[:, 0:n], is1[:, 0:n],
        p1n[:, 0:1], ALU.mult),
            reads=[b_is1, b_p1n], writes=[b_B1])
    sch.run("vector", lambda v: v.tensor_single_scalar(
        B2ap if B2ap is not None else B2[:, 0:n], is2[:, 0:n],
        p2n[:, 0:1], ALU.mult),
            reads=[b_is2, b_p2n], writes=[b_B2])


def mk_top2_tmp(nc, ctx, tag, nlog):
    tmp = {}
    for name, w in [("m1", 1), ("nm1", 1), ("e8", nlog), ("m1e", 1),
                    ("is1", nlog), ("ms2", nlog), ("m2e", 1), ("is2", nlog),
                    ("den", 1), ("rden", 1), ("p1n", 1), ("p2n", 1)]:
        h = ctx.enter_context(nc.sbuf_tensor(_PFX + f"{tag}_{name}", [128, w], F32))
        tmp[name] = (h, Buf(f"{tag}_{name}"))
    return tmp



def build_launch_R(n_tiles, own_range):
    nc = bass.Bass("TRN2", target_bir_lowering=False)
    xr = nc.dram_tensor("xr", [n_tiles * 128, D], F32, kind="ExternalInput").ap()
    rw = nc.dram_tensor("rw", [D, 8], F32, kind="ExternalInput").ap()
    rb = nc.dram_tensor("rb", [1, 8], F32, kind="ExternalInput").ap()
    o_sp = nc.dram_tensor("o_sp", [8, 2], F32, kind="ExternalOutput").ap()
    with ExitStack() as ctx:
        sch = Sched(nc, ctx)
        consts, cb = declare_consts(nc, sch, ctx)
        emit_R(nc, sch, consts, cb, xr, rw, rb, o_sp, Buf("osp"),
               n_tiles, own_range, final_wait=True)
    return nc


def emit_R(nc, sch, consts, cb, xr, rw, rb, o_sp, b_osp, n_tiles, own_range,
           final_wait=False):
    global _PFX
    _PFX = "r_"
    with ExitStack() as ctx:
        e = ctx.enter_context

        NSLOT = 3
        NP = 2
        xs = [e(nc.sbuf_tensor(_PFX + f"xs{i}", [128, D], F32)) for i in range(NSLOT)]
        xb = [Buf(f"xs{i}") for i in range(NSLOT)]
        xn16 = [e(nc.sbuf_tensor(_PFX + f"xn16_{i}", [128, D], BF16)) for i in range(NP)]
        b_xn16 = [Buf(f"xn16_{i}") for i in range(NP)]
        xnf = [e(nc.sbuf_tensor(_PFX + f"xnf_{i}", [128, D], BF16)) for i in range(NP)]
        b_xnf = [Buf(f"xnf_{i}") for i in range(NP)]
        rw_sb = e(nc.sbuf_tensor(_PFX + "rw_sb", [128, 64], BF16))
        rb_sb = e(nc.sbuf_tensor(_PFX + "rb_sb", [1, 8], BF16))
        b_w = Buf("rw")
        lgt = [e(nc.sbuf_tensor(_PFX + f"lgt_{i}", [128, 8], F32)) for i in range(NP)]
        b_lgt = [Buf(f"lgt_{i}") for i in range(NP)]
        B1 = [e(nc.sbuf_tensor(_PFX + f"B1_{i}", [128, 8], F32)) for i in range(NP)]
        B2 = [e(nc.sbuf_tensor(_PFX + f"B2_{i}", [128, 8], F32)) for i in range(NP)]
        b_B1 = [Buf(f"B1_{i}") for i in range(NP)]
        b_B2 = [Buf(f"B2_{i}") for i in range(NP)]
        spart = e(nc.sbuf_tensor(_PFX + "spart", [8, 2], F32))
        b_sp = Buf("spart")

        pt = [e(nc.psum_tensor(_PFX + f"pt_{i}", [128, 1024], BF16)) for i in range(NP)]
        b_pt = [Buf(f"pt_{i}") for i in range(NP)]
        pr = [e(nc.psum_tensor(_PFX + f"pr_{i}", [128, 8], F32)) for i in range(NP)]
        b_pr = [Buf(f"pr_{i}") for i in range(NP)]
        pS1 = e(nc.psum_tensor(_PFX + "pS1", [8, 1], F32))
        pS2 = e(nc.psum_tensor(_PFX + "pS2", [8, 1], F32))
        b_pS = Buf("pS")

        ln_tmp = [mk_ln_tmp(nc, ctx, f"lnR{i}") for i in range(NP)]
        t2_tmp = [mk_top2_tmp(nc, ctx, f"t2R{i}", 8) for i in range(NP)]

        sch.dma("gpsimd", "w", rw_sb[:, :].rearrange("p (c e) -> p c e", c=8),
                rw.rearrange("(c p) e -> p c e", p=128), writes=[b_w])
        sch.dma("gpsimd", "w", rb_sb[:, :], rb, writes=[b_w])

        own_set = set(own_range)
        first_own, last_own = min(own_range), max(own_range)

        st2 = [None]

        def stage2_r(t, p):
            def t_emit(tns, p=p):
                last = None
                for ch in range(8):
                    last = tns.transpose(pt[p][:, ch * 128:(ch + 1) * 128],
                                         xn16[p][:, ch * 128:(ch + 1) * 128],
                                         consts["identb"][:, :])
                return last
            sch.run("tensor", t_emit, reads=[b_xn16[p], cb], writes=[b_pt[p]])

            def v_emit(v, p=p):
                last = None
                for ch in range(8):
                    last = v.tensor_copy(xnf[p][:, ch * 128:(ch + 1) * 128],
                                         pt[p][:, ch * 128:(ch + 1) * 128])
                return last
            sch.run("vector", v_emit, reads=[b_pt[p]], writes=[b_xnf[p]])

            def r_emit(tns, p=p):
                for ch in range(8):
                    tns.matmul(pr[p][:, :], xnf[p][:, ch * 128:(ch + 1) * 128],
                               rw_sb[:, ch * 8:(ch + 1) * 8],
                               start=(ch == 0), stop=False)
                return tns.matmul(pr[p][:, :], consts["ones_row"][:, :],
                                  rb_sb[:, :], start=False, stop=True)
            sch.run("tensor", r_emit, reads=[b_xnf[p], b_w, cb], writes=[b_pr[p]])
            sch.run("vector", lambda v, p=p: v.tensor_copy(lgt[p][:, :],
                                                           pr[p][:, :]),
                    reads=[b_pr[p]], writes=[b_lgt[p]])
            emit_top2(nc, sch, lgt[p], b_lgt[p], 8, t2_tmp[p],
                      {"B1": (B1[p], b_B1[p]), "B2": (B2[p], b_B2[p])}, None)
            if t in own_set:
                def s_emit(tns, p=p, first=(t == first_own), last=(t == last_own)):
                    tns.matmul(pS1[:, :], B1[p][:, :], consts["ones_col"][:, :],
                               start=first, stop=last)
                    return tns.matmul(pS2[:, :], B2[p][:, :],
                                      consts["ones_col"][:, :],
                                      start=first, stop=last)
                sch.run("tensor", s_emit, reads=[b_B1[p], b_B2[p], cb],
                        writes=[b_pS])

        for t in range(n_tiles):
            sl = t % NSLOT
            p = t % NP
            sch.dma("sync", f"x{sl}", xs[sl][:, :],
                    xr[t * 128:(t + 1) * 128, :], writes=[xb[sl]])
            emit_ln_tile(nc, sch, cb, xb[sl], xs[sl], ln_tmp[p],
                         want_f32=None,
                         want_b16=(xn16[p], slice(0, D), b_xn16[p]),
                         consts=consts)
            if st2[0]:
                st2[0]()
            st2[0] = (lambda t=t, p=p: stage2_r(t, p))
        st2[0]()

        def fin_emit(v):
            v.tensor_copy(spart[:, 0:1], pS1[:, :])
            return v.tensor_copy(spart[:, 1:2], pS2[:, :])
        sch.run("vector", fin_emit, reads=[b_pS], writes=[b_sp])
        tok = sch.dma("scalar", "spout", o_sp, spart[:, :], reads=[b_sp],
                      writes=[b_osp])
        if final_wait:
            sch._wait("scalar", tok)


class RSteps:

    def __init__(self, nc, sch, consts, cb, xr, rw, rb, o_sp, b_osp, n_tiles):
        self.nc, self.sch = nc, sch
        self.o_sp, self.b_osp = o_sp, b_osp
        self.ctx = ExitStack()
        e = self.ctx.enter_context
        global _PFX
        old_pfx = _PFX
        _PFX = "r_"
        NSLOT = 4
        NP = 3
        xs = [e(nc.sbuf_tensor(f"r_xs{i}", [128, D], F32)) for i in range(NSLOT)]
        xb = [Buf(f"r_xs{i}") for i in range(NSLOT)]
        xn16 = [e(nc.sbuf_tensor(f"r_xn16_{i}", [128, D], BF16))
                for i in range(NP)]
        b_xn16 = [Buf(f"r_xn16_{i}") for i in range(NP)]
        xnf = [e(nc.sbuf_tensor(f"r_xnf_{i}", [128, D], BF16))
               for i in range(NP)]
        b_xnf_lo = [Buf(f"r_xnfl_{i}") for i in range(NP)]
        b_xnf_hi = [Buf(f"r_xnfh_{i}") for i in range(NP)]
        rw_sb = e(nc.sbuf_tensor("r_rw_sb", [128, 64], BF16))
        rb_sb = e(nc.sbuf_tensor("r_rb_sb", [1, 8], BF16))
        b_w = Buf("r_rw")
        lgt = [e(nc.sbuf_tensor(f"r_lgt_{i}", [128, 8], F32)) for i in range(NP)]
        b_lgt = [Buf(f"r_lgt_{i}") for i in range(NP)]
        B12 = [e(nc.sbuf_tensor(f"r_B12_{i}", [128, 16], F32)) for i in range(NP)]
        b_B12 = [Buf(f"r_B12_{i}") for i in range(NP)]
        sp16 = e(nc.sbuf_tensor("r_sp16", [16, 1], F32))
        b_sp16 = Buf("r_sp16")
        pt = [e(nc.psum_tensor(f"r_pt_{i}", [128, 1024], BF16)) for i in range(NP)]
        b_pt = [Buf(f"r_pt_{i}") for i in range(NP)]
        pr = e(nc.psum_tensor("r_pr", [128, 8], F32))
        b_pr = Buf("r_pr")
        pS = e(nc.psum_tensor("r_pS", [16, 1], F32))
        b_pS = Buf("r_pS")
        ln_tmp = [mk_ln_tmp(nc, self.ctx, f"lnR{i}") for i in range(NP)]
        t2_tmp = [mk_top2_tmp(nc, self.ctx, f"t2R{i}", 8) for i in range(NP)]
        _PFX = old_pfx

        sch.dma("gpsimd", "w", rw_sb[:, :].rearrange("p (c e) -> p c e", c=8),
                rw.rearrange("(c p) e -> p c e", p=128), writes=[b_w])
        sch.dma("gpsimd", "w", rb_sb[:, :], rb, writes=[b_w])

        st2 = [None]

        def stage2(t, p):
            def t_emit(tns, p=p):
                last = None
                for ch in range(8):
                    last = tns.transpose(pt[p][:, ch * 128:(ch + 1) * 128],
                                         xn16[p][:, ch * 128:(ch + 1) * 128],
                                         consts["identb"][:, :])
                return last
            sch.run("tensor", t_emit, reads=[b_xn16[p], cb], writes=[b_pt[p]])
            sch.run("vector",
                    lambda v, p=p: v.tensor_copy(xnf[p][:, 0:512],
                                                 pt[p][:, 0:512]),
                    reads=[b_pt[p]], writes=[b_xnf_lo[p]])
            sch.run("vector",
                    lambda v, p=p: v.tensor_copy(xnf[p][:, 512:1024],
                                                 pt[p][:, 512:1024]),
                    reads=[b_pt[p]], writes=[b_xnf_hi[p]])

            def r_emit(tns, p=p):
                for ch in range(8):
                    tns.matmul(pr[:, :], xnf[p][:, ch * 128:(ch + 1) * 128],
                               rw_sb[:, ch * 8:(ch + 1) * 8],
                               start=(ch == 0), stop=False)
                return tns.matmul(pr[:, :], consts["ones_row"][:, :],
                                  rb_sb[:, :], start=False, stop=True)
            sch.run("tensor", r_emit,
                    reads=[b_xnf_lo[p], b_xnf_hi[p], b_w, cb], writes=[b_pr])
            sch.run("vector", lambda v, p=p: v.tensor_copy(lgt[p][:, :],
                                                           pr[:, :]),
                    reads=[b_pr], writes=[b_lgt[p]])
            emit_top2(nc, sch, lgt[p], b_lgt[p], 8, t2_tmp[p],
                      {"B1": (B12[p], b_B12[p]), "B2": (B12[p], b_B12[p]),
                       "B1ap": bass.AP(B12[p], 0, [[16, 128], [2, 8]]),
                       "B2ap": bass.AP(B12[p], 1, [[16, 128], [2, 8]])}, None)

            def s_emit(tns, p=p, first=(t == 0), last=(t == n_tiles - 1)):
                return tns.matmul(pS[:, :], B12[p][:, :],
                                  consts["ones_col"][:, :],
                                  start=first, stop=last)
            sch.run("tensor", s_emit, reads=[b_B12[p], cb], writes=[b_pS])

        def tile_step(t):
            sl = t % NSLOT
            p = t % NP
            sch.dma("sync", f"x{sl}", xs[sl][:, :],
                    xr[t * 128:(t + 1) * 128, :], writes=[xb[sl]])
            emit_ln_tile(nc, sch, cb, xb[sl], xs[sl], ln_tmp[p],
                         want_f32=None,
                         want_b16=(xn16[p], slice(0, D), b_xn16[p]),
                         consts=consts)
            if st2[0] is not None:
                st2[0]()
            st2[0] = (lambda t=t, p=p: stage2(t, p))

        def fin_step():
            st2[0]()
            sch.run("vector", lambda v: v.tensor_copy(sp16[:, :], pS[:, :]),
                    reads=[b_pS], writes=[b_sp16])
            sch.dma("scalar", "spout", o_sp[:, :], sp16[:, :],
                    reads=[b_sp16], writes=[b_osp])

        self.steps = [(lambda t=t: tile_step(t)) for t in range(n_tiles)]
        self.steps.append(fin_step)

    def __call__(self):
        if self.steps:
            self.steps.pop(0)()

    def flush(self):
        while self.steps:
            self.steps.pop(0)()
        self.ctx.close()


def host_consts_named():
    return {"c_" + k: v for k, v in host_consts().items()}


def host_R_inputs(x_own, L0_ffn_router_w, L0_ffn_router_b):
    d = host_consts_named()
    d["xr"] = to_f32(x_own)
    d["rw"] = to_f32(L0_ffn_router_w)
    d["rb"] = to_f32(L0_ffn_router_b).reshape(1, 8)
    return d



NQT = 5
QT_N = [512, 512, 512, 512, 256]
QT_START = [0, 512, 1024, 1536, 2048]
UKB = [19, 23, 27, 31, 32]


def a0_maskpos(k):
    pos = set()
    for h in (0, 1):
        q0 = 2048 * h - 128 + 512 * k
        d0 = q0 // 128
        for j in range(4):
            kb = d0 + j
            if 0 <= kb < UKB[k]:
                pos.add(kb)
    return sorted(pos)


A0_MASKPOS = [a0_maskpos(k) for k in range(NQT)]
A0_NMASK = max(len(p) for p in A0_MASKPOS)
A0_DEBUG = None


def build_launch_A0():
    nc = bass.Bass("TRN2", target_bir_lowering=False)
    xf = nc.dram_tensor("xf", [SEQ, D], F32, kind="ExternalInput").ap()
    xe = nc.dram_tensor("xe", [T_EXT, D], F32, kind="ExternalInput").ap()
    wq_r = nc.dram_tensor("wq_r", [NHP, 8, 128, 128], F32, kind="ExternalInput").ap()
    wk_r = nc.dram_tensor("wk_r", [NHP, 8, 128, 128], F32, kind="ExternalInput").ap()
    wv_r = nc.dram_tensor("wv_r", [NHP, 8, 128, 128], F32, kind="ExternalInput").ap()
    woT_r = nc.dram_tensor("woT_r", [8, 128, D], F32, kind="ExternalInput").ap()
    dmask = nc.dram_tensor("dmask", [NQT, A0_NMASK, 128, 512], F32,
                           kind="ExternalInput").ap()
    bcol = nc.dram_tensor("bcol", [128, NQT, 32], F32, kind="ExternalInput").ap()
    vslab = nc.dram_tensor("vslab", [128, (SEQ // 128) * 4 * 65], F32,
                           kind="ExternalInput").ap()
    o_s = nc.dram_tensor("o_s", [T_EXT, D], F32, kind="ExternalOutput").ap()
    with ExitStack() as ctx:
        sch = Sched(nc, ctx)
        consts, cb = declare_consts(nc, sch, ctx)
        emit_A0(nc, sch, consts, cb, xf, xe, wq_r, wk_r, wv_r, woT_r, dmask,
                bcol, vslab, o_s, Buf("os"), final_wait=True)
    return nc


def emit_A0(nc, sch, consts, cb, xf, xe, wq_r, wk_r, wv_r, woT_r, dmask,
            bcol, vslab, o_s, b_os, final_wait=False, steal_mk=None):
    dbg = {}
    if A0_DEBUG is not None:
        for kb in A0_DEBUG[3]:
            dbg[kb] = nc.dram_tensor(f"dbg_pt_{kb}", [128, 512], F32,
                                     kind="ExternalOutput").ap()
    global _PFX
    _PFX = "a0_"
    with ExitStack() as ctx:
        e = ctx.enter_context

        qf = e(nc.sbuf_tensor(_PFX + "qf", [128, NHP * T_EXT], BF16))
        b_qf = Buf("qf")
        off = e(nc.sbuf_tensor(_PFX + "off", [128, NHP * T_EXT], BF16))
        b_off = Buf("off")
        bcol_sb = e(nc.sbuf_tensor(_PFX + "bcol_sb", [128, NQT * 32], F32))
        b_bcol = Buf("bcol")
        sch.dma("gpsimd", "bcol", bcol_sb[:, :].rearrange("p (q k) -> p q k", q=NQT),
                bcol, writes=[b_bcol])

        ln_tmp = mk_ln_tmp(nc, ctx, "lnA0")

        with ExitStack() as ph:
            pe = ph.enter_context
            xnf_e = pe(nc.sbuf_tensor(_PFX + "xnf_e", [128, 8 * T_EXT], BF16))
            b_xnfe = Buf("xnf_e")
            ptb = pe(nc.psum_tensor(_PFX + "ptb", [128, 1024], BF16))
            b_ptb = Buf("ptb")
            NSLOT = 3
            xs = [pe(nc.sbuf_tensor(_PFX + f"a1x{i}", [128, D], F32)) for i in range(NSLOT)]
            xb = [Buf(f"a1x{i}") for i in range(NSLOT)]
            xn16 = [pe(nc.sbuf_tensor(_PFX + f"a1xn16_{i}", [128, D], BF16))
                    for i in range(2)]
            b_xn16 = [Buf(f"a1xn16_{i}") for i in range(2)]
            ln2 = [ln_tmp, mk_ln_tmp(nc, ph, "lnA0b")]
            wq_sb = pe(nc.sbuf_tensor(_PFX + "wq_sb", [128, NHP * 8 * 128], BF16))
            b_wq = Buf("wq")
            sch.dma("gpsimd", "wq",
                    wq_sb[:, :].rearrange("p (hp dc c) -> p hp dc c", hp=NHP, dc=8),
                    wq_r.rearrange("hp dc p c -> p hp dc c"), writes=[b_wq])
            pqk = pe(nc.psum_tensor(_PFX + "pqk", [128, 1024], F32))
            b_pqk = [Buf("pqk0"), Buf("pqk1")]
            rst = steal_mk() if steal_mk is not None else None
            _steal = rst if rst is not None else (lambda: None)
            st2 = [None]
            for t in range(NT_EXT):
                _steal()
                _steal()
                _steal()
                sl = t % NSLOT
                p = t % 2
                sch.dma("sync", f"a1x{sl}", xs[sl][:, :],
                        xe[t * 128:(t + 1) * 128, :], writes=[xb[sl]])
                emit_ln_tile(nc, sch, cb, xb[sl], xs[sl], ln2[p],
                             want_f32=None,
                             want_b16=(xn16[p], slice(0, D), b_xn16[p]),
                             consts=consts)
                if st2[0]:
                    st2[0]()
                st2[0] = (lambda t=t, p=p: emit_transpose8(
                    nc, sch, xn16[p], b_xn16[p], ptb, b_ptb, xnf_e,
                    [ch * T_EXT + t * 128 for ch in range(8)],
                    b_xnfe, consts["identb"], cb))
            st2[0]()
            for hp in range(NHP):
                for g in range(NQT):
                    _steal()
                    _steal()
                    n = QT_N[g]
                    st = QT_START[g]
                    bank = (hp * NQT + g) % 2
                    po = pqk[:, bank * 512:bank * 512 + n]

                    def mm(tns, hp=hp, st=st, n=n, po=po):
                        last = None
                        for dc in range(8):
                            last = tns.matmul(
                                po, wq_sb[:, (hp * 8 + dc) * 128:(hp * 8 + dc + 1) * 128],
                                xnf_e[:, dc * T_EXT + st:dc * T_EXT + st + n],
                                start=(dc == 0), stop=(dc == 7))
                        return last
                    sch.run("tensor", mm, reads=[b_wq, b_xnfe, cb],
                            writes=[b_pqk[bank]])
                    sch.run("vector",
                            lambda v, hp=hp, st=st, n=n, po=po:
                            v.tensor_copy(qf[:, hp * T_EXT + st:hp * T_EXT + st + n], po),
                            reads=[b_pqk[bank]], writes=[b_qf])
            if rst is not None:
                rst.flush()

        sch.fence()
        with ExitStack() as ph:
            pe = ph.enter_context
            xnf_f = pe(nc.sbuf_tensor(_PFX + "xnf_f", [128, 8 * SEQ], BF16))
            b_xnff = Buf("xnf_f")
            with ExitStack() as pa3:
                a3e = pa3.enter_context
                ptb = a3e(nc.psum_tensor(_PFX + "ptb2", [128, 1024], BF16))
                b_ptb = Buf("ptb2")
                NSLOT = 3
                xs = [a3e(nc.sbuf_tensor(_PFX + f"a3x{i}", [128, D], F32))
                      for i in range(NSLOT)]
                xb = [Buf(f"a3x{i}") for i in range(NSLOT)]
                xn16 = [a3e(nc.sbuf_tensor(_PFX + f"a3xn16_{i}", [128, D], BF16))
                        for i in range(2)]
                b_xn16 = [Buf(f"a3xn16_{i}") for i in range(2)]
                ln3 = [ln_tmp, mk_ln_tmp(nc, pa3, "lnA0c")]
                st3 = [None]
                for t in range(SEQ // 128):
                    sl = t % NSLOT
                    p = t % 2
                    sch.dma("sync", f"a3x{sl}", xs[sl][:, :],
                            xf[t * 128:(t + 1) * 128, :], writes=[xb[sl]])
                    emit_ln_tile(nc, sch, cb, xb[sl], xs[sl], ln3[p],
                                 want_f32=None,
                                 want_b16=(xn16[p], slice(0, D), b_xn16[p]),
                                 consts=consts)
                    if st3[0]:
                        st3[0]()
                    st3[0] = (lambda t=t, p=p: emit_transpose8(
                        nc, sch, xn16[p], b_xn16[p], ptb, b_ptb, xnf_f,
                        [ch * SEQ + t * 128 for ch in range(8)],
                        b_xnff, consts["identb"], cb))
                st3[0]()
            sch.fence()

            NKT = SEQ // 128
            for hg in range(4):
                sch.fence()
                with ExitStack() as hgc:
                    he = hgc.enter_context
                    wk_sb = he(nc.sbuf_tensor(_PFX + f"wk{hg}", [128, 2 * 8 * 128], BF16))
                    wv_sb = he(nc.sbuf_tensor(_PFX + f"wv{hg}", [128, 2 * 8 * 128], BF16))
                    b_wkv = Buf("wkv")
                    for i, hp in enumerate((2 * hg, 2 * hg + 1)):
                        sch.dma("gpsimd", "wkv",
                                wk_sb[:, i * 1024:(i + 1) * 1024].rearrange(
                                    "p (dc c) -> p dc c", dc=8),
                                wk_r[hp].rearrange("dc p c -> p dc c"),
                                writes=[b_wkv])
                        sch.dma("gpsimd", "wkv",
                                wv_sb[:, i * 1024:(i + 1) * 1024].rearrange(
                                    "p (dc c) -> p dc c", dc=8),
                                wv_r[hp].rearrange("dc p c -> p dc c"),
                                writes=[b_wkv])
                    kf = he(nc.sbuf_tensor(_PFX + f"kf{hg}", [128, 2 * SEQ], BF16))
                    b_kf = Buf("kf")
                    vaug = he(nc.sbuf_tensor(_PFX + f"va{hg}", [128, NKT * 4 * 65], BF16))
                    b_va = Buf("vaug")
                    pp = he(nc.psum_tensor(_PFX + f"pp{hg}", [128, 1024], F32))
                    b_pp = [Buf("pp0"), Buf("pp1")]
                    for i in range(2):
                        for g in range(8):
                            bank = (i * 8 + g) % 2
                            po = pp[:, bank * 512:(bank + 1) * 512]

                            def mm(tns, i=i, g=g, po=po):
                                last = None
                                for dc in range(8):
                                    last = tns.matmul(
                                        po,
                                        wk_sb[:, (i * 8 + dc) * 128:(i * 8 + dc + 1) * 128],
                                        xnf_f[:, dc * SEQ + g * 512:dc * SEQ + (g + 1) * 512],
                                        start=(dc == 0), stop=(dc == 7))
                                return last
                            sch.run("tensor", mm, reads=[b_wkv, b_xnff, cb],
                                    writes=[b_pp[bank]])
                            sch.run("vector",
                                    lambda v, i=i, g=g, po=po:
                                    v.tensor_copy(kf[:, i * SEQ + g * 512:i * SEQ + (g + 1) * 512], po),
                                    reads=[b_pp[bank]], writes=[b_kf])
                    VROW = NKT * 4 * 65
                    sch.dma("gpsimd", "vslab", vaug[:, :], vslab, writes=[b_va])
                    for kt in range(NKT):
                        for i in range(2):
                            bank = (kt * 2 + i) % 2
                            po = pp[:, bank * 512:bank * 512 + 128]

                            def mm(tns, kt=kt, i=i, po=po):
                                last = None
                                for dc in range(8):
                                    last = tns.matmul(
                                        po,
                                        xnf_f[:, dc * SEQ + kt * 128:dc * SEQ + (kt + 1) * 128],
                                        wv_sb[:, (i * 8 + dc) * 128:(i * 8 + dc + 1) * 128],
                                        start=(dc == 0), stop=(dc == 7))
                                return last
                            sch.run("tensor", mm, reads=[b_wkv, b_xnff, cb],
                                    writes=[b_pp[bank]])
                            base = (kt * 4 + i * 2) * 65

                            def ev(v, po=po, base=base):
                                return v.tensor_copy(
                                    bass.AP(vaug, base,
                                            [[VROW, 128], [65, 2], [1, 64]]),
                                    po.rearrange("p (h e) -> p h e", h=2))
                            sch.run("vector", ev, reads=[b_pp[bank]], writes=[b_va])

                    dm_sb = [he(nc.sbuf_tensor(_PFX + f"dm{hg}_0", [128, A0_NMASK * 512], BF16))]
                    b_dm = [Buf("dm0")]
                    pT = [he(nc.sbuf_tensor(_PFX + f"pT{hg}_{i}", [128, 512], BF16))
                          for i in range(8)]
                    b_pT = [Buf(f"pTs{i}") for i in range(4)]

                    recip = [he(nc.sbuf_tensor(_PFX + f"rc{hg}_{i}", [1, 512], F32))
                             for i in range(2)]
                    b_rc = [Buf("rc0"), Buf("rc1")]
                    bc_sb = [he(nc.sbuf_tensor(_PFX + f"bc{hg}_{i}", [64, 512], F32))
                             for i in range(2)]
                    b_bc = [Buf("bc0"), Buf("bc1")]
                    pS = [he(nc.psum_tensor(_PFX + f"pS{hg}_{i}", [128, 1024], F32))
                          for i in range(2)]
                    b_pS = [Buf("pS0"), Buf("pS1")]
                    pO = [he(nc.psum_tensor(_PFX + f"pO{hg}_{i}", [65, 512], F32))
                          for i in range(2)]
                    b_pO = [Buf("pO0"), Buf("pO1")]

                    for k in range(NQT):
                        n = QT_N[k]
                        qst = QT_START[k]
                        dsl = 0
                        sch.dma("gpsimd", f"dm{dsl}",
                                dm_sb[dsl][:, :].rearrange("p (m c) -> p m c",
                                                           m=A0_NMASK),
                                dmask[k].rearrange("m p c -> p m c"),
                                writes=[b_dm[dsl]])
                        mpos = {kb: j for j, kb in enumerate(A0_MASKPOS[k])}
                        for hlp in range(2):
                          ngrp = (UKB[k] + 1) // 2
                          for g in range(ngrp + 1):
                           for stage in range(4):
                            for hl in (2 * hlp, 2 * hlp + 1):
                                h = 4 * hg + hl
                                hp_loc = hl // 2
                                half = hl % 2
                                obank = hl % 2
                                par = hl % 2
                                q_ap = qf[64 * half:64 * half + 64,
                                          (2 * hg + hp_loc) * T_EXT + qst:
                                          (2 * hg + hp_loc) * T_EXT + qst + n]
                                kbs = [kb for kb in (2 * g, 2 * g + 1)
                                       if kb < UKB[k]]
                                pkbs = [kb for kb in (2 * g - 2, 2 * g - 1)
                                        if 0 <= kb < UKB[k]]
                                slot = par
                                pti = (g % 2) * 4 + par * 2
                                ppti = ((g - 1) % 2) * 4 + par * 2
                                if stage == 0 and g < ngrp:
                                    def qk_emit(t, kbs=kbs, slot=slot, q=q_ap,
                                                n=n, half=half, hp_loc=hp_loc):
                                        last = None
                                        for i, kb in enumerate(kbs):
                                            last = t.matmul(
                                                pS[slot][:, i * 512:i * 512 + n],
                                                kf[64 * half:64 * half + 64,
                                                   hp_loc * SEQ + kb * 128:
                                                   hp_loc * SEQ + (kb + 1) * 128],
                                                q, start=True, stop=True)
                                        return last
                                    sch.run("tensor", qk_emit,
                                            reads=[b_kf, b_qf],
                                            writes=[b_pS[slot]])
                                elif stage == 1 and g < ngrp:
                                    if any(kb in mpos for kb in kbs):
                                        def m_emit(v, kbs=kbs, slot=slot, n=n):
                                            last = None
                                            for i, kb in enumerate(kbs):
                                                if kb in mpos:
                                                    ap = pS[slot][:, i * 512:i * 512 + n]
                                                    last = v.tensor_tensor(
                                                        ap, ap,
                                                        dm_sb[dsl][:, mpos[kb] * 512:
                                                                   mpos[kb] * 512 + n],
                                                        ALU.add)
                                            return last
                                        sch.run("vector", m_emit,
                                                reads=[b_dm[dsl]],
                                                writes=[b_pS[slot]])
                                elif stage == 2 and g < ngrp:
                                    def e_emit(sce, kbs=kbs, slot=slot, n=n,
                                               k=k, pti=pti):
                                        last = None
                                        for i, kb in enumerate(kbs):
                                            src = pS[slot][:, i * 512:i * 512 + n]
                                            last = sce.activation(
                                                pT[pti + i][:, 0:n], src,
                                                AF.Exp, scale=0.125,
                                                bias=bcol_sb[:, k * 32 + kb:
                                                             k * 32 + kb + 1])
                                        return last
                                    sch.run("scalar", e_emit,
                                            reads=[b_pS[slot], b_bcol],
                                            writes=[b_pT[(g % 2) * 2 + par]])
                                elif stage == 3:
                                    if g >= 1:
                                        def pv_emit(t, pkbs=pkbs, n=n, k=k,
                                                    hl=hl, ob=obank, ppti=ppti):
                                            last = None
                                            for i, kb in enumerate(pkbs):
                                                last = t.matmul(
                                                    pO[ob][:, 0:n],
                                                    vaug[:, (kb * 4 + hl) * 65:
                                                         (kb * 4 + hl) * 65 + 65],
                                                    pT[ppti + i][:, 0:n],
                                                    start=(kb == 0),
                                                    stop=(kb == UKB[k] - 1))
                                            return last
                                        sch.run("tensor", pv_emit,
                                                reads=[b_va,
                                                       b_pT[((g - 1) % 2) * 2 + par]],
                                                writes=[b_pO[obank]])
                                    if g == ngrp:
                                        sch.run("vector",
                                                lambda v, n=n, ob=obank:
                                                v.reciprocal(recip[ob][:, 0:n],
                                                             pO[ob][64:65, 0:n]),
                                                reads=[b_pO[obank]],
                                                writes=[b_rc[obank]])
                                        sch.run("tensor",
                                                lambda t, n=n, ob=obank:
                                                t.matmul(pS[ob][0:64, 0:n],
                                                         consts["ones_rowf"][:, 0:64],
                                                         recip[ob][:, 0:n],
                                                         start=True, stop=True),
                                                reads=[b_rc[obank]],
                                                writes=[b_pS[obank]])
                                        sch.run("vector",
                                                lambda v, n=n, ob=obank:
                                                v.tensor_copy(bc_sb[ob][:, 0:n],
                                                              pS[ob][0:64, 0:n]),
                                                reads=[b_pS[obank]],
                                                writes=[b_bc[obank]])
                                        ocol = (2 * hg + hp_loc) * T_EXT + qst
                                        sch.run("vector",
                                                lambda v, n=n, ob=obank,
                                                half=half, ocol=ocol:
                                                v.tensor_tensor(
                                                    off[64 * half:64 * half + 64,
                                                        ocol:ocol + n],
                                                    pO[ob][0:64, 0:n],
                                                    bc_sb[ob][0:64, 0:n],
                                                    ALU.mult),
                                                reads=[b_pO[obank], b_bc[obank]],
                                                writes=[b_off])

        sch.fence()
        with ExitStack() as ph:
            pe = ph.enter_context
            wo_sb = pe(nc.sbuf_tensor(_PFX + "wo_sb", [128, 8 * D], BF16))
            b_wo = Buf("wo")
            sch.dma("gpsimd", "wo",
                    wo_sb[:, :].rearrange("p (cc o) -> p cc o", cc=8),
                    woT_r.rearrange("cc p o -> p cc o"), writes=[b_wo])
            ps = pe(nc.psum_tensor(_PFX + "psC", [128, 1024], F32))
            b_ps = [Buf("psC0"), Buf("psC1")]
            s_sb = [pe(nc.sbuf_tensor(_PFX + f"s_sb{i}", [128, 512], F32)) for i in range(2)]
            b_ssb = [Buf("ssb0"), Buf("ssb1")]
            for t in range(NT_EXT):
                for oc in range(2):
                    bank = (t * 2 + oc) % 2
                    po = ps[:, bank * 512:(bank + 1) * 512]

                    def mm(tns, t=t, oc=oc, po=po):
                        last = None
                        for cc in range(8):
                            last = tns.matmul(
                                po, off[:, cc * T_EXT + t * 128:cc * T_EXT + (t + 1) * 128],
                                wo_sb[:, cc * D + oc * 512:cc * D + (oc + 1) * 512],
                                start=(cc == 0), stop=(cc == 7))
                        return last
                    sch.run("tensor", mm, reads=[b_off, b_wo, cb],
                            writes=[b_ps[bank]])
                    sch.run("vector",
                            lambda v, bank=bank, po=po: v.tensor_copy(
                                s_sb[bank][:, :], po),
                            reads=[b_ps[bank]], writes=[b_ssb[bank]])
                    sch.dma("scalar", f"so{bank}",
                            o_s[t * 128:(t + 1) * 128, oc * 512:(oc + 1) * 512],
                            s_sb[bank][:, :], reads=[b_ssb[bank]],
                            writes=[b_os])
            if final_wait:
                for kk in ("so0", "so1"):
                    sch._wait("scalar",
                              (f"dma_{kk}", sch.dsems[kk], sch.dcnt[kk]))
        sch.fence()


def a0_vslab():
    slab = np.zeros((128, (SEQ // 128) * 4 * 65), np.float32)
    slab[:, 64::65] = 1.0
    return slab


def host_A0_inputs(core, x, wq, wk, wv, wo):
    b, h = core // 2, core % 2
    xb = to_f32(x[b])
    own0 = 2048 * h
    xe = np.zeros((T_EXT, D), np.float32)
    lo, hi = own0 - HALO, own0 + T_OWN + HALO
    slo, shi = max(lo, 0), min(hi, SEQ)
    xe[slo - lo:shi - lo] = xb[slo:shi]
    wq_r = to_f32(wq).reshape(8, 2, 64, 8, 128).transpose(0, 3, 4, 1, 2) \
        .reshape(8, 8, 128, 128)
    wk_r = to_f32(wk).reshape(8, 2, 64, 8, 128).transpose(0, 3, 4, 1, 2) \
        .reshape(8, 8, 128, 128)
    wv_r = to_f32(wv).reshape(8, 2, 64, 8, 128).transpose(0, 3, 4, 1, 2) \
        .reshape(8, 8, 128, 128)
    woT_r = to_f32(wo).T.reshape(8, 128, D)
    q0_ext = own0 - HALO
    dm = np.zeros((NQT, A0_NMASK, 128, 512), np.float32)
    bc = np.zeros((NQT, 32, 128), np.float32)
    for k in range(NQT):
        n = QT_N[k]
        q_abs = q0_ext + QT_START[k] + np.arange(n)
        d0 = (q0_ext + QT_START[k]) // 128
        diag = set(range(d0, d0 + 4))
        for j, kb in enumerate(A0_MASKPOS[k]):
            key_abs = kb * 128 + np.arange(128)
            m = np.where((key_abs[:, None] <= q_abs[None, :]) |
                         (q_abs[None, :] < 0), 0.0, -1e9)
            dm[k, j, :, 0:n] = m
        for kb in range(UKB[k]):
            if kb in A0_MASKPOS[k]:
                continue
            if kb > d0 + 3:
                bc[k, kb, :] = -1.25e8
    bc = np.ascontiguousarray(bc.transpose(2, 0, 1))
    return dict(xf=xb, xe=xe, wq_r=np.ascontiguousarray(wq_r),
                wk_r=np.ascontiguousarray(wk_r),
                wv_r=np.ascontiguousarray(wv_r),
                woT_r=np.ascontiguousarray(woT_r), dmask=dm, bcol=bc,
                vslab=a0_vslab(), **host_consts_named())



F0_GRP = [(0, 512), (512, 512), (1024, 512), (1536, 512), (2048, 256)]


def build_launch_F0():
    nc = bass.Bass("TRN2", target_bir_lowering=False)
    xe = nc.dram_tensor("xe", [T_EXT, D], F32, kind="ExternalInput").ap()
    se = nc.dram_tensor("se", [T_EXT, D], F32, kind="ExternalInput").ap()
    spall = nc.dram_tensor("spall", [8, 16], F32, kind="ExternalInput").ap()
    flag = nc.dram_tensor("flag", [8, 2], F32, kind="ExternalInput").ap()
    rw = nc.dram_tensor("rw", [D, 8], F32, kind="ExternalInput").ap()
    rb = nc.dram_tensor("rb", [1, 8], F32, kind="ExternalInput").ap()
    w1_r = nc.dram_tensor("w1_r", [8, 8, 128, 8, 128], F32, kind="ExternalInput").ap()
    w2_r = nc.dram_tensor("w2_r", [8, 8, 128, 2, 512], F32, kind="ExternalInput").ap()
    b1_r = nc.dram_tensor("b1_r", [128, 8, 8], F32, kind="ExternalInput").ap()
    b2_r = nc.dram_tensor("b2_r", [8, D], F32, kind="ExternalInput").ap()
    eye8 = nc.dram_tensor("eye8", [8, 8 * 128], F32, kind="ExternalInput").ap()
    o_x1 = nc.dram_tensor("o_x1", [T_EXT, D], F32, kind="ExternalOutput").ap()
    with ExitStack() as ctx:
        sch = Sched(nc, ctx)
        consts, cb = declare_consts(nc, sch, ctx)
        emit_F0(nc, sch, consts, cb, xe, se, None, spall, flag, rw, rb, w1_r,
                w2_r, b1_r, b2_r, eye8, o_x1, Buf("ox1"), spall_direct=False,
                final_wait=True)
    return nc


def emit_F0(nc, sch, consts, cb, xe, se, b_se, spall, flag, rw, rb, w1_r,
            w2_r, b1_r, b2_r, eye8, o_x1, b_ox1, spall_direct=False,
            b_spall=None, final_wait=False):
    global _PFX
    _PFX = "f0_"
    with ExitStack() as ctx:
        e = ctx.enter_context

        xnf16 = e(nc.sbuf_tensor(_PFX + "xnf16", [128, 8 * T_EXT], BF16))
        b_xnf = Buf("xnf16")
        acc = e(nc.sbuf_tensor(_PFX + "acc", [128, T_EXT * 8], F32))
        b_acc = Buf("acc")
        mask_all = e(nc.sbuf_tensor(_PFX + "mask_all", [128, NT_EXT * 8], F32))
        b_mask = Buf("mask")
        maskT = e(nc.sbuf_tensor(_PFX + "maskT", [8, T_EXT], F32))
        b_maskT = Buf("maskT")
        spall_sb = e(nc.sbuf_tensor(_PFX + "spall_sb", [8, 16], F32))
        flag_sb = e(nc.sbuf_tensor(_PFX + "flag_sb", [8, 2], F32))
        ssum = e(nc.sbuf_tensor(_PFX + "ssum", [8, 2], F32))
        b_ssum = Buf("ssum")
        eye8_sb = e(nc.sbuf_tensor(_PFX + "eye8_sb", [8, 8 * 128], F32))
        b_eye = Buf("eye8")
        rw_sb = e(nc.sbuf_tensor(_PFX + "rw_sb", [128, 64], F32))
        rb_sb = e(nc.sbuf_tensor(_PFX + "rb_sb", [1, 8], F32))
        b_w = Buf("rw")
        b1_sb = e(nc.sbuf_tensor(_PFX + "b1_sb", [128, 64], F32))
        b2_sb = e(nc.sbuf_tensor(_PFX + "b2_sb", [8, D], F32))
        b_b12 = Buf("b12")

        sch.dma("gpsimd", "ssumld", spall_sb[:, 0:2] if spall_direct
                else spall_sb[:, :], spall,
                reads=[b_spall] if b_spall else None, writes=[b_ssum])
        sch.dma("gpsimd", "ssumld", flag_sb[:, :], flag, writes=[b_ssum])
        sch.dma("gpsimd", "eyeld", eye8_sb[:, :], eye8, writes=[b_eye])
        sch.dma("gpsimd", "rwld", rw_sb[:, :].rearrange("p (c e) -> p c e", c=8),
                rw.rearrange("(c p) e -> p c e", p=128), writes=[b_w])
        sch.dma("gpsimd", "rwld", rb_sb[:, :], rb, writes=[b_w])
        sch.dma("gpsimd", "b12ld", b1_sb[:, :].rearrange("p (e h) -> p e h", e=8),
                b1_r, writes=[b_b12])
        sch.dma("gpsimd", "b12ld", b2_sb[:, :], b2_r, writes=[b_b12])

        ln_tmp = mk_ln_tmp(nc, ctx, "lnF0")
        t2_tmp = mk_top2_tmp(nc, ctx, "t2F0", 8)

        with ExitStack() as ph:
            pe = ph.enter_context
            NSLOT = 3
            xs = [pe(nc.sbuf_tensor(_PFX + f"f0x{i}", [128, D], F32)) for i in range(NSLOT)]
            xb = [Buf(f"f0x{i}") for i in range(NSLOT)]
            xn32 = pe(nc.sbuf_tensor(_PFX + "xn32", [128, D], F32))
            b_xn32 = Buf("xn32")
            xn16 = pe(nc.sbuf_tensor(_PFX + "xn16", [128, D], BF16))
            b_xn16 = Buf("xn16")
            xnf32 = pe(nc.sbuf_tensor(_PFX + "xnf32", [128, D], F32))
            b_xnf32 = Buf("xnf32")
            lgt = pe(nc.sbuf_tensor(_PFX + "lgt", [128, 8], F32))
            b_lgt = Buf("lgt")
            B1 = pe(nc.sbuf_tensor(_PFX + "B1", [128, 8], F32))
            B2 = pe(nc.sbuf_tensor(_PFX + "B2", [128, 8], F32))
            b_B1, b_B2 = Buf("B1"), Buf("B2")
            ptf = pe(nc.psum_tensor(_PFX + "ptf", [128, 1024], F32))
            b_ptf = Buf("ptf")
            ptb = pe(nc.psum_tensor(_PFX + "ptb", [128, 1024], BF16))
            b_ptb = Buf("ptb")
            pr = pe(nc.psum_tensor(_PFX + "pr", [128, 8], F32))
            b_pr = Buf("pr")
            pmt = pe(nc.psum_tensor(_PFX + "pmt", [8, 128], F32))
            b_pmt = Buf("pmt")

            ln2 = [ln_tmp, mk_ln_tmp(nc, ph, "lnF0b")]
            xn16b = [xn16, pe(nc.sbuf_tensor(_PFX + "xn16b", [128, D], BF16))]
            b_xn16b = [b_xn16, Buf("xn16b")]
            xn32b = [xn32, pe(nc.sbuf_tensor(_PFX + "xn32b", [128, D], F32))]
            b_xn32b = [b_xn32, Buf("xn32b")]
            st2 = [None]

            def stage2_f0(t, p):
                emit_transpose8(nc, sch, xn16b[p], b_xn16b[p], ptb, b_ptb,
                                xnf16, [ch * T_EXT + t * 128 for ch in range(8)],
                                b_xnf, consts["identb"], cb)
                def t_emit(tns):
                    last = None
                    for ch in range(8):
                        last = tns.transpose(ptf[:, ch * 128:(ch + 1) * 128],
                                             xn32b[p][:, ch * 128:(ch + 1) * 128],
                                             consts["identf"][:, :])
                    return last
                sch.run("tensor", t_emit, reads=[b_xn32b[p], cb], writes=[b_ptf])

                def v_emit(v):
                    last = None
                    for ch in range(8):
                        last = v.tensor_copy(xnf32[:, ch * 128:(ch + 1) * 128],
                                             ptf[:, ch * 128:(ch + 1) * 128])
                    return last
                sch.run("vector", v_emit, reads=[b_ptf], writes=[b_xnf32])

                def r_emit(tns):
                    for ch in range(8):
                        tns.matmul(pr[:, :], xnf32[:, ch * 128:(ch + 1) * 128],
                                   rw_sb[:, ch * 8:(ch + 1) * 8],
                                   start=(ch == 0), stop=False)
                    return tns.matmul(pr[:, :], consts["ones_rowf"][:, :],
                                      rb_sb[:, :], start=False, stop=True)
                sch.run("tensor", r_emit, reads=[b_xnf32, b_w, cb], writes=[b_pr])
                sch.run("vector", lambda v: v.tensor_copy(lgt[:, :], pr[:, :]),
                        reads=[b_pr], writes=[b_lgt])
                emit_top2(nc, sch, lgt, b_lgt, 8, t2_tmp,
                          {"B1": (B1, b_B1), "B2": (B2, b_B2)}, None)
                sch.run("vector",
                        lambda v: v.tensor_add(
                            mask_all[:, t * 8:(t + 1) * 8], B1[:, :], B2[:, :]),
                        reads=[b_B1, b_B2], writes=[b_mask])

            for t in range(NT_EXT):
                sl = t % NSLOT
                p = t % 2
                sch.dma("sync", f"f0x{sl}", xs[sl][:, :],
                        xe[t * 128:(t + 1) * 128, :], writes=[xb[sl]])
                emit_ln_tile(nc, sch, cb, xb[sl], xs[sl], ln2[p],
                             want_f32=(xn32b[p], slice(0, D), b_xn32b[p]),
                             want_b16=(xn16b[p], slice(0, D), b_xn16b[p]),
                             consts=consts)
                if st2[0]:
                    st2[0]()
                st2[0] = (lambda t=t, p=p: stage2_f0(t, p))
            st2[0]()

            if spall_direct:
                sch.run("vector", lambda v: v.tensor_copy(ssum[:, :],
                                                          spall_sb[:, 0:2]),
                        reads=[b_ssum], writes=[b_ssum])
            else:
                def red_emit(v):
                    v.reduce_sum(ssum[:, 0:1],
                                 bass.AP(spall_sb, 0, [[16, 8], [2, 8]]),
                                 axis=AX.X)
                    return v.reduce_sum(ssum[:, 1:2],
                                        bass.AP(spall_sb, 1, [[16, 8], [2, 8]]),
                                        axis=AX.X)
                sch.run("vector", red_emit, reads=[b_ssum], writes=[b_ssum])
            sch.run("vector", lambda v: v.tensor_mul(ssum[:, :], ssum[:, :],
                                                     flag_sb[:, :]),
                    reads=[b_ssum], writes=[b_ssum])
            sch.run("vector", lambda v: v.tensor_add(
                mask_all[0:8, 8:10], mask_all[0:8, 8:10], ssum[:, :]),
                    reads=[b_ssum], writes=[b_mask])
            sch.run("vector", lambda v: v.tensor_scalar_min(
                mask_all[0:8, 8:16], mask_all[0:8, 8:16], 2560.0),
                    reads=[], writes=[b_mask])

            for t in range(NT_EXT):
                sch.run("tensor",
                        lambda tns, t=t: tns.transpose(
                            pmt[:, :], mask_all[:, t * 8:(t + 1) * 8],
                            consts["identf"][:, :]),
                        reads=[b_mask, cb], writes=[b_pmt])
                sch.run("vector",
                        lambda v, t=t: v.tensor_copy(
                            maskT[:, t * 128:(t + 1) * 128], pmt[0:8, :]),
                        reads=[b_pmt], writes=[b_maskT])

        sch.fence()
        with ExitStack() as ph:
            pe = ph.enter_context
            pz = pe(nc.psum_tensor(_PFX + "pz", [128, 1024], F32))
            pz2 = pe(nc.psum_tensor(_PFX + "pz2", [128, 1024], F32))
            b_pz = [Buf(f"pz{i}") for i in range(4)]
            py = pe(nc.psum_tensor(_PFX + "py", [128, 1024], F32))
            b_py = [Buf("py0"), Buf("py1")]
            pBe = pe(nc.psum_tensor(_PFX + "pBe", [128, 512], F32))
            b_pBe = Buf("pBe")
            Be = [pe(nc.sbuf_tensor(_PFX + "Be", [128, 512], F32))] * 2
            b_Be = [Buf("Be")] * 2
            zz = [pe(nc.sbuf_tensor(_PFX + "zz", [128, 512], F32))] * 2
            b_zz = [Buf("zz")] * 2
            hf = pe(nc.sbuf_tensor(_PFX + "hf", [128, 8 * 512], BF16))
            b_hf = Buf("hf")
            w1_sb = [pe(nc.sbuf_tensor(_PFX + f"w1s{i}", [128, 8192], BF16)) for i in range(2)]
            w2_sb = [pe(nc.sbuf_tensor(_PFX + f"w2s{i}", [128, 8192], BF16)) for i in range(2)]
            b_w1 = [Buf("w1s0"), Buf("w1s1")]
            b_w2 = [Buf("w2s0"), Buf("w2s1")]

            for t in range(NT_EXT):
                for dch in range(2):
                    bank = (t * 2 + dch) % 2
                    po = py[:, bank * 512:(bank + 1) * 512]
                    sch.run("tensor",
                            lambda tns, t=t, dch=dch, po=po:
                            tns.matmul(po, maskT[:, t * 128:(t + 1) * 128],
                                       b2_sb[:, dch * 512:(dch + 1) * 512],
                                       start=True, stop=True),
                            reads=[b_maskT, b_b12], writes=[b_py[bank]])
                    sch.run("vector",
                            lambda v, t=t, dch=dch, po=po:
                            v.tensor_copy(
                                acc[:, t * 1024 + dch * 512:t * 1024 + (dch + 1) * 512],
                                po),
                            reads=[b_py[bank]], writes=[b_acc])

            for ei in range(8):
                wsl = ei % 2
                sch.dma("gpsimd", f"w1_{wsl}",
                        w1_sb[wsl][:, :].rearrange("p (dc hc c) -> p dc hc c",
                                                   dc=8, hc=8),
                        w1_r[ei].rearrange("dc d hc c -> d dc hc c"),
                        writes=[b_w1[wsl]])
                sch.dma("gpsimd", f"w2_{wsl}",
                        w2_sb[wsl][:, :].rearrange("p (hc dch o) -> p hc dch o",
                                                   hc=8, dch=2),
                        w2_r[ei].rearrange("hc h dch o -> h hc dch o"),
                        writes=[b_w2[wsl]])
                for gi, (gst, gn) in enumerate(F0_GRP):
                    gp = (ei * len(F0_GRP) + gi) % 2
                    sch.run("tensor",
                            lambda tns, ei=ei, gst=gst, gn=gn:
                            tns.matmul(pBe[:, 0:gn],
                                       eye8_sb[:, ei * 128:(ei + 1) * 128],
                                       maskT[:, gst:gst + gn],
                                       start=True, stop=True),
                            reads=[b_eye, b_maskT], writes=[b_pBe])
                    sch.run("vector",
                            lambda v, gn=gn, gp=gp: v.tensor_copy(
                                Be[gp][:, 0:gn], pBe[:, 0:gn]),
                            reads=[b_pBe], writes=[b_Be[gp]])
                    for hc in range(8):
                        bank = hc % 4
                        zp = hc % 2
                        pzt = pz if bank < 2 else pz2
                        po = pzt[:, (bank % 2) * 512:(bank % 2) * 512 + gn]

                        def zmm(tns, wsl=wsl, hc=hc, gst=gst, gn=gn, po=po):
                            last = None
                            for dc in range(8):
                                last = tns.matmul(
                                    po,
                                    w1_sb[wsl][:, (dc * 8 + hc) * 128:(dc * 8 + hc + 1) * 128],
                                    xnf16[:, dc * T_EXT + gst:dc * T_EXT + gst + gn],
                                    start=(dc == 0), stop=(dc == 7))
                            return last
                        sch.run("tensor", zmm, reads=[b_w1[wsl], b_xnf, cb],
                                writes=[b_pz[bank]])
                        sch.run("vector",
                                lambda v, gn=gn, po=po, gp=gp, zp=zp:
                                v.tensor_tensor(zz[zp][:, 0:gn], po,
                                                Be[gp][:, 0:gn],
                                                ALU.mult),
                                reads=[b_pz[bank], b_Be[gp]], writes=[b_zz[zp]])
                        sch.run("scalar",
                                lambda s, hc=hc, gn=gn, ei=ei, zp=zp:
                                s.activation(hf[:, hc * 512:hc * 512 + gn],
                                             zz[zp][:, 0:gn], AF.Gelu,
                                             bias=b1_sb[:, ei * 8 + hc:ei * 8 + hc + 1],
                                             scale=1.0),
                                reads=[b_zz[zp], b_b12], writes=[b_hf])
                    ntt = (gn + 127) // 128
                    for ttl in range(ntt):
                        t = gi * 4 + ttl
                        for dch in range(2):
                            bank = (ttl * 2 + dch) % 2
                            po = py[:, bank * 512:(bank + 1) * 512]

                            def ymm(tns, wsl=wsl, ttl=ttl, dch=dch, po=po):
                                last = None
                                for hc in range(8):
                                    last = tns.matmul(
                                        po,
                                        hf[:, hc * 512 + ttl * 128:hc * 512 + (ttl + 1) * 128],
                                        w2_sb[wsl][:, (hc * 2 + dch) * 512:(hc * 2 + dch + 1) * 512],
                                        start=(hc == 0), stop=(hc == 7))
                                return last
                            sch.run("tensor", ymm, reads=[b_w2[wsl], b_hf, cb],
                                    writes=[b_py[bank]])
                            sch.run("vector",
                                    lambda v, t=t, ei=ei, dch=dch, po=po:
                                    v.scalar_tensor_tensor(
                                        acc[:, t * 1024 + dch * 512:t * 1024 + (dch + 1) * 512],
                                        po,
                                        mask_all[:, t * 8 + ei:t * 8 + ei + 1],
                                        acc[:, t * 1024 + dch * 512:t * 1024 + (dch + 1) * 512],
                                        ALU.mult, ALU.add),
                                    reads=[b_py[bank], b_mask], writes=[b_acc])

        sch.fence()
        with ExitStack() as ph:
            pe = ph.enter_context
            NSLOT = 2
            xs = [pe(nc.sbuf_tensor(_PFX + f"c4x{i}", [128, D], F32)) for i in range(NSLOT)]
            ss = [pe(nc.sbuf_tensor(_PFX + f"c4s{i}", [128, D], F32)) for i in range(NSLOT)]
            xbs = [Buf(f"c4x{i}") for i in range(NSLOT)]
            sbs = [Buf(f"c4s{i}") for i in range(NSLOT)]
            gsil = pe(nc.sbuf_tensor(_PFX + "gsil", [128, D], F32))
            b_g = Buf("gsil")
            x1t = [pe(nc.sbuf_tensor(_PFX + f"x1t{i}", [128, D], F32)) for i in range(NSLOT)]
            b_x1 = [Buf(f"x1t{i}") for i in range(NSLOT)]
            for t in range(NT_EXT):
                sl = t % NSLOT
                sch.dma("sync", f"c4x{sl}", xs[sl][:, :],
                        xe[t * 128:(t + 1) * 128, :], writes=[xbs[sl]])
                sch.dma("sync", f"c4s{sl}", ss[sl][:, :],
                        se[t * 128:(t + 1) * 128, :],
                        reads=[b_se] if b_se else None, writes=[sbs[sl]])
                sch.run("scalar",
                        lambda s, t=t: s.activation(
                            gsil[:, :], acc[:, t * 1024:(t + 1) * 1024], AF.Silu),
                        reads=[b_acc], writes=[b_g])
                sch.run("vector",
                        lambda v, sl=sl: v.tensor_mul(gsil[:, :], gsil[:, :],
                                                      ss[sl][:, :]),
                        reads=[sbs[sl]], writes=[b_g])
                sch.run("vector",
                        lambda v, sl=sl: v.tensor_add(x1t[sl][:, :], gsil[:, :],
                                                      xs[sl][:, :]),
                        reads=[b_g, xbs[sl]], writes=[b_x1[sl]])
                sch.dma("scalar", f"fo{sl}", o_x1[t * 128:(t + 1) * 128, :],
                        x1t[sl][:, :], reads=[b_x1[sl]], writes=[b_ox1])
            if final_wait:
                for kk in ("fo0", "fo1"):
                    sch._wait("scalar",
                              (f"dma_{kk}", sch.dsems[kk], sch.dcnt[kk]))
        sch.fence()


def host_F0_inputs(core, x, se, spart_all, rw, rb, w1, b1, w2, b2):
    b, h = core // 2, core % 2
    xb = to_f32(x[b])
    own0 = 2048 * h
    xe = np.zeros((T_EXT, D), np.float32)
    lo, hi = own0 - HALO, own0 + T_OWN + HALO
    slo, shi = max(lo, 0), min(hi, SEQ)
    xe[slo - lo:shi - lo] = xb[slo:shi]
    flag = np.full((8, 2), 1.0 if core == 0 else 0.0, np.float32)
    eye = np.zeros((8, 8 * 128), np.float32)
    for ei in range(8):
        eye[ei, ei * 128:(ei + 1) * 128] = 1.0
    w1_r = to_f32(w1).reshape(8, 8, 128, 8, 128)
    w2_r = to_f32(w2).reshape(8, 8, 128, 2, 512)
    b1_r = to_f32(b1).reshape(8, 8, 128).transpose(2, 0, 1)
    spall = to_f32(spart_all).transpose(1, 0, 2).reshape(8, 16)
    return dict(xe=xe, se=to_f32(se), spall=np.ascontiguousarray(spall),
                flag=flag, rw=to_f32(rw), rb=to_f32(rb).reshape(1, 8),
                w1_r=np.ascontiguousarray(w1_r),
                w2_r=np.ascontiguousarray(w2_r),
                b1_r=np.ascontiguousarray(b1_r), b2_r=to_f32(b2),
                eye8=eye, **host_consts_named())



VR1 = NT_EXT * 16 * 65
A1_DEBUG = None


def build_launch_A1():
    nc = bass.Bass("TRN2", target_bir_lowering=False)
    x1e = nc.dram_tensor("x1e", [T_EXT, D], F32, kind="ExternalInput").ap()
    wqkv_r = nc.dram_tensor("wqkv_r", [8, 128, 24, 128], F32, kind="ExternalInput").ap()
    qkvb_c = nc.dram_tensor("qkvb_c", [128, 24], F32, kind="ExternalInput").ap()
    qkvb_row = nc.dram_tensor("qkvb_row", [1, 3072], F32, kind="ExternalInput").ap()
    crope_r = nc.dram_tensor("crope_r", [128, T_EXT], F32, kind="ExternalInput").ap()
    srope_r = nc.dram_tensor("srope_r", [128, T_EXT], F32, kind="ExternalInput").ap()
    p2_r = nc.dram_tensor("p2_r", [128, 128], F32, kind="ExternalInput").ap()
    wmask_r = nc.dram_tensor("wmask_r", [6, 128, 512], F32, kind="ExternalInput").ap()
    vslab1 = nc.dram_tensor("vslab1", [128, VR1], F32, kind="ExternalInput").ap()
    bcol1 = nc.dram_tensor("bcol1", [128, NT_EXT], F32, kind="ExternalInput").ap()
    wo1_r = nc.dram_tensor("wo1_r", [8, 128, D], F32, kind="ExternalInput").ap()
    wob_row = nc.dram_tensor("wob_row", [1, D], F32, kind="ExternalInput").ap()
    o_s1 = nc.dram_tensor("o_s1", [T_OWN, D], F32, kind="ExternalOutput").ap()
    with ExitStack() as ctx:
        sch = Sched(nc, ctx)
        consts, cb = declare_consts(nc, sch, ctx)
        emit_A1(nc, sch, consts, cb, x1e, None, wqkv_r, qkvb_c, qkvb_row,
                crope_r, srope_r, p2_r, wmask_r, vslab1, bcol1, wo1_r,
                wob_row, o_s1, Buf("os1"), final_wait=True)
    return nc


def emit_A1(nc, sch, consts, cb, x1e, b_x1e, wqkv_r, qkvb_c, qkvb_row,
            crope_r, srope_r, p2_r, wmask_r, vslab1, bcol1, wo1_r, wob_row,
            o_s1, b_os1, final_wait=False):
    global A1_DBG_APS
    A1_DBG_APS = {}
    if A1_DEBUG is not None:
        for j in A1_DEBUG[2]:
            A1_DBG_APS[j] = nc.dram_tensor(f"dbg1_pt_{j}", [128, 512], F32,
                                           kind="ExternalOutput").ap()
        A1_DBG_APS["rc"] = nc.dram_tensor("dbg1_rc", [1, 512], F32,
                                          kind="ExternalOutput").ap()
    global _PFX
    _PFX = "a1_"
    with ExitStack() as ctx:
        e = ctx.enter_context

        Q1 = e(nc.sbuf_tensor(_PFX + "Q1", [128, 8 * T_EXT], BF16))
        K1 = e(nc.sbuf_tensor(_PFX + "K1", [128, 8 * T_EXT], BF16))
        b_Q1, b_K1 = Buf("Q1"), Buf("K1")
        vaug1 = e(nc.sbuf_tensor(_PFX + "vaug1", [128, VR1], BF16))
        b_va = Buf("va1")
        b_off = Buf("off1")

        ln_tmp = mk_ln_tmp(nc, ctx, "lnA1")

        with ExitStack() as ph:
            pe = ph.enter_context
            xnf1 = pe(nc.sbuf_tensor(_PFX + "xnf1", [128, 8 * T_EXT], BF16))
            b_xnf = Buf("xnf1")
            with ExitStack() as p1s:
                p1e = p1s.enter_context
                ptb = p1e(nc.psum_tensor(_PFX + "ptb", [128, 1024], BF16))
                b_ptb = Buf("ptb")
                NSLOT = 3
                xs = [p1e(nc.sbuf_tensor(_PFX + f"a1x{i}", [128, D], F32))
                      for i in range(NSLOT)]
                xb = [Buf(f"a1x{i}") for i in range(NSLOT)]
                xn16 = [p1e(nc.sbuf_tensor(_PFX + f"xn16_{i}", [128, D], BF16))
                        for i in range(2)]
                b_xn16 = [Buf(f"xn16_{i}") for i in range(2)]
                ln2 = [ln_tmp, mk_ln_tmp(nc, p1s, "lnA1b")]
                st2 = [None]
                for t in range(NT_EXT):
                    sl = t % NSLOT
                    p = t % 2
                    sch.dma("sync", f"a1x{sl}", xs[sl][:, :],
                            x1e[t * 128:(t + 1) * 128, :],
                            reads=[b_x1e] if b_x1e else None, writes=[xb[sl]])
                    emit_ln_tile(nc, sch, cb, xb[sl], xs[sl], ln2[p],
                                 want_f32=None,
                                 want_b16=(xn16[p], slice(0, D), b_xn16[p]),
                                 consts=consts)
                    if st2[0]:
                        st2[0]()
                    st2[0] = (lambda t=t, p=p: emit_transpose8(
                        nc, sch, xn16[p], b_xn16[p], ptb, b_ptb, xnf1,
                        [ch * T_EXT + t * 128 for ch in range(8)],
                        b_xnf, consts["identb"], cb))
                st2[0]()
            sch.fence()

            with ExitStack() as p2a:
                ae = p2a.enter_context
                wqk_sb = ae(nc.sbuf_tensor(_PFX + "wqk_sb", [128, 16 * 8 * 128], BF16))
                b_wqk = Buf("wqk")
                sch.dma("gpsimd", "wqk",
                        wqk_sb[:, :].rearrange("p (dc cc c) -> p dc cc c",
                                               cc=16, dc=8),
                        wqkv_r.rearrange("dc p cc c -> p dc cc c")[:, :, 0:16, :],
                        writes=[b_wqk])
                qb_sb = ae(nc.sbuf_tensor(_PFX + "qb_sb", [128, 24], F32))
                b_qb = Buf("qb")
                sch.dma("gpsimd", "qb", qb_sb[:, :], qkvb_c, writes=[b_qb])
                crope = ae(nc.sbuf_tensor(_PFX + "crope", [128, T_EXT], BF16))
                srope = ae(nc.sbuf_tensor(_PFX + "srope", [128, T_EXT], BF16))
                b_rope = Buf("rope")
                sch.dma("gpsimd", "rope", crope[:, :], crope_r, writes=[b_rope])
                sch.dma("gpsimd", "rope", srope[:, :], srope_r, writes=[b_rope])
                p2sb = ae(nc.sbuf_tensor(_PFX + "p2sb", [128, 128], BF16))
                b_p2 = Buf("p2")
                sch.dma("gpsimd", "p2", p2sb[:, :], p2_r, writes=[b_p2])

                pqk = ae(nc.psum_tensor(_PFX + "pqk", [128, 1024], F32))
                b_pqk = [Buf("pqk0"), Buf("pqk1")]
                prot = ae(nc.psum_tensor(_PFX + "prot", [128, 512], F32))
                b_prot = Buf("prot")
                qk16 = ae(nc.sbuf_tensor(_PFX + "qk16", [128, 512], BF16))
                b_qk16 = Buf("qk16")
                t1 = ae(nc.sbuf_tensor(_PFX + "t1", [128, 512], F32))
                b_t1 = Buf("t1")
                t2 = ae(nc.sbuf_tensor(_PFX + "t2", [128, 512], F32))
                b_t2 = Buf("t2")
                for cc in range(16):
                    dst, b_dst = (Q1, b_Q1) if cc < 8 else (K1, b_K1)
                    hp = cc % 8
                    for g in range(NQT):
                        n = QT_N[g]
                        st = QT_START[g]
                        bank = (cc * NQT + g) % 2
                        po = pqk[:, bank * 512:bank * 512 + n]

                        def mm(tns, cc=cc, st=st, n=n, po=po):
                            last = None
                            for dc in range(8):
                                last = tns.matmul(
                                    po,
                                    wqk_sb[:, (dc * 16 + cc) * 128:(dc * 16 + cc + 1) * 128],
                                    xnf1[:, dc * T_EXT + st:dc * T_EXT + st + n],
                                    start=(dc == 0), stop=(dc == 7))
                            return last
                        sch.run("tensor", mm, reads=[b_wqk, b_xnf, cb],
                                writes=[b_pqk[bank]])
                        sch.run("vector",
                                lambda v, cc=cc, n=n, po=po:
                                v.tensor_scalar_add(qk16[:, 0:n], po,
                                                    qb_sb[:, cc:cc + 1]),
                                reads=[b_pqk[bank], b_qb], writes=[b_qk16])
                        sch.run("tensor",
                                lambda t, n=n:
                                t.matmul(prot[:, 0:n], p2sb[:, :], qk16[:, 0:n],
                                         start=True, stop=True),
                                reads=[b_p2, b_qk16], writes=[b_prot])
                        sch.run("vector",
                                lambda v, st=st, n=n:
                                v.tensor_mul(t1[:, 0:n], qk16[:, 0:n],
                                             crope[:, st:st + n]),
                                reads=[b_qk16, b_rope], writes=[b_t1])
                        sch.run("vector",
                                lambda v, st=st, n=n:
                                v.tensor_tensor(t2[:, 0:n], prot[:, 0:n],
                                                srope[:, st:st + n], ALU.mult),
                                reads=[b_prot, b_rope], writes=[b_t2])
                        sch.run("vector",
                                lambda v, hp=hp, st=st, n=n, dst=dst:
                                v.tensor_add(dst[:, hp * T_EXT + st:hp * T_EXT + st + n],
                                             t1[:, 0:n], t2[:, 0:n]),
                                reads=[b_t1, b_t2], writes=[b_dst])

            sch.fence()
            with ExitStack() as p2b:
                be = p2b.enter_context
                wv_sb = be(nc.sbuf_tensor(_PFX + "wv1_sb", [128, 8 * 8 * 128], BF16))
                b_wv = Buf("wv1")
                sch.dma("gpsimd", "wv1",
                        wv_sb[:, :].rearrange("p (dc cc c) -> p dc cc c",
                                              cc=8, dc=8),
                        wqkv_r.rearrange("dc p cc c -> p dc cc c")[:, :, 16:24, :],
                        writes=[b_wv])
                vbrow = be(nc.sbuf_tensor(_PFX + "vbrow", [1, 3072], BF16))
                b_vb = Buf("vbrow")
                sch.dma("gpsimd", "vb", vbrow[:, :], qkvb_row, writes=[b_vb])
                sch.dma("gpsimd", "vslab1", vaug1[:, :], vslab1, writes=[b_va])
                ppv = be(nc.psum_tensor(_PFX + "ppv", [128, 1024], F32))
                b_ppv = [Buf("ppv0"), Buf("ppv1")]
                for kt in range(NT_EXT):
                    for hp in range(8):
                        bank = (kt * 8 + hp) % 2
                        po = ppv[:, bank * 512:bank * 512 + 128]

                        def mm(tns, kt=kt, hp=hp, po=po):
                            for dc in range(8):
                                tns.matmul(
                                    po,
                                    xnf1[:, dc * T_EXT + kt * 128:dc * T_EXT + (kt + 1) * 128],
                                    wv_sb[:, (dc * 8 + hp) * 128:(dc * 8 + hp + 1) * 128],
                                    start=(dc == 0), stop=False)
                            return tns.matmul(
                                po, consts["ones_row"][:, :],
                                vbrow[:, (16 + hp) * 128:(17 + hp) * 128],
                                start=False, stop=True)
                        sch.run("tensor", mm, reads=[b_wv, b_xnf, b_vb, cb],
                                writes=[b_ppv[bank]])
                        base = (kt * 16 + hp * 2) * 65

                        def ev(v, po=po, base=base):
                            return v.tensor_copy(
                                bass.AP(vaug1, base,
                                        [[VR1, 128], [65, 2], [1, 64]]),
                                po.rearrange("p (h c) -> p h c", h=2))
                        sch.run("vector", ev, reads=[b_ppv[bank]], writes=[b_va])

        sch.fence()
        with ExitStack() as ph34:
            off1 = ph34.enter_context(
                nc.sbuf_tensor(_PFX + "off1", [128, 8 * T_OWN], BF16))
            run_p34(nc, sch, ctx, ph34, consts, cb, Q1, K1, b_Q1, b_K1,
                    vaug1, b_va, off1, b_off, wmask_r, wo1_r, wob_row, o_s1,
                    bcol1, b_os1, final_wait)
        sch.fence()


def run_p34(nc, sch, ctx, ph34, consts, cb, Q1, K1, b_Q1, b_K1, vaug1, b_va,
            off1, b_off, wmask_r, wo1_r, wob_row, o_s1, bcol1, b_os1,
            final_wait):
        with ExitStack() as ph:
            pe = ph.enter_context
            wm_sb = pe(nc.sbuf_tensor(_PFX + "wm_sb", [128, 6 * 512], F32))
            b_wm = Buf("wm")
            bc1_sb = pe(nc.sbuf_tensor(_PFX + "bc1_sb", [128, NT_EXT], F32))
            b_bc1 = Buf("bc1c")
            sch.dma("gpsimd", "bc1", bc1_sb[:, :], bcol1, writes=[b_bc1])
            sch.dma("gpsimd", "wm",
                    wm_sb[:, :].rearrange("p (j c) -> p j c", j=6),
                    wmask_r.rearrange("j p c -> p j c"), writes=[b_wm])
            pT = [pe(nc.sbuf_tensor(_PFX + f"pT1_{i}", [128, 512], BF16))
                  for i in range(12)]
            b_pT = [Buf("pT6a"), Buf("pT6b")]
            smsk = [pe(nc.sbuf_tensor(_PFX + f"smsk1_{i}", [128, 512], F32))
                    for i in range(12)]
            b_smsk = [Buf("sm6a"), Buf("sm6b")]
            recip = [pe(nc.sbuf_tensor(_PFX + f"rc1_{i}", [1, 512], F32))
                     for i in range(2)]
            b_rc = [Buf("rc0"), Buf("rc1")]
            bc_sb = [pe(nc.sbuf_tensor(_PFX + f"bc1_{i}", [64, 512], F32))
                     for i in range(2)]
            b_bc = [Buf("bc0"), Buf("bc1")]
            pS = [pe(nc.psum_tensor(_PFX + f"pS1p_{i}", [128, 1024], F32))
                  for i in range(3)]
            b_pS = Buf("pS6")
            pO = [pe(nc.psum_tensor(_PFX + f"pO1_{i}", [65, 512], F32)) for i in range(2)]
            b_pO = [Buf("pO0"), Buf("pO1")]

            pending = [None]

            def flush():
                if pending[0] is not None:
                    pending[0]()
                    pending[0] = None

            for k in range(4):
                for h in range(16):
                    hp, half = h // 2, h % 2
                    obank = (k * 16 + h) % 2
                    par = h % 2
                    q_ap = Q1[64 * half:64 * half + 64,
                              hp * T_EXT + 128 + 512 * k:
                              hp * T_EXT + 128 + 512 * k + 512]

                    def qk_emit(t, k=k, hp=hp, half=half, q=q_ap):
                        last = None
                        for j in range(6):
                            kbe = 4 * k + j
                            last = t.matmul(
                                pS[j // 2][:, (j % 2) * 512:(j % 2 + 1) * 512],
                                K1[64 * half:64 * half + 64,
                                   hp * T_EXT + kbe * 128:hp * T_EXT + (kbe + 1) * 128],
                                q, start=True, stop=True)
                        return last
                    sch.run("tensor", qk_emit, reads=[b_K1, b_Q1], writes=[b_pS])

                    def m_emit(v, par=par):
                        last = None
                        for j in range(6):
                            last = v.tensor_tensor(
                                smsk[par * 6 + j][:, :],
                                pS[j // 2][:, (j % 2) * 512:(j % 2 + 1) * 512],
                                wm_sb[:, j * 512:(j + 1) * 512], ALU.add)
                        return last
                    sch.run("vector", m_emit, reads=[b_pS, b_wm],
                            writes=[b_smsk[par]])

                    def e_emit(sce, k=k, par=par):
                        last = None
                        for j in range(6):
                            kbe = 4 * k + j
                            last = sce.activation(
                                pT[par * 6 + j][:, :], smsk[par * 6 + j][:, :],
                                AF.Exp, scale=0.125,
                                bias=bc1_sb[:, kbe:kbe + 1])
                        return last
                    sch.run("scalar", e_emit, reads=[b_smsk[par], b_bc1],
                            writes=[b_pT[par]])

                    def tail(k=k, h=h, hp=hp, half=half, ob=obank, par=par):
                        def pv_emit(t):
                            last = None
                            for j in range(6):
                                kbe = 4 * k + j
                                last = t.matmul(
                                    pO[ob][:, :],
                                    vaug1[:, (kbe * 16 + h) * 65:(kbe * 16 + h) * 65 + 65],
                                    pT[par * 6 + j][:, :], start=(j == 0),
                                    stop=(j == 5))
                            return last
                        sch.run("tensor", pv_emit, reads=[b_va, b_pT[par]],
                                writes=[b_pO[ob]])
                        sch.run("vector",
                                lambda v: v.reciprocal(recip[ob][:, :],
                                                       pO[ob][64:65, :]),
                                reads=[b_pO[ob]], writes=[b_rc[ob]])
                        sch.run("tensor",
                                lambda t: t.matmul(
                                    pS[0][0:64, 0:512],
                                    consts["ones_rowf"][:, 0:64],
                                    recip[ob][:, :], start=True, stop=True),
                                reads=[b_rc[ob]], writes=[b_pS])
                        sch.run("vector",
                                lambda v: v.tensor_copy(bc_sb[ob][:, :],
                                                        pS[0][0:64, 0:512]),
                                reads=[b_pS], writes=[b_bc[ob]])
                        ocol = hp * T_OWN + 512 * k
                        sch.run("vector",
                                lambda v: v.tensor_tensor(
                                    off1[64 * half:64 * half + 64, ocol:ocol + 512],
                                    pO[ob][0:64, :], bc_sb[ob][:, :], ALU.mult),
                                reads=[b_pO[ob], b_bc[ob]], writes=[b_off])
                    flush()
                    pending[0] = tail
            flush()
        sch.fence()
        with ExitStack() as ph:
            pe = ph.enter_context
            wo_sb = pe(nc.sbuf_tensor(_PFX + "wo1_sb", [128, 8 * D], BF16))
            b_wo = Buf("wo1")
            sch.dma("gpsimd", "wo1",
                    wo_sb[:, :].rearrange("p (cc o) -> p cc o", cc=8),
                    wo1_r.rearrange("cc p o -> p cc o"), writes=[b_wo])
            wob_sb = pe(nc.sbuf_tensor(_PFX + "wob_sb", [1, D], BF16))
            b_wob = Buf("wob")
            sch.dma("gpsimd", "wob", wob_sb[:, :], wob_row, writes=[b_wob])
            ps = pe(nc.psum_tensor(_PFX + "psD", [128, 1024], F32))
            b_ps = [Buf("psD0"), Buf("psD1")]
            s_sb = [pe(nc.sbuf_tensor(_PFX + f"s1sb{i}", [128, 512], F32)) for i in range(2)]
            b_ssb = [Buf("s1sb0"), Buf("s1sb1")]
            for t in range(NT_OWN):
                for oc in range(2):
                    bank = (t * 2 + oc) % 2
                    po = ps[:, bank * 512:(bank + 1) * 512]

                    def mm(tns, t=t, oc=oc, po=po):
                        for cc in range(8):
                            tns.matmul(
                                po,
                                off1[:, cc * T_OWN + t * 128:cc * T_OWN + (t + 1) * 128],
                                wo_sb[:, cc * D + oc * 512:cc * D + (oc + 1) * 512],
                                start=(cc == 0), stop=False)
                        return tns.matmul(po, consts["ones_row"][:, :],
                                          wob_sb[:, oc * 512:(oc + 1) * 512],
                                          start=False, stop=True)
                    sch.run("tensor", mm, reads=[b_off, b_wo, b_wob, cb],
                            writes=[b_ps[bank]])
                    sch.run("vector",
                            lambda v, bank=bank, po=po:
                            v.tensor_copy(s_sb[bank][:, :], po),
                            reads=[b_ps[bank]], writes=[b_ssb[bank]])
                    sch.dma("scalar", f"s1o{bank}",
                            o_s1[t * 128:(t + 1) * 128, oc * 512:(oc + 1) * 512],
                            s_sb[bank][:, :], reads=[b_ssb[bank]],
                            writes=[b_os1])
            if final_wait:
                for kk in ("s1o0", "s1o1"):
                    sch._wait("scalar",
                              (f"dma_{kk}", sch.dsems[kk], sch.dcnt[kk]))


def host_A1_inputs(core, x1_full, Wqkv, bqkv, Wo, bo, rope_cache):
    b, h = core // 2, core % 2
    own0 = 2048 * h
    lo, hi = own0 - HALO, own0 + T_OWN + HALO
    slo, shi = max(lo, 0), min(hi, SEQ)
    x1e = np.zeros((T_EXT, D), np.float32)
    x1e[slo - lo:shi - lo] = to_f32(x1_full[b])[slo:shi]
    wqkv_r = to_f32(Wqkv).reshape(8, 128, 24, 128)
    qkvb_c = to_f32(bqkv).reshape(24, 128).T
    qkvb_row = to_f32(bqkv).reshape(1, 3072)
    cache = to_f32(rope_cache)
    pos = (lo + np.arange(T_EXT)) % 256
    pos = np.where(pos < 0, pos + 256, pos)
    cosv = cache[pos][:, :, 0]
    sinv = cache[pos][:, :, 1]
    cos64 = np.repeat(cosv, 2, axis=1)
    sin64 = np.repeat(sinv, 2, axis=1)
    sgn = np.tile(np.array([-1.0, 1.0], np.float32), 32)
    sin64s = sin64 * sgn[None, :]
    crope = np.concatenate([cos64, cos64], axis=1).T
    srope = np.concatenate([sin64s, sin64s], axis=1).T
    p2 = np.zeros((128, 128), np.float32)
    for i in range(64):
        p2[2 * i, 2 * i + 1] = 1.0
        p2[2 * i + 1, 2 * i] = 1.0
    wm = np.zeros((6, 128, 512), np.float32)
    for j in range(6):
        kr = np.arange(128)[:, None]
        ql = np.arange(512)[None, :]
        rel = 128 * (j - 1) + kr - ql
        wm[j] = np.where(np.abs(rel) <= 128, 0.0, -1e9)
    vones = np.zeros(T_EXT, np.float32)
    vones[slo - lo:shi - lo] = 1.0
    vs = np.zeros((128, VR1), np.float32)
    for kt in range(NT_EXT):
        for hh in range(16):
            vs[:, (kt * 16 + hh) * 65 + 64] = vones[kt * 128:(kt + 1) * 128]
    wo1_r = to_f32(Wo).reshape(8, 128, D)
    wob = to_f32(bo).reshape(1, D)
    bc1 = np.zeros((128, NT_EXT), np.float32)
    for kt in range(NT_EXT):
        bc1[:, kt] = np.where(vones[kt * 128:(kt + 1) * 128] > 0, 0.0, -1.25e8)
    return dict(x1e=x1e, bcol1=bc1,
                wqkv_r=wqkv_r, qkvb_c=np.ascontiguousarray(qkvb_c),
                qkvb_row=qkvb_row, crope_r=np.ascontiguousarray(crope),
                srope_r=np.ascontiguousarray(srope), p2_r=p2, wmask_r=wm,
                vslab1=vs, wo1_r=wo1_r, wob_row=wob, **host_consts_named())



F1_GRP = [(0, 512), (512, 512), (1024, 512), (1536, 512)]


def build_launch_F1():
    nc = bass.Bass("TRN2", target_bir_lowering=False)
    x1o = nc.dram_tensor("x1o", [T_OWN, D], F32, kind="ExternalInput").ap()
    s1 = nc.dram_tensor("s1", [T_OWN, D], F32, kind="ExternalInput").ap()
    r1w = nc.dram_tensor("r1w", [D, 9], F32, kind="ExternalInput").ap()
    r1b = nc.dram_tensor("r1b", [1, 9], F32, kind="ExternalInput").ap()
    w1_r = nc.dram_tensor("w1f_r", [8, 8, 128, 32, 128], F32, kind="ExternalInput").ap()
    w2_r = nc.dram_tensor("w2f_r", [8, 32, 128, 2, 512], F32, kind="ExternalInput").ap()
    o_out = nc.dram_tensor("o_out", [T_OWN, D], F32, kind="ExternalOutput").ap()
    with ExitStack() as ctx:
        sch = Sched(nc, ctx)
        consts, cb = declare_consts(nc, sch, ctx)
        emit_F1(nc, sch, consts, cb, x1o, None, s1, None, r1w, r1b, w1_r,
                w2_r, o_out, final_wait=True)
    return nc


F1_SPARSE = True
F1_C = 1024


def emit_F1(*a, **kw):
    if F1_SPARSE:
        return emit_F1_sparse(*a, **kw)
    return emit_F1_dense(*a, **kw)


def emit_F1_sparse(nc, sch, consts, cb, x1o, b_x1o, s1, b_s1, r1w, r1b,
                   w1_r, w2_r, o_out, final_wait=False):
    global _PFX
    _PFX = "f1_"
    C = F1_C
    SLOTS = 8 * C
    slab = nc.dram_tensor("f1_slab", [SLOTS, D], BF16).ap()
    yslab = nc.dram_tensor("f1_yslab", [SLOTS, D], BF16).ap()
    b_slab = Buf("slab")
    b_ysl = [Buf(f"ysl{e}") for e in range(8)]
    with ExitStack() as ctx:
        e_ = ctx.enter_context
        b_acc = Buf("acc")
        gates = e_(nc.sbuf_tensor(_PFX + "gates", [128, NT_OWN * 9], F32))
        b_gates = Buf("gates")
        stats = e_(nc.sbuf_tensor(_PFX + "stats", [128, NT_OWN * 2], F32))
        b_stats = Buf("stats")
        idx_sb = e_(nc.sbuf_tensor(_PFX + "idx", [128, NT_OWN * 9], mybir.dt.uint32))
        b_idx = Buf("idx")
        baseT = e_(nc.sbuf_tensor(_PFX + "baseT", [1, 9], F32))
        b_base = Buf("baseT")
        r1w_sb = e_(nc.sbuf_tensor(_PFX + "r1w_sb", [128, 72], F32))
        r1b_sb = e_(nc.sbuf_tensor(_PFX + "r1b_sb", [1, 9], F32))
        b_rw = Buf("r1w")
        sch.dma("gpsimd", "rwld", r1w_sb[:, :].rearrange("p (c e) -> p c e", c=8),
                r1w.rearrange("(c p) e -> p c e", p=128), writes=[b_rw])
        sch.dma("gpsimd", "rwld", r1b_sb[:, :], r1b, writes=[b_rw])

        ln_tmp = mk_ln_tmp(nc, ctx, "lnF1")
        breg = ctx.enter_context(nc.gpsimd.register("f1bnd"))
        sch.run("gpsimd", lambda g: g.reg_mov(breg, C - 1))
        for zi in range(SLOTS // 128):
            sch.dma("gpsimd", "zf", slab[zi * 128:(zi + 1) * 128, :],
                    consts["zeros_t"][:, :], reads=[cb], writes=[b_slab])
            sch.dma("gpsimd", f"zfy{zi // 8}", yslab[zi * 128:(zi + 1) * 128, :],
                    consts["zeros_t"][:, :], reads=[cb], writes=[b_ysl[zi // 8]])

        with ExitStack() as ph:
            pe = ph.enter_context
            t2_tmp = mk_top2_tmp(nc, ph, "t2F1", 9)
            NSLOT = 3
            xs = [pe(nc.sbuf_tensor(_PFX + f"f1x{i}", [128, D], F32))
                  for i in range(NSLOT)]
            xb = [Buf(f"f1x{i}") for i in range(NSLOT)]
            xn16b = [pe(nc.sbuf_tensor(_PFX + f"xn16_{i}", [128, D], BF16))
                     for i in range(2)]
            b_xn16b = [Buf(f"xn16_{i}") for i in range(2)]
            xn32b = [pe(nc.sbuf_tensor(_PFX + f"xn32_{i}", [128, D], F32))
                     for i in range(2)]
            b_xn32b = [Buf(f"xn32_{i}") for i in range(2)]
            xnf32 = pe(nc.sbuf_tensor(_PFX + "xnf32", [128, D], F32))
            b_xnf32 = Buf("xnf32")
            lgt = pe(nc.sbuf_tensor(_PFX + "lgt", [128, 9], F32))
            b_lgt = Buf("lgt")
            B1 = pe(nc.sbuf_tensor(_PFX + "B1", [128, 9], F32))
            B2 = pe(nc.sbuf_tensor(_PFX + "B2", [128, 9], F32))
            b_B1, b_B2 = Buf("B1"), Buf("B2")
            ind = pe(nc.sbuf_tensor(_PFX + "ind", [128, 9], BF16))
            b_ind = Buf("ind")
            indv = pe(nc.sbuf_tensor(_PFX + "indv", [128, 9], F32))
            b_indv = Buf("indv")
            over = pe(nc.sbuf_tensor(_PFX + "over", [128, 9], F32))
            b_over = Buf("over")
            df = pe(nc.sbuf_tensor(_PFX + "df", [128, 9], F32))
            b_df = Buf("df")
            ptf = pe(nc.psum_tensor(_PFX + "ptf", [128, 1024], F32))
            b_ptf = Buf("ptf")
            pr = pe(nc.psum_tensor(_PFX + "pr", [128, 9], F32))
            b_pr = Buf("pr")
            pPx = pe(nc.psum_tensor(_PFX + "pPx", [128, 9], F32))
            b_pPx = Buf("pPx")
            pcs = pe(nc.psum_tensor(_PFX + "pcs", [1, 9], F32))
            b_pcs = Buf("pcs")

            sch.run("vector", lambda v: v.memzero(baseT[:, :]), writes=[b_base])

            st2 = [None]

            def stage2(t, p):
                def t_emit(tns):
                    last = None
                    for ch in range(8):
                        last = tns.transpose(ptf[:, ch * 128:(ch + 1) * 128],
                                             xn32b[p][:, ch * 128:(ch + 1) * 128],
                                             consts["identf"][:, :])
                    return last
                sch.run("tensor", t_emit, reads=[b_xn32b[p], cb], writes=[b_ptf])

                def v_emit(v):
                    last = None
                    for ch in range(8):
                        last = v.tensor_copy(xnf32[:, ch * 128:(ch + 1) * 128],
                                             ptf[:, ch * 128:(ch + 1) * 128])
                    return last
                sch.run("vector", v_emit, reads=[b_ptf], writes=[b_xnf32])

                def r_emit(tns):
                    for ch in range(8):
                        tns.matmul(pr[:, :], xnf32[:, ch * 128:(ch + 1) * 128],
                                   r1w_sb[:, ch * 9:(ch + 1) * 9],
                                   start=(ch == 0), stop=False)
                    return tns.matmul(pr[:, :], consts["ones_rowf"][:, :],
                                      r1b_sb[:, :], start=False, stop=True)
                sch.run("tensor", r_emit, reads=[b_xnf32, b_rw, cb], writes=[b_pr])
                sch.run("vector", lambda v: v.tensor_copy(lgt[:, :], pr[:, :]),
                        reads=[b_pr], writes=[b_lgt])
                emit_top2(nc, sch, lgt, b_lgt, 9, t2_tmp,
                          {"B1": (B1, b_B1), "B2": (B2, b_B2)}, None,
                          with_z=True)
                sch.run("vector",
                        lambda v: v.tensor_add(
                            gates[:, t * 9:(t + 1) * 9], B1[:, :], B2[:, :]),
                        reads=[b_B1, b_B2], writes=[b_gates])
                sch.run("vector",
                        lambda v: v.tensor_single_scalar(
                            ind[:, :], gates[:, t * 9:(t + 1) * 9], 0.0,
                            ALU.is_gt),
                        reads=[b_gates], writes=[b_ind])
                sch.run("tensor",
                        lambda tns: tns.matmul(pcs[:, :],
                                               consts["ones_colb"][:, :],
                                               ind[:, :], start=True, stop=True),
                        reads=[b_ind, cb], writes=[b_pcs])

                def px_emit(tns):
                    tns.matmul(pPx[:, :], consts["lt_excl"][:, :], ind[:, :],
                               start=True, stop=False)
                    return tns.matmul(pPx[:, :], consts["ones_rowf"][:, :],
                                      baseT[:, :], start=False, stop=True)
                sch.run("tensor", px_emit, reads=[b_ind, b_base, cb],
                        writes=[b_pPx])
                sch.run("vector",
                        lambda v: v.tensor_scalar(indv[:, :], ind[:, :],
                                                  -1.0, 1.0, ALU.mult, ALU.add),
                        reads=[b_ind], writes=[b_indv])
                sch.run("vector",
                        lambda v: v.tensor_single_scalar(
                            over[:, :], pPx[:, :], float(C), ALU.is_ge),
                        reads=[b_pPx], writes=[b_over])
                sch.run("vector",
                        lambda v: v.scalar_tensor_tensor(
                            df[:, :], indv[:, :], 3e9, pPx[:, :],
                            ALU.mult, ALU.add),
                        reads=[b_indv, b_pPx], writes=[b_df])
                sch.run("vector",
                        lambda v: v.scalar_tensor_tensor(
                            df[:, :], over[:, :], 3e9, df[:, :],
                            ALU.mult, ALU.add),
                        reads=[b_over], writes=[b_df])
                sch.run("vector",
                        lambda v: v.tensor_copy(
                            idx_sb[:, t * 9:(t + 1) * 9], df[:, :]),
                        reads=[b_df], writes=[b_idx])
                sch.run("vector",
                        lambda v: v.tensor_tensor(baseT[:, :], baseT[:, :],
                                                  pcs[:, :], ALU.add),
                        reads=[b_pcs], writes=[b_base])
                for ei in range(8):
                    sch.idma("scat",
                             lambda g, t=t, p=p, ei=ei: g.indirect_dma_start(
                                 slab,
                                 bass.IndirectOffsetOnAxis(
                                     idx_sb[:, t * 9 + ei:t * 9 + ei + 1], 0),
                                 xn16b[p][:, :], None,
                                 element_offset=ei * C * D,
                                 bounds_check=breg, oob_is_err=False),
                             reads=[b_idx, b_xn16b[p]], writes=[b_slab])

            for t in range(NT_OWN):
                sl = t % NSLOT
                p = t % 2
                sch.dma("sync", f"f1x{sl}", xs[sl][:, :],
                        x1o[t * 128:(t + 1) * 128, :],
                        reads=[b_x1o] if b_x1o else None, writes=[xb[sl]])
                emit_ln_tile(nc, sch, cb, xb[sl], xs[sl],
                             ln_tmp if p == 0 else mk_f1_ln2(nc, ph),
                             want_f32=(xn32b[p], slice(0, D), b_xn32b[p]),
                             want_b16=(xn16b[p], slice(0, D), b_xn16b[p]),
                             stats_out=(stats, 2 * t, b_stats),
                             consts=consts)
                if st2[0]:
                    st2[0]()
                st2[0] = (lambda t=t, p=p: stage2(t, p))
            st2[0]()

        sch.fence()
        with ExitStack() as ph:
            pe = ph.enter_context
            pz = pe(nc.psum_tensor(_PFX + "pz", [128, 1024], F32))
            b_pz = [Buf("pz0"), Buf("pz1")]
            py = pe(nc.psum_tensor(_PFX + "py", [128, 1024], F32))
            b_py = [Buf("py0"), Buf("py1")]
            ptb = pe(nc.psum_tensor(_PFX + "ptb", [128, 1024], BF16))
            b_ptb = Buf("ptb")
            xsl = [pe(nc.sbuf_tensor(_PFX + f"xsl{i}", [128, D], BF16))
                   for i in range(2)]
            b_xsl = [Buf(f"xsl{i}") for i in range(2)]
            xnfE = [pe(nc.sbuf_tensor(_PFX + f"xnfE{i}", [128, 8 * C], BF16))
                    for i in range(2)]
            b_xnfE = [Buf(f"xnfE{i}") for i in range(2)]
            hf = pe(nc.sbuf_tensor(_PFX + "hf", [128, 8 * C], BF16))
            b_hf = Buf("hf")
            w1_sb = [pe(nc.sbuf_tensor(_PFX + f"w1s{i}", [128, 8192], BF16))
                     for i in range(2)]
            w2_sb = [pe(nc.sbuf_tensor(_PFX + f"w2s{i}", [128, 8192], BF16))
                     for i in range(2)]
            b_w1 = [Buf("w1s0"), Buf("w1s1")]
            b_w2 = [Buf("w2s0"), Buf("w2s1")]
            yacc = pe(nc.sbuf_tensor(_PFX + "yacc", [128, C * 8], BF16))
            b_yacc = Buf("yacc")
            acc = pe(nc.sbuf_tensor(_PFX + "acc", [128, NT_OWN * 1024], F32))
            ygat = [pe(nc.sbuf_tensor(_PFX + f"ygat{i}", [128, D], BF16))
                    for i in range(2)]
            b_ygat = [Buf(f"ygat{i}") for i in range(2)]
            for i in range(2):
                sch.run("vector", lambda v, i=i: v.tensor_copy(
                    ygat[i][:, :], consts["zeros_t"][:, :]),
                        reads=[cb], writes=[b_ygat[i]])

            def load_slab(ei):
                ep = ei % 2
                for st in range(8):
                    ssl = st % 2
                    sch.dma("sync", f"xsl{ssl}", xsl[ssl][:, :],
                            slab[(ei * C + st * 128):(ei * C + (st + 1) * 128), :],
                            reads=[b_slab], writes=[b_xsl[ssl]])

            def trans_slab(ei):
                ep = ei % 2
                for st in range(8):
                    ssl = st % 2
                    emit_transpose8(nc, sch, xsl[ssl], b_xsl[ssl], ptb, b_ptb,
                                    xnfE[ep],
                                    [ch * C + st * 128 for ch in range(8)],
                                    b_xnfE[ep], consts["identb"], cb)

            def gather_acc(ei):
                for t in range(NT_OWN):
                    gsl = t % 2
                    sch.idma(f"ygat{gsl}",
                             lambda g, t=t, ei=ei, gsl=gsl:
                             g.indirect_dma_start(
                                 ygat[gsl][:, :], None, yslab,
                                 bass.IndirectOffsetOnAxis(
                                     idx_sb[:, t * 9 + ei:t * 9 + ei + 1], 0),
                                 element_offset=ei * C * D,
                                 bounds_check=breg, oob_is_err=False),
                             reads=[b_ysl[ei], b_idx], writes=[b_ygat[gsl]])
                    if ei == 0:
                        sch.run("vector",
                                lambda v, t=t, ei=ei, gsl=gsl:
                                v.tensor_single_scalar(
                                    acc[:, t * 1024:(t + 1) * 1024],
                                    ygat[gsl][:, :],
                                    gates[:, t * 9 + ei:t * 9 + ei + 1],
                                    ALU.mult),
                                reads=[b_ygat[gsl], b_gates], writes=[b_acc])
                    else:
                        sch.run("vector",
                                lambda v, t=t, ei=ei, gsl=gsl:
                                v.scalar_tensor_tensor(
                                    acc[:, t * 1024:(t + 1) * 1024],
                                    ygat[gsl][:, :],
                                    gates[:, t * 9 + ei:t * 9 + ei + 1],
                                    acc[:, t * 1024:(t + 1) * 1024],
                                    ALU.mult, ALU.add),
                                reads=[b_ygat[gsl], b_gates], writes=[b_acc])

            def load_w(gidx):
                ei, hs = gidx // 4, gidx % 4
                wsl = gidx % 2
                sch.dma("gpsimd", f"w1_{wsl}",
                        w1_sb[wsl][:, :].rearrange(
                            "p (dc hc c) -> p dc hc c", dc=8, hc=8),
                        w1_r[ei].rearrange(
                            "dc d hc c -> d dc hc c")[:, :, hs * 8:(hs + 1) * 8, :],
                        writes=[b_w1[wsl]])

            def load_w2(gidx):
                ei, hs = gidx // 4, gidx % 4
                wsl = gidx % 2
                sch.dma("gpsimd", f"w2_{wsl}",
                        w2_sb[wsl][:, :].rearrange(
                            "p (hc dch o) -> p hc dch o", hc=8, dch=2),
                        w2_r[ei].rearrange(
                            "hc h dch o -> h hc dch o")[:, hs * 8:(hs + 1) * 8, :, :],
                        writes=[b_w2[wsl]])

            load_slab(0)
            trans_slab(0)
            load_w(0)
            load_w2(0)
            for ei in range(8):
                ep = ei % 2
                for hs in range(4):
                    gidx = ei * 4 + hs
                    wsl = gidx % 2
                    if gidx + 1 < 32:
                        load_w(gidx + 1)
                        load_w2(gidx + 1)
                    if hs == 1 and ei + 1 < 8:
                        load_slab(ei + 1)
                    if hs == 3 and ei + 1 < 8:
                        trans_slab(ei + 1)
                    for hc in range(8):
                        for gi in range(C // 512):
                            bank = (hc * 2 + gi) % 2
                            po = pz[:, bank * 512:(bank + 1) * 512]

                            def zmm(tns, wsl=wsl, hc=hc, gi=gi, po=po, ep=ep):
                                last = None
                                for dc in range(8):
                                    last = tns.matmul(
                                        po,
                                        w1_sb[wsl][:, (dc * 8 + hc) * 128:(dc * 8 + hc + 1) * 128],
                                        xnfE[ep][:, dc * C + gi * 512:dc * C + (gi + 1) * 512],
                                        start=(dc == 0), stop=(dc == 7))
                                return last
                            sch.run("tensor", zmm,
                                    reads=[b_w1[wsl], b_xnfE[ep], cb],
                                    writes=[b_pz[bank]])
                            sch.run("scalar",
                                    lambda s, hc=hc, gi=gi, po=po:
                                    s.activation(
                                        hf[:, hc * C + gi * 512:hc * C + (gi + 1) * 512],
                                        po, AF.Gelu),
                                    reads=[b_pz[bank]], writes=[b_hf])
                    for tt in range(C // 128):
                        for dch in range(2):
                            bank = (tt * 2 + dch) % 2
                            po = py[:, bank * 512:(bank + 1) * 512]

                            def ymm(tns, wsl=wsl, tt=tt, dch=dch, po=po):
                                last = None
                                for hc in range(8):
                                    last = tns.matmul(
                                        po,
                                        hf[:, hc * C + tt * 128:hc * C + (tt + 1) * 128],
                                        w2_sb[wsl][:, (hc * 2 + dch) * 512:(hc * 2 + dch + 1) * 512],
                                        start=(hc == 0), stop=(hc == 7))
                                return last
                            sch.run("tensor", ymm, reads=[b_w2[wsl], b_hf, cb],
                                    writes=[b_py[bank]])
                            ysl_ap = yacc[:, tt * 1024 + dch * 512:
                                          tt * 1024 + (dch + 1) * 512]
                            if hs == 0:
                                sch.run("vector",
                                        lambda v, po=po, a=ysl_ap:
                                        v.tensor_copy(a, po),
                                        reads=[b_py[bank]], writes=[b_yacc])
                            else:
                                sch.run("vector",
                                        lambda v, po=po, a=ysl_ap:
                                        v.tensor_tensor(a, a, po, ALU.add),
                                        reads=[b_py[bank]], writes=[b_yacc])
                            if hs == 3:
                                sch.dma("scalar" if dch == 0 else "sync",
                                        f"yo{bank}",
                                        yslab[ei * C + tt * 128:
                                              ei * C + (tt + 1) * 128,
                                              dch * 512:(dch + 1) * 512],
                                        ysl_ap, reads=[b_yacc],
                                        writes=[b_ysl[ei]])
                gather_acc(ei)

        sch.fence()
        with ExitStack() as ph:
            pe = ph.enter_context
            NSLOT = 2
            xs = [pe(nc.sbuf_tensor(_PFX + f"c3x{i}", [128, D], F32))
                  for i in range(NSLOT)]
            ss = [pe(nc.sbuf_tensor(_PFX + f"c3s{i}", [128, D], F32))
                  for i in range(NSLOT)]
            xbs = [Buf(f"c3x{i}") for i in range(NSLOT)]
            sbs = [Buf(f"c3s{i}") for i in range(NSLOT)]
            xn1t = pe(nc.sbuf_tensor(_PFX + "xn1t", [128, D], BF16))
            b_xn1t = Buf("xn1t")
            gsil = pe(nc.sbuf_tensor(_PFX + "gsil", [128, D], F32))
            b_g = Buf("gsil")
            outt = [pe(nc.sbuf_tensor(_PFX + f"outt{i}", [128, D], F32))
                    for i in range(NSLOT)]
            b_ot = [Buf(f"outt{i}") for i in range(NSLOT)]
            for t in range(NT_OWN):
                sl = t % NSLOT
                sch.dma("sync", f"c3x{sl}", xs[sl][:, :],
                        x1o[t * 128:(t + 1) * 128, :],
                        reads=[b_x1o] if b_x1o else None, writes=[xbs[sl]])
                sch.dma("sync", f"c3s{sl}", ss[sl][:, :],
                        s1[t * 128:(t + 1) * 128, :],
                        reads=[b_s1] if b_s1 else None, writes=[sbs[sl]])

                def nrm(s, t=t, sl=sl):
                    return s.activation(xn1t[:, :], xs[sl][:, :], AF.Identity,
                                        bias=stats[:, 2 * t + 1:2 * t + 2],
                                        scale=stats[:, 2 * t:2 * t + 1])
                sch.run("scalar", nrm, reads=[xbs[sl], b_stats], writes=[b_xn1t])
                sch.run("vector",
                        lambda v, t=t: v.scalar_tensor_tensor(
                            acc[:, t * 1024:(t + 1) * 1024], xn1t[:, :],
                            gates[:, t * 9 + 8:t * 9 + 9],
                            acc[:, t * 1024:(t + 1) * 1024],
                            ALU.mult, ALU.add),
                        reads=[b_xn1t, b_gates], writes=[b_acc])
                sch.run("scalar",
                        lambda s, t=t: s.activation(
                            gsil[:, :], acc[:, t * 1024:(t + 1) * 1024], AF.Silu),
                        reads=[b_acc], writes=[b_g])
                sch.run("vector",
                        lambda v, sl=sl: v.tensor_mul(gsil[:, :], gsil[:, :],
                                                      ss[sl][:, :]),
                        reads=[sbs[sl]], writes=[b_g])
                sch.run("vector",
                        lambda v, sl=sl: v.tensor_add(outt[sl][:, :], gsil[:, :],
                                                      xs[sl][:, :]),
                        reads=[b_g, xbs[sl]], writes=[b_ot[sl]])
                sch.dma("scalar", f"fno{sl}", o_out[t * 128:(t + 1) * 128, :],
                        outt[sl][:, :], reads=[b_ot[sl]])
            if final_wait:
                for kk in ("fno0", "fno1"):
                    sch._wait("scalar",
                              (f"dma_{kk}", sch.dsems[kk], sch.dcnt[kk]))


_f1_ln2_cache = {}


def mk_f1_ln2(nc, ctx):
    key = id(nc)
    if key not in _f1_ln2_cache:
        _f1_ln2_cache[key] = mk_ln_tmp(nc, ctx, "lnF1b")
    return _f1_ln2_cache[key]


def emit_F1_dense(nc, sch, consts, cb, x1o, b_x1o, s1, b_s1, r1w, r1b, w1_r, w2_r,
            o_out, final_wait=False):
    global _PFX
    _PFX = "f1_"
    with ExitStack() as ctx:
        e = ctx.enter_context

        xnf16 = e(nc.sbuf_tensor(_PFX + "xnf16", [128, 8 * T_OWN], BF16))
        b_xnf = Buf("xnf16")
        acc = e(nc.sbuf_tensor(_PFX + "acc", [128, NT_OWN * 1024], F32))
        b_acc = Buf("acc")
        gates = e(nc.sbuf_tensor(_PFX + "gates", [128, NT_OWN * 9], F32))
        b_gates = Buf("gates")
        stats = e(nc.sbuf_tensor(_PFX + "stats", [128, NT_OWN * 2], F32))
        b_stats = Buf("stats")
        r1w_sb = e(nc.sbuf_tensor(_PFX + "r1w_sb", [128, 72], F32))
        r1b_sb = e(nc.sbuf_tensor(_PFX + "r1b_sb", [1, 9], F32))
        b_rw = Buf("r1w")
        sch.dma("gpsimd", "rwld", r1w_sb[:, :].rearrange("p (c e) -> p c e", c=8),
                r1w.rearrange("(c p) e -> p c e", p=128), writes=[b_rw])
        sch.dma("gpsimd", "rwld", r1b_sb[:, :], r1b, writes=[b_rw])

        ln_tmp = mk_ln_tmp(nc, ctx, "lnF1")
        t2_tmp = mk_top2_tmp(nc, ctx, "t2F1", 9)

        with ExitStack() as ph:
            pe = ph.enter_context
            NSLOT = 3
            xs = [pe(nc.sbuf_tensor(_PFX + f"f1x{i}", [128, D], F32)) for i in range(NSLOT)]
            xb = [Buf(f"f1x{i}") for i in range(NSLOT)]
            xn32 = pe(nc.sbuf_tensor(_PFX + "xn32", [128, D], F32))
            b_xn32 = Buf("xn32")
            xn16 = pe(nc.sbuf_tensor(_PFX + "xn16", [128, D], BF16))
            b_xn16 = Buf("xn16")
            xnf32 = pe(nc.sbuf_tensor(_PFX + "xnf32", [128, D], F32))
            b_xnf32 = Buf("xnf32")
            lgt = pe(nc.sbuf_tensor(_PFX + "lgt", [128, 9], F32))
            b_lgt = Buf("lgt")
            B1 = pe(nc.sbuf_tensor(_PFX + "B1", [128, 9], F32))
            B2 = pe(nc.sbuf_tensor(_PFX + "B2", [128, 9], F32))
            b_B1, b_B2 = Buf("B1"), Buf("B2")
            ptf = pe(nc.psum_tensor(_PFX + "ptf", [128, 1024], F32))
            b_ptf = Buf("ptf")
            ptb = pe(nc.psum_tensor(_PFX + "ptb", [128, 1024], BF16))
            b_ptb = Buf("ptb")
            pr = pe(nc.psum_tensor(_PFX + "pr", [128, 9], F32))
            b_pr = Buf("pr")

            ln2 = [ln_tmp, mk_ln_tmp(nc, ph, "lnF1b")]
            xn16b = [xn16, pe(nc.sbuf_tensor(_PFX + "xn16b", [128, D], BF16))]
            b_xn16b = [b_xn16, Buf("xn16b")]
            xn32b = [xn32, pe(nc.sbuf_tensor(_PFX + "xn32b", [128, D], F32))]
            b_xn32b = [b_xn32, Buf("xn32b")]
            st2 = [None]

            def stage2_f1(t, p):
                emit_transpose8(nc, sch, xn16b[p], b_xn16b[p], ptb, b_ptb,
                                xnf16, [ch * T_OWN + t * 128 for ch in range(8)],
                                b_xnf, consts["identb"], cb)
                def t_emit(tns):
                    last = None
                    for ch in range(8):
                        last = tns.transpose(ptf[:, ch * 128:(ch + 1) * 128],
                                             xn32b[p][:, ch * 128:(ch + 1) * 128],
                                             consts["identf"][:, :])
                    return last
                sch.run("tensor", t_emit, reads=[b_xn32b[p], cb], writes=[b_ptf])

                def v_emit(v):
                    last = None
                    for ch in range(8):
                        last = v.tensor_copy(xnf32[:, ch * 128:(ch + 1) * 128],
                                             ptf[:, ch * 128:(ch + 1) * 128])
                    return last
                sch.run("vector", v_emit, reads=[b_ptf], writes=[b_xnf32])

                def r_emit(tns):
                    for ch in range(8):
                        tns.matmul(pr[:, :], xnf32[:, ch * 128:(ch + 1) * 128],
                                   r1w_sb[:, ch * 9:(ch + 1) * 9],
                                   start=(ch == 0), stop=False)
                    return tns.matmul(pr[:, :], consts["ones_rowf"][:, :],
                                      r1b_sb[:, :], start=False, stop=True)
                sch.run("tensor", r_emit, reads=[b_xnf32, b_rw, cb], writes=[b_pr])
                sch.run("vector", lambda v: v.tensor_copy(lgt[:, :], pr[:, :]),
                        reads=[b_pr], writes=[b_lgt])
                emit_top2(nc, sch, lgt, b_lgt, 9, t2_tmp,
                          {"B1": (B1, b_B1), "B2": (B2, b_B2)}, None,
                          with_z=True)
                sch.run("vector",
                        lambda v: v.tensor_add(
                            gates[:, t * 9:(t + 1) * 9], B1[:, :], B2[:, :]),
                        reads=[b_B1, b_B2], writes=[b_gates])

            for t in range(NT_OWN):
                sl = t % NSLOT
                p = t % 2
                sch.dma("sync", f"f1x{sl}", xs[sl][:, :],
                        x1o[t * 128:(t + 1) * 128, :],
                        reads=[b_x1o] if b_x1o else None, writes=[xb[sl]])
                emit_ln_tile(nc, sch, cb, xb[sl], xs[sl], ln2[p],
                             want_f32=(xn32b[p], slice(0, D), b_xn32b[p]),
                             want_b16=(xn16b[p], slice(0, D), b_xn16b[p]),
                             stats_out=(stats, 2 * t, b_stats),
                             consts=consts)
                if st2[0]:
                    st2[0]()
                st2[0] = (lambda t=t, p=p: stage2_f1(t, p))
            st2[0]()

        sch.fence()
        with ExitStack() as ph:
            pe = ph.enter_context
            pz = pe(nc.psum_tensor(_PFX + "pz", [128, 1024], F32))
            b_pz = [Buf("pz0"), Buf("pz1")]
            py = pe(nc.psum_tensor(_PFX + "py", [128, 1024], F32))
            b_py = [Buf("py0"), Buf("py1")]
            hf = pe(nc.sbuf_tensor(_PFX + "hf", [128, 8 * T_OWN], BF16))
            b_hf = Buf("hf")
            w1_sb = [pe(nc.sbuf_tensor(_PFX + f"w1s{i}", [128, 8192], BF16)) for i in range(2)]
            w2_sb = [pe(nc.sbuf_tensor(_PFX + f"w2s{i}", [128, 8192], BF16)) for i in range(2)]
            b_w1 = [Buf("w1s0"), Buf("w1s1")]
            b_w2 = [Buf("w2s0"), Buf("w2s1")]

            first = True
            for ei in range(8):
                for hs in range(4):
                    wsl = (ei * 4 + hs) % 2
                    sch.dma("gpsimd", f"w1_{wsl}",
                            w1_sb[wsl][:, :].rearrange(
                                "p (dc hc c) -> p dc hc c", dc=8, hc=8),
                            w1_r[ei].rearrange(
                                "dc d hc c -> d dc hc c")[:, :, hs * 8:(hs + 1) * 8, :],
                            writes=[b_w1[wsl]])
                    sch.dma("gpsimd", f"w2_{wsl}",
                            w2_sb[wsl][:, :].rearrange(
                                "p (hc dch o) -> p hc dch o", hc=8, dch=2),
                            w2_r[ei].rearrange(
                                "hc h dch o -> h hc dch o")[:, hs * 8:(hs + 1) * 8, :, :],
                            writes=[b_w2[wsl]])
                    for hc in range(8):
                        for gi, (gst, gn) in enumerate(F1_GRP):
                            bank = (hc * 4 + gi) % 2
                            po = pz[:, bank * 512:(bank + 1) * 512]

                            def zmm(tns, wsl=wsl, hc=hc, gst=gst, po=po):
                                last = None
                                for dc in range(8):
                                    last = tns.matmul(
                                        po,
                                        w1_sb[wsl][:, (dc * 8 + hc) * 128:(dc * 8 + hc + 1) * 128],
                                        xnf16[:, dc * T_OWN + gst:dc * T_OWN + gst + 512],
                                        start=(dc == 0), stop=(dc == 7))
                                return last
                            sch.run("tensor", zmm, reads=[b_w1[wsl], b_xnf, cb],
                                    writes=[b_pz[bank]])
                            sch.run("scalar",
                                    lambda s, hc=hc, gst=gst, po=po:
                                    s.activation(
                                        hf[:, hc * T_OWN + gst:hc * T_OWN + gst + 512],
                                        po, AF.Gelu),
                                    reads=[b_pz[bank]], writes=[b_hf])
                    for tt in range(NT_OWN):
                        for dch in range(2):
                            bank = (tt * 2 + dch) % 2
                            po = py[:, bank * 512:(bank + 1) * 512]

                            def ymm(tns, wsl=wsl, tt=tt, dch=dch, po=po):
                                last = None
                                for hc in range(8):
                                    last = tns.matmul(
                                        po,
                                        hf[:, hc * T_OWN + tt * 128:hc * T_OWN + (tt + 1) * 128],
                                        w2_sb[wsl][:, (hc * 2 + dch) * 512:(hc * 2 + dch + 1) * 512],
                                        start=(hc == 0), stop=(hc == 7))
                                return last
                            sch.run("tensor", ymm, reads=[b_w2[wsl], b_hf, cb],
                                    writes=[b_py[bank]])
                            aslice = acc[:, tt * 1024 + dch * 512:tt * 1024 + (dch + 1) * 512]
                            if first:
                                sch.run("vector",
                                        lambda v, tt=tt, ei=ei, po=po, a=aslice:
                                        v.tensor_single_scalar(
                                            a, po,
                                            gates[:, tt * 9 + ei:tt * 9 + ei + 1],
                                            ALU.mult),
                                        reads=[b_py[bank], b_gates], writes=[b_acc])
                            else:
                                sch.run("vector",
                                        lambda v, tt=tt, ei=ei, po=po, a=aslice:
                                        v.scalar_tensor_tensor(
                                            a, po,
                                            gates[:, tt * 9 + ei:tt * 9 + ei + 1],
                                            a, ALU.mult, ALU.add),
                                        reads=[b_py[bank], b_gates], writes=[b_acc])
                    first = False

        sch.fence()
        with ExitStack() as ph:
            pe = ph.enter_context
            NSLOT = 2
            xs = [pe(nc.sbuf_tensor(_PFX + f"c3x{i}", [128, D], F32)) for i in range(NSLOT)]
            ss = [pe(nc.sbuf_tensor(_PFX + f"c3s{i}", [128, D], F32)) for i in range(NSLOT)]
            xbs = [Buf(f"c3x{i}") for i in range(NSLOT)]
            sbs = [Buf(f"c3s{i}") for i in range(NSLOT)]
            xn1t = pe(nc.sbuf_tensor(_PFX + "xn1t", [128, D], BF16))
            b_xn1t = Buf("xn1t")
            gsil = pe(nc.sbuf_tensor(_PFX + "gsil", [128, D], F32))
            b_g = Buf("gsil")
            outt = [pe(nc.sbuf_tensor(_PFX + f"outt{i}", [128, D], F32)) for i in range(NSLOT)]
            b_ot = [Buf(f"outt{i}") for i in range(NSLOT)]
            for t in range(NT_OWN):
                sl = t % NSLOT
                sch.dma("sync", f"c3x{sl}", xs[sl][:, :],
                        x1o[t * 128:(t + 1) * 128, :],
                        reads=[b_x1o] if b_x1o else None, writes=[xbs[sl]])
                sch.dma("sync", f"c3s{sl}", ss[sl][:, :],
                        s1[t * 128:(t + 1) * 128, :],
                        reads=[b_s1] if b_s1 else None, writes=[sbs[sl]])
                def nrm(s, t=t, sl=sl):
                    return s.activation(xn1t[:, :], xs[sl][:, :], AF.Identity,
                                        bias=stats[:, 2 * t + 1:2 * t + 2],
                                        scale=stats[:, 2 * t:2 * t + 1])
                sch.run("scalar", nrm, reads=[xbs[sl], b_stats], writes=[b_xn1t])
                sch.run("vector",
                        lambda v, t=t: v.scalar_tensor_tensor(
                            acc[:, t * 1024:(t + 1) * 1024], xn1t[:, :],
                            gates[:, t * 9 + 8:t * 9 + 9],
                            acc[:, t * 1024:(t + 1) * 1024],
                            ALU.mult, ALU.add),
                        reads=[b_xn1t, b_gates], writes=[b_acc])
                sch.run("scalar",
                        lambda s, t=t: s.activation(
                            gsil[:, :], acc[:, t * 1024:(t + 1) * 1024], AF.Silu),
                        reads=[b_acc], writes=[b_g])
                sch.run("vector",
                        lambda v, sl=sl: v.tensor_mul(gsil[:, :], gsil[:, :],
                                                      ss[sl][:, :]),
                        reads=[sbs[sl]], writes=[b_g])
                sch.run("vector",
                        lambda v, sl=sl: v.tensor_add(outt[sl][:, :], gsil[:, :],
                                                      xs[sl][:, :]),
                        reads=[b_g, xbs[sl]], writes=[b_ot[sl]])
                sch.dma("scalar", f"fno{sl}", o_out[t * 128:(t + 1) * 128, :],
                        outt[sl][:, :], reads=[b_ot[sl]])
            if final_wait:
                for kk in ("fno0", "fno1"):
                    sch._wait("scalar",
                              (f"dma_{kk}", sch.dsems[kk], sch.dcnt[kk]))


def host_F1_inputs(core, x1_full, s1_full, r1w, r1b, w1, w2):
    b, h = core // 2, core % 2
    own0 = 2048 * h
    x1e = to_f32(x1_full[b])[own0:own0 + T_OWN]
    s1e = to_f32(s1_full)
    w1_r = to_f32(w1).reshape(8, 8, 128, 32, 128)
    w2_r = to_f32(w2).reshape(8, 32, 128, 2, 512)
    return dict(x1o=x1e, s1=s1e, r1w=to_f32(r1w), r1b=to_f32(r1b).reshape(1, 9),
                w1f_r=w1_r, w2f_r=w2_r, **host_consts_named())



def build_fused():
    nc = bass.Bass("TRN2", target_bir_lowering=False)
    T = lambda n, sh: nc.dram_tensor(n, sh, F32, kind="ExternalInput").ap()
    x_all = T("x_all", [BATCH * SEQ, D])
    xf = T("xf", [SEQ, D])
    xe = T("xe", [T_EXT, D])
    wq_r = T("wq_r", [NHP, 8, 128, 128])
    wk_r = T("wk_r", [NHP, 8, 128, 128])
    wv_r = T("wv_r", [NHP, 8, 128, 128])
    woT_r = T("woT_r", [8, 128, D])
    dmask = T("dmask", [NQT, A0_NMASK, 128, 512])
    bcol = T("bcol", [128, NQT, 32])
    vslab = T("vslab", [128, (SEQ // 128) * 4 * 65])
    rw = T("rw", [D, 8])
    rb = T("rb", [1, 8])
    flag = T("flag", [8, 2])
    w1_r = T("w1_r", [8, 8, 128, 8, 128])
    w2_r = T("w2_r", [8, 8, 128, 2, 512])
    b1_r = T("b1_r", [128, 8, 8])
    b2_r = T("b2_r", [8, D])
    eye8 = T("eye8", [8, 8 * 128])
    wqkv_r = T("wqkv_r", [8, 128, 24, 128])
    qkvb_c = T("qkvb_c", [128, 24])
    qkvb_row = T("qkvb_row", [1, 3072])
    crope_r = T("crope_r", [128, T_EXT])
    srope_r = T("srope_r", [128, T_EXT])
    p2_r = T("p2_r", [128, 128])
    wmask_r = T("wmask_r", [6, 128, 512])
    vslab1 = T("vslab1", [128, VR1])
    bcol1 = T("bcol1", [128, NT_EXT])
    wo1_r = T("wo1_r", [8, 128, D])
    wob_row = T("wob_row", [1, D])
    r1w = T("r1w", [D, 9])
    r1b = T("r1b", [1, 9])
    w1f_r = T("w1f_r", [8, 8, 128, 32, 128])
    w2f_r = T("w2f_r", [8, 32, 128, 2, 512])
    o_out = nc.dram_tensor("o_out", [T_OWN, D], F32, kind="ExternalOutput").ap()
    sp_sc = nc.dram_tensor("sp_sc", [16, 1], F32).ap()
    s_sc = nc.dram_tensor("s_sc", [T_EXT, D], F32).ap()
    x1_sc = nc.dram_tensor("x1_sc", [T_EXT, D], F32).ap()
    s1_sc = nc.dram_tensor("s1_sc", [T_OWN, D], F32).ap()
    b_sp = Buf("sp_sc")
    b_s = Buf("s_sc")
    b_x1 = Buf("x1_sc")
    b_s1 = Buf("s1_sc")

    with ExitStack() as ctx:
        sch = Sched(nc, ctx)
        consts, cb = declare_consts(nc, sch, ctx)
        emit_A0(nc, sch, consts, cb, xf, xe, wq_r, wk_r, wv_r, woT_r, dmask,
                bcol, vslab, s_sc, b_s,
                steal_mk=lambda: RSteps(nc, sch, consts, cb, x_all, rw, rb,
                                        sp_sc, b_sp, BATCH * SEQ // 128))
        emit_F0(nc, sch, consts, cb, xe, s_sc, b_s,
                sp_sc.rearrange("(e k) one -> e (k one)", k=2), flag, rw, rb,
                w1_r, w2_r, b1_r, b2_r, eye8, x1_sc, b_x1,
                spall_direct=True, b_spall=b_sp)
        emit_A1(nc, sch, consts, cb, x1_sc, b_x1, wqkv_r, qkvb_c, qkvb_row,
                crope_r, srope_r, p2_r, wmask_r, vslab1, bcol1, wo1_r,
                wob_row, s1_sc, b_s1)
        emit_F1(nc, sch, consts, cb, x1_sc[HALO:HALO + T_OWN, :], b_x1,
                s1_sc, b_s1, r1w, r1b, w1f_r, w2f_r, o_out, final_wait=True)
    return nc


def host_fused_inputs(core, inputs):
    x = to_f32(inputs["L0_attn_x"])
    d = host_A0_inputs(core, x, inputs["L0_attn_wq"], inputs["L0_attn_wk"],
                       inputs["L0_attn_wv"], inputs["L0_attn_wo"])
    d["x_all"] = x.reshape(-1, D)
    d["rw"] = to_f32(inputs["L0_ffn_router_w"])
    d["rb"] = to_f32(inputs["L0_ffn_router_b"]).reshape(1, 8)
    d["flag"] = np.full((8, 2), 1.0 if core == 0 else 0.0, np.float32)
    d["w1_r"] = np.ascontiguousarray(to_f32(inputs["L0_ffn_w1"]).reshape(8, 8, 128, 8, 128))
    d["w2_r"] = np.ascontiguousarray(to_f32(inputs["L0_ffn_w2"]).reshape(8, 8, 128, 2, 512))
    d["b1_r"] = np.ascontiguousarray(to_f32(inputs["L0_ffn_b1"]).reshape(8, 8, 128).transpose(2, 0, 1))
    d["b2_r"] = to_f32(inputs["L0_ffn_b2"])
    eye = np.zeros((8, 8 * 128), np.float32)
    for ei in range(8):
        eye[ei, ei * 128:(ei + 1) * 128] = 1.0
    d["eye8"] = eye
    d1 = host_A1_inputs(core, np.broadcast_to(np.float32(0.0), (BATCH, SEQ, D)),
                        inputs["L1_attn_Wqkv_kernel"], inputs["L1_attn_Wqkv_bias"],
                        inputs["L1_attn_Wo_kernel"], inputs["L1_attn_Wo_bias"],
                        inputs["L1_attn_rope_cache"])
    del d1["x1e"]
    d.update(d1)
    d["r1w"] = to_f32(inputs["L1_ffn_router_w"])
    d["r1b"] = to_f32(inputs["L1_ffn_router_b"]).reshape(1, 9)
    d["w1f_r"] = np.ascontiguousarray(to_f32(inputs["L1_ffn_w1"]).reshape(8, 8, 128, 32, 128))
    d["w2f_r"] = np.ascontiguousarray(to_f32(inputs["L1_ffn_w2"]).reshape(8, 32, 128, 2, 512))
    return d



FUSED = True


def kernel(**inputs):
    if FUSED:
        nc = build_fused()
        in_maps = [host_fused_inputs(c, inputs) for c in range(NCORES)]
        res = run_bass_kernel_spmd(nc, in_maps, core_ids=list(range(NCORES)))
        out = np.empty((BATCH, SEQ, D), np.float32)
        for c in range(NCORES):
            b, h = c // 2, c % 2
            out[b, 2048 * h:2048 * h + T_OWN] = res.results[c]["o_out"]
        return out
    return kernel_multilaunch(**inputs)


def kernel_multilaunch(**inputs):
    x = to_f32(inputs["L0_attn_x"])
    flat = x.reshape(-1, D)

    nc_r = build_launch_R(NT_OWN, list(range(NT_OWN)))
    in_maps = [host_R_inputs(flat[c * T_OWN:(c + 1) * T_OWN],
                             inputs["L0_ffn_router_w"],
                             inputs["L0_ffn_router_b"])
               for c in range(NCORES)]
    res = run_bass_kernel_spmd(nc_r, in_maps, core_ids=list(range(NCORES)))
    spart_all = np.stack([res.results[c]["o_sp"] for c in range(NCORES)])

    nc_a0 = build_launch_A0()
    in_maps = [host_A0_inputs(c, x, inputs["L0_attn_wq"], inputs["L0_attn_wk"],
                              inputs["L0_attn_wv"], inputs["L0_attn_wo"])
               for c in range(NCORES)]
    res = run_bass_kernel_spmd(nc_a0, in_maps, core_ids=list(range(NCORES)))
    s_ext = [res.results[c]["o_s"] for c in range(NCORES)]

    nc_f0 = build_launch_F0()
    in_maps = [host_F0_inputs(c, x, s_ext[c], spart_all,
                              inputs["L0_ffn_router_w"],
                              inputs["L0_ffn_router_b"], inputs["L0_ffn_w1"],
                              inputs["L0_ffn_b1"], inputs["L0_ffn_w2"],
                              inputs["L0_ffn_b2"])
               for c in range(NCORES)]
    res = run_bass_kernel_spmd(nc_f0, in_maps, core_ids=list(range(NCORES)))
    x1 = np.empty((BATCH, SEQ, D), np.float32)
    for c in range(NCORES):
        b, h = c // 2, c % 2
        x1[b, 2048 * h:2048 * h + T_OWN] = \
            res.results[c]["o_x1"][HALO:HALO + T_OWN]

    nc_a1 = build_launch_A1()
    in_maps = [host_A1_inputs(c, x1, inputs["L1_attn_Wqkv_kernel"],
                              inputs["L1_attn_Wqkv_bias"],
                              inputs["L1_attn_Wo_kernel"],
                              inputs["L1_attn_Wo_bias"],
                              inputs["L1_attn_rope_cache"])
               for c in range(NCORES)]
    res = run_bass_kernel_spmd(nc_a1, in_maps, core_ids=list(range(NCORES)))
    s1 = [res.results[c]["o_s1"] for c in range(NCORES)]

    nc_f1 = build_launch_F1()
    in_maps = [host_F1_inputs(c, x1, s1[c], inputs["L1_ffn_router_w"],
                              inputs["L1_ffn_router_b"], inputs["L1_ffn_w1"],
                              inputs["L1_ffn_w2"])
               for c in range(NCORES)]
    res = run_bass_kernel_spmd(nc_f1, in_maps, core_ids=list(range(NCORES)))
    out = np.empty((BATCH, SEQ, D), np.float32)
    for c in range(NCORES):
        b, h = c // 2, c % 2
        out[b, 2048 * h:2048 * h + T_OWN] = res.results[c]["o_out"]
    return out
